# version 07
# speedup vs baseline: 1.3011x; 1.0482x over previous
import contextlib
import numpy as np
import concourse.bass as bass
import concourse.mybir as mybir
from concourse.bass_utils import run_bass_kernel_spmd

F32 = mybir.dt.float32
BF16 = mybir.dt.bfloat16
AF = mybir.ActivationFunctionType
ALU = mybir.AluOpType
AX = mybir.AxisListType

T = 4096
D = 1024
NCORES = 8
CH = 512
NCH = T // CH
DT = D // 128
EPS_LN = 1e-5
ENGS = ("sync", "gpsimd", "tensor", "scalar", "vector")


class Ctx:

    def __init__(self, nc, es, block):
        self.nc = nc
        self.es = es
        self.block = block
        self.sems = {}
        self.c = {}
        self.tag = {}
        self.q = {e: [] for e in ENGS}
        for s in ("LD", "LW", "MM", "VE", "SE", "GP", "ZS", "ST", "QS", "KS", "VS", "XN"):
            self.sem(s)

    def sem(self, name):
        if name not in self.sems:
            self.sems[name] = self.es.enter_context(self.nc.semaphore(name))
            self.c[name] = 0
        return self.sems[name]

    def res(self, name, k=1):
        self.c[name] += k
        return self.c[name]

    def t(self, key):
        return self.tag.get(key, 0)

    def on(self, eng, body):
        self.q[eng].append(body)

    def wait(self, e, name, val):
        if val > 0:
            e.wait_ge(self.sems[name], val)

    def fin(self, instr, name, k=1):
        instr.then_inc(self.sems[name], k)

    def sett(self, key, sem, val):
        self.tag[key] = (sem, val)

    def tt(self, key):
        return self.tag.get(key, None)

    def waitp(self, eng, pair):
        if pair:
            self.wait(eng, pair[0], pair[1])

    DATA_SEMS = ("LD", "LW", "MM", "VE", "SE", "GP", "ZS", "ST", "QS", "KS",
                 "VS", "XN")

    def barrier(self):
        if "BAR" not in self.sems:
            self.sem("BAR")
        bar0 = self.c["BAR"]
        dma_counts = [(s, self.c[s]) for s in
                      ("LD", "LW", "ST", "QS", "KS", "VS", "XN", "ZS")
                      if self.c.get(s, 0) > 0]
        noclear = True
        self.res("BAR", 5)
        target = bar0 + 5

        for e in ENGS:
            def b(eng, e=e, dma_counts=dma_counts, bar0=bar0, noclear=noclear):
                if e == "sync":
                    for s, v in dma_counts:
                        self.wait(eng, s, v)
                    eng.nop().then_inc(self.sems["BAR"], 1)
                    if not noclear:
                        eng.wait_ge(self.sems["BAR"], bar0 + 5)
                        for s in self.DATA_SEMS:
                            eng.sem_clear(self.sems[s])
                        eng.nop().then_inc(self.sems["BAR"], 1)
                else:
                    eng.drain().then_inc(self.sems["BAR"], 1)

            self.on(e, b)
        self.flush()
        if not noclear:
            for s in self.DATA_SEMS:
                self.c[s] = 0
        self.tag = {}
        for e in ENGS:
            def b(eng, target=target):
                self.wait(eng, "BAR", target)

            self.on(e, b)

    def flush(self):
        for name in ENGS:
            items = self.q[name]
            if not items:
                continue
            self.q[name] = []
            dec = getattr(self.block, name)

            def emit(eng, items=items):
                for b in items:
                    b(eng)

            dec(emit)




def q_ln_chunk(ctx, bufs, src_f32, dst_bf16, ld_sem, ld_tag, pre_wait=None,
               ready_ve=None, extra_bc=None, extra_stats=None):
    sq = bufs["sq"]
    nsq = bufs.get("nsq", 1)
    bc = bufs["bc"]
    rowm, rowv, rows = bufs["rowm"], bufs["rowv"], bufs["rows"]
    onesb, ones1f, eps1 = bufs["onesb"], bufs["ones1f"], bufs["eps1"]
    ps_stats, ps_bc = bufs["ps_stats"], bufs["ps_bc"]

    if src_f32 is not None:
        cast_tag = ctx.res("VE")

        def b(vector, pre_wait=pre_wait, ld_tag=ld_tag):
            if pre_wait is not None:
                pre_wait(vector)
            ctx.wait(vector, ld_sem, ld_tag)
            ctx.fin(vector.tensor_copy(dst_bf16, src_f32), "VE")

        ctx.on("vector", b)
        ctx.tag["ln_cast"] = cast_tag
    else:
        cast_tag = ready_ve
        ctx.tag["ln_cast"] = ready_ve
    cast_p = cast_tag if isinstance(cast_tag, tuple) else ("VE", cast_tag)

    stats_free = ctx.t("stats_free")
    sq_guard = ctx.t("sq_guard")
    sq_mm = {}
    for a in range(DT):
        se_tag = ctx.res("SE")

        def b(scalar, a=a, sq_guard=sq_guard, cast_p=cast_p):
            if a < nsq:
                ctx.waitp(scalar, cast_p)
                ctx.wait(scalar, "MM", sq_guard)
            else:
                ctx.wait(scalar, "MM", sq_mm[a - nsq])
            ctx.fin(scalar.square(sq[:, a % nsq, :], dst_bf16[:, a, :]), "SE")

        ctx.on("scalar", b)
        sq_mm[a] = ctx.res("MM")

        def b(tensor, a=a, se_tag=se_tag, cast_p=cast_p, stats_free=stats_free,
              extra_stats=extra_stats):
            if a == 0:
                ctx.waitp(tensor, cast_p)
                ctx.wait(tensor, "VE", stats_free)
                if extra_stats is not None:
                    extra_stats(tensor)
            ctx.wait(tensor, "SE", se_tag)
            tensor.matmul(
                ps_stats[:, 0:CH], onesb[:], dst_bf16[:, a, :],
                start=(a == 0), stop=(a == DT - 1),
            )
            ctx.fin(
                tensor.matmul(
                    ps_stats[:, CH : 2 * CH], onesb[:], sq[:, a % nsq, :],
                    start=(a == 0), stop=(a == DT - 1),
                ),
                "MM",
            )

        ctx.on("tensor", b)

    stats_tag = ctx.c["MM"]
    var_tag = ctx.res("VE")
    ctx.tag["stats_free"] = var_tag

    def b(vector, stats_tag=stats_tag):
        ctx.wait(vector, "MM", stats_tag)
        vector.tensor_scalar_mul(rowm[:], ps_stats[:, 0:CH], 1.0 / D)
        vector.tensor_scalar_mul(rowv[:], ps_stats[:, CH : 2 * CH], 1.0 / D)
        vector.drain()
        vector.tensor_tensor(rows[:], rowm[:], rowm[:], op=ALU.mult)
        vector.drain()
        ctx.fin(
            vector.tensor_tensor(rowv[:], rowv[:], rows[:], op=ALU.subtract), "VE"
        )

    ctx.on("vector", b)

    rstd_tag = ctx.res("SE")

    def b(scalar, var_tag=var_tag):
        ctx.wait(scalar, "VE", var_tag)
        scalar.activation(rows[:], rowv[:], AF.Ln, bias=eps1[:])
        scalar.drain()
        ctx.fin(scalar.activation(rows[:], rows[:], AF.Exp, scale=-0.5), "SE")

    ctx.on("scalar", b)

    rows_tag = ctx.res("VE")

    def b(vector, rstd_tag=rstd_tag):
        ctx.wait(vector, "SE", rstd_tag)
        vector.drain()
        ctx.fin(
            vector.scalar_tensor_tensor(
                rowm[:], rowm[:], -1.0, rows[:], op0=ALU.mult, op1=ALU.mult
            ),
            "VE",
        )

    ctx.on("vector", b)

    bc_free = ctx.t("bc_free")
    bc_mm_tag = ctx.res("MM")

    def b(tensor, rows_tag=rows_tag, bc_free=bc_free, extra_bc=extra_bc):
        ctx.wait(tensor, "VE", rows_tag)
        ctx.wait(tensor, "SE", bc_free)
        if extra_bc is not None:
            extra_bc(tensor)
        tensor.matmul(ps_bc[:, 0:CH], ones1f[:], rows[:], start=True, stop=True)
        ctx.fin(
            tensor.matmul(ps_bc[:, CH : 2 * CH], ones1f[:], rowm[:],
                          start=True, stop=True),
            "MM",
        )

    ctx.on("tensor", b)

    bc_tag = ctx.res("SE")
    ctx.tag["bc_free"] = bc_tag

    def b(scalar, bc_mm_tag=bc_mm_tag):
        ctx.wait(scalar, "MM", bc_mm_tag)
        ctx.fin(scalar.copy(bc[:], ps_bc[:]), "SE")

    ctx.on("scalar", b)

    out_tag = ctx.res("VE")

    def b(vector, bc_tag=bc_tag):
        ctx.wait(vector, "SE", bc_tag)
        for a in range(DT):
            vector.tensor_tensor(
                dst_bf16[:, a, :], dst_bf16[:, a, :], bc[:, 0:CH], op=ALU.mult
            )
        vector.drain()
        for a in range(DT):
            mm = vector.tensor_tensor(
                dst_bf16[:, a, :], dst_bf16[:, a, :], bc[:, CH : 2 * CH], op=ALU.add
            )
        ctx.fin(mm, "VE")

    ctx.on("vector", b)
    return out_tag




def tile_x(x2d):
    return np.ascontiguousarray(
        x2d.reshape(NCH, CH, DT, 128).transpose(0, 3, 2, 1)
    )


def untile_y(yt):
    return np.ascontiguousarray(yt.transpose(0, 3, 2, 1).reshape(T, D))


def tile_w(w):
    K, N = w.shape
    return np.ascontiguousarray(w.reshape(K // 128, 128, N).transpose(1, 0, 2))



NE = 8
HD = 2048
HT = HD // 128


def moe_drams(nc, pfx="", io=True):
    A = {}
    if io:
        A["x"] = nc.dram_tensor(pfx + "x", [NCH, 128, DT, CH], F32, kind="ExternalInput").ap()
    A["wg"] = nc.dram_tensor(pfx + "wg", [128, DT, NE], F32, kind="ExternalInput").ap()
    A["bg"] = nc.dram_tensor(pfx + "bg", [NE, 1], F32, kind="ExternalInput").ap()
    A["w1"] = nc.dram_tensor(pfx + "w1", [NE, 128, DT, HD], F32, kind="ExternalInput").ap()
    A["b1t"] = nc.dram_tensor(pfx + "b1t", [128, HT, NE], F32, kind="ExternalInput").ap()
    A["w2"] = nc.dram_tensor(pfx + "w2", [NE, 128, HT, D], F32, kind="ExternalInput").ap()
    A["b2"] = nc.dram_tensor(pfx + "b2", [NE, D], F32, kind="ExternalInput").ap()
    A["idf"] = nc.dram_tensor(pfx + "idf", [8, 8], F32, kind="ExternalInput").ap()
    A["idb"] = nc.dram_tensor(pfx + "idb", [128, 128], BF16, kind="ExternalInput").ap()
    A["eb8"] = nc.dram_tensor(pfx + "eb8", [NE, NE * 128], BF16, kind="ExternalInput").ap()
    if io:
        A["y"] = nc.dram_tensor(pfx + "y", [NCH, 128, DT, CH], F32, kind="ExternalOutput").ap()
    A["xnh"] = nc.dram_tensor(pfx + "xnh", [NCH, 128, DT, CH], BF16, kind="Internal").ap()

    return A


def emit_moe(nc, block, ctx, es, A):
    sb = lambda *a: es.enter_context(nc.sbuf_tensor(*a, allow_name_mangling=True))
    psa = lambda *a: es.enter_context(nc.psum_tensor(*a, allow_name_mangling=True))

    xf = sb("xf", [128, DT, CH], F32)
    xnc = sb("xnc", [128, 2, DT, CH], BF16)
    sq = sb("sqb", [128, 1, CH], BF16)
    bc = sb("bcb", [128, 2 * CH], BF16)
    rowm = sb("rowm", [1, CH], F32)
    rowv = sb("rowv", [1, CH], F32)
    rows = sb("rows", [1, CH], F32)
    onesb = sb("onesb", [128, 1], BF16)
    eps1 = sb("eps1", [1, 1], F32)
    ones1f = sb("ones1f", [1, 128], F32)
    idf_t = sb("idf_t", [8, 8], F32)
    idb_t = sb("idb_t", [128, 128], BF16)
    eb8_t = sb("eb8_t", [NE, NE * 128], BF16)
    wg_t = sb("wg_t", [128, DT, NE], BF16)
    bg_t = sb("bg_t", [NE, 1], F32)
    b1t_t = sb("b1t_t", [128, HT, NE], F32)
    b2_t = sb("b2_t", [NE, D], BF16)
    lg = sb("lg", [NE, CH], F32)
    lgn = sb("lgn", [128, 4 * NE], F32)
    gt1 = sb("gt1", [128, 4 * NE], F32)
    gt2 = sb("gt2", [128, 4 * NE], F32)
    gmax = sb("gmax", [128, 4], F32)
    gsum = sb("gsum", [128, 4], F32)
    gm1p = sb("gm1p", [128, 4], F32)
    gm2 = sb("gm2", [128, 4], F32)
    gws = sb("gws", [128, 4], F32)
    combT = sb("combT", [NE, T], BF16)
    comb_n = sb("comb_n", [128, 4, NE], BF16)
    w1_sb = sb("w1_sb", [128, DT, HD], BF16)
    w2_sb = sb("w2_sb", [128, HT, D], BF16)
    h_bf = sb("h_bf", [128, HT, CH], BF16)
    outb = (
        h_bf[:]
        .rearrange("p a n -> p (a n)")
        .bitcast(F32)
        .rearrange("p (a n) -> p a n", a=DT)
    )
    cbb = sb("cbb", [128, 2, CH], BF16)
    acc = sb("acc", [128, DT, T], BF16)

    ps_stats = psa("ps_stats", [1, 2 * CH], F32)
    ps_bc = psa("ps_bc", [128, 2 * CH], F32)
    psA = psa("psA", [128, CH], F32)
    ps_g = psA[0:NE, 0:CH]
    ps_t = psA[:, 0 : 4 * NE]
    ps_ct = psA[0:NE, 0 : CH // 2].rearrange("p n -> p (n)").bitcast(BF16)
    ps_h = [psa("ps_h0", [128, CH], F32), psa("ps_h1", [128, CH], F32)]
    ps_y = [ps_bc[:, 0:CH], ps_bc[:, CH : 2 * CH]]


    bufs = dict(
        sq=sq, bc=bc, rowm=rowm, rowv=rowv, rows=rows, onesb=onesb,
        ones1f=ones1f, ps_stats=ps_stats, ps_bc=ps_bc, eps1=eps1,
    )

    ctx.res("LD", 64)

    def b(sync):
        sync.dma_start(out=idf_t[:], in_=A["idf"]).then_inc(ctx.sems["LD"], 16)
        sync.dma_start(out=idb_t[:], in_=A["idb"]).then_inc(ctx.sems["LD"], 16)
        sync.dma_start(out=bg_t[:], in_=A["bg"]).then_inc(ctx.sems["LD"], 16)
        sync.dma_start(out=b1t_t[:], in_=A["b1t"]).then_inc(ctx.sems["LD"], 16)

    ctx.on("sync", b)
    ctx.res("LW", 48)

    def b(gpsimd):
        gpsimd.dma_start(out=eb8_t[:], in_=A["eb8"]).then_inc(ctx.sems["LW"], 16)
        gpsimd.dma_start(out=wg_t[:], in_=A["wg"]).then_inc(ctx.sems["LW"], 16)
        gpsimd.dma_start(out=b2_t[:], in_=A["b2"]).then_inc(ctx.sems["LW"], 16)

    ctx.on("gpsimd", b)
    init_ve = ctx.res("VE")

    def b(vector):
        vector.memset(onesb[:], 1.0)
        vector.memset(eps1[:], EPS_LN)
        ctx.fin(vector.memset(ones1f[:], 1.0), "VE")

    ctx.on("vector", b)

    for c in range(NCH):
        xn = xnc[:, c % 2, :, :]
        xf_free = ctx.t("xf_free")
        ld_tag = ctx.res("LD", 16)

        def b(sync, c=c, xf_free=xf_free):
            ctx.wait(sync, "VE", xf_free)
            sync.dma_start(out=xf[:], in_=A["x"][c]).then_inc(ctx.sems["LD"], 16)

        ctx.on("sync", b)

        gm_prev = ctx.t(f"gate_mm_{c-2}")

        def pre(vector, c=c, gm_prev=gm_prev, init_ve=init_ve):
            if c == 0:
                ctx.wait(vector, "VE", init_ve)
            if c >= 2:
                ctx.wait(vector, "XN", 16 * (c - 1))
                ctx.wait(vector, "MM", gm_prev)

        ve_tag = q_ln_chunk(ctx, bufs, xf[:], xn, "LD", ld_tag, pre)
        ctx.tag["xf_free"] = ctx.t("ln_cast")

        ctx.res("XN", 16)

        def b(gpsimd, c=c, xn=xn, ve_tag=ve_tag):
            ctx.wait(gpsimd, "VE", ve_tag)
            gpsimd.dma_start(out=A["xnh"][c], in_=xn).then_inc(ctx.sems["XN"], 16)

        ctx.on("gpsimd", b)

        psA_free = ctx.t("psA_free")
        g_tag = ctx.res("MM")
        ctx.tag[f"gate_mm_{c}"] = g_tag

        def b(tensor, c=c, xn=xn, ve_tag=ve_tag, psA_free=psA_free):
            ctx.wait(tensor, "LW", 32)
            ctx.wait(tensor, "VE", ve_tag)
            ctx.wait(tensor, "SE", psA_free)
            for a in range(DT):
                mm = tensor.matmul(
                    ps_g, wg_t[:, a, :], xn[:, a, :],
                    start=(a == 0), stop=(a == DT - 1),
                )
            ctx.fin(mm, "MM")

        ctx.on("tensor", b)

        lg_tag = ctx.res("SE")

        def b(scalar, g_tag=g_tag):
            ctx.wait(scalar, "MM", g_tag)
            ctx.fin(
                scalar.activation(lg[:], ps_g, AF.Identity, bias=bg_t[:]), "SE"
            )

        ctx.on("scalar", b)

        t_tag = ctx.res("MM")

        def b(tensor, lg_tag=lg_tag):
            ctx.wait(tensor, "SE", lg_tag)
            for i in range(4):
                mm = tensor.transpose(
                    ps_t[:, i * NE : (i + 1) * NE],
                    lg[:, i * 128 : (i + 1) * 128], idf_t[:],
                )
            ctx.fin(mm, "MM")

        ctx.on("tensor", b)

        lgn_tag = ctx.res("SE")

        def b(scalar, t_tag=t_tag):
            ctx.wait(scalar, "MM", t_tag)
            ctx.fin(scalar.copy(lgn[:], ps_t), "SE")

        ctx.on("scalar", b)

        nm_tag = ctx.res("VE")

        def b(vector, lgn_tag=lgn_tag):
            ctx.wait(vector, "SE", lgn_tag)
            for i in range(4):
                mm = vector.reduce_max(
                    gmax[:, i : i + 1], lgn[:, i * NE : (i + 1) * NE],
                    axis=AX.X, negate=True,
                )
            ctx.fin(mm, "VE")

        ctx.on("vector", b)

        e_tag = ctx.res("SE")

        def b(scalar, nm_tag=nm_tag):
            ctx.wait(scalar, "VE", nm_tag)
            for i in range(4):
                mm = scalar.activation(
                    gt1[:, i * NE : (i + 1) * NE],
                    lgn[:, i * NE : (i + 1) * NE],
                    AF.Exp, bias=gmax[:, i : i + 1],
                )
            ctx.fin(mm, "SE")

        ctx.on("scalar", b)

        comb_tag = ctx.res("VE")

        def b(vector, e_tag=e_tag):
            ctx.wait(vector, "SE", e_tag)
            sl = lambda t, i: t[:, i * NE : (i + 1) * NE]
            for i in range(4):
                vector.reduce_sum(gsum[:, i : i + 1], sl(gt1, i), axis=AX.X)
            vector.drain()
            vector.reciprocal(gsum[:], gsum[:])
            vector.drain()
            for i in range(4):
                vector.tensor_scalar_mul(sl(gt1, i), sl(gt1, i), gsum[:, i : i + 1])
            vector.drain()
            for i in range(4):
                vector.reduce_max(gm1p[:, i : i + 1], sl(gt1, i), axis=AX.X)
            vector.drain()
            for i in range(4):
                vector.tensor_scalar(
                    sl(gt2, i), sl(gt1, i), gm1p[:, i : i + 1], None, op0=ALU.is_lt
                )
            vector.drain()
            vector.tensor_tensor(gt2[:], gt1[:], gt2[:], op=ALU.mult)
            vector.drain()
            for i in range(4):
                vector.reduce_max(gm2[:, i : i + 1], sl(gt2, i), axis=AX.X)
            vector.drain()
            for i in range(4):
                vector.tensor_scalar(
                    sl(gt2, i), sl(gt1, i), gm2[:, i : i + 1], None, op0=ALU.is_ge
                )
            vector.tensor_tensor(gws[:], gm1p[:], gm2[:], op=ALU.add)
            vector.drain()
            vector.reciprocal(gws[:], gws[:])
            vector.tensor_tensor(gt1[:], gt1[:], gt2[:], op=ALU.mult)
            vector.drain()
            for i in range(4):
                mm = vector.tensor_scalar_mul(
                    comb_n[:, i, :], sl(gt1, i), gws[:, i : i + 1]
                )
            ctx.fin(mm, "VE")

        ctx.on("vector", b)

        ct_tag = ctx.res("MM")

        def b(tensor, comb_tag=comb_tag):
            ctx.wait(tensor, "VE", comb_tag)
            for i in range(4):
                mm = tensor.transpose(
                    ps_ct[:, i * 128 : (i + 1) * 128], comb_n[:, i, :], idb_t[:]
                )
            ctx.fin(mm, "MM")

        ctx.on("tensor", b)

        psA_free2 = ctx.res("SE")
        ctx.tag["psA_free"] = psA_free2

        def b(scalar, c=c, ct_tag=ct_tag):
            ctx.wait(scalar, "MM", ct_tag)
            ctx.fin(scalar.copy(combT[:, c * CH : (c + 1) * CH], ps_ct), "SE")

        ctx.on("scalar", b)
        ctx.flush()

    stageA_se = ctx.c["SE"]
    stageA_mm = ctx.c["MM"]
    stageA_xn = ctx.c["XN"]

    for e in range(NE):
        w_used = ctx.t("w_used")
        w_tag = ctx.res("LW", 32)

        def b(gpsimd, e=e, w_used=w_used):
            if e > 0:
                ctx.wait(gpsimd, "MM", w_used)
            gpsimd.dma_start(out=w1_sb[:], in_=A["w1"][e]).then_inc(
                ctx.sems["LW"], 16
            )
            gpsimd.dma_start(out=w2_sb[:], in_=A["w2"][e]).then_inc(
                ctx.sems["LW"], 16
            )

        ctx.on("gpsimd", b)

        for c in range(NCH):
            it = e * NCH + c
            xn = xnc[:, it % 2, :, :]
            cbuf = it % 2

            xn_used = ctx.t(f"xn_used_{it-2}")
            xn_tag = ctx.res("LD", 16)

            def b(sync, e=e, c=c, it=it, xn=xn, xn_used=xn_used):
                ctx.wait(sync, "XN", 16 * (c + 1))
                if it < 2:
                    ctx.wait(sync, "XN", stageA_xn)
                    ctx.wait(sync, "MM", stageA_mm)
                else:
                    ctx.wait(sync, "MM", xn_used)
                sync.dma_start(out=xn, in_=A["xnh"][c]).then_inc(ctx.sems["LD"], 16)

            ctx.on("sync", b)

            y_evac0 = ctx.t("y_evac_0")
            cbm_tag = ctx.res("MM")

            def b(tensor, e=e, c=c, it=it, w_tag=w_tag, y_evac0=y_evac0):
                if c == 0:
                    ctx.wait(tensor, "LW", w_tag)
                if it == 0:
                    ctx.wait(tensor, "SE", stageA_se)
                ctx.wait(tensor, "VE", y_evac0)
                ctx.fin(
                    tensor.matmul(
                        ps_y[0], eb8_t[:, e * 128 : (e + 1) * 128],
                        combT[:, c * CH : (c + 1) * CH],
                        start=True, stop=True,
                    ),
                    "MM",
                )

            ctx.on("tensor", b)

            cb_used = ctx.t(f"cb_used_{it-2}")
            cb_tag = ctx.res("SE")

            def b(scalar, cbuf=cbuf, cbm_tag=cbm_tag, cb_used=cb_used):
                ctx.wait(scalar, "MM", cbm_tag)
                ctx.wait(scalar, "GP", cb_used)
                ctx.fin(scalar.copy(cbb[:, cbuf, :], ps_y[0]), "SE")

            ctx.on("scalar", b)

            for m in range(HT):
                bank = m % 2
                h_evac = ctx.t(f"h_evac_{bank}")
                g_tag2 = ctx.res("MM")

                def b(tensor, m=m, bank=bank, xn=xn, xn_tag=xn_tag,
                      h_evac=h_evac):
                    if m == 0:
                        ctx.wait(tensor, "LD", xn_tag)
                    ctx.wait(tensor, "VE", h_evac)
                    for a in range(DT):
                        mm = tensor.matmul(
                            ps_h[bank][:],
                            w1_sb[:, a, m * 128 : (m + 1) * 128],
                            xn[:, a, :], start=(a == 0), stop=(a == DT - 1),
                        )
                    ctx.fin(mm, "MM")

                ctx.on("tensor", b)

                hm_used = ctx.t(f"hm_used_{it-1}")
                st_used = ctx.c["ST"]
                i = ctx.res("VE")
                ctx.tag[f"h_evac_{bank}"] = i

                def b(vector, e=e, c=c, m=m, bank=bank, it=it, g_tag2=g_tag2,
                      hm_used=hm_used, st_used=st_used):
                    ctx.wait(vector, "MM", g_tag2)
                    if m == 0:
                        ctx.wait(vector, "GP", hm_used)
                        if e == NE - 1 and c > 0:
                            ctx.wait(vector, "ST", st_used)
                    ctx.fin(
                        vector.tensor_scalar(
                            h_bf[:, m, :], ps_h[bank][:],
                            b1t_t[:, m, e : e + 1], 0.0,
                            op0=ALU.add, op1=ALU.max,
                        ),
                        "VE",
                    )

                ctx.on("vector", b)

            h_done = ctx.c["VE"]
            hm_tag = ctx.res("GP")
            ctx.tag[f"cb_used_{it}"] = hm_tag
            ctx.tag[f"hm_used_{it}"] = hm_tag

            def b(gpsimd, cbuf=cbuf, h_done=h_done, cb_tag=cb_tag):
                ctx.wait(gpsimd, "SE", cb_tag)
                ctx.wait(gpsimd, "VE", h_done)
                for m in range(HT):
                    mm = gpsimd.tensor_mul(
                        h_bf[:, m, :], h_bf[:, m, :], cbb[:, cbuf, :]
                    )
                ctx.fin(mm, "GP")

            ctx.on("gpsimd", b)

            for o in range(DT):
                bank = o % 2
                y_evac = ctx.t(f"y_evac_{bank}")
                y_tag = ctx.res("MM")

                def b(tensor, e=e, c=c, o=o, bank=bank, hm_tag=hm_tag,
                      cb_tag=cb_tag, y_evac=y_evac):
                    if o == 0:
                        ctx.wait(tensor, "GP", hm_tag)
                        ctx.wait(tensor, "SE", cb_tag)
                    ctx.wait(tensor, "VE", y_evac)
                    for hh in range(HT):
                        mm = tensor.matmul(
                            ps_y[bank],
                            w2_sb[:, hh, o * 128 : (o + 1) * 128],
                            h_bf[:, hh, :],
                            start=(hh == 0), stop=(hh == HT - 1 and e != 0),
                        )
                    if e == 0:
                        mm = tensor.matmul(
                            ps_y[bank], b2_t[:, o * 128 : (o + 1) * 128],
                            combT[:, c * CH : (c + 1) * CH],
                            start=False, stop=True,
                        )
                    ctx.fin(mm, "MM")

                ctx.on("tensor", b)

                i = ctx.res("VE")
                ctx.tag[f"y_evac_{bank}"] = i

                def b(vector, e=e, c=c, o=o, bank=bank, y_tag=y_tag):
                    ctx.wait(vector, "MM", y_tag)
                    at = acc[:, o, c * CH : (c + 1) * CH]
                    if e == 0:
                        ctx.fin(vector.tensor_copy(at, ps_y[bank]), "VE")
                    else:
                        ctx.fin(
                            vector.tensor_tensor(at, at, ps_y[bank], op=ALU.add),
                            "VE",
                        )

                ctx.on("vector", b)

            ctx.tag[f"xn_used_{it}"] = ctx.c["MM"]

            if e == NE - 1:
                guard = ctx.t("xf_free") if c == 0 else ctx.t("out_done")
                xr_tag = ctx.res("LD", 16)

                def b(sync, c=c, guard=guard):
                    ctx.wait(sync, "VE", guard)
                    sync.dma_start(out=xf[:], in_=A["x"][c]).then_inc(
                        ctx.sems["LD"], 16
                    )

                ctx.on("sync", b)

                out_tag = ctx.res("VE")
                ctx.tag["out_done"] = out_tag

                def b(vector, c=c, xr_tag=xr_tag):
                    ctx.wait(vector, "LD", xr_tag)
                    vector.drain()
                    for o in range(DT):
                        mm = vector.scalar_tensor_tensor(
                            outb[:, o, :], acc[:, o, c * CH : (c + 1) * CH],
                            0.5, xf[:, o, :], op0=ALU.mult, op1=ALU.add,
                        )
                    ctx.fin(mm, "VE")

                ctx.on("vector", b)

                ctx.res("ST", 16)

                def b(sync, c=c, out_tag=out_tag):
                    ctx.wait(sync, "VE", out_tag)
                    sync.dma_start(out=A["y"][c], in_=outb).then_inc(
                        ctx.sems["ST"], 16
                    )

                ctx.on("sync", b)
            ctx.flush()

        ctx.tag["w_used"] = ctx.c["MM"]

    st_final = ctx.c["ST"]

    def b(sync):
        ctx.wait(sync, "ST", st_final)

    ctx.on("sync", b)
    ctx.flush()


def build_moe():
    nc = bass.Bass("TRN2", target_bir_lowering=False)
    A = moe_drams(nc)
    es = contextlib.ExitStack()
    with es, nc.Block() as block:
        ctx = Ctx(nc, es, block)
        emit_moe(nc, block, ctx, es, A)
    return nc


def moe_host_inputs(xt_core, W):
    import ml_dtypes

    eb8 = np.zeros((NE, NE * 128), dtype=ml_dtypes.bfloat16)
    for e in range(NE):
        eb8[e, e * 128 : (e + 1) * 128] = 1.0
    w1 = np.ascontiguousarray(
        np.asarray(W["W1"]).reshape(NE, DT, 128, HD).transpose(0, 2, 1, 3)
    )
    w2 = np.ascontiguousarray(
        np.asarray(W["W2"]).reshape(NE, HT, 128, D).transpose(0, 2, 1, 3)
    )
    b1t = np.ascontiguousarray(
        np.asarray(W["b1"]).reshape(NE, HT, 128).transpose(2, 1, 0)
    )
    return {
        "x": xt_core,
        "wg": tile_w(np.asarray(W["Wg"])),
        "bg": np.ascontiguousarray(np.asarray(W["bg"]).reshape(NE, 1)),
        "w1": w1,
        "b1t": b1t,
        "w2": w2,
        "b2": np.ascontiguousarray(np.asarray(W["b2"])),
        "idf": np.eye(8, dtype=np.float32),
        "idb": np.eye(128, dtype=np.float32).astype(ml_dtypes.bfloat16),
        "eb8": eb8,
    }



DL = 256
NH = 16


def mla_drams(nc, pfx="", io=True):
    A = {}
    if io:
        A["x"] = nc.dram_tensor(pfx + "x", [NCH, 128, DT, CH], F32, kind="ExternalInput").ap()
    A["wdkv"] = nc.dram_tensor(pfx + "wdkv", [128, DT, DL], F32, kind="ExternalInput").ap()
    A["wuk"] = nc.dram_tensor(pfx + "wuk", [128, 2, D], F32, kind="ExternalInput").ap()
    A["wuv"] = nc.dram_tensor(pfx + "wuv", [128, 2, D], F32, kind="ExternalInput").ap()
    A["wuq"] = nc.dram_tensor(pfx + "wuq", [128, DT, D], F32, kind="ExternalInput").ap()
    A["wo"] = nc.dram_tensor(pfx + "wo", [128, DT, D], F32, kind="ExternalInput").ap()
    A["mask4"] = nc.dram_tensor(pfx + "mask4", [128, 4, CH], BF16, kind="ExternalInput").ap()
    if io:
        A["y"] = nc.dram_tensor(pfx + "y", [NCH, 128, DT, CH], F32, kind="ExternalOutput").ap()
    A["qth"] = nc.dram_tensor(pfx + "qth", [NCH, 128, DT, CH], BF16, kind="Internal").ap()
    A["kth"] = nc.dram_tensor(pfx + "kth", [NCH, 128, DT, CH], BF16, kind="Internal").ap()
    A["vah"] = nc.dram_tensor(pfx + "vah", [NCH, 128, 4, NH * 65], BF16, kind="Internal").ap()

    return A


def emit_mla(nc, block, ctx, es, A):
    sb = lambda *a: es.enter_context(nc.sbuf_tensor(*a, allow_name_mangling=True))
    psa = lambda *a: es.enter_context(nc.psum_tensor(*a, allow_name_mangling=True))

    xf = sb("xf", [128, DT, CH], F32)
    xnc = sb("xnc", [128, 2, DT, CH], BF16)
    sq = sb("sqb", [128, 1, CH], BF16)
    bc = sb("bcb", [128, 2 * CH], BF16)
    rowm = sb("rowm", [1, CH], F32)
    rowv = sb("rowv", [1, CH], F32)
    rows = sb("rows", [1, CH], F32)
    onesb = sb("onesb", [128, 1], BF16)
    eps1 = sb("eps1", [1, 1], F32)
    ones1f = sb("ones1f", [1, 128], F32)
    neg10 = sb("neg10", [128, 1], F32)
    ckvT = sb("ckvT", [128, 2, T], BF16)
    wdkv_t = sb("wdkv_t", [128, DT, DL], BF16)
    wuk_t = sb("wuk_t", [128, 2, D], BF16)
    wuv_t = sb("wuv_t", [128, 2, D], BF16)
    wuq_t = sb("wuq_t", [128, DT, D], BF16)
    wo_t = sb("wo_t", [128, DT, D], BF16)
    qt2 = sb("qt2", [128, 2, DT, CH], BF16)
    kt2 = sb("kt2", [128, 2, DT, CH], BF16)
    va2 = sb("va2", [128, 2, 4, NH * 65], BF16)
    pb = sb("pb", [128, 2, CH], BF16)
    mask_t = sb("mask_t", [128, 4, CH], BF16)
    ctxEO = [sb("ctxE", [128, DT, CH], F32), sb("ctxO", [128, DT, CH], F32)]
    ctx_bf = sb("ctx_bf", [128, DT, CH], BF16)
    rden = sb("rden", [1, 4, CH], F32)

    outb = ctxEO[1][:]

    ps_stats = psa("ps_stats", [1, 2 * CH], F32)
    ps_bc = psa("ps_bc", [128, 2 * CH], F32)
    ps_w = psa("ps_w", [128, 2 * CH], F32)
    ps_pv = [psa("ps_pv0", [65, CH], F32), psa("ps_pv1", [65, CH], F32)]
    ps_s = [ps_w[:, 0:CH], ps_w[:, CH : 2 * CH]]
    ps_o = [ps_bc[:, 0:CH], ps_bc[:, CH : 2 * CH]]
    ps_s4 = [ps_w[:, 0:CH], ps_w[:, CH : 2 * CH],
             ps_bc[:, 0:CH], ps_bc[:, CH : 2 * CH]]
    pb4 = xnc[:, 0, 0:4, :]

    qtb = qt2[:, 0, :, :]
    ktb = kt2[:, 0, :, :]
    vab = va2[:, 0, :, :]


    bufs = dict(
        sq=sq, bc=bc, rowm=rowm, rowv=rowv, rows=rows, onesb=onesb,
        ones1f=ones1f, ps_stats=ps_stats, ps_bc=ps_bc, eps1=eps1,
    )

    w_tag = ctx.res("LW", 96)

    def b(gpsimd):
        for dst, srcd in [(wdkv_t, A["wdkv"]), (wuk_t, A["wuk"]), (wuv_t, A["wuv"]),
                          (wuq_t, A["wuq"]), (wo_t, A["wo"]), (mask_t, A["mask4"])]:
            gpsimd.dma_start(out=dst[:], in_=srcd).then_inc(ctx.sems["LW"], 16)

    ctx.on("gpsimd", b)

    init_ve = ctx.res("VE")

    def b(vector):
        vector.memset(onesb[:], 1.0)
        vector.memset(eps1[:], EPS_LN)
        vector.memset(neg10[:], -10.0)
        vector.memset(
            va2[:].rearrange("p b w (h e) -> p b w h e", e=65)[:, :, :, :, 64:65],
            1.0,
        )
        ctx.fin(vector.memset(ones1f[:], 1.0), "VE")

    ctx.on("vector", b)

    for c in range(NCH):
        xn = xnc[:, c % 2, :, :]
        csl = slice(c * CH, (c + 1) * CH)
        xf_free = ctx.t("xf_free")
        ld_tag = ctx.res("LD", 16)

        def b(sync, c=c, xf_free=xf_free):
            ctx.wait(sync, "VE", xf_free)
            sync.dma_start(out=xf[:], in_=A["x"][c]).then_inc(ctx.sems["LD"], 16)

        ctx.on("sync", b)

        xn_used = ctx.t(f"xn_used_{c-2}")

        def pre(vector, c=c, xn_used=xn_used, init_ve=init_ve):
            if c == 0:
                ctx.wait(vector, "VE", init_ve)
            if c >= 2:
                ctx.wait(vector, "MM", xn_used)

        ve_tag = q_ln_chunk(ctx, bufs, xf[:], xn, "LD", ld_tag, pre)
        ctx.tag["xf_free"] = ctx.t("ln_cast")

        for m2 in range(2):
            psw_free = ctx.t(f"psw_free_{m2 % 2}")
            mm_tag = ctx.res("MM")

            def b(tensor, c=c, m2=m2, xn=xn, ve_tag=ve_tag, psw_free=psw_free,
                  w_tag=w_tag):
                if c == 0 and m2 == 0:
                    ctx.wait(tensor, "LW", w_tag)
                ctx.wait(tensor, "VE", ve_tag)
                ctx.wait(tensor, "SE", psw_free)
                for a in range(DT):
                    mm = tensor.matmul(
                        ps_s[m2 % 2], wdkv_t[:, a, m2 * 128 : (m2 + 1) * 128],
                        xn[:, a, :], start=(a == 0), stop=(a == DT - 1),
                    )
                ctx.fin(mm, "MM")

            ctx.on("tensor", b)
            i = ctx.res("SE")
            ctx.tag[f"psw_free_{m2 % 2}"] = i

            def b(scalar, m2=m2, csl=csl, mm_tag=mm_tag):
                ctx.wait(scalar, "MM", mm_tag)
                ctx.fin(scalar.copy(ckvT[:, m2, csl], ps_s[m2 % 2]), "SE")

            ctx.on("scalar", b)

        ckv_se = ctx.c["SE"]

        for ot in range(DT):
            psw_free = ctx.t(f"psw_free_{ot % 2}")
            q_spill = ctx.t(f"q_spill_{c-1}")
            mm_tag = ctx.res("MM")

            def b(tensor, c=c, ot=ot, xn=xn, psw_free=psw_free, q_spill=q_spill):
                ctx.wait(tensor, "SE", psw_free)
                if ot == 0:
                    ctx.wait(tensor, "QS", q_spill)
                for a in range(DT):
                    mm = tensor.matmul(
                        ps_s[ot % 2], wuq_t[:, a, ot * 128 : (ot + 1) * 128],
                        xn[:, a, :], start=(a == 0), stop=(a == DT - 1),
                    )
                ctx.fin(mm, "MM")

            ctx.on("tensor", b)
            i = ctx.res("SE")
            ctx.tag[f"psw_free_{ot % 2}"] = i

            def b(scalar, ot=ot, mm_tag=mm_tag):
                ctx.wait(scalar, "MM", mm_tag)
                ctx.fin(scalar.copy(qtb[:, ot, :], ps_s[ot % 2]), "SE")

            ctx.on("scalar", b)

        q_se = ctx.c["SE"]
        ctx.res("QS", 16)
        ctx.tag[f"q_spill_{c}"] = ctx.c["QS"]

        def b(gpsimd, c=c, q_se=q_se):
            ctx.wait(gpsimd, "SE", q_se)
            gpsimd.dma_start(out=A["qth"][c], in_=qtb).then_inc(ctx.sems["QS"], 16)

        ctx.on("gpsimd", b)

        for ot in range(DT):
            psw_free = ctx.t(f"psw_free_{ot % 2}")
            k_spill = ctx.t(f"k_spill_{c-1}")
            mm_tag = ctx.res("MM")

            def b(tensor, c=c, ot=ot, csl=csl, ckv_se=ckv_se, psw_free=psw_free,
                  k_spill=k_spill):
                if ot == 0:
                    ctx.wait(tensor, "SE", ckv_se)
                    ctx.wait(tensor, "KS", k_spill)
                ctx.wait(tensor, "SE", psw_free)
                for c2 in range(2):
                    mm = tensor.matmul(
                        ps_s[ot % 2], wuk_t[:, c2, ot * 128 : (ot + 1) * 128],
                        ckvT[:, c2, csl], start=(c2 == 0), stop=(c2 == 1),
                    )
                ctx.fin(mm, "MM")

            ctx.on("tensor", b)
            i = ctx.res("SE")
            ctx.tag[f"psw_free_{ot % 2}"] = i

            def b(scalar, ot=ot, mm_tag=mm_tag):
                ctx.wait(scalar, "MM", mm_tag)
                ctx.fin(scalar.copy(ktb[:, ot, :], ps_s[ot % 2]), "SE")

            ctx.on("scalar", b)

        k_se = ctx.c["SE"]
        ctx.res("KS", 16)
        ctx.tag[f"k_spill_{c}"] = ctx.c["KS"]

        def b(gpsimd, c=c, k_se=k_se):
            ctx.wait(gpsimd, "SE", k_se)
            gpsimd.dma_start(out=A["kth"][c], in_=ktb).then_inc(ctx.sems["KS"], 16)

        ctx.on("gpsimd", b)

        for tt in range(4):
            psbc_free = ctx.t("psbc_free")
            v_spill = ctx.t(f"v_spill_{c-1}")
            mm_tag = ctx.res("MM")

            def b(tensor, c=c, tt=tt, ckv_se=ckv_se, psbc_free=psbc_free,
                  v_spill=v_spill):
                if tt == 0:
                    ctx.wait(tensor, "SE", ckv_se)
                    ctx.wait(tensor, "VS", v_spill)
                ctx.wait(tensor, "VE", psbc_free)
                for half in range(2):
                    for c2 in range(2):
                        mm = tensor.matmul(
                            ps_bc[:, half * CH : (half + 1) * CH],
                            ckvT[:, c2, c * CH + tt * 128 : c * CH + (tt + 1) * 128],
                            wuv_t[:, c2, half * CH : (half + 1) * CH],
                            start=(c2 == 0), stop=(c2 == 1),
                        )
                ctx.fin(mm, "MM")

            ctx.on("tensor", b)
            i = ctx.res("VE")
            ctx.tag["psbc_free"] = i

            def b(vector, tt=tt, mm_tag=mm_tag):
                ctx.wait(vector, "MM", mm_tag)
                ctx.fin(
                    vector.tensor_copy(
                        vab.rearrange("p w (h e) -> p w h e", e=65)[
                            :, tt, :, 0:64
                        ],
                        ps_bc[:].rearrange("p (h e) -> p h e", e=64),
                    ),
                    "VE",
                )

            ctx.on("vector", b)

        v_ve = ctx.c["VE"]
        ctx.res("VS", 16)
        ctx.tag[f"v_spill_{c}"] = ctx.c["VS"]

        def b(gpsimd, c=c, v_ve=v_ve):
            ctx.wait(gpsimd, "VE", v_ve)
            gpsimd.dma_start(out=A["vah"][c], in_=vab).then_inc(ctx.sems["VS"], 16)

        ctx.on("gpsimd", b)

        ctx.tag[f"xn_used_{c}"] = ctx.c["MM"]
        ctx.flush()

    stageN_mm = ctx.c["MM"]
    stageN_ve = ctx.c["VE"]
    stageN_se = ctx.c["SE"]

    for qc in range(NCH):
        qb = qc % 2
        qt_used = ctx.t(f"qt2_used_{qc-2}")
        out_done = ctx.t("out_done")
        xf_free = ctx.t("xf_free")
        q_spill = ctx.t(f"q_spill_{qc}")
        q_ld = ctx.res("LD", 32)

        def b(sync, qc=qc, qb=qb, qt_used=qt_used, out_done=out_done,
              xf_free=xf_free, q_spill=q_spill):
            ctx.wait(sync, "QS", q_spill)
            if qc < 2:
                ctx.wait(sync, "MM", stageN_mm)
                ctx.wait(sync, "VE", stageN_ve)
            else:
                ctx.wait(sync, "MM", qt_used)
            sync.dma_start(out=qt2[:, qb, :, :], in_=A["qth"][qc]).then_inc(
                ctx.sems["LD"], 16
            )
            ctx.wait(sync, "VE", out_done)
            if qc == 0:
                ctx.wait(sync, "VE", xf_free)
            sync.dma_start(out=xf[:], in_=A["x"][qc]).then_inc(ctx.sems["LD"], 16)

        ctx.on("sync", b)

        for kc in range(qc + 1):
            it = qc * (qc + 1) // 2 + kc
            kb = it % 2
            diag = kc == qc

            kv_used = ctx.t(f"kv2_used_{it-2}")
            k_spill = ctx.t(f"k_spill_{kc}")
            v_spill = ctx.t(f"v_spill_{kc}")
            kv_ld = ctx.res("LD", 32)

            def b(sync, qc=qc, kc=kc, it=it, kb=kb, kv_used=kv_used,
                  k_spill=k_spill, v_spill=v_spill):
                ctx.wait(sync, "KS", k_spill)
                ctx.wait(sync, "VS", v_spill)
                if it >= 2:
                    ctx.wait(sync, "MM", kv_used)
                elif qc < 2:
                    ctx.wait(sync, "MM", stageN_mm)
                sync.dma_start(out=kt2[:, kb, :, :], in_=A["kth"][kc]).then_inc(
                    ctx.sems["LD"], 16
                )
                sync.dma_start(out=va2[:, kb, :, :], in_=A["vah"][kc]).then_inc(
                    ctx.sems["LD"], 16
                )

            ctx.on("sync", b)

            se_base = ctx.c["SE"]
            rbc_read = ctx.t("rbc_read")

            mm_tags = {}
            ve_mask_tags = {}
            n_ve_pre = ctx.c["VE"]

            LAGM = 3
            msched = []
            for j in range(NH * 4 + LAGM):
                if j < NH * 4:
                    msched.append(("S", j))
                if j >= LAGM:
                    msched.append(("P", j - LAGM))
            for kind, j in msched:
                if kind == "S":
                    mm_tags[("s", j // 4, j % 4)] = ctx.res("MM")
                else:
                    mm_tags[("pv", j // 4, j % 4)] = ctx.res("MM")
            for h in range(NH):
                for kt in range(4):
                    ctx.res("SE")
                    if diag:
                        ve_mask_tags[(h, kt)] = ctx.res("VE")
                mm_tags[("acc", h)] = ctx.res("VE")

            pv_tail = [ctx.t(f"pv_tail_{i}") for i in range(4)]
            for i in range(4):
                ctx.tag[f"pv_tail_{i}"] = mm_tags[("pv", NH - 1, i)]
            pso_p = [ctx.t("pso_free_0"), ctx.t("pso_free_1")]
            pv_used_prev = [ctx.t("pv_used_0"), ctx.t("pv_used_1")]
            ctx.tag["pv_used_0"] = mm_tags[("acc", NH - 2)]
            ctx.tag["pv_used_1"] = mm_tags[("acc", NH - 1)]
            st_used = ctx.t("out_stored")

            def b(tensor, qc=qc, kc=kc, it=it, kb=kb, qb=qb, diag=diag,
                  kv_ld=kv_ld, q_ld=q_ld, se_base=se_base, mm_tags=mm_tags,
                  ve_mask_tags=ve_mask_tags, msched=msched, pso_p=pso_p,
                  rbc_read=rbc_read, pv_used_prev=pv_used_prev):
                def eS(j):
                    h, kt = j // 4, j % 4
                    a = h // 2
                    base = 64 * (h % 2)
                    if j == 0:
                        ctx.wait(tensor, "LD", kv_ld)
                        ctx.wait(tensor, "LD", q_ld - 16)
                        if kc == 0:
                            if qc == 0:
                                ctx.wait(tensor, "SE", stageN_se)
                            ctx.wait(tensor, "VE", rbc_read)
                    if kc == 0 and j in (2, 3):
                        ctx.wait(tensor, "VE", pso_p[j - 2])
                    if j >= 4:
                        ctx.wait(tensor, "SE", se_base + j - 3)
                    ctx.fin(
                        tensor.matmul(
                            ps_s4[j % 4],
                            kt2[base : base + 64, kb, a,
                                kt * 128 : (kt + 1) * 128],
                            qt2[base : base + 64, qb, a, :],
                            start=True, stop=True,
                            tile_position=(base, 0),
                        ),
                        "MM",
                    )

                def eP(j):
                    h, kt = j // 4, j % 4
                    if diag:
                        ctx.wait(tensor, "VE", ve_mask_tags[(h, kt)])
                    else:
                        ctx.wait(tensor, "SE", se_base + j + 1)
                    if kt == 0:
                        ctx.wait(
                            tensor, "VE",
                            mm_tags[("acc", h - 2)] if h >= 2
                            else pv_used_prev[h % 2],
                        )
                    ctx.fin(
                        tensor.matmul(
                            ps_pv[h % 2][:],
                            va2[:, kb, kt, h * 65 : (h + 1) * 65],
                            pb4[:, j % 4, :],
                            start=(kt == 0), stop=(kt == 3),
                        ),
                        "MM",
                    )

                for kind, j in msched:
                    (eS if kind == "S" else eP)(j)

            ctx.on("tensor", b)

            def b(scalar, diag=diag, mm_tags=mm_tags, pv_tail=pv_tail):
                for h in range(NH):
                    for kt in range(4):
                        j = h * 4 + kt
                        ctx.wait(scalar, "MM", mm_tags[("s", h, kt)])
                        if j >= 4:
                            ctx.wait(scalar, "MM",
                                     mm_tags[("pv", (j - 4) // 4, (j - 4) % 4)])
                        else:
                            ctx.wait(scalar, "MM", pv_tail[j])
                        ctx.fin(
                            scalar.activation(
                                pb4[:, j % 4, :], ps_s4[j % 4], AF.Exp,
                                bias=neg10[:], scale=0.125,
                            ),
                            "SE",
                        )

            ctx.on("scalar", b)

            def b(vector, qc=qc, kc=kc, diag=diag, mm_tags=mm_tags,
                  se_base=se_base, st_used=st_used):
                for h in range(NH):
                    for kt in range(4):
                        j = h * 4 + kt
                        if diag:
                            ctx.wait(vector, "SE", se_base + j + 1)
                            ctx.fin(
                                vector.tensor_tensor(
                                    pb4[:, j % 4, :], pb4[:, j % 4, :],
                                    mask_t[:, kt, :], op=ALU.mult,
                                ),
                                "VE",
                            )
                    ctx.wait(vector, "MM", mm_tags[("pv", h, 3)])
                    if kc == 0 and h < 2:
                        ctx.wait(vector, "ST", st_used)
                    dst = ctxEO[h % 2][0:65, h // 2, :]
                    if kc == 0:
                        ctx.fin(vector.tensor_copy(dst, ps_pv[h % 2][:]), "VE")
                    else:
                        ctx.fin(
                            vector.tensor_tensor(
                                dst, dst, ps_pv[h % 2][:], op=ALU.add
                            ),
                            "VE",
                        )

            ctx.on("vector", b)
            ctx.tag[f"kv2_used_{it}"] = ctx.c["MM"]
            ctx.flush()

        ctx.tag[f"qt2_used_{qc}"] = ctx.c["MM"]
        acc_ve = ctx.c["VE"]

        for batch in range(4):
            par = batch // 2
            alo = 4 * (batch % 2)
            rbc_used = ctx.t("rbc_used")
            r_ve = ctx.res("VE")

            def b(vector, batch=batch, par=par, alo=alo, acc_ve=acc_ve,
                  rbc_used=rbc_used):
                if batch == 0:
                    ctx.wait(vector, "VE", acc_ve)
                ctx.wait(vector, "MM", rbc_used)
                vector.drain()
                ctx.fin(
                    vector.reciprocal(
                        rden[:, :, :], ctxEO[par][64:65, alo : alo + 4, :]
                    ),
                    "VE",
                )

            ctx.on("vector", b)

            for i4 in range(4):
                a = alo + i4
                h = 2 * a + par
                base = 64 * par
                rbc_read2 = ctx.t("rbc_read")
                b_tag = ctx.res("MM")
                ctx.tag["rbc_used"] = b_tag

                def b(tensor, i4=i4, r_ve=r_ve, rbc_read2=rbc_read2):
                    if i4 == 0:
                        ctx.wait(tensor, "VE", r_ve)
                    ctx.wait(tensor, "VE", rbc_read2)
                    ctx.fin(
                        tensor.matmul(
                            ps_s[0][0:64, :], ones1f[:, 0:64],
                            rden[:, i4, :], start=True, stop=True,
                        ),
                        "MM",
                    )

                ctx.on("tensor", b)

                i = ctx.res("VE")
                ctx.tag["rbc_read"] = i

                def b(vector, par=par, a=a, base=base, b_tag=b_tag):
                    ctx.wait(vector, "MM", b_tag)
                    ctx.fin(
                        vector.tensor_tensor(
                            ctx_bf[base : base + 64, a, :],
                            ctxEO[par][0:64, a, :],
                            ps_s[0][0:64, :], op=ALU.mult,
                        ),
                        "VE",
                    )

                ctx.on("vector", b)

        norm_ve = ctx.c["VE"]

        for ot in range(DT):
            pso_free = ctx.t(f"pso_free_{ot % 2}")
            o_tag = ctx.res("MM")

            def b(tensor, qc=qc, ot=ot, norm_ve=norm_ve, pso_free=pso_free):
                if ot == 0:
                    ctx.wait(tensor, "VE", norm_ve)
                ctx.wait(tensor, "VE", pso_free)
                for a in range(DT):
                    mm = tensor.matmul(
                        ps_o[ot % 2], wo_t[:, a, ot * 128 : (ot + 1) * 128],
                        ctx_bf[:, a, :], start=(a == 0), stop=(a == DT - 1),
                    )
                ctx.fin(mm, "MM")

            ctx.on("tensor", b)

            st_prev = ctx.t("out_stored")
            i = ctx.res("VE")
            ctx.tag[f"pso_free_{ot % 2}"] = i
            if ot == DT - 1:
                ctx.tag["out_done"] = i

            def b(vector, qc=qc, ot=ot, o_tag=o_tag, q_ld=q_ld, st_prev=st_prev):
                ctx.wait(vector, "MM", o_tag)
                ctx.wait(vector, "LD", q_ld)
                if ot == 0:
                    ctx.wait(vector, "ST", st_prev)
                ctx.fin(
                    vector.tensor_tensor(
                        outb[:, ot, :], ps_o[ot % 2], xf[:, ot, :], op=ALU.add
                    ),
                    "VE",
                )

            ctx.on("vector", b)

        out_ve = ctx.c["VE"]
        ctx.res("ST", 16)
        ctx.tag["out_stored"] = ctx.c["ST"]

        def b(sync, qc=qc, out_ve=out_ve):
            ctx.wait(sync, "VE", out_ve)
            sync.dma_start(out=A["y"][qc], in_=outb).then_inc(ctx.sems["ST"], 16)

        ctx.on("sync", b)
        ctx.flush()

    st_final = ctx.c["ST"]

    def b(sync):
        ctx.wait(sync, "ST", st_final)

    ctx.on("sync", b)
    ctx.flush()


def build_mla():
    nc = bass.Bass("TRN2", target_bir_lowering=False)
    A = mla_drams(nc)
    es = contextlib.ExitStack()
    with es, nc.Block() as block:
        ctx = Ctx(nc, es, block)
        emit_mla(nc, block, ctx, es, A)
    return nc


def lora_drams(nc, pfx="", io=True):
    A = {}
    if io:
        A["x"] = nc.dram_tensor(pfx + "x", [NCH, 128, DT, CH], F32, kind="ExternalInput").ap()
    A["wqt"] = nc.dram_tensor(pfx + "wqt", [128, DT, D], F32, kind="ExternalInput").ap()
    A["wkt"] = nc.dram_tensor(pfx + "wkt", [128, DT, D], F32, kind="ExternalInput").ap()
    A["wvt"] = nc.dram_tensor(pfx + "wvt", [128, DT, D], F32, kind="ExternalInput").ap()
    A["wot"] = nc.dram_tensor(pfx + "wot", [128, DT, D], F32, kind="ExternalInput").ap()
    A["lat"] = nc.dram_tensor(pfx + "lat", [128, 4, DT, 16], F32, kind="ExternalInput").ap()
    A["lbt"] = nc.dram_tensor(pfx + "lbt", [16, 4, D], F32, kind="ExternalInput").ap()
    A["mask4"] = nc.dram_tensor(pfx + "mask4", [128, 4, CH], BF16, kind="ExternalInput").ap()
    if io:
        A["y"] = nc.dram_tensor(pfx + "y", [NCH, 128, DT, CH], F32, kind="ExternalOutput").ap()
    A["qth"] = nc.dram_tensor(pfx + "qth", [NCH, 128, DT, CH], BF16, kind="Internal").ap()
    A["kth"] = nc.dram_tensor(pfx + "kth", [NCH, 128, DT, CH], BF16, kind="Internal").ap()
    A["vah"] = nc.dram_tensor(pfx + "vah", [NCH, 128, 4, NH * 65], BF16, kind="Internal").ap()

    return A


def emit_lora(nc, block, ctx, es, A):
    sb = lambda *a: es.enter_context(nc.sbuf_tensor(*a, allow_name_mangling=True))
    psa = lambda *a: es.enter_context(nc.psum_tensor(*a, allow_name_mangling=True))

    xf = sb("xf", [128, DT, CH], F32)
    xnc = sb("xnc", [128, 2, DT, CH], BF16)
    sq = sb("sqb", [128, 1, CH], BF16)
    bc = sb("bcb", [128, 2 * CH], BF16)
    rowm = sb("rowm", [1, CH], F32)
    rowv = sb("rowv", [1, CH], F32)
    rows = sb("rows", [1, CH], F32)
    onesb = sb("onesb", [128, 1], BF16)
    eps1 = sb("eps1", [1, 1], F32)
    ones1f = sb("ones1f", [1, 128], F32)
    neg10 = sb("neg10", [128, 1], F32)
    wq_t = sb("wq_t", [128, DT, D], BF16)
    wk_t = sb("wk_t", [128, DT, D], BF16)
    wv_t = sb("wv_t", [128, DT, D], BF16)
    la_t = sb("la_t", [128, 4, DT, 16], BF16)
    lb_t = sb("lb_t", [16, 4, D], BF16)
    lraw = sb("lraw", [16, 3, CH], BF16)
    lo_raw = sb("lo_raw", [16, CH], BF16)
    wo_t = wq_t
    qt2 = sb("qt2", [128, 2, DT, CH], BF16)
    kt2 = sb("kt2", [128, 2, DT, CH], BF16)
    va2 = sb("va2", [128, 2, 4, NH * 65], BF16)
    pb = sb("pb", [128, 2, CH], BF16)
    mask_t = sb("mask_t", [128, 4, CH], BF16)
    ctxEO = [sb("ctxE", [128, DT, CH], F32), sb("ctxO", [128, DT, CH], F32)]
    ctx_bf = sb("ctx_bf", [128, DT, CH], BF16)
    rden = sb("rden", [1, 4, CH], F32)

    outb = ctxEO[1][:]

    ps_stats = psa("ps_stats", [1, 2 * CH], F32)
    ps_bc = psa("ps_bc", [128, 2 * CH], F32)
    ps_w = psa("ps_w", [128, 2 * CH], F32)
    ps_pv = [psa("ps_pv0", [65, CH], F32), psa("ps_pv1", [65, CH], F32)]
    ps_s = [ps_w[:, 0:CH], ps_w[:, CH : 2 * CH]]
    ps_o = [ps_bc[:, 0:CH], ps_bc[:, CH : 2 * CH]]
    ps_s4 = [ps_w[:, 0:CH], ps_w[:, CH : 2 * CH],
             ps_bc[:, 0:CH], ps_bc[:, CH : 2 * CH]]
    pb4 = xnc[:, 0, 0:4, :]

    qtb = qt2[:, 0, :, :]
    ktb = kt2[:, 0, :, :]
    vab = va2[:, 0, :, :]


    bufs = dict(
        sq=sq, bc=bc, rowm=rowm, rowv=rowv, rows=rows, onesb=onesb,
        ones1f=ones1f, ps_stats=ps_stats, ps_bc=ps_bc, eps1=eps1,
    )

    w_tag = ctx.res("LW", 96)

    def b(gpsimd):
        for dst, srcd in [(wq_t, A["wqt"]), (wk_t, A["wkt"]), (wv_t, A["wvt"]),
                          (la_t, A["lat"]), (lb_t, A["lbt"]), (mask_t, A["mask4"])]:
            gpsimd.dma_start(out=dst[:], in_=srcd).then_inc(ctx.sems["LW"], 16)

    ctx.on("gpsimd", b)

    init_ve = ctx.res("VE")

    def b(vector):
        vector.memset(onesb[:], 1.0)
        vector.memset(eps1[:], EPS_LN)
        vector.memset(neg10[:], -10.0)
        vector.memset(
            va2[:].rearrange("p b w (h e) -> p b w h e", e=65)[:, :, :, :, 64:65],
            1.0,
        )
        ctx.fin(vector.memset(ones1f[:], 1.0), "VE")

    ctx.on("vector", b)

    for c in range(NCH):
        xn = xnc[:, c % 2, :, :]
        csl = slice(c * CH, (c + 1) * CH)
        xf_free = ctx.t("xf_free")
        ld_tag = ctx.res("LD", 16)

        def b(sync, c=c, xf_free=xf_free):
            ctx.wait(sync, "VE", xf_free)
            sync.dma_start(out=xf[:], in_=A["x"][c]).then_inc(ctx.sems["LD"], 16)

        ctx.on("sync", b)

        xn_used = ctx.t(f"xn_used_{c-2}")

        def pre(vector, c=c, xn_used=xn_used, init_ve=init_ve):
            if c == 0:
                ctx.wait(vector, "VE", init_ve)
            if c >= 2:
                ctx.wait(vector, "MM", xn_used)

        ve_tag = q_ln_chunk(ctx, bufs, xf[:], xn, "LD", ld_tag, pre)
        ctx.tag["xf_free"] = ctx.t("ln_cast")

        lraw_used = ctx.t("lraw_used")
        for w3 in range(3):
            psw_free = ctx.t(f"psw_free_{w3 % 2}")
            mm_tag = ctx.res("MM")

            def b(tensor, c=c, w3=w3, xn=xn, ve_tag=ve_tag, psw_free=psw_free,
                  w_tag=w_tag):
                if c == 0 and w3 == 0:
                    ctx.wait(tensor, "LW", w_tag)
                ctx.wait(tensor, "VE", ve_tag)
                ctx.wait(tensor, "SE", psw_free)
                for a in range(DT):
                    mm = tensor.matmul(
                        ps_s[w3 % 2][0:16, :], la_t[:, w3, a, :],
                        xn[:, a, :], start=(a == 0), stop=(a == DT - 1),
                    )
                ctx.fin(mm, "MM")

            ctx.on("tensor", b)
            i = ctx.res("SE")
            ctx.tag[f"psw_free_{w3 % 2}"] = i

            def b(scalar, w3=w3, mm_tag=mm_tag, lraw_used=lraw_used):
                ctx.wait(scalar, "MM", mm_tag)
                if w3 == 0:
                    ctx.wait(scalar, "MM", lraw_used)
                ctx.fin(
                    scalar.activation(
                        lraw[:, w3, :], ps_s[w3 % 2][0:16, :], AF.Identity,
                        scale=1.0 / 16.0,
                    ),
                    "SE",
                )

            ctx.on("scalar", b)

        ckv_se = ctx.c["SE"]

        for ot in range(DT):
            psw_free = ctx.t(f"psw_free_{ot % 2}")
            q_spill = ctx.t(f"q_spill_{c-1}")
            mm_tag = ctx.res("MM")

            def b(tensor, c=c, ot=ot, xn=xn, psw_free=psw_free, q_spill=q_spill,
                  ckv_se=ckv_se):
                ctx.wait(tensor, "SE", psw_free)
                if ot == 0:
                    ctx.wait(tensor, "QS", q_spill)
                    ctx.wait(tensor, "SE", ckv_se)
                for a in range(DT):
                    tensor.matmul(
                        ps_s[ot % 2], wq_t[:, a, ot * 128 : (ot + 1) * 128],
                        xn[:, a, :], start=(a == 0), stop=False,
                    )
                mm = tensor.matmul(
                    ps_s[ot % 2], lb_t[:, 0, ot * 128 : (ot + 1) * 128],
                    lraw[:, 0, :], start=False, stop=True,
                )
                ctx.fin(mm, "MM")

            ctx.on("tensor", b)
            i = ctx.res("SE")
            ctx.tag[f"psw_free_{ot % 2}"] = i

            def b(scalar, ot=ot, mm_tag=mm_tag):
                ctx.wait(scalar, "MM", mm_tag)
                ctx.fin(scalar.copy(qtb[:, ot, :], ps_s[ot % 2]), "SE")

            ctx.on("scalar", b)

        q_se = ctx.c["SE"]
        ctx.res("QS", 16)
        ctx.tag[f"q_spill_{c}"] = ctx.c["QS"]

        def b(gpsimd, c=c, q_se=q_se):
            ctx.wait(gpsimd, "SE", q_se)
            gpsimd.dma_start(out=A["qth"][c], in_=qtb).then_inc(ctx.sems["QS"], 16)

        ctx.on("gpsimd", b)

        for ot in range(DT):
            psw_free = ctx.t(f"psw_free_{ot % 2}")
            k_spill = ctx.t(f"k_spill_{c-1}")
            mm_tag = ctx.res("MM")

            def b(tensor, c=c, ot=ot, xn=xn, ckv_se=ckv_se, psw_free=psw_free,
                  k_spill=k_spill):
                if ot == 0:
                    ctx.wait(tensor, "KS", k_spill)
                ctx.wait(tensor, "SE", psw_free)
                for a in range(DT):
                    tensor.matmul(
                        ps_s[ot % 2], wk_t[:, a, ot * 128 : (ot + 1) * 128],
                        xn[:, a, :], start=(a == 0), stop=False,
                    )
                mm = tensor.matmul(
                    ps_s[ot % 2], lb_t[:, 1, ot * 128 : (ot + 1) * 128],
                    lraw[:, 1, :], start=False, stop=True,
                )
                ctx.fin(mm, "MM")

            ctx.on("tensor", b)
            i = ctx.res("SE")
            ctx.tag[f"psw_free_{ot % 2}"] = i

            def b(scalar, ot=ot, mm_tag=mm_tag):
                ctx.wait(scalar, "MM", mm_tag)
                ctx.fin(scalar.copy(ktb[:, ot, :], ps_s[ot % 2]), "SE")

            ctx.on("scalar", b)

        k_se = ctx.c["SE"]
        ctx.res("KS", 16)
        ctx.tag[f"k_spill_{c}"] = ctx.c["KS"]

        def b(gpsimd, c=c, k_se=k_se):
            ctx.wait(gpsimd, "SE", k_se)
            gpsimd.dma_start(out=A["kth"][c], in_=ktb).then_inc(ctx.sems["KS"], 16)

        ctx.on("gpsimd", b)

        for tt in range(4):
            psbc_free = ctx.t("psbc_free")
            v_spill = ctx.t(f"v_spill_{c-1}")
            mm_tag = ctx.res("MM")

            def b(tensor, c=c, tt=tt, xn=xn, ckv_se=ckv_se, psbc_free=psbc_free,
                  v_spill=v_spill):
                if tt == 0:
                    ctx.wait(tensor, "VS", v_spill)
                ctx.wait(tensor, "VE", psbc_free)
                for half in range(2):
                    for a in range(DT):
                        tensor.matmul(
                            ps_bc[:, half * CH : (half + 1) * CH],
                            xn[:, a, tt * 128 : (tt + 1) * 128],
                            wv_t[:, a, half * CH : (half + 1) * CH],
                            start=(a == 0), stop=False,
                        )
                    mm = tensor.matmul(
                        ps_bc[:, half * CH : (half + 1) * CH],
                        lraw[:, 2, tt * 128 : (tt + 1) * 128],
                        lb_t[:, 2, half * CH : (half + 1) * CH],
                        start=False, stop=True,
                    )
                ctx.fin(mm, "MM")

            ctx.on("tensor", b)
            i = ctx.res("VE")
            ctx.tag["psbc_free"] = i

            def b(vector, tt=tt, mm_tag=mm_tag):
                ctx.wait(vector, "MM", mm_tag)
                ctx.fin(
                    vector.tensor_copy(
                        vab.rearrange("p w (h e) -> p w h e", e=65)[
                            :, tt, :, 0:64
                        ],
                        ps_bc[:].rearrange("p (h e) -> p h e", e=64),
                    ),
                    "VE",
                )

            ctx.on("vector", b)

        v_ve = ctx.c["VE"]
        ctx.res("VS", 16)
        ctx.tag[f"v_spill_{c}"] = ctx.c["VS"]

        def b(gpsimd, c=c, v_ve=v_ve):
            ctx.wait(gpsimd, "VE", v_ve)
            gpsimd.dma_start(out=A["vah"][c], in_=vab).then_inc(ctx.sems["VS"], 16)

        ctx.on("gpsimd", b)

        ctx.tag[f"xn_used_{c}"] = ctx.c["MM"]
        ctx.tag["lraw_used"] = ctx.c["MM"]
        ctx.flush()

    stageN_mm = ctx.c["MM"]
    stageN_ve = ctx.c["VE"]
    stageN_se = ctx.c["SE"]

    wo_tag = ctx.res("LW", 16)

    def b(gpsimd):
        ctx.wait(gpsimd, "MM", stageN_mm)
        gpsimd.dma_start(out=wo_t[:], in_=A["wot"]).then_inc(ctx.sems["LW"], 16)

    ctx.on("gpsimd", b)

    for qc in range(NCH):
        qb = qc % 2
        qt_used = ctx.t(f"qt2_used_{qc-2}")
        out_done = ctx.t("out_done")
        xf_free = ctx.t("xf_free")
        q_spill = ctx.t(f"q_spill_{qc}")
        q_ld = ctx.res("LD", 32)

        def b(sync, qc=qc, qb=qb, qt_used=qt_used, out_done=out_done,
              xf_free=xf_free, q_spill=q_spill):
            ctx.wait(sync, "QS", q_spill)
            if qc < 2:
                ctx.wait(sync, "MM", stageN_mm)
                ctx.wait(sync, "VE", stageN_ve)
            else:
                ctx.wait(sync, "MM", qt_used)
            sync.dma_start(out=qt2[:, qb, :, :], in_=A["qth"][qc]).then_inc(
                ctx.sems["LD"], 16
            )
            ctx.wait(sync, "VE", out_done)
            if qc == 0:
                ctx.wait(sync, "VE", xf_free)
            sync.dma_start(out=xf[:], in_=A["x"][qc]).then_inc(ctx.sems["LD"], 16)

        ctx.on("sync", b)

        for kc in range(qc + 1):
            it = qc * (qc + 1) // 2 + kc
            kb = it % 2
            diag = kc == qc

            kv_used = ctx.t(f"kv2_used_{it-2}")
            k_spill = ctx.t(f"k_spill_{kc}")
            v_spill = ctx.t(f"v_spill_{kc}")
            kv_ld = ctx.res("LD", 32)

            def b(sync, qc=qc, kc=kc, it=it, kb=kb, kv_used=kv_used,
                  k_spill=k_spill, v_spill=v_spill):
                ctx.wait(sync, "KS", k_spill)
                ctx.wait(sync, "VS", v_spill)
                if it >= 2:
                    ctx.wait(sync, "MM", kv_used)
                elif qc < 2:
                    ctx.wait(sync, "MM", stageN_mm)
                sync.dma_start(out=kt2[:, kb, :, :], in_=A["kth"][kc]).then_inc(
                    ctx.sems["LD"], 16
                )
                sync.dma_start(out=va2[:, kb, :, :], in_=A["vah"][kc]).then_inc(
                    ctx.sems["LD"], 16
                )

            ctx.on("sync", b)

            se_base = ctx.c["SE"]
            rbc_read = ctx.t("rbc_read")

            mm_tags = {}
            ve_mask_tags = {}
            n_ve_pre = ctx.c["VE"]

            LAGM = 3
            msched = []
            for j in range(NH * 4 + LAGM):
                if j < NH * 4:
                    msched.append(("S", j))
                if j >= LAGM:
                    msched.append(("P", j - LAGM))
            for kind, j in msched:
                if kind == "S":
                    mm_tags[("s", j // 4, j % 4)] = ctx.res("MM")
                else:
                    mm_tags[("pv", j // 4, j % 4)] = ctx.res("MM")
            for h in range(NH):
                for kt in range(4):
                    ctx.res("SE")
                    if diag:
                        ve_mask_tags[(h, kt)] = ctx.res("VE")
                mm_tags[("acc", h)] = ctx.res("VE")

            pv_tail = [ctx.t(f"pv_tail_{i}") for i in range(4)]
            for i in range(4):
                ctx.tag[f"pv_tail_{i}"] = mm_tags[("pv", NH - 1, i)]
            pso_p = [ctx.t("pso_free_0"), ctx.t("pso_free_1")]
            pv_used_prev = [ctx.t("pv_used_0"), ctx.t("pv_used_1")]
            ctx.tag["pv_used_0"] = mm_tags[("acc", NH - 2)]
            ctx.tag["pv_used_1"] = mm_tags[("acc", NH - 1)]
            st_used = ctx.t("out_stored")

            def b(tensor, qc=qc, kc=kc, it=it, kb=kb, qb=qb, diag=diag,
                  kv_ld=kv_ld, q_ld=q_ld, se_base=se_base, mm_tags=mm_tags,
                  ve_mask_tags=ve_mask_tags, msched=msched, pso_p=pso_p,
                  rbc_read=rbc_read, pv_used_prev=pv_used_prev):
                def eS(j):
                    h, kt = j // 4, j % 4
                    a = h // 2
                    base = 64 * (h % 2)
                    if j == 0:
                        ctx.wait(tensor, "LD", kv_ld)
                        ctx.wait(tensor, "LD", q_ld - 16)
                        if kc == 0:
                            if qc == 0:
                                ctx.wait(tensor, "SE", stageN_se)
                            ctx.wait(tensor, "VE", rbc_read)
                    if kc == 0 and j in (2, 3):
                        ctx.wait(tensor, "VE", pso_p[j - 2])
                    if j >= 4:
                        ctx.wait(tensor, "SE", se_base + j - 3)
                    ctx.fin(
                        tensor.matmul(
                            ps_s4[j % 4],
                            kt2[base : base + 64, kb, a,
                                kt * 128 : (kt + 1) * 128],
                            qt2[base : base + 64, qb, a, :],
                            start=True, stop=True,
                            tile_position=(base, 0),
                        ),
                        "MM",
                    )

                def eP(j):
                    h, kt = j // 4, j % 4
                    if diag:
                        ctx.wait(tensor, "VE", ve_mask_tags[(h, kt)])
                    else:
                        ctx.wait(tensor, "SE", se_base + j + 1)
                    if kt == 0:
                        ctx.wait(
                            tensor, "VE",
                            mm_tags[("acc", h - 2)] if h >= 2
                            else pv_used_prev[h % 2],
                        )
                        if h == 0 and kc == 0:
                            ctx.wait(tensor, "SE", ctx.t("lo_done_se"))
                    ctx.fin(
                        tensor.matmul(
                            ps_pv[h % 2][:],
                            va2[:, kb, kt, h * 65 : (h + 1) * 65],
                            pb4[:, j % 4, :],
                            start=(kt == 0), stop=(kt == 3),
                        ),
                        "MM",
                    )

                for kind, j in msched:
                    (eS if kind == "S" else eP)(j)

            ctx.on("tensor", b)

            def b(scalar, diag=diag, mm_tags=mm_tags, pv_tail=pv_tail):
                for h in range(NH):
                    for kt in range(4):
                        j = h * 4 + kt
                        ctx.wait(scalar, "MM", mm_tags[("s", h, kt)])
                        if j >= 4:
                            ctx.wait(scalar, "MM",
                                     mm_tags[("pv", (j - 4) // 4, (j - 4) % 4)])
                        else:
                            ctx.wait(scalar, "MM", pv_tail[j])
                        ctx.fin(
                            scalar.activation(
                                pb4[:, j % 4, :], ps_s4[j % 4], AF.Exp,
                                bias=neg10[:], scale=0.125,
                            ),
                            "SE",
                        )

            ctx.on("scalar", b)

            def b(vector, qc=qc, kc=kc, diag=diag, mm_tags=mm_tags,
                  se_base=se_base, st_used=st_used):
                for h in range(NH):
                    for kt in range(4):
                        j = h * 4 + kt
                        if diag:
                            ctx.wait(vector, "SE", se_base + j + 1)
                            ctx.fin(
                                vector.tensor_tensor(
                                    pb4[:, j % 4, :], pb4[:, j % 4, :],
                                    mask_t[:, kt, :], op=ALU.mult,
                                ),
                                "VE",
                            )
                    ctx.wait(vector, "MM", mm_tags[("pv", h, 3)])
                    if kc == 0 and h < 2:
                        ctx.wait(vector, "ST", st_used)
                    dst = ctxEO[h % 2][0:65, h // 2, :]
                    if kc == 0:
                        ctx.fin(vector.tensor_copy(dst, ps_pv[h % 2][:]), "VE")
                    else:
                        ctx.fin(
                            vector.tensor_tensor(
                                dst, dst, ps_pv[h % 2][:], op=ALU.add
                            ),
                            "VE",
                        )

            ctx.on("vector", b)
            ctx.tag[f"kv2_used_{it}"] = ctx.c["MM"]
            ctx.flush()

        ctx.tag[f"qt2_used_{qc}"] = ctx.c["MM"]
        acc_ve = ctx.c["VE"]

        for batch in range(4):
            par = batch // 2
            alo = 4 * (batch % 2)
            rbc_used = ctx.t("rbc_used")
            r_ve = ctx.res("VE")

            def b(vector, batch=batch, par=par, alo=alo, acc_ve=acc_ve,
                  rbc_used=rbc_used):
                if batch == 0:
                    ctx.wait(vector, "VE", acc_ve)
                ctx.wait(vector, "MM", rbc_used)
                vector.drain()
                ctx.fin(
                    vector.reciprocal(
                        rden[:, :, :], ctxEO[par][64:65, alo : alo + 4, :]
                    ),
                    "VE",
                )

            ctx.on("vector", b)

            for i4 in range(4):
                a = alo + i4
                h = 2 * a + par
                base = 64 * par
                rbc_read2 = ctx.t("rbc_read")
                b_tag = ctx.res("MM")
                ctx.tag["rbc_used"] = b_tag

                def b(tensor, i4=i4, r_ve=r_ve, rbc_read2=rbc_read2):
                    if i4 == 0:
                        ctx.wait(tensor, "VE", r_ve)
                    ctx.wait(tensor, "VE", rbc_read2)
                    ctx.fin(
                        tensor.matmul(
                            ps_s[0][0:64, :], ones1f[:, 0:64],
                            rden[:, i4, :], start=True, stop=True,
                        ),
                        "MM",
                    )

                ctx.on("tensor", b)

                i = ctx.res("VE")
                ctx.tag["rbc_read"] = i

                def b(vector, par=par, a=a, base=base, b_tag=b_tag):
                    ctx.wait(vector, "MM", b_tag)
                    ctx.fin(
                        vector.tensor_tensor(
                            ctx_bf[base : base + 64, a, :],
                            ctxEO[par][0:64, a, :],
                            ps_s[0][0:64, :], op=ALU.mult,
                        ),
                        "VE",
                    )

                ctx.on("vector", b)

        norm_ve = ctx.c["VE"]

        lo_guard = ctx.t("pv_used_0")
        lo_mm = ctx.res("MM")

        def b(tensor, qc=qc, norm_ve=norm_ve, lo_guard=lo_guard, wo_tag=wo_tag):
            ctx.wait(tensor, "VE", norm_ve)
            if qc == 0:
                ctx.wait(tensor, "LW", wo_tag)
            ctx.wait(tensor, "VE", lo_guard)
            for a in range(DT):
                mm = tensor.matmul(
                    ps_pv[0][0:16, :], la_t[:, 3, a, :], ctx_bf[:, a, :],
                    start=(a == 0), stop=(a == DT - 1),
                )
            ctx.fin(mm, "MM")

        ctx.on("tensor", b)

        lo_se = ctx.res("SE")
        ctx.tag["lo_done_se"] = lo_se

        def b(scalar, lo_mm=lo_mm):
            ctx.wait(scalar, "MM", lo_mm)
            ctx.fin(
                scalar.activation(
                    lo_raw[:], ps_pv[0][0:16, :], AF.Identity, scale=1.0 / 16.0
                ),
                "SE",
            )

        ctx.on("scalar", b)

        for ot in range(DT):
            pso_free = ctx.t(f"pso_free_{ot % 2}")
            o_tag = ctx.res("MM")

            def b(tensor, qc=qc, ot=ot, pso_free=pso_free, lo_se=lo_se):
                if ot == 0:
                    ctx.wait(tensor, "SE", lo_se)
                ctx.wait(tensor, "VE", pso_free)
                for a in range(DT):
                    tensor.matmul(
                        ps_o[ot % 2], wo_t[:, a, ot * 128 : (ot + 1) * 128],
                        ctx_bf[:, a, :], start=(a == 0), stop=False,
                    )
                mm = tensor.matmul(
                    ps_o[ot % 2], lb_t[:, 3, ot * 128 : (ot + 1) * 128],
                    lo_raw[:], start=False, stop=True,
                )
                ctx.fin(mm, "MM")

            ctx.on("tensor", b)

            st_prev = ctx.t("out_stored")
            i = ctx.res("VE")
            ctx.tag[f"pso_free_{ot % 2}"] = i
            if ot == DT - 1:
                ctx.tag["out_done"] = i

            def b(vector, qc=qc, ot=ot, o_tag=o_tag, q_ld=q_ld, st_prev=st_prev):
                ctx.wait(vector, "MM", o_tag)
                ctx.wait(vector, "LD", q_ld)
                if ot == 0:
                    ctx.wait(vector, "ST", st_prev)
                ctx.fin(
                    vector.tensor_tensor(
                        outb[:, ot, :], ps_o[ot % 2], xf[:, ot, :], op=ALU.add
                    ),
                    "VE",
                )

            ctx.on("vector", b)

        out_ve = ctx.c["VE"]
        ctx.res("ST", 16)
        ctx.tag["out_stored"] = ctx.c["ST"]

        def b(sync, qc=qc, out_ve=out_ve):
            ctx.wait(sync, "VE", out_ve)
            sync.dma_start(out=A["y"][qc], in_=outb).then_inc(ctx.sems["ST"], 16)

        ctx.on("sync", b)
        ctx.flush()

    st_final = ctx.c["ST"]

    def b(sync):
        ctx.wait(sync, "ST", st_final)

    ctx.on("sync", b)
    ctx.flush()


def build_lora():
    nc = bass.Bass("TRN2", target_bir_lowering=False)
    A = lora_drams(nc)
    es = contextlib.ExitStack()
    with es, nc.Block() as block:
        ctx = Ctx(nc, es, block)
        emit_lora(nc, block, ctx, es, A)
    return nc


def lora_host_inputs(xt_core, W):
    import ml_dtypes

    m4 = np.zeros((128, 4, CH), dtype=np.float32)
    for kt in range(4):
        for kl in range(128):
            m4[kl, kt, :] = (np.arange(CH) >= (128 * kt + kl)).astype(np.float32)
    lat = np.stack(
        [tile_w(np.asarray(W["a" + nm]).T) for nm in ["q", "k", "v", "o"]], axis=1
    )
    lbt = np.stack(
        [np.asarray(W["b" + nm]).T for nm in ["q", "k", "v", "o"]], axis=1
    )
    return {
        "x": xt_core,
        "wqt": tile_w(np.asarray(W["wq"]).T),
        "wkt": tile_w(np.asarray(W["wk"]).T),
        "wvt": tile_w(np.asarray(W["wv"]).T),
        "wot": tile_w(np.asarray(W["wo"]).T),
        "lat": np.ascontiguousarray(lat),
        "lbt": np.ascontiguousarray(lbt),
        "mask4": m4.astype(ml_dtypes.bfloat16),
    }


def mla_host_inputs(xt_core, W):
    import ml_dtypes

    m4 = np.zeros((128, 4, CH), dtype=np.float32)
    for kt in range(4):
        for kl in range(128):
            m4[kl, kt, :] = (np.arange(CH) >= (128 * kt + kl)).astype(np.float32)
    return {
        "x": xt_core,
        "wdkv": tile_w(np.asarray(W["W_DKV"])),
        "wuk": tile_w(np.asarray(W["W_UK"])),
        "wuv": tile_w(np.asarray(W["W_UV"])),
        "wuq": tile_w(np.asarray(W["W_UQ"])),
        "wo": tile_w(np.asarray(W["W_O"])),
        "mask4": m4.astype(ml_dtypes.bfloat16),
    }



DEPTH6 = 6
DI = 2730
DI2 = 5460
MT1 = 22


def l1ffn_drams(nc, pfx="", io=True):
    A = {}
    if io:
        A["x"] = nc.dram_tensor(pfx + "x", [NCH, 128, DT, CH], F32, kind="ExternalInput").ap()
    A["wqkv"] = nc.dram_tensor(pfx + "wqkv", [DEPTH6, 128, DT, 3 * D], F32, kind="ExternalInput").ap()
    A["wod"] = nc.dram_tensor(pfx + "wod", [DEPTH6, 128, DT, D], F32, kind="ExternalInput").ap()
    A["w1d"] = nc.dram_tensor(pfx + "w1d", [DEPTH6, 128, DT, DI2], F32, kind="ExternalInput").ap()
    A["w2da"] = nc.dram_tensor(pfx + "w2da", [DEPTH6, 128, 21, D], F32, kind="ExternalInput").ap()
    A["w2db"] = nc.dram_tensor(pfx + "w2db", [DEPTH6, 42, D], F32, kind="ExternalInput").ap()
    A["qscol_d"] = nc.dram_tensor(pfx + "qscol", [128, DEPTH6, 2], F32, kind="ExternalInput").ap()
    A["kscol_d"] = nc.dram_tensor(pfx + "kscol", [128, DEPTH6, 2], F32, kind="ExternalInput").ap()
    A["lng_d"] = nc.dram_tensor(pfx + "lng", [128, DEPTH6, DT], F32, kind="ExternalInput").ap()
    A["lnb_d"] = nc.dram_tensor(pfx + "lnb", [128, DEPTH6, DT], F32, kind="ExternalInput").ap()
    A["b1a_d"] = nc.dram_tensor(pfx + "b1a", [128, DEPTH6, MT1], F32, kind="ExternalInput").ap()
    A["b1g_d"] = nc.dram_tensor(pfx + "b1g", [128, DEPTH6, MT1], F32, kind="ExternalInput").ap()
    A["b2_d"] = nc.dram_tensor(pfx + "b2c", [128, DEPTH6, DT], F32, kind="ExternalInput").ap()
    A["tabq_d"] = nc.dram_tensor(pfx + "tabq", [128, 2, 2, T], BF16, kind="ExternalInput").ap()
    A["tabk_d"] = nc.dram_tensor(pfx + "tabk", [128, 2, 2, T], BF16, kind="ExternalInput").ap()
    A["hp16_d"] = nc.dram_tensor(pfx + "hp16", [128, 4, 16], BF16, kind="ExternalInput").ap()
    A["hpbc_d"] = nc.dram_tensor(pfx + "hpbc", [16, 4, 128], BF16, kind="ExternalInput").ap()
    A["mdiag_d"] = nc.dram_tensor(pfx + "mdiag", [128, 128], F32, kind="ExternalInput").ap()
    A["mpast_d"] = nc.dram_tensor(pfx + "mpast", [128, 128], BF16, kind="ExternalInput").ap()
    if io:
        A["y"] = nc.dram_tensor(pfx + "y", [NCH, 128, DT, CH], F32, kind="ExternalOutput").ap()
    A["zh"] = [
        nc.dram_tensor(pfx + "zh0", [NCH, 128, DT, CH], F32, kind="Internal").ap(),
        nc.dram_tensor(pfx + "zh1", [NCH, 128, DT, CH], F32, kind="Internal").ap(),
    ]

    return A


def emit_l1ffn(nc, block, ctx, es, A, depth=DEPTH6):
    sb = lambda *a: es.enter_context(nc.sbuf_tensor(*a, allow_name_mangling=True))
    psa = lambda *a: es.enter_context(nc.psum_tensor(*a, allow_name_mangling=True))

    zf = sb("zf", [128, DT, CH], F32)
    zsb = sb("zsb", [128, DT, CH], BF16)
    pcol = sb("pcol", [128, 4, 1], F32)
    sq = sb("sqb", [128, 1, CH], BF16)
    bcb = sb("bcb", [128, 2 * CH], BF16)
    rowm = sb("rowm", [1, CH], F32)
    rowv = sb("rowv", [1, CH], F32)
    rows = sb("rows", [1, CH], F32)
    rowt2 = [rowm, rowv]
    onesb = sb("onesb", [128, 1], BF16)
    eps1 = sb("eps1", [1, 1], F32)
    ones1f = sb("ones1f", [1, 128], F32)
    w1_sb = sb("w1_sb", [128, DT, DI2], BF16)
    w2_sb = sb("w2_sb", [128, MT1, D], BF16)
    w1flat = w1_sb[:].rearrange("p a n -> p (a n)")
    w2flat = w2_sb[:].rearrange("p a n -> p (a n)")
    wqkv_sb = w1flat[:, 0 : DT * 3 * D].rearrange("p (a n) -> p a n", n=3 * D)
    wo_sb = w1flat[:, DT * 3 * D : DT * 4 * D].rearrange("p (a n) -> p a n", n=D)
    kt_ring = w2flat[:, 0 : 2 * DT * CH].rearrange("p (b a n) -> p b a n", b=2, n=CH)
    va_ring = w2flat[:, 2 * DT * CH : 2 * DT * CH + 8 * 1040].rearrange(
        "p (w f) -> p w f", f=1040
    )
    h2_sb = sb("h2_sb", [128, MT1, CH], BF16)
    h2flat = h2_sb[:].rearrange("p a n -> p (a n)")
    rqb = h2flat[:, 0 : DT * CH].rearrange("p (a n) -> p a n", n=CH)
    qraw = h2flat[:, DT * CH : 2 * DT * CH].rearrange("p (a n) -> p a n", n=CH)
    sc0 = h2flat[:, 2 * DT * CH : 2 * DT * CH + CH]
    sc1 = h2flat[:, 2 * DT * CH + CH : 2 * DT * CH + 2 * CH]
    sc2 = h2flat[:, 2 * DT * CH + 2 * CH : 2 * DT * CH + 3 * CH]
    xv = h2flat[:, 0 : 2 * DT * CH].bitcast(F32).rearrange("p (a n) -> p a n", n=CH)
    abuf = sb("abuf", [128, 2, CH], BF16)
    tabs = sb("tabs", [128, 2, 2, CH], BF16)
    gbuf = tabs[:, :, 0, :]
    rsrow = sb("rsrow", [16, CH], BF16)
    ctx_bf = sb("ctx_bf", [128, DT, CH], BF16)
    pT2 = h2flat[:, 2 * DT * CH + 3 * CH :].rearrange(
        "p (s r n) -> p s r n", s=4, r=3
    )
    rbsb = sb("rbsb", [64, 2, CH], BF16)
    ctxu = sb("ctxu", [64, 2, CH], BF16)
    qscol = sb("qscol_t", [128, DEPTH6, 2], F32)
    kscol = sb("kscol_t", [128, DEPTH6, 2], F32)
    lng = sb("lng_t", [128, DEPTH6, DT], F32)
    lnb = sb("lnb_t", [128, DEPTH6, DT], F32)
    b1a = sb("b1a_t", [128, DEPTH6, MT1], F32)
    b1g = sb("b1g_t", [128, DEPTH6, MT1], F32)
    b2c = sb("b2c_t", [128, DEPTH6, DT], F32)
    hp16 = sb("hp16_t", [128, 4, 16], BF16)
    hpbc = sb("hpbc_t", [16, 4, 128], BF16)
    mdiag = sb("mdiag_t", [128, 128], F32)
    mpast = sb("mpast_t", [128, 128], BF16)

    psA = psa("psA", [128, 2 * CH], F32)
    psB = psa("psB", [128, 2 * CH], F32)
    psC = psa("psC", [128, 2 * CH], F32)
    psD = psa("psD", [65, 2 * CH], F32)
    ps_stats = psA[0:1, :]
    ps_nq = psA[0:16, 0:CH]
    ps_rb2 = [psA[0:64, CH : 2 * CH], psA[64:128, 0:CH]]
    psBh = [psB[:, 0:CH], psB[:, CH : 2 * CH]]
    psCh = [psC[:, 0:CH], psC[:, CH : 2 * CH]]
    ps_st2 = [psC[:, 0:384], psC[:, CH : CH + 384],
              psB[:, 0:384], psB[:, CH : CH + 384]]
    psDh = [psD[:, 0:CH], psD[:, CH : 2 * CH]]


    bufs = dict(
        sq=sq, bc=bcb, rowm=rowm, rowv=rowv, rows=rows, onesb=onesb,
        ones1f=ones1f, ps_stats=ps_stats, ps_bc=psB, eps1=eps1,
    )

    const_ld = ctx.res("LD", 176)

    def b(sync):
        for dst, srcd in [
            (qscol, A["qscol_d"]), (kscol, A["kscol_d"]), (lng, A["lng_d"]), (lnb, A["lnb_d"]),
            (b1a, A["b1a_d"]), (b1g, A["b1g_d"]), (b2c, A["b2_d"]), (hp16, A["hp16_d"]),
            (hpbc, A["hpbc_d"]), (mdiag, A["mdiag_d"]), (mpast, A["mpast_d"]),
        ]:
            sync.dma_start(out=dst[:], in_=srcd).then_inc(ctx.sems["LD"], 16)

    ctx.on("sync", b)

    init_ve = ctx.res("VE")

    def b(vector):
        ctx.wait(vector, "LD", const_ld)
        vector.memset(onesb[:], 1.0)
        vector.memset(eps1[:], EPS_LN)
        ctx.fin(vector.memset(ones1f[:], 1.0), "VE")

    ctx.on("vector", b)

    def q_zload(zin, c, first_wait=0, mm_guard=0):
        zf_free = ctx.t("zf_free")
        z_store = ctx.t("z_store")
        z_phase = ctx.t("z_phase")
        pcol_prev = ctx.t("pcol_ve")
        ld_tag = ctx.res("LD", 16)

        def b(sync, c=c, zf_free=zf_free, z_store=z_store,
              first_wait=first_wait, pcol_prev=pcol_prev):
            ctx.wait(sync, "VE", zf_free)
            ctx.wait(sync, "VE", pcol_prev)
            ctx.wait(sync, "ZS", z_store)
            if first_wait:
                ctx.wait(sync, "VE", first_wait)
            sync.dma_start(out=zf[:], in_=zin[c]).then_inc(ctx.sems["LD"], 16)

        ctx.on("sync", b)

        zsb_used = ctx.t("zsb_used")
        ld2 = ctx.res("ZS", 32)

        def b(gpsimd, c=c, z_phase=z_phase, mm_guard=mm_guard,
              zsb_used=zsb_used):
            ctx.wait(gpsimd, "ZS", z_phase)
            ctx.wait(gpsimd, "MM", zsb_used)
            ctx.wait(gpsimd, "MM", mm_guard)
            gpsimd.dma_start(
                out=zsb[:, 0:4, :], in_=zin[c][:, 0:4, :]
            ).then_inc(ctx.sems["ZS"], 16)
            gpsimd.dma_start(
                out=zsb[:, 4:8, 1:CH], in_=zin[c][:, 4:8, 0 : CH - 1]
            ).then_inc(ctx.sems["ZS"], 16)

        ctx.on("gpsimd", b)

        zs_ve = ctx.res("VE")
        pv_tag = ctx.res("VE")
        ctx.tag["pcol_ve"] = pv_tag

        def b(vector, c=c, ld2=ld2, ld_tag=ld_tag):
            ctx.wait(vector, "ZS", ld2)
            if c == 0:
                ctx.fin(vector.memset(zsb[:, 4:8, 0:1], 0.0), "VE")
            else:
                ctx.fin(
                    vector.tensor_copy(
                        zsb[:, 4:8, 0:1].rearrange("p a n -> p (a n)"),
                        pcol[:].rearrange("p a n -> p (a n)"),
                    ),
                    "VE",
                )
            ctx.wait(vector, "LD", ld_tag)
            ctx.fin(
                vector.tensor_copy(
                    pcol[:].rearrange("p a n -> p (a n)"),
                    zf[:, 4:8, CH - 1 : CH].rearrange("p a n -> p (a n)"),
                ),
                "VE",
            )

        ctx.on("vector", b)
        return ld_tag, ("VE", zs_ve)

    def q_qk(L, c, which, dst, w_off, ve_need):
        raw_tags = {}
        for ot in range(DT):
            Btag = ctx.tt(f"Bps{ot % 2}")
            mm_tag = ctx.res("MM")

            def b(tensor, ot=ot, ve_need=ve_need, Btag=Btag, w_off=w_off):
                if ot == 0:
                    ctx.waitp(tensor, ve_need)
                ctx.waitp(tensor, Btag)
                for a in range(DT):
                    mm = tensor.matmul(
                        psBh[ot % 2],
                        wqkv_sb[:, a, w_off + ot * 128 : w_off + (ot + 1) * 128],
                        zsb[:, a, :], start=(a == 0), stop=(a == DT - 1),
                    )
                ctx.fin(mm, "MM")

            ctx.on("tensor", b)

            qraw_free = ctx.t("qraw_free")
            i = ctx.res("SE")
            ctx.sett(f"Bps{ot % 2}", "SE", i)
            raw_tags[ot] = i

            def b(scalar, ot=ot, mm_tag=mm_tag, qraw_free=qraw_free):
                ctx.wait(scalar, "MM", mm_tag)
                if ot < 2:
                    ctx.wait(scalar, "VE", qraw_free)
                ctx.fin(scalar.copy(qraw[:, ot, :], psBh[ot % 2]), "SE")

            ctx.on("scalar", b)

        raw_se = ctx.c["SE"]

        sq_mm = {}
        for ot in range(DT):
            sqg = ctx.t("sq_guard")
            se_tag = ctx.res("SE")

            def b(scalar, ot=ot, sqg=sqg):
                if ot == 0:
                    ctx.wait(scalar, "MM", sqg)
                else:
                    ctx.wait(scalar, "MM", sq_mm[ot - 1])
                ctx.fin(scalar.square(sq[:, 0, :], qraw[:, ot, :]), "SE")

            ctx.on("scalar", b)

            Anq = ctx.t("Anq")
            stats_free = ctx.t("stats_free")
            sq_mm[ot] = ctx.res("MM")

            def b(tensor, ot=ot, se_tag=se_tag, Anq=Anq, stats_free=stats_free):
                ctx.wait(tensor, "SE", se_tag)
                if ot == 0:
                    ctx.wait(tensor, "VE", Anq)
                    ctx.wait(tensor, "VE", stats_free)
                ctx.fin(
                    tensor.matmul(
                        ps_nq, hp16[:, ot % 4, :], sq[:, 0, :],
                        start=(ot == 0), stop=(ot == DT - 1),
                    ),
                    "MM",
                )

            ctx.on("tensor", b)

        ctx.tag["sq_guard"] = ctx.c["MM"]
        nq_mm = ctx.c["MM"]
        rs_free = ctx.t("rsrow_free")
        nq_ve = ctx.res("VE")
        ctx.tag["Anq"] = nq_ve

        def b(vector, nq_mm=nq_mm, rs_free=rs_free):
            ctx.wait(vector, "MM", nq_mm)
            ctx.wait(vector, "MM", rs_free)
            ctx.fin(
                vector.tensor_scalar(rsrow[:], ps_nq, 1e-24, None, op0=ALU.max),
                "VE",
            )

        ctx.on("vector", b)

        rs_se = ctx.res("SE")

        def b(scalar, nq_ve=nq_ve):
            ctx.wait(scalar, "VE", nq_ve)
            scalar.activation(rsrow[:], rsrow[:], AF.Ln)
            scalar.drain()
            ctx.fin(scalar.activation(rsrow[:], rsrow[:], AF.Exp, scale=-0.5),
                    "SE")

        ctx.on("scalar", b)

        tabs_free = ctx.t("tabs_free")
        tab_ld = ctx.res("LD", 16)

        def b(sync, c=c, which=which, tabs_free=tabs_free):
            ctx.wait(sync, "VE", tabs_free)
            tab = A["tabq_d"] if which == "q" else A["tabk_d"]
            sync.dma_start(
                out=tabs[:], in_=tab[:, :, :, c * CH : (c + 1) * CH]
            ).then_inc(ctx.sems["LD"], 16)

        ctx.on("sync", b)

        scol = qscol if which == "q" else kscol
        rope_done = 0
        rope_slots = [psCh[0], psCh[1], psBh[0], psBh[1]]
        rope_names = ["Cps0", "Cps1", "Bps0", "Bps1"]
        for ot in range(DT):
            g = 0 if ot < 4 else 1
            gp = 1 - g
            pair = ot + 4 if ot < 4 else ot - 4
            sl = ot % 4
            Ctag = ctx.tt(rope_names[sl])
            bc_mm = ctx.res("MM")

            def b(tensor, sl=sl, rs_se=rs_se, Ctag=Ctag):
                ctx.wait(tensor, "SE", rs_se)
                ctx.waitp(tensor, Ctag)
                ctx.fin(
                    tensor.matmul(
                        rope_slots[sl], hpbc[:, sl, :], rsrow[:],
                        start=True, stop=True,
                    ),
                    "MM",
                )

            ctx.on("tensor", b)

            i = ctx.res("VE")
            ctx.sett(rope_names[sl], "VE", i)
            rope_done = i

            def b(vector, L=L, ot=ot, sl=sl, g=g, gp=gp, pair=pair, bc_mm=bc_mm,
                  tab_ld=tab_ld, raw_se=raw_se, which=which):
                ctx.wait(vector, "MM", bc_mm)
                if ot == 0:
                    ctx.wait(vector, "LD", tab_ld)
                    ctx.wait(vector, "SE", raw_se)
                vector.scalar_tensor_tensor(
                    sc0, qraw[:, ot, :], scol[:, L, g : g + 1], tabs[:, g, 0, :],
                    op0=ALU.mult, op1=ALU.mult,
                )
                vector.scalar_tensor_tensor(
                    sc1, qraw[:, pair, :], scol[:, L, gp : gp + 1],
                    tabs[:, g, 1, :], op0=ALU.mult, op1=ALU.mult,
                )
                vector.drain()
                vector.tensor_tensor(sc2, sc0, sc1, op=ALU.add)
                vector.drain()
                ctx.fin(
                    vector.tensor_tensor(
                        dst[:, ot, :], sc2, rope_slots[sl], op=ALU.mult
                    ),
                    "VE",
                )

            ctx.on("vector", b)

        ctx.tag["rsrow_free"] = ctx.c["MM"]
        ctx.tag["qraw_free"] = rope_done
        ctx.tag["tabs_free"] = rope_done
        return rope_done

    for c in range(NCH):
        zf_free = ctx.t("zf_free")
        z_store = ctx.t("z_store")
        ld_tag = ctx.res("LD", 16)

        def b(sync, c=c, zf_free=zf_free, z_store=z_store, init_ve=init_ve):
            if c == 0:
                ctx.wait(sync, "VE", init_ve)
            ctx.wait(sync, "VE", zf_free)
            ctx.wait(sync, "ZS", z_store)
            sync.dma_start(out=zf[:], in_=A["x"][c]).then_inc(ctx.sems["LD"], 16)

        ctx.on("sync", b)

        ve_tag = q_ln_chunk(ctx, bufs, zf[:], zsb[:], "LD", ld_tag)

        z0_ve = ctx.res("VE")
        ctx.tag["zf_free"] = z0_ve

        def b(vector, ve_tag=ve_tag):
            ctx.wait(vector, "VE", ve_tag)
            vector.drain()
            for a in range(DT):
                vector.tensor_tensor(
                    zf[:, a, :], zf[:, a, :], bcb[:, 0:CH], op=ALU.mult
                )
            vector.drain()
            for a in range(DT):
                mm = vector.tensor_tensor(
                    zf[:, a, :], zf[:, a, :], bcb[:, CH : 2 * CH], op=ALU.add
                )
            ctx.fin(mm, "VE")

        ctx.on("vector", b)

        ctx.res("ZS", 16)
        ctx.tag["z_store"] = ctx.c["ZS"]

        def b(gpsimd, c=c, z0_ve=z0_ve):
            ctx.wait(gpsimd, "VE", z0_ve)
            gpsimd.dma_start(out=A["zh"][0][c], in_=zf[:]).then_inc(
                ctx.sems["ZS"], 16
            )

        ctx.on("gpsimd", b)
        ctx.flush()

    for L in range(depth):
        for phase in ("attn", "ffn"):
            step = 2 * L + (0 if phase == "attn" else 1)
            zin = A["zh"][step % 2]
            zout = A["zh"][(step + 1) % 2]
            ctx.tag["sq_guard"] = ctx.c["MM"]
            phase_mm = ctx.c["MM"]
            phase_se = ctx.c["SE"]
            phase_ve = ctx.c["VE"]
            phase_gp = ctx.c["GP"]

            ctx.tag["z_phase"] = ctx.c["ZS"]
            w_tag = ctx.res("LW", 32 if phase == "attn" else 48)

            phase_ld = ctx.c["LD"]

            def b(gpsimd, L=L, phase=phase, phase_mm=phase_mm,
                  phase_se=phase_se, phase_ve=phase_ve, phase_ld=phase_ld):
                ctx.wait(gpsimd, "MM", phase_mm)
                ctx.wait(gpsimd, "SE", phase_se)
                ctx.wait(gpsimd, "VE", phase_ve)
                ctx.wait(gpsimd, "LD", phase_ld)
                if phase == "attn":
                    gpsimd.dma_start(out=wqkv_sb, in_=A["wqkv"][L]).then_inc(
                        ctx.sems["LW"], 16
                    )
                    gpsimd.dma_start(out=wo_sb, in_=A["wod"][L]).then_inc(
                        ctx.sems["LW"], 16
                    )
                else:
                    gpsimd.dma_start(out=w1_sb[:], in_=A["w1d"][L]).then_inc(
                        ctx.sems["LW"], 16
                    )
                    gpsimd.dma_start(out=w2_sb[:, 0:21, :], in_=A["w2da"][L]).then_inc(
                        ctx.sems["LW"], 16
                    )
                    gpsimd.dma_start(out=w2_sb[0:42, 21, :], in_=A["w2db"][L]).then_inc(
                        ctx.sems["LW"], 16
                    )

            ctx.on("gpsimd", b)

            if phase == "attn":
                for c in range(NCH):
                    s_prev = ctx.t(f"s_done_{c-1}")
                    ld_tag, zs_ve = q_zload(zin, c, mm_guard=s_prev)

                    if c == 0:
                        def b(tensor, w_tag=w_tag):
                            ctx.wait(tensor, "LW", w_tag)
                        ctx.on("tensor", b)

                    q_qk(L, c, "k", kt_ring[:, c % 2, :, :], D, zs_ve)
                    q_ve = q_qk(L, c, "q", rqb, 0, zs_ve)

                    if c == 0 and L == 0:
                        ones_ve = ctx.res("VE")

                        def b(vector, phase_mm=phase_mm):
                            ctx.wait(vector, "MM", phase_mm)
                            ctx.fin(
                                vector.memset(
                                    va_ring.rearrange(
                                        "p w (h e) -> p w h e", e=65
                                    )[:, :, :, 64:65],
                                    1.0,
                                ),
                                "VE",
                            )

                        ctx.on("vector", b)
                    elif c == 0:
                        ones_ve = ctx.res("VE")
                        lw_now = ctx.c["LW"]

                        def b(vector, lw_now=lw_now):
                            ctx.wait(vector, "LW", lw_now)
                            ctx.fin(
                                vector.memset(
                                    va_ring.rearrange(
                                        "p w (h e) -> p w h e", e=65
                                    )[:, :, :, 64:65],
                                    1.0,
                                ),
                                "VE",
                            )

                        ctx.on("vector", b)
                    else:
                        ones_ve = 0

                    v_prev = 0
                    for tt4 in range(4):
                        Cp0 = ctx.tt("Cps0") if tt4 == 0 else None
                        Cp1 = ctx.tt("Cps1") if tt4 == 0 else None
                        vp = v_prev
                        v_mm = ctx.res("MM")

                        def b(tensor, c=c, tt4=tt4, zs_ve=zs_ve,
                              ones_ve=ones_ve, Cp0=Cp0, Cp1=Cp1, vp=vp):
                            if tt4 == 0:
                                ctx.waitp(tensor, zs_ve)
                                ctx.wait(tensor, "VE", ones_ve)
                                ctx.waitp(tensor, Cp0)
                                ctx.waitp(tensor, Cp1)
                            else:
                                ctx.wait(tensor, "VE", vp)
                            for half in range(2):
                                for a in range(DT):
                                    mm = tensor.matmul(
                                        psC[:, half * CH : (half + 1) * CH],
                                        zsb[:, a, tt4 * 128 : (tt4 + 1) * 128],
                                        wqkv_sb[
                                            :, a,
                                            2 * D + half * CH
                                            : 2 * D + (half + 1) * CH,
                                        ],
                                        start=(a == 0), stop=(a == DT - 1),
                                    )
                            ctx.fin(mm, "MM")

                        ctx.on("tensor", b)

                        i = ctx.res("VE")
                        v_prev = i
                        if tt4 == 3:
                            ctx.sett("Cps0", "VE", i)
                            ctx.sett("Cps1", "VE", i)

                        def b(vector, c=c, tt4=tt4, v_mm=v_mm):
                            ctx.wait(vector, "MM", v_mm)
                            ctx.fin(
                                vector.tensor_copy(
                                    va_ring.rearrange(
                                        "p (b w) f -> p b w f", b=2
                                    )[:, c % 2, tt4, :].rearrange(
                                        "p (h e) -> p h e", e=65
                                    )[:, :, 0:64],
                                    psC[:].rearrange("p (h e) -> p h e", e=64),
                                ),
                                "VE",
                            )

                        ctx.on("vector", b)

                    kvv_ve = ctx.c["VE"]
                    ctx.tag["zsb_used"] = ctx.c["MM"]

                    units = []
                    for hi in range(4):
                        for pr in ((0, 2), (1, 3)):
                            for j in range(4):
                                for hm in pr:
                                    h = 4 * hi + hm
                                    units.append(
                                        (h, j, [4 * c + j - 2 + r for r in range(3)])
                                    )
                    dslot = lambda h: (h % 4) // 2
                    st_free0 = [ctx.t("st_free_se0"), ctx.t("st_free_se1"),
                                0, 0]
                    bps_first = [ctx.tt("Bps0"), ctx.tt("Bps1")]
                    rb_prev0 = [ctx.t("rb_se0"), ctx.t("rb_se1")]
                    D_prev = [ctx.t("D0se"), ctx.t("D1se")]
                    wo_done = ctx.t("wo_done")
                    cm_prev0 = [ctx.t("gp_cm0"), ctx.t("gp_cm1")]
                    mm_s = {}
                    mm_pv = {}
                    mm_rb = {}
                    ve_dm = {}
                    se_exp = {}
                    se_r = {}
                    se_cu = {}
                    gp_pm = {}
                    gp_cm = {}
                    lhseq = [u[0] for u in units if u[1] == 3]
                    hslot = {h: i % 2 for i, h in enumerate(lhseq)}
                    cu_after = {}
                    _pend = []
                    for ui, (h, j, gk) in enumerate(units):
                        if _pend:
                            cu_after[ui] = _pend.pop(0)
                        if j == 3:
                            _pend.append(h)
                    tail_cu = list(_pend)
                    for ui, (h, j, gk) in enumerate(units):
                        ve_dm[(h, j)] = ctx.res("VE")
                        se_exp[(h, j)] = ctx.res("SE")
                        if gk[0] >= 0:
                            gp_pm[(h, j)] = ctx.res("GP")
                        if j == 3:
                            se_r[h] = ctx.res("SE")
                        if ui in cu_after:
                            se_cu[cu_after[ui]] = ctx.res("SE")
                            gp_cm[cu_after[ui]] = ctx.res("GP")
                    for h in tail_cu:
                        se_cu[h] = ctx.res("SE")
                        gp_cm[h] = ctx.res("GP")
                    LAG = 3
                    sched = []
                    rbq = []
                    nu = len(units)
                    for ui in range(nu + LAG):
                        if ui < nu:
                            sched.append(("S", ui))
                        if ui >= LAG:
                            if rbq:
                                sched.append(("RB", rbq.pop(0)))
                            sched.append(("PV", ui - LAG))
                            if units[ui - LAG][1] == 3:
                                rbq.append(units[ui - LAG][0])
                    for h in rbq:
                        sched.append(("RB", h))
                    for kind, v in sched:
                        if kind == "S":
                            mm_s[units[v][:2]] = ctx.res("MM")
                        elif kind == "PV":
                            mm_pv[units[v][:2]] = ctx.res("MM")
                        else:
                            mm_rb[v] = ctx.res("MM")
                    uex = lambda k: se_exp[(units[k][0], units[k][1])]
                    ctx.tag["st_free_se0"] = uex(nu - 4)
                    ctx.tag["st_free_se1"] = uex(nu - 3)
                    ctx.sett("Cps0", "SE", uex(nu - 4))
                    ctx.sett("Cps1", "SE", uex(nu - 3))
                    ctx.sett("Bps0", "SE", uex(nu - 2))
                    ctx.sett("Bps1", "SE", uex(nu - 1))
                    dlast = {}
                    for h in lhseq:
                        dlast[dslot(h)] = h
                    ctx.tag["rb_se"] = se_cu[lhseq[-1]]
                    ctx.tag["rb_se0"] = se_cu[lhseq[-2]]
                    ctx.tag["rb_se1"] = se_cu[lhseq[-1]]
                    ctx.tag["D0se"] = se_cu[dlast[0]]
                    ctx.tag["D1se"] = se_cu[dlast[1]]
                    ctx.tag["gp_cm0"] = gp_cm[lhseq[-2]]
                    ctx.tag["gp_cm1"] = gp_cm[lhseq[-1]]
                    attn_gp = max(gp_cm.values())

                    def b(tensor, c=c, units=units, kvv_ve=kvv_ve, q_ve=q_ve,
                          mm_s=mm_s, mm_pv=mm_pv, se_exp=se_exp, gp_pm=gp_pm,
                          se_cu=se_cu, se_r=se_r, mm_rb=mm_rb,
                          st_free0=st_free0, bps_first=bps_first,
                          rb_prev0=rb_prev0, D_prev=D_prev,
                          dslot=dslot, hslot=hslot, sched=sched):
                        prev_exp = list(st_free0)
                        prev_cu = {0: D_prev[0], 1: D_prev[1]}
                        prev_rb = list(rb_prev0)

                        def emit_S(ui):
                            h, j, gk = units[ui]
                            b0 = 32 * (h % 4)
                            tg0, tg1 = h // 4, 4 + h // 4
                            slot = ui % 4
                            if ui in (2, 3):
                                ctx.waitp(tensor, bps_first[ui - 2])
                            ctx.wait(tensor, "SE", prev_exp[slot])
                            for r in range(3):
                                g = gk[r]
                                if g < 0:
                                    continue
                                kc, ktl = g // 4, g % 4
                                for gg, tgx in ((0, tg0), (1, tg1)):
                                    mm = tensor.matmul(
                                        ps_st2[slot][:, r * 128 : (r + 1) * 128],
                                        kt_ring[
                                            b0 : b0 + 32, kc % 2, tgx,
                                            ktl * 128 : (ktl + 1) * 128,
                                        ],
                                        rqb[
                                            b0 : b0 + 32, tgx,
                                            j * 128 : (j + 1) * 128,
                                        ],
                                        start=(gg == 0), stop=(gg == 1),
                                        tile_position=(b0, 0),
                                    )
                            ctx.fin(mm, "MM")
                            prev_exp[slot] = se_exp[(h, j)]

                        def emit_PV(ui):
                            h, j, gk = units[ui]
                            slot = ui % 4
                            ds = dslot(h)
                            if gk[0] >= 0:
                                ctx.wait(tensor, "GP", gp_pm[(h, j)])
                            else:
                                ctx.wait(tensor, "SE", se_exp[(h, j)])
                            if j == 0:
                                ctx.wait(tensor, "SE", prev_cu[ds])
                            first = True
                            for r in range(3):
                                g = gk[r]
                                if g < 0:
                                    continue
                                kc, ktl = g // 4, g % 4
                                mm = tensor.matmul(
                                    psDh[ds][:, j * 128 : (j + 1) * 128],
                                    va_ring.rearrange(
                                        "p (b w) f -> p b w f", b=2
                                    )[:, kc % 2, ktl, h * 65 : (h + 1) * 65],
                                    pT2[:, slot, r, :],
                                    start=first, stop=(r == 2),
                                )
                                first = False
                            ctx.fin(mm, "MM")
                            if j == 3:
                                prev_cu[ds] = se_cu[h]

                        def emit_RB(h):
                            hs = hslot[h]
                            ctx.wait(tensor, "SE", se_r[h])
                            ctx.wait(tensor, "SE", prev_rb[hs])
                            prev_rb[hs] = se_cu[h]
                            ctx.fin(
                                tensor.matmul(
                                    ps_rb2[hs], ones1f[:, 0:64],
                                    rowt2[hs][:],
                                    start=True, stop=True,
                                ),
                                "MM",
                            )

                        ctx.wait(tensor, "VE", kvv_ve)
                        ctx.wait(tensor, "VE", q_ve)
                        for kind, v in sched:
                            if kind == "S":
                                emit_S(v)
                            elif kind == "PV":
                                emit_PV(v)
                            else:
                                emit_RB(v)

                    ctx.on("tensor", b)

                    def b(scalar, units=units, ve_dm=ve_dm, mm_pv=mm_pv,
                          mm_rb=mm_rb, gp_cm=gp_cm, cm_prev0=cm_prev0,
                          dslot=dslot, hslot=hslot, cu_after=cu_after,
                          tail_cu=tail_cu):
                        pv_hist = [0, 0, 0, 0]
                        prev_cm = list(cm_prev0)

                        def emit_cu(h):
                            ds = dslot(h)
                            hs = hslot[h]
                            ctx.wait(scalar, "MM", mm_rb[h])
                            ctx.wait(scalar, "GP", prev_cm[hs])
                            prev_cm[hs] = gp_cm[h]
                            scalar.copy(rbsb[:, hs, :], ps_rb2[hs])
                            ctx.fin(
                                scalar.copy(
                                    ctxu[:, hs, :], psDh[ds][0:64, :]
                                ),
                                "SE",
                            )

                        for ui, (h, j, gk) in enumerate(units):
                            slot = ui % 4
                            ctx.wait(scalar, "VE", ve_dm[(h, j)])
                            ctx.wait(scalar, "MM", pv_hist[slot])
                            ctx.fin(
                                scalar.activation(
                                    pT2[:, slot, :, :].rearrange(
                                        "p a n -> p (a n)"
                                    ),
                                    ps_st2[slot][:], AF.Exp, scale=8.0,
                                ),
                                "SE",
                            )
                            pv_hist[slot] = mm_pv[(h, j)]
                            if j == 3:
                                ds = dslot(h)
                                hs = hslot[h]
                                ctx.wait(scalar, "MM", mm_pv[(h, 3)])
                                scalar.activation(
                                    rowt2[hs][:], psDh[ds][64:65, :], AF.Ln
                                )
                                scalar.drain()
                                ctx.fin(
                                    scalar.activation(
                                        rowt2[hs][:], rowt2[hs][:],
                                        AF.Exp, scale=-1.0
                                    ),
                                    "SE",
                                )
                            if ui in cu_after:
                                emit_cu(cu_after[ui])
                        for h in tail_cu:
                            emit_cu(h)

                    ctx.on("scalar", b)

                    def b(vector, units=units, mm_s=mm_s):
                        for ui, (h, j, gk) in enumerate(units):
                            slot = ui % 4
                            ctx.wait(vector, "MM", mm_s[(h, j)])
                            ctx.fin(
                                vector.tensor_tensor(
                                    ps_st2[slot][:, 256:384],
                                    ps_st2[slot][:, 256:384],
                                    mdiag[:], op=ALU.add,
                                ),
                                "VE",
                            )

                    ctx.on("vector", b)

                    def b(gpsimd, units=units, se_exp=se_exp, se_cu=se_cu,
                          gp_pm=gp_pm, wo_done=wo_done, hslot=hslot,
                          cu_after=cu_after, tail_cu=tail_cu):
                        first = [True]

                        def emit_cm(h):
                            hs = hslot[h]
                            ctx.wait(gpsimd, "SE", se_cu[h])
                            if first[0]:
                                ctx.wait(gpsimd, "MM", wo_done)
                                first[0] = False
                            ctx.fin(
                                gpsimd.tensor_mul(
                                    ctx_bf[
                                        64 * (h % 2) : 64 * (h % 2) + 64,
                                        h // 2, :,
                                    ],
                                    ctxu[:, hs, :], rbsb[:, hs, :],
                                ),
                                "GP",
                            )

                        for ui, (h, j, gk) in enumerate(units):
                            slot = ui % 4
                            if gk[0] >= 0:
                                ctx.wait(gpsimd, "SE", se_exp[(h, j)])
                                ctx.fin(
                                    gpsimd.tensor_mul(
                                        pT2[:, slot, 0, :],
                                        pT2[:, slot, 0, :], mpast[:],
                                    ),
                                    "GP",
                                )
                            if ui in cu_after:
                                emit_cm(cu_after[ui])
                        for h in tail_cu:
                            emit_cm(h)

                    ctx.on("gpsimd", b)

                    for ot in range(DT):
                        Bp = ctx.tt(f"Bps{ot % 2}")
                        o_mm = ctx.res("MM")
                        if ot == DT - 1:
                            ctx.tag["wo_done"] = o_mm

                        def b(tensor, ot=ot, attn_gp=attn_gp, Bp=Bp):
                            if ot == 0:
                                ctx.wait(tensor, "GP", attn_gp)
                            ctx.waitp(tensor, Bp)
                            for a in range(DT):
                                mm = tensor.matmul(
                                    psBh[ot % 2],
                                    wo_sb[:, a, ot * 128 : (ot + 1) * 128],
                                    ctx_bf[:, a, :],
                                    start=(a == 0), stop=(a == DT - 1),
                                )
                            ctx.fin(mm, "MM")

                        ctx.on("tensor", b)

                        i = ctx.res("VE")
                        ctx.sett(f"Bps{ot % 2}", "VE", i)
                        if ot == DT - 1:
                            ctx.tag["zf_free"] = i

                        def b(vector, ot=ot, o_mm=o_mm):
                            ctx.wait(vector, "MM", o_mm)
                            ctx.fin(
                                vector.tensor_tensor(
                                    zf[:, ot, :], zf[:, ot, :], psBh[ot % 2],
                                    op=ALU.add,
                                ),
                                "VE",
                            )

                        ctx.on("vector", b)

                    zn_ve = ctx.c["VE"]
                    ctx.res("ZS", 16)
                    ctx.tag["z_store"] = ctx.c["ZS"]
                    ctx.tag[f"s_done_{c}"] = ctx.c["MM"]

                    def b(gpsimd, c=c, zn_ve=zn_ve, zout=zout):
                        ctx.wait(gpsimd, "VE", zn_ve)
                        gpsimd.dma_start(out=zout[c], in_=zf[:]).then_inc(
                            ctx.sems["ZS"], 16
                        )

                    ctx.on("gpsimd", b)
                    ctx.flush()

            else:
                for c in range(NCH):
                    ld_tag, zs_ve = q_zload(zin, c)

                    Bp0 = ctx.tt("Bps0")
                    Bp1 = ctx.tt("Bps1")
                    Anq = ctx.t("Anq")
                    rb_se = ctx.t("rb_se")

                    def extra_bc(tensor, Bp0=Bp0, Bp1=Bp1):
                        ctx.waitp(tensor, Bp0)
                        ctx.waitp(tensor, Bp1)

                    def extra_stats(tensor, Anq=Anq, rb_se=rb_se):
                        ctx.wait(tensor, "VE", Anq)
                        ctx.wait(tensor, "SE", rb_se)

                    ve_tag = q_ln_chunk(
                        ctx, dict(bufs, sq=ctx_bf[:, 0:2, :], nsq=2),
                        None, zsb[:], "LD", ld_tag,
                        ready_ve=zs_ve, extra_bc=extra_bc,
                        extra_stats=extra_stats,
                    )

                    xn_ve = ctx.res("VE")

                    def b(vector, L=L, ve_tag=ve_tag):
                        ctx.wait(vector, "VE", ve_tag)
                        vector.drain()
                        for a in range(DT):
                            mm = vector.tensor_scalar(
                                zsb[:, a, :], zsb[:, a, :],
                                lng[:, L, a : a + 1], lnb[:, L, a : a + 1],
                                op0=ALU.mult, op1=ALU.add,
                            )
                        ctx.fin(mm, "VE")

                    ctx.on("vector", b)

                    for m in range(MT1):
                        rows_n = 128 if m < MT1 - 1 else 42
                        for part in range(2):
                            coff = part * DI + m * 128
                            Bp = ctx.tt(f"Bps{part}")
                            hg_mm = ctx.res("MM")

                            def b(tensor, c=c, m=m, part=part, coff=coff,
                                  rows_n=rows_n, xn_ve=xn_ve, Bp=Bp,
                                  w_tag=w_tag):
                                if m == 0 and part == 0:
                                    ctx.wait(tensor, "VE", xn_ve)
                                    if c == 0:
                                        ctx.wait(tensor, "LW", w_tag)
                                ctx.waitp(tensor, Bp)
                                for a in range(DT):
                                    mm = tensor.matmul(
                                        psBh[part][0:rows_n, :],
                                        w1_sb[:, a, coff : coff + rows_n],
                                        zsb[:, a, :],
                                        start=(a == 0), stop=(a == DT - 1),
                                    )
                                ctx.fin(mm, "MM")

                            ctx.on("tensor", b)

                            if part == 0:
                                ab_gp = ctx.t(f"ab_gp_{m % 2}")
                                i = ctx.res("VE")
                                ctx.sett("Bps0", "VE", i)

                                def b(vector, L=L, m=m, rows_n=rows_n,
                                      hg_mm=hg_mm, ab_gp=ab_gp):
                                    ctx.wait(vector, "MM", hg_mm)
                                    ctx.wait(vector, "GP", ab_gp)
                                    ctx.fin(
                                        vector.tensor_scalar(
                                            abuf[0:rows_n, m % 2, :],
                                            psBh[0][0:rows_n, :],
                                            b1a[0:rows_n, L, m : m + 1], None,
                                            op0=ALU.add,
                                        ),
                                        "VE",
                                    )

                                ctx.on("vector", b)
                            else:
                                gb_gp = ctx.t(f"ab_gp_{m % 2}")
                                i = ctx.res("SE")
                                ctx.sett("Bps1", "SE", i)

                                def b(scalar, L=L, m=m, rows_n=rows_n,
                                      hg_mm=hg_mm, gb_gp=gb_gp):
                                    ctx.wait(scalar, "MM", hg_mm)
                                    ctx.wait(scalar, "GP", gb_gp)
                                    ctx.fin(
                                        scalar.activation(
                                            gbuf[0:rows_n, m % 2, :],
                                            psBh[1][0:rows_n, :],
                                            AF.Gelu,
                                            bias=b1g[0:rows_n, L, m : m + 1],
                                        ),
                                        "SE",
                                    )

                                ctx.on("scalar", b)

                        ab_ve = ctx.c["VE"]
                        g_se = ctx.c["SE"]
                        h2_used = ctx.t("h2_used")
                        i = ctx.res("GP")
                        ctx.tag[f"ab_gp_{m % 2}"] = i

                        def b(gpsimd, m=m, rows_n=rows_n, ab_ve=ab_ve,
                              g_se=g_se, h2_used=h2_used):
                            ctx.wait(gpsimd, "VE", ab_ve)
                            ctx.wait(gpsimd, "SE", g_se)
                            ctx.wait(gpsimd, "MM", h2_used)
                            ctx.fin(
                                gpsimd.tensor_mul(
                                    h2_sb[0:rows_n, m, :],
                                    abuf[0:rows_n, m % 2, :],
                                    gbuf[0:rows_n, m % 2, :],
                                ),
                                "GP",
                            )

                        ctx.on("gpsimd", b)

                    h2_gp = ctx.c["GP"]
                    ctx.tag["zsb_used"] = ctx.c["MM"]

                    wslots = [psCh[0], psCh[1], psBh[0], psBh[1]]
                    wnames = ["Cps0", "Cps1", "Bps0", "Bps1"]
                    for ot in range(DT):
                        sl = ot % 4
                        Cp = ctx.tt(wnames[sl])
                        o_mm = ctx.res("MM")
                        if ot == DT - 1:
                            ctx.tag["h2_used"] = o_mm

                        def b(tensor, ot=ot, sl=sl, h2_gp=h2_gp, Cp=Cp):
                            if ot == 0:
                                ctx.wait(tensor, "GP", h2_gp)
                            ctx.waitp(tensor, Cp)
                            for m in range(MT1):
                                rows_n = 128 if m < MT1 - 1 else 42
                                mm = tensor.matmul(
                                    wslots[sl],
                                    w2_sb[0:rows_n, m, ot * 128 : (ot + 1) * 128],
                                    h2_sb[0:rows_n, m, :],
                                    start=(m == 0), stop=(m == MT1 - 1),
                                )
                            ctx.fin(mm, "MM")

                        ctx.on("tensor", b)

                        i = ctx.res("VE")
                        ctx.sett(wnames[sl], "VE", i)
                        if ot == DT - 1:
                            ctx.tag["zf_free"] = i

                        def b(vector, L=L, ot=ot, sl=sl, o_mm=o_mm):
                            ctx.wait(vector, "MM", o_mm)
                            ctx.fin(
                                vector.scalar_tensor_tensor(
                                    zf[:, ot, :], wslots[sl],
                                    b2c[:, L, ot : ot + 1], zf[:, ot, :],
                                    op0=ALU.add, op1=ALU.add,
                                ),
                                "VE",
                            )

                        ctx.on("vector", b)

                    zn_ve = ctx.c["VE"]
                    ctx.res("ZS", 16)
                    ctx.tag["z_store"] = ctx.c["ZS"]

                    def b(gpsimd, c=c, zn_ve=zn_ve, zout=zout):
                        ctx.wait(gpsimd, "VE", zn_ve)
                        gpsimd.dma_start(out=zout[c], in_=zf[:]).then_inc(
                            ctx.sems["ZS"], 16
                        )

                    ctx.on("gpsimd", b)
                    ctx.flush()

    for c in range(NCH):
        zf_free = ctx.t("zf_free")
        z_store = ctx.t("z_store")
        xv_guard_mm = ctx.c["MM"]
        xv_guard_gp = ctx.c["GP"]
        xv_free = ctx.t("xv_free")
        ld_tag = ctx.res("LD", 32)

        def b(sync, c=c, zf_free=zf_free, z_store=z_store,
              xv_guard_mm=xv_guard_mm, xv_guard_gp=xv_guard_gp,
              xv_free=xv_free):
            ctx.wait(sync, "VE", zf_free)
            ctx.wait(sync, "ZS", z_store)
            sync.dma_start(out=zf[:], in_=A["zh"][0][c]).then_inc(ctx.sems["LD"], 16)
            if c == 0:
                ctx.wait(sync, "MM", xv_guard_mm)
                ctx.wait(sync, "GP", xv_guard_gp)
            else:
                ctx.wait(sync, "VE", xv_free)
            sync.dma_start(out=xv, in_=A["x"][c]).then_inc(ctx.sems["LD"], 16)

        ctx.on("sync", b)

        st_used = ctx.c["ST"]
        out_ve = ctx.res("VE")
        ctx.tag["zf_free"] = out_ve
        ctx.tag["xv_free"] = out_ve

        def b(vector, c=c, ld_tag=ld_tag, st_used=st_used):
            ctx.wait(vector, "LD", ld_tag)
            if c > 0:
                ctx.wait(vector, "ST", st_used)
            vector.drain()
            for ot in range(DT):
                mm = vector.scalar_tensor_tensor(
                    zf[:, ot, :], zf[:, ot, :], 0.5, xv[:, ot, :],
                    op0=ALU.mult, op1=ALU.add,
                )
            ctx.fin(mm, "VE")

        ctx.on("vector", b)

        ctx.res("ST", 16)

        def b(sync, c=c, out_ve=out_ve):
            ctx.wait(sync, "VE", out_ve)
            sync.dma_start(out=A["y"][c], in_=zf[:]).then_inc(ctx.sems["ST"], 16)

        ctx.on("sync", b)
        ctx.flush()

    st_final = ctx.c["ST"]

    def b(sync):
        ctx.wait(sync, "ST", st_final)

    ctx.on("sync", b)
    ctx.flush()


def build_l1ffn(depth=DEPTH6):
    nc = bass.Bass("TRN2", target_bir_lowering=False)
    A = l1ffn_drams(nc)
    es = contextlib.ExitStack()
    with es, nc.Block() as block:
        ctx = Ctx(nc, es, block)
        emit_l1ffn(nc, block, ctx, es, A, depth=depth)
    return nc


def l1ffn_host_inputs(xt_core, W):
    import ml_dtypes

    DIM_HEAD = 64
    WINDOW = 256
    perm = np.zeros(D, dtype=np.int64)
    for n in range(D):
        i, p = n // 128, n % 128
        g = i // 4
        h = 4 * (i % 4) + p // 32
        d = 32 * g + (p % 32)
        perm[n] = h * 64 + d
    Wqkv = np.asarray(W["Wqkv"])
    wqkv_p = Wqkv.copy()
    wqkv_p[:, :, 0:D] = Wqkv[:, :, 0:D][:, :, perm]
    wqkv_p[:, :, D : 2 * D] = Wqkv[:, :, D : 2 * D][:, :, perm]

    qscol = np.zeros((128, DEPTH6, 2), dtype=np.float32)
    kscol = np.zeros((128, DEPTH6, 2), dtype=np.float32)
    qs = np.asarray(W["q_scale"])
    ks = np.asarray(W["k_scale"])
    for p in range(128):
        for g in range(2):
            qscol[p, :, g] = qs[:, 32 * g + p % 32]
            kscol[p, :, g] = ks[:, 32 * g + p % 32]

    half = np.arange(0, DIM_HEAD, 2, dtype=np.float64)
    inv_freq = 1.0 / 10000.0 ** (half / DIM_HEAD)
    t = np.arange(T, dtype=np.float64)
    freqs = t[:, None] * inv_freq[None, :]
    base_scale = (half + 0.4 * DIM_HEAD) / (1.4 * DIM_HEAD)
    power = (t - T // 2) / (WINDOW // 2)
    scale = base_scale[None, :] ** power[:, None]
    cos_t = np.cos(freqs)
    sin_t = np.sin(freqs)
    tabq = np.zeros((128, 2, 2, T), dtype=np.float64)
    tabk = np.zeros((128, 2, 2, T), dtype=np.float64)
    for p in range(128):
        dp = p % 32
        for g in range(2):
            sgn = -1.0 if g == 0 else 1.0
            tabq[p, g, 0] = cos_t[:, dp] * scale[:, dp]
            tabq[p, g, 1] = sgn * sin_t[:, dp] * scale[:, dp]
            tabk[p, g, 0] = cos_t[:, dp] / scale[:, dp]
            tabk[p, g, 1] = sgn * sin_t[:, dp] / scale[:, dp]

    hp16 = np.zeros((128, 4, 16), dtype=np.float32)
    hpbc = np.zeros((16, 4, 128), dtype=np.float32)
    for p in range(128):
        for v in range(4):
            h = 4 * v + p // 32
            hp16[p, v, h] = 1.0
            hpbc[h, v, p] = 1.0

    mdiag = np.zeros((128, 128), dtype=np.float32)
    mpast = np.zeros((128, 128), dtype=np.float32)
    for kl in range(128):
        mdiag[kl, :] = np.where(kl > np.arange(128), -1e30, 0.0)
        mpast[kl, :] = (kl >= np.arange(128)).astype(np.float32)

    def padb1(bpart):
        out = np.zeros((DEPTH6, MT1 * 128), dtype=np.float32)
        out[:, :DI] = bpart
        return np.ascontiguousarray(out.reshape(DEPTH6, MT1, 128).transpose(2, 0, 1))

    def tile_wd(w):
        d0, K, N = w.shape
        return np.ascontiguousarray(
            w.reshape(d0, K // 128, 128, N).transpose(0, 2, 1, 3)
        )

    bf = lambda a: np.ascontiguousarray(a).astype(ml_dtypes.bfloat16)
    b1 = np.asarray(W["b1"])
    W2 = np.asarray(W["W2"])
    return {
        "x": xt_core,
        "wqkv": tile_wd(wqkv_p),
        "wod": tile_wd(np.asarray(W["Wo"])),
        "w1d": tile_wd(np.asarray(W["W1"])),
        "w2da": tile_wd(W2[:, 0 : 21 * 128, :]),
        "w2db": np.ascontiguousarray(W2[:, 21 * 128 :, :]),
        "qscol": qscol,
        "kscol": kscol,
        "lng": np.ascontiguousarray(
            np.asarray(W["ln_g"]).reshape(DEPTH6, DT, 128).transpose(2, 0, 1)
        ),
        "lnb": np.ascontiguousarray(
            np.asarray(W["ln_b"]).reshape(DEPTH6, DT, 128).transpose(2, 0, 1)
        ),
        "b1a": padb1(b1[:, :DI]),
        "b1g": padb1(b1[:, DI:]),
        "b2c": np.ascontiguousarray(
            np.asarray(W["b2"]).reshape(DEPTH6, DT, 128).transpose(2, 0, 1)
        ),
        "tabq": bf(tabq),
        "tabk": bf(tabk),
        "hp16": bf(hp16),
        "hpbc": bf(hpbc),
        "mdiag": mdiag,
        "mpast": bf(mpast),
    }









def build_fused(nstages=6):
    nc = bass.Bass("TRN2", target_bir_lowering=False)
    Am = moe_drams(nc, "moe_", io=False)
    Aa = mla_drams(nc, "mla_", io=False)
    Al = lora_drams(nc, "lora_", io=False)
    Af = l1ffn_drams(nc, "l1_", io=False)
    xin = nc.dram_tensor("x", [NCH, 128, DT, CH], F32, kind="ExternalInput").ap()
    yout = nc.dram_tensor("y", [NCH, 128, DT, CH], F32, kind="ExternalOutput").ap()
    xh = [
        nc.dram_tensor("xh0", [NCH, 128, DT, CH], F32, kind="Internal").ap(),
        nc.dram_tensor("xh1", [NCH, 128, DT, CH], F32, kind="Internal").ap(),
    ]
    es = contextlib.ExitStack()
    with es, nc.Block() as block:
        ctx = Ctx(nc, es, block)
        seq = [
            (emit_moe, Am, {}),
            (emit_mla, Aa, {}),
            (emit_moe, Am, {}),
            (emit_l1ffn, Af, {}),
            (emit_lora, Al, {}),
            (emit_l1ffn, Af, {}),
        ]
        seq = seq[:nstages]
        cur = xin
        for i, (fn, A, kw) in enumerate(seq):
            out = yout if i == len(seq) - 1 else xh[i % 2]
            A2 = dict(A)
            A2["x"] = cur
            A2["y"] = out
            with contextlib.ExitStack() as es_stage:
                fn(nc, block, ctx, es_stage, A2, **kw)
            if i < len(seq) - 1:
                ctx.barrier()
            cur = out
    return nc


def fused_host_maps(xt, moe_W, mla_W, lora_W, l1_W, cores):
    maps = []
    for b in cores:
        m = {}
        for pfx, hm in [
            ("moe_", moe_host_inputs(xt[b], moe_W)),
            ("mla_", mla_host_inputs(xt[b], mla_W)),
            ("lora_", lora_host_inputs(xt[b], lora_W)),
            ("l1_", l1ffn_host_inputs(xt[b], l1_W)),
        ]:
            for k, v in hm.items():
                if k == "x":
                    continue
                m[pfx + k] = v
        m["x"] = xt[b]
        maps.append(m)
    return maps


_NC_CACHE = {}


def _nc(name, fn, *a, **kw):
    if name not in _NC_CACHE:
        _NC_CACHE[name] = fn(*a, **kw)
    return _NC_CACHE[name]


def kernel(**inputs):
    inp = {k: np.asarray(v) for k, v in inputs.items()}
    cores = list(range(NCORES))
    x = np.ascontiguousarray(inp["L0_attn_x"])
    xt = [tile_x(np.ascontiguousarray(x[b])) for b in cores]

    moe_W = {
        "Wg": inp["L0_ffn_Wg"], "bg": inp["L0_ffn_bg"], "W1": inp["L0_ffn_W1"],
        "b1": inp["L0_ffn_b1"], "W2": inp["L0_ffn_W2"], "b2": inp["L0_ffn_b2"],
    }
    mla_W = {
        "W_DKV": inp["L0_attn_W_DKV"], "W_UK": inp["L0_attn_W_UK"],
        "W_UV": inp["L0_attn_W_UV"], "W_UQ": inp["L0_attn_W_UQ"],
        "W_O": inp["L0_attn_W_O"],
    }
    lora_W = {}
    for nm in ["q", "k", "v", "o"]:
        lora_W["w" + nm] = inp["L1_attn_w" + nm]
        lora_W["a" + nm] = inp["L1_attn_a" + nm]
        lora_W["b" + nm] = inp["L1_attn_b" + nm]
    l1_W = {
        "Wqkv": inp["L1_ffn_Wqkv"], "Wo": inp["L1_ffn_Wo"],
        "W1": inp["L1_ffn_W1"], "W2": inp["L1_ffn_W2"],
        "q_scale": inp["L1_ffn_q_scale"], "k_scale": inp["L1_ffn_k_scale"],
        "ln_g": inp["L1_ffn_ln_g"], "ln_b": inp["L1_ffn_ln_b"],
        "b1": inp["L1_ffn_b1"], "b2": inp["L1_ffn_b2"],
    }

    nc = _nc("fused", build_fused)
    maps = fused_host_maps(xt, moe_W, mla_W, lora_W, l1_W, cores)
    res = run_bass_kernel_spmd(nc, maps, core_ids=cores)
    return np.stack(
        [untile_y(res.results[b]["y"]) for b in cores]
    ).astype(np.float32)


# version 08
# speedup vs baseline: 1.3505x; 1.0379x over previous
import contextlib
import numpy as np
import concourse.bass as bass
import concourse.mybir as mybir
from concourse.bass_utils import run_bass_kernel_spmd

F32 = mybir.dt.float32
BF16 = mybir.dt.bfloat16
AF = mybir.ActivationFunctionType
ALU = mybir.AluOpType
AX = mybir.AxisListType

T = 4096
D = 1024
NCORES = 8
CH = 512
NCH = T // CH
DT = D // 128
EPS_LN = 1e-5
ENGS = ("sync", "gpsimd", "tensor", "scalar", "vector")


class Ctx:

    def __init__(self, nc, es, block):
        self.nc = nc
        self.es = es
        self.block = block
        self.sems = {}
        self.c = {}
        self.tag = {}
        self.q = {e: [] for e in ENGS}
        for s in ("LD", "LW", "MM", "VE", "SE", "GP", "ZS", "ST", "QS", "KS", "VS", "XN"):
            self.sem(s)

    def sem(self, name):
        if name not in self.sems:
            self.sems[name] = self.es.enter_context(self.nc.semaphore(name))
            self.c[name] = 0
        return self.sems[name]

    def res(self, name, k=1):
        self.c[name] += k
        return self.c[name]

    def t(self, key):
        return self.tag.get(key, 0)

    def on(self, eng, body):
        self.q[eng].append(body)

    def wait(self, e, name, val):
        if val > 0:
            e.wait_ge(self.sems[name], val)

    def fin(self, instr, name, k=1):
        instr.then_inc(self.sems[name], k)

    def sett(self, key, sem, val):
        self.tag[key] = (sem, val)

    def tt(self, key):
        return self.tag.get(key, None)

    def waitp(self, eng, pair):
        if pair:
            self.wait(eng, pair[0], pair[1])

    DATA_SEMS = ("LD", "LW", "MM", "VE", "SE", "GP", "ZS", "ST", "QS", "KS",
                 "VS", "XN")

    def barrier(self):
        if "BAR" not in self.sems:
            self.sem("BAR")
        bar0 = self.c["BAR"]
        dma_counts = [(s, self.c[s]) for s in
                      ("LD", "LW", "ST", "QS", "KS", "VS", "XN", "ZS")
                      if self.c.get(s, 0) > 0]
        noclear = True
        self.res("BAR", 5)
        target = bar0 + 5

        for e in ENGS:
            def b(eng, e=e, dma_counts=dma_counts, bar0=bar0, noclear=noclear):
                if e == "sync":
                    for s, v in dma_counts:
                        self.wait(eng, s, v)
                    eng.nop().then_inc(self.sems["BAR"], 1)
                    if not noclear:
                        eng.wait_ge(self.sems["BAR"], bar0 + 5)
                        for s in self.DATA_SEMS:
                            eng.sem_clear(self.sems[s])
                        eng.nop().then_inc(self.sems["BAR"], 1)
                else:
                    eng.drain().then_inc(self.sems["BAR"], 1)

            self.on(e, b)
        self.flush()
        if not noclear:
            for s in self.DATA_SEMS:
                self.c[s] = 0
        self.tag = {}
        for e in ENGS:
            def b(eng, target=target):
                self.wait(eng, "BAR", target)

            self.on(e, b)

    def flush(self):
        for name in ENGS:
            items = self.q[name]
            if not items:
                continue
            self.q[name] = []
            dec = getattr(self.block, name)

            def emit(eng, items=items):
                for b in items:
                    b(eng)

            dec(emit)




def q_ln_chunk(ctx, bufs, src_f32, dst_bf16, ld_sem, ld_tag, pre_wait=None,
               ready_ve=None, extra_bc=None, extra_stats=None):
    sq = bufs["sq"]
    nsq = bufs.get("nsq", 1)
    bc = bufs["bc"]
    rowm, rowv, rows = bufs["rowm"], bufs["rowv"], bufs["rows"]
    onesb, ones1f, eps1 = bufs["onesb"], bufs["ones1f"], bufs["eps1"]
    ps_stats, ps_bc = bufs["ps_stats"], bufs["ps_bc"]

    if src_f32 is not None:
        cast_tag = ctx.res("VE")

        def b(vector, pre_wait=pre_wait, ld_tag=ld_tag):
            if pre_wait is not None:
                pre_wait(vector)
            ctx.wait(vector, ld_sem, ld_tag)
            ctx.fin(vector.tensor_copy(dst_bf16, src_f32), "VE")

        ctx.on("vector", b)
        ctx.tag["ln_cast"] = cast_tag
    else:
        cast_tag = ready_ve
        ctx.tag["ln_cast"] = ready_ve
    cast_p = cast_tag if isinstance(cast_tag, tuple) else ("VE", cast_tag)

    stats_free = ctx.t("stats_free")
    sq_guard = ctx.t("sq_guard")
    sq_mm = {}
    for a in range(DT):
        se_tag = ctx.res("SE")

        def b(scalar, a=a, sq_guard=sq_guard, cast_p=cast_p):
            if a < nsq:
                ctx.waitp(scalar, cast_p)
                ctx.wait(scalar, "MM", sq_guard)
            else:
                ctx.wait(scalar, "MM", sq_mm[a - nsq])
            ctx.fin(scalar.square(sq[:, a % nsq, :], dst_bf16[:, a, :]), "SE")

        ctx.on("scalar", b)
        sq_mm[a] = ctx.res("MM")

        def b(tensor, a=a, se_tag=se_tag, cast_p=cast_p, stats_free=stats_free,
              extra_stats=extra_stats):
            if a == 0:
                ctx.waitp(tensor, cast_p)
                ctx.wait(tensor, "VE", stats_free)
                if extra_stats is not None:
                    extra_stats(tensor)
            ctx.wait(tensor, "SE", se_tag)
            tensor.matmul(
                ps_stats[:, 0:CH], onesb[:], dst_bf16[:, a, :],
                start=(a == 0), stop=(a == DT - 1),
            )
            ctx.fin(
                tensor.matmul(
                    ps_stats[:, CH : 2 * CH], onesb[:], sq[:, a % nsq, :],
                    start=(a == 0), stop=(a == DT - 1),
                ),
                "MM",
            )

        ctx.on("tensor", b)

    stats_tag = ctx.c["MM"]
    var_tag = ctx.res("VE")
    ctx.tag["stats_free"] = var_tag

    def b(vector, stats_tag=stats_tag):
        ctx.wait(vector, "MM", stats_tag)
        vector.tensor_scalar_mul(rowm[:], ps_stats[:, 0:CH], 1.0 / D)
        vector.tensor_scalar_mul(rowv[:], ps_stats[:, CH : 2 * CH], 1.0 / D)
        vector.drain()
        vector.tensor_tensor(rows[:], rowm[:], rowm[:], op=ALU.mult)
        vector.drain()
        ctx.fin(
            vector.tensor_tensor(rowv[:], rowv[:], rows[:], op=ALU.subtract), "VE"
        )

    ctx.on("vector", b)

    rstd_tag = ctx.res("SE")

    def b(scalar, var_tag=var_tag):
        ctx.wait(scalar, "VE", var_tag)
        scalar.activation(rows[:], rowv[:], AF.Ln, bias=eps1[:])
        scalar.drain()
        ctx.fin(scalar.activation(rows[:], rows[:], AF.Exp, scale=-0.5), "SE")

    ctx.on("scalar", b)

    rows_tag = ctx.res("VE")

    def b(vector, rstd_tag=rstd_tag):
        ctx.wait(vector, "SE", rstd_tag)
        vector.drain()
        ctx.fin(
            vector.scalar_tensor_tensor(
                rowm[:], rowm[:], -1.0, rows[:], op0=ALU.mult, op1=ALU.mult
            ),
            "VE",
        )

    ctx.on("vector", b)

    bc_free = ctx.t("bc_free")
    bc_mm_tag = ctx.res("MM")

    def b(tensor, rows_tag=rows_tag, bc_free=bc_free, extra_bc=extra_bc):
        ctx.wait(tensor, "VE", rows_tag)
        ctx.wait(tensor, "SE", bc_free)
        if extra_bc is not None:
            extra_bc(tensor)
        tensor.matmul(ps_bc[:, 0:CH], ones1f[:], rows[:], start=True, stop=True)
        ctx.fin(
            tensor.matmul(ps_bc[:, CH : 2 * CH], ones1f[:], rowm[:],
                          start=True, stop=True),
            "MM",
        )

    ctx.on("tensor", b)

    bc_tag = ctx.res("SE")
    ctx.tag["bc_free"] = bc_tag

    def b(scalar, bc_mm_tag=bc_mm_tag):
        ctx.wait(scalar, "MM", bc_mm_tag)
        ctx.fin(scalar.copy(bc[:], ps_bc[:]), "SE")

    ctx.on("scalar", b)

    out_tag = ctx.res("VE")

    def b(vector, bc_tag=bc_tag):
        ctx.wait(vector, "SE", bc_tag)
        for a in range(DT):
            vector.tensor_tensor(
                dst_bf16[:, a, :], dst_bf16[:, a, :], bc[:, 0:CH], op=ALU.mult
            )
        vector.drain()
        for a in range(DT):
            mm = vector.tensor_tensor(
                dst_bf16[:, a, :], dst_bf16[:, a, :], bc[:, CH : 2 * CH], op=ALU.add
            )
        ctx.fin(mm, "VE")

    ctx.on("vector", b)
    return out_tag




def tile_x(x2d):
    return np.ascontiguousarray(
        x2d.reshape(NCH, CH, DT, 128).transpose(0, 3, 2, 1)
    )


def untile_y(yt):
    return np.ascontiguousarray(yt.transpose(0, 3, 2, 1).reshape(T, D))


def tile_w(w):
    K, N = w.shape
    return np.ascontiguousarray(w.reshape(K // 128, 128, N).transpose(1, 0, 2))



NE = 8
HD = 2048
HT = HD // 128


def moe_drams(nc, pfx="", io=True):
    A = {}
    if io:
        A["x"] = nc.dram_tensor(pfx + "x", [NCH, 128, DT, CH], F32, kind="ExternalInput").ap()
    A["wg"] = nc.dram_tensor(pfx + "wg", [128, DT, NE], F32, kind="ExternalInput").ap()
    A["bg"] = nc.dram_tensor(pfx + "bg", [NE, 1], F32, kind="ExternalInput").ap()
    A["w1"] = nc.dram_tensor(pfx + "w1", [NE, 128, DT, HD], F32, kind="ExternalInput").ap()
    A["b1t"] = nc.dram_tensor(pfx + "b1t", [128, HT, NE], F32, kind="ExternalInput").ap()
    A["w2"] = nc.dram_tensor(pfx + "w2", [NE, 128, HT, D], F32, kind="ExternalInput").ap()
    A["b2"] = nc.dram_tensor(pfx + "b2", [NE, D], F32, kind="ExternalInput").ap()
    A["idf"] = nc.dram_tensor(pfx + "idf", [8, 8], F32, kind="ExternalInput").ap()
    A["idb"] = nc.dram_tensor(pfx + "idb", [128, 128], BF16, kind="ExternalInput").ap()
    A["eb8"] = nc.dram_tensor(pfx + "eb8", [NE, NE * 128], BF16, kind="ExternalInput").ap()
    if io:
        A["y"] = nc.dram_tensor(pfx + "y", [NCH, 128, DT, CH], F32, kind="ExternalOutput").ap()
    A["xnh"] = nc.dram_tensor(pfx + "xnh", [NCH, 128, DT, CH], BF16, kind="Internal").ap()

    return A


def emit_moe(nc, block, ctx, es, A):
    sb = lambda *a: es.enter_context(nc.sbuf_tensor(*a, allow_name_mangling=True))
    psa = lambda *a: es.enter_context(nc.psum_tensor(*a, allow_name_mangling=True))

    xf = sb("xf", [128, DT, CH], F32)
    xnc = sb("xnc", [128, 2, DT, CH], BF16)
    sq = sb("sqb", [128, 1, CH], BF16)
    bc = sb("bcb", [128, 2 * CH], BF16)
    rowm = sb("rowm", [1, CH], F32)
    rowv = sb("rowv", [1, CH], F32)
    rows = sb("rows", [1, CH], F32)
    onesb = sb("onesb", [128, 1], BF16)
    eps1 = sb("eps1", [1, 1], F32)
    ones1f = sb("ones1f", [1, 128], F32)
    idf_t = sb("idf_t", [8, 8], F32)
    idb_t = sb("idb_t", [128, 128], BF16)
    eb8_t = sb("eb8_t", [NE, NE * 128], BF16)
    wg_t = sb("wg_t", [128, DT, NE], BF16)
    bg_t = sb("bg_t", [NE, 1], F32)
    b1t_t = sb("b1t_t", [128, HT, NE], F32)
    b2_t = sb("b2_t", [NE, D], BF16)
    lg = sb("lg", [NE, CH], F32)
    lgn = sb("lgn", [128, 4 * NE], F32)
    gt1 = sb("gt1", [128, 4 * NE], F32)
    gt2 = sb("gt2", [128, 4 * NE], F32)
    gmax = sb("gmax", [128, 4], F32)
    gsum = sb("gsum", [128, 4], F32)
    gm1p = sb("gm1p", [128, 4], F32)
    gm2 = sb("gm2", [128, 4], F32)
    gws = sb("gws", [128, 4], F32)
    combT = sb("combT", [NE, T], BF16)
    comb_n = sb("comb_n", [128, 4, NE], BF16)
    w1_sb = sb("w1_sb", [128, DT, HD], BF16)
    w2_sb = sb("w2_sb", [128, HT, D], BF16)
    h_bf = sb("h_bf", [128, HT, CH], BF16)
    outb = (
        h_bf[:]
        .rearrange("p a n -> p (a n)")
        .bitcast(F32)
        .rearrange("p (a n) -> p a n", a=DT)
    )
    cbb = sb("cbb", [128, 2, CH], BF16)
    acc = sb("acc", [128, DT, T], BF16)

    ps_stats = psa("ps_stats", [1, 2 * CH], F32)
    ps_bc = psa("ps_bc", [128, 2 * CH], F32)
    psA = psa("psA", [128, CH], F32)
    ps_g = psA[0:NE, 0:CH]
    ps_t = psA[:, 0 : 4 * NE]
    ps_ct = psA[0:NE, 0 : CH // 2].rearrange("p n -> p (n)").bitcast(BF16)
    ps_h = [psa("ps_h0", [128, CH], F32), psa("ps_h1", [128, CH], F32)]
    ps_y = [ps_bc[:, 0:CH], ps_bc[:, CH : 2 * CH]]


    bufs = dict(
        sq=sq, bc=bc, rowm=rowm, rowv=rowv, rows=rows, onesb=onesb,
        ones1f=ones1f, ps_stats=ps_stats, ps_bc=ps_bc, eps1=eps1,
    )

    ctx.res("LD", 64)

    def b(sync):
        sync.dma_start(out=idf_t[:], in_=A["idf"]).then_inc(ctx.sems["LD"], 16)
        sync.dma_start(out=idb_t[:], in_=A["idb"]).then_inc(ctx.sems["LD"], 16)
        sync.dma_start(out=bg_t[:], in_=A["bg"]).then_inc(ctx.sems["LD"], 16)
        sync.dma_start(out=b1t_t[:], in_=A["b1t"]).then_inc(ctx.sems["LD"], 16)

    ctx.on("sync", b)
    ctx.res("LW", 48)

    def b(gpsimd):
        gpsimd.dma_start(out=eb8_t[:], in_=A["eb8"]).then_inc(ctx.sems["LW"], 16)
        gpsimd.dma_start(out=wg_t[:], in_=A["wg"]).then_inc(ctx.sems["LW"], 16)
        gpsimd.dma_start(out=b2_t[:], in_=A["b2"]).then_inc(ctx.sems["LW"], 16)

    ctx.on("gpsimd", b)
    init_ve = ctx.res("VE")

    def b(vector):
        vector.memset(onesb[:], 1.0)
        vector.memset(eps1[:], EPS_LN)
        ctx.fin(vector.memset(ones1f[:], 1.0), "VE")

    ctx.on("vector", b)

    for c in range(NCH):
        xn = xnc[:, c % 2, :, :]
        xf_free = ctx.t("xf_free")
        ld_tag = ctx.res("LD", 16)

        def b(sync, c=c, xf_free=xf_free):
            ctx.wait(sync, "VE", xf_free)
            sync.dma_start(out=xf[:], in_=A["x"][c]).then_inc(ctx.sems["LD"], 16)

        ctx.on("sync", b)

        gm_prev = ctx.t(f"gate_mm_{c-2}")

        def pre(vector, c=c, gm_prev=gm_prev, init_ve=init_ve):
            if c == 0:
                ctx.wait(vector, "VE", init_ve)
            if c >= 2:
                ctx.wait(vector, "XN", 16 * (c - 1))
                ctx.wait(vector, "MM", gm_prev)

        ve_tag = q_ln_chunk(ctx, bufs, xf[:], xn, "LD", ld_tag, pre)
        ctx.tag["xf_free"] = ctx.t("ln_cast")

        ctx.res("XN", 16)

        def b(gpsimd, c=c, xn=xn, ve_tag=ve_tag):
            ctx.wait(gpsimd, "VE", ve_tag)
            gpsimd.dma_start(out=A["xnh"][c], in_=xn).then_inc(ctx.sems["XN"], 16)

        ctx.on("gpsimd", b)

        psA_free = ctx.t("psA_free")
        g_tag = ctx.res("MM")
        ctx.tag[f"gate_mm_{c}"] = g_tag

        def b(tensor, c=c, xn=xn, ve_tag=ve_tag, psA_free=psA_free):
            ctx.wait(tensor, "LW", 32)
            ctx.wait(tensor, "VE", ve_tag)
            ctx.wait(tensor, "SE", psA_free)
            for a in range(DT):
                mm = tensor.matmul(
                    ps_g, wg_t[:, a, :], xn[:, a, :],
                    start=(a == 0), stop=(a == DT - 1),
                )
            ctx.fin(mm, "MM")

        ctx.on("tensor", b)

        lg_tag = ctx.res("SE")

        def b(scalar, g_tag=g_tag):
            ctx.wait(scalar, "MM", g_tag)
            ctx.fin(
                scalar.activation(lg[:], ps_g, AF.Identity, bias=bg_t[:]), "SE"
            )

        ctx.on("scalar", b)

        t_tag = ctx.res("MM")

        def b(tensor, lg_tag=lg_tag):
            ctx.wait(tensor, "SE", lg_tag)
            for i in range(4):
                mm = tensor.transpose(
                    ps_t[:, i * NE : (i + 1) * NE],
                    lg[:, i * 128 : (i + 1) * 128], idf_t[:],
                )
            ctx.fin(mm, "MM")

        ctx.on("tensor", b)

        lgn_tag = ctx.res("SE")

        def b(scalar, t_tag=t_tag):
            ctx.wait(scalar, "MM", t_tag)
            ctx.fin(scalar.copy(lgn[:], ps_t), "SE")

        ctx.on("scalar", b)

        nm_tag = ctx.res("VE")

        def b(vector, lgn_tag=lgn_tag):
            ctx.wait(vector, "SE", lgn_tag)
            for i in range(4):
                mm = vector.reduce_max(
                    gmax[:, i : i + 1], lgn[:, i * NE : (i + 1) * NE],
                    axis=AX.X, negate=True,
                )
            ctx.fin(mm, "VE")

        ctx.on("vector", b)

        e_tag = ctx.res("SE")

        def b(scalar, nm_tag=nm_tag):
            ctx.wait(scalar, "VE", nm_tag)
            for i in range(4):
                mm = scalar.activation(
                    gt1[:, i * NE : (i + 1) * NE],
                    lgn[:, i * NE : (i + 1) * NE],
                    AF.Exp, bias=gmax[:, i : i + 1],
                )
            ctx.fin(mm, "SE")

        ctx.on("scalar", b)

        comb_tag = ctx.res("VE")

        def b(vector, e_tag=e_tag):
            ctx.wait(vector, "SE", e_tag)
            sl = lambda t, i: t[:, i * NE : (i + 1) * NE]
            for i in range(4):
                vector.reduce_sum(gsum[:, i : i + 1], sl(gt1, i), axis=AX.X)
            vector.drain()
            vector.reciprocal(gsum[:], gsum[:])
            vector.drain()
            for i in range(4):
                vector.tensor_scalar_mul(sl(gt1, i), sl(gt1, i), gsum[:, i : i + 1])
            vector.drain()
            for i in range(4):
                vector.reduce_max(gm1p[:, i : i + 1], sl(gt1, i), axis=AX.X)
            vector.drain()
            for i in range(4):
                vector.tensor_scalar(
                    sl(gt2, i), sl(gt1, i), gm1p[:, i : i + 1], None, op0=ALU.is_lt
                )
            vector.drain()
            vector.tensor_tensor(gt2[:], gt1[:], gt2[:], op=ALU.mult)
            vector.drain()
            for i in range(4):
                vector.reduce_max(gm2[:, i : i + 1], sl(gt2, i), axis=AX.X)
            vector.drain()
            for i in range(4):
                vector.tensor_scalar(
                    sl(gt2, i), sl(gt1, i), gm2[:, i : i + 1], None, op0=ALU.is_ge
                )
            vector.tensor_tensor(gws[:], gm1p[:], gm2[:], op=ALU.add)
            vector.drain()
            vector.reciprocal(gws[:], gws[:])
            vector.tensor_tensor(gt1[:], gt1[:], gt2[:], op=ALU.mult)
            vector.drain()
            for i in range(4):
                mm = vector.tensor_scalar_mul(
                    comb_n[:, i, :], sl(gt1, i), gws[:, i : i + 1]
                )
            ctx.fin(mm, "VE")

        ctx.on("vector", b)

        ct_tag = ctx.res("MM")

        def b(tensor, comb_tag=comb_tag):
            ctx.wait(tensor, "VE", comb_tag)
            for i in range(4):
                mm = tensor.transpose(
                    ps_ct[:, i * 128 : (i + 1) * 128], comb_n[:, i, :], idb_t[:]
                )
            ctx.fin(mm, "MM")

        ctx.on("tensor", b)

        psA_free2 = ctx.res("SE")
        ctx.tag["psA_free"] = psA_free2

        def b(scalar, c=c, ct_tag=ct_tag):
            ctx.wait(scalar, "MM", ct_tag)
            ctx.fin(scalar.copy(combT[:, c * CH : (c + 1) * CH], ps_ct), "SE")

        ctx.on("scalar", b)
        ctx.flush()

    stageA_se = ctx.c["SE"]
    stageA_mm = ctx.c["MM"]
    stageA_xn = ctx.c["XN"]

    for e in range(NE):
        w_used = ctx.t("w_used")
        w_tag = ctx.res("LW", 32)

        def b(gpsimd, e=e, w_used=w_used):
            if e > 0:
                ctx.wait(gpsimd, "MM", w_used)
            gpsimd.dma_start(out=w1_sb[:], in_=A["w1"][e]).then_inc(
                ctx.sems["LW"], 16
            )
            gpsimd.dma_start(out=w2_sb[:], in_=A["w2"][e]).then_inc(
                ctx.sems["LW"], 16
            )

        ctx.on("gpsimd", b)

        for c in range(NCH):
            it = e * NCH + c
            xn = xnc[:, it % 2, :, :]
            cbuf = it % 2

            xn_used = ctx.t(f"xn_used_{it-2}")
            xn_tag = ctx.res("LD", 16)

            def b(sync, e=e, c=c, it=it, xn=xn, xn_used=xn_used):
                ctx.wait(sync, "XN", 16 * (c + 1))
                if it < 2:
                    ctx.wait(sync, "XN", stageA_xn)
                    ctx.wait(sync, "MM", stageA_mm)
                else:
                    ctx.wait(sync, "MM", xn_used)
                sync.dma_start(out=xn, in_=A["xnh"][c]).then_inc(ctx.sems["LD"], 16)

            ctx.on("sync", b)

            y_evac0 = ctx.t("y_evac_0")
            cbm_tag = ctx.res("MM")

            def b(tensor, e=e, c=c, it=it, w_tag=w_tag, y_evac0=y_evac0):
                if c == 0:
                    ctx.wait(tensor, "LW", w_tag)
                if it == 0:
                    ctx.wait(tensor, "SE", stageA_se)
                ctx.wait(tensor, "VE", y_evac0)
                ctx.fin(
                    tensor.matmul(
                        ps_y[0], eb8_t[:, e * 128 : (e + 1) * 128],
                        combT[:, c * CH : (c + 1) * CH],
                        start=True, stop=True,
                    ),
                    "MM",
                )

            ctx.on("tensor", b)

            cb_used = ctx.t(f"cb_used_{it-2}")
            cb_tag = ctx.res("SE")

            def b(scalar, cbuf=cbuf, cbm_tag=cbm_tag, cb_used=cb_used):
                ctx.wait(scalar, "MM", cbm_tag)
                ctx.wait(scalar, "GP", cb_used)
                ctx.fin(scalar.copy(cbb[:, cbuf, :], ps_y[0]), "SE")

            ctx.on("scalar", b)

            for m in range(HT):
                bank = m % 2
                h_evac = ctx.t(f"h_evac_{bank}")
                g_tag2 = ctx.res("MM")

                def b(tensor, m=m, bank=bank, xn=xn, xn_tag=xn_tag,
                      h_evac=h_evac):
                    if m == 0:
                        ctx.wait(tensor, "LD", xn_tag)
                    ctx.wait(tensor, "VE", h_evac)
                    for a in range(DT):
                        mm = tensor.matmul(
                            ps_h[bank][:],
                            w1_sb[:, a, m * 128 : (m + 1) * 128],
                            xn[:, a, :], start=(a == 0), stop=(a == DT - 1),
                        )
                    ctx.fin(mm, "MM")

                ctx.on("tensor", b)

                hm_used = ctx.t(f"hm_used_{it-1}")
                st_used = ctx.c["ST"]
                i = ctx.res("VE")
                ctx.tag[f"h_evac_{bank}"] = i

                def b(vector, e=e, c=c, m=m, bank=bank, it=it, g_tag2=g_tag2,
                      hm_used=hm_used, st_used=st_used):
                    ctx.wait(vector, "MM", g_tag2)
                    if m == 0:
                        ctx.wait(vector, "GP", hm_used)
                        if e == NE - 1 and c > 0:
                            ctx.wait(vector, "ST", st_used)
                    ctx.fin(
                        vector.tensor_scalar(
                            h_bf[:, m, :], ps_h[bank][:],
                            b1t_t[:, m, e : e + 1], 0.0,
                            op0=ALU.add, op1=ALU.max,
                        ),
                        "VE",
                    )

                ctx.on("vector", b)

            h_done = ctx.c["VE"]
            hm_tag = ctx.res("GP")
            ctx.tag[f"cb_used_{it}"] = hm_tag
            ctx.tag[f"hm_used_{it}"] = hm_tag

            def b(gpsimd, cbuf=cbuf, h_done=h_done, cb_tag=cb_tag):
                ctx.wait(gpsimd, "SE", cb_tag)
                ctx.wait(gpsimd, "VE", h_done)
                for m in range(HT):
                    mm = gpsimd.tensor_mul(
                        h_bf[:, m, :], h_bf[:, m, :], cbb[:, cbuf, :]
                    )
                ctx.fin(mm, "GP")

            ctx.on("gpsimd", b)

            for o in range(DT):
                bank = o % 2
                y_evac = ctx.t(f"y_evac_{bank}")
                y_tag = ctx.res("MM")

                def b(tensor, e=e, c=c, o=o, bank=bank, hm_tag=hm_tag,
                      cb_tag=cb_tag, y_evac=y_evac):
                    if o == 0:
                        ctx.wait(tensor, "GP", hm_tag)
                        ctx.wait(tensor, "SE", cb_tag)
                    ctx.wait(tensor, "VE", y_evac)
                    for hh in range(HT):
                        mm = tensor.matmul(
                            ps_y[bank],
                            w2_sb[:, hh, o * 128 : (o + 1) * 128],
                            h_bf[:, hh, :],
                            start=(hh == 0), stop=(hh == HT - 1 and e != 0),
                        )
                    if e == 0:
                        mm = tensor.matmul(
                            ps_y[bank], b2_t[:, o * 128 : (o + 1) * 128],
                            combT[:, c * CH : (c + 1) * CH],
                            start=False, stop=True,
                        )
                    ctx.fin(mm, "MM")

                ctx.on("tensor", b)

                i = ctx.res("VE")
                ctx.tag[f"y_evac_{bank}"] = i

                def b(vector, e=e, c=c, o=o, bank=bank, y_tag=y_tag):
                    ctx.wait(vector, "MM", y_tag)
                    at = acc[:, o, c * CH : (c + 1) * CH]
                    if e == 0:
                        ctx.fin(vector.tensor_copy(at, ps_y[bank]), "VE")
                    else:
                        ctx.fin(
                            vector.tensor_tensor(at, at, ps_y[bank], op=ALU.add),
                            "VE",
                        )

                ctx.on("vector", b)

            ctx.tag[f"xn_used_{it}"] = ctx.c["MM"]

            if e == NE - 1:
                guard = ctx.t("xf_free") if c == 0 else ctx.t("out_done")
                xr_tag = ctx.res("LD", 16)

                def b(sync, c=c, guard=guard):
                    ctx.wait(sync, "VE", guard)
                    sync.dma_start(out=xf[:], in_=A["x"][c]).then_inc(
                        ctx.sems["LD"], 16
                    )

                ctx.on("sync", b)

                out_tag = ctx.res("VE")
                ctx.tag["out_done"] = out_tag

                def b(vector, c=c, xr_tag=xr_tag):
                    ctx.wait(vector, "LD", xr_tag)
                    vector.drain()
                    for o in range(DT):
                        mm = vector.scalar_tensor_tensor(
                            outb[:, o, :], acc[:, o, c * CH : (c + 1) * CH],
                            0.5, xf[:, o, :], op0=ALU.mult, op1=ALU.add,
                        )
                    ctx.fin(mm, "VE")

                ctx.on("vector", b)

                ctx.res("ST", 16)

                def b(sync, c=c, out_tag=out_tag):
                    ctx.wait(sync, "VE", out_tag)
                    sync.dma_start(out=A["y"][c], in_=outb).then_inc(
                        ctx.sems["ST"], 16
                    )

                ctx.on("sync", b)
            ctx.flush()

        ctx.tag["w_used"] = ctx.c["MM"]

    st_final = ctx.c["ST"]

    def b(sync):
        ctx.wait(sync, "ST", st_final)

    ctx.on("sync", b)
    ctx.flush()


def build_moe():
    nc = bass.Bass("TRN2", target_bir_lowering=False)
    A = moe_drams(nc)
    es = contextlib.ExitStack()
    with es, nc.Block() as block:
        ctx = Ctx(nc, es, block)
        emit_moe(nc, block, ctx, es, A)
    return nc


def moe_host_inputs(xt_core, W):
    import ml_dtypes

    eb8 = np.zeros((NE, NE * 128), dtype=ml_dtypes.bfloat16)
    for e in range(NE):
        eb8[e, e * 128 : (e + 1) * 128] = 1.0
    w1 = np.ascontiguousarray(
        np.asarray(W["W1"]).reshape(NE, DT, 128, HD).transpose(0, 2, 1, 3)
    )
    w2 = np.ascontiguousarray(
        np.asarray(W["W2"]).reshape(NE, HT, 128, D).transpose(0, 2, 1, 3)
    )
    b1t = np.ascontiguousarray(
        np.asarray(W["b1"]).reshape(NE, HT, 128).transpose(2, 1, 0)
    )
    return {
        "x": xt_core,
        "wg": tile_w(np.asarray(W["Wg"])),
        "bg": np.ascontiguousarray(np.asarray(W["bg"]).reshape(NE, 1)),
        "w1": w1,
        "b1t": b1t,
        "w2": w2,
        "b2": np.ascontiguousarray(np.asarray(W["b2"])),
        "idf": np.eye(8, dtype=np.float32),
        "idb": np.eye(128, dtype=np.float32).astype(ml_dtypes.bfloat16),
        "eb8": eb8,
    }



DL = 256
NH = 16


def mla_drams(nc, pfx="", io=True):
    A = {}
    if io:
        A["x"] = nc.dram_tensor(pfx + "x", [NCH, 128, DT, CH], F32, kind="ExternalInput").ap()
    A["wdkv"] = nc.dram_tensor(pfx + "wdkv", [128, DT, DL], F32, kind="ExternalInput").ap()
    A["wuk"] = nc.dram_tensor(pfx + "wuk", [128, 2, D], F32, kind="ExternalInput").ap()
    A["wuv"] = nc.dram_tensor(pfx + "wuv", [128, 2, D], F32, kind="ExternalInput").ap()
    A["wuq"] = nc.dram_tensor(pfx + "wuq", [128, DT, D], F32, kind="ExternalInput").ap()
    A["wo"] = nc.dram_tensor(pfx + "wo", [128, DT, D], F32, kind="ExternalInput").ap()
    A["mask4"] = nc.dram_tensor(pfx + "mask4", [128, 4, CH], BF16, kind="ExternalInput").ap()
    if io:
        A["y"] = nc.dram_tensor(pfx + "y", [NCH, 128, DT, CH], F32, kind="ExternalOutput").ap()
    A["qth"] = nc.dram_tensor(pfx + "qth", [NCH, 128, DT, CH], BF16, kind="Internal").ap()
    A["kth"] = nc.dram_tensor(pfx + "kth", [NCH, 128, DT, CH], BF16, kind="Internal").ap()
    A["vah"] = nc.dram_tensor(pfx + "vah", [NCH, 128, 4, NH * 65], BF16, kind="Internal").ap()

    return A


def emit_mla(nc, block, ctx, es, A):
    sb = lambda *a: es.enter_context(nc.sbuf_tensor(*a, allow_name_mangling=True))
    psa = lambda *a: es.enter_context(nc.psum_tensor(*a, allow_name_mangling=True))

    xf = sb("xf", [128, DT, CH], F32)
    xnc = sb("xnc", [128, 2, DT, CH], BF16)
    sq = sb("sqb", [128, 1, CH], BF16)
    bc = sb("bcb", [128, 2 * CH], BF16)
    rowm = sb("rowm", [1, CH], F32)
    rowv = sb("rowv", [1, CH], F32)
    rows = sb("rows", [1, CH], F32)
    onesb = sb("onesb", [128, 1], BF16)
    eps1 = sb("eps1", [1, 1], F32)
    ones1f = sb("ones1f", [1, 128], F32)
    neg10 = sb("neg10", [128, 1], F32)
    ckvT = sb("ckvT", [128, 2, T], BF16)
    wdkv_t = sb("wdkv_t", [128, DT, DL], BF16)
    wuk_t = sb("wuk_t", [128, 2, D], BF16)
    wuv_t = sb("wuv_t", [128, 2, D], BF16)
    wuq_t = sb("wuq_t", [128, DT, D], BF16)
    wo_t = sb("wo_t", [128, DT, D], BF16)
    qt2 = sb("qt2", [128, 2, DT, CH], BF16)
    kt2 = sb("kt2", [128, 2, DT, CH], BF16)
    va2 = sb("va2", [128, 2, 4, NH * 65], BF16)
    pb = sb("pb", [128, 2, CH], BF16)
    mask_t = sb("mask_t", [128, 4, CH], BF16)
    ctxEO = [sb("ctxE", [128, DT, CH], F32), sb("ctxO", [128, DT, CH], F32)]
    ctx_bf = sb("ctx_bf", [128, DT, CH], BF16)
    rden = sb("rden", [1, 4, CH], F32)

    outb = ctxEO[1][:]

    ps_stats = psa("ps_stats", [1, 2 * CH], F32)
    ps_bc = psa("ps_bc", [128, 2 * CH], F32)
    ps_w = psa("ps_w", [128, 2 * CH], F32)
    ps_pv = [psa("ps_pv0", [65, CH], F32), psa("ps_pv1", [65, CH], F32)]
    ps_s = [ps_w[:, 0:CH], ps_w[:, CH : 2 * CH]]
    ps_o = [ps_bc[:, 0:CH], ps_bc[:, CH : 2 * CH]]
    ps_s4 = [ps_w[:, 0:CH], ps_w[:, CH : 2 * CH],
             ps_bc[:, 0:CH], ps_bc[:, CH : 2 * CH]]
    pb4 = xnc[:, 0, 0:4, :]

    qtb = qt2[:, 0, :, :]
    ktb = kt2[:, 0, :, :]
    vab = va2[:, 0, :, :]


    bufs = dict(
        sq=sq, bc=bc, rowm=rowm, rowv=rowv, rows=rows, onesb=onesb,
        ones1f=ones1f, ps_stats=ps_stats, ps_bc=ps_bc, eps1=eps1,
    )

    w_tag = ctx.res("LW", 96)

    def b(gpsimd):
        for dst, srcd in [(wdkv_t, A["wdkv"]), (wuk_t, A["wuk"]), (wuv_t, A["wuv"]),
                          (wuq_t, A["wuq"]), (wo_t, A["wo"]), (mask_t, A["mask4"])]:
            gpsimd.dma_start(out=dst[:], in_=srcd).then_inc(ctx.sems["LW"], 16)

    ctx.on("gpsimd", b)

    init_ve = ctx.res("VE")

    def b(vector):
        vector.memset(onesb[:], 1.0)
        vector.memset(eps1[:], EPS_LN)
        vector.memset(neg10[:], -10.0)
        vector.memset(
            va2[:].rearrange("p b w (h e) -> p b w h e", e=65)[:, :, :, :, 64:65],
            1.0,
        )
        ctx.fin(vector.memset(ones1f[:], 1.0), "VE")

    ctx.on("vector", b)

    for c in range(NCH):
        xn = xnc[:, c % 2, :, :]
        csl = slice(c * CH, (c + 1) * CH)
        xf_free = ctx.t("xf_free")
        ld_tag = ctx.res("LD", 16)

        def b(sync, c=c, xf_free=xf_free):
            ctx.wait(sync, "VE", xf_free)
            sync.dma_start(out=xf[:], in_=A["x"][c]).then_inc(ctx.sems["LD"], 16)

        ctx.on("sync", b)

        xn_used = ctx.t(f"xn_used_{c-2}")

        def pre(vector, c=c, xn_used=xn_used, init_ve=init_ve):
            if c == 0:
                ctx.wait(vector, "VE", init_ve)
            if c >= 2:
                ctx.wait(vector, "MM", xn_used)

        ve_tag = q_ln_chunk(ctx, bufs, xf[:], xn, "LD", ld_tag, pre)
        ctx.tag["xf_free"] = ctx.t("ln_cast")

        for m2 in range(2):
            psw_free = ctx.t(f"psw_free_{m2 % 2}")
            mm_tag = ctx.res("MM")

            def b(tensor, c=c, m2=m2, xn=xn, ve_tag=ve_tag, psw_free=psw_free,
                  w_tag=w_tag):
                if c == 0 and m2 == 0:
                    ctx.wait(tensor, "LW", w_tag)
                ctx.wait(tensor, "VE", ve_tag)
                ctx.wait(tensor, "SE", psw_free)
                for a in range(DT):
                    mm = tensor.matmul(
                        ps_s[m2 % 2], wdkv_t[:, a, m2 * 128 : (m2 + 1) * 128],
                        xn[:, a, :], start=(a == 0), stop=(a == DT - 1),
                    )
                ctx.fin(mm, "MM")

            ctx.on("tensor", b)
            i = ctx.res("SE")
            ctx.tag[f"psw_free_{m2 % 2}"] = i

            def b(scalar, m2=m2, csl=csl, mm_tag=mm_tag):
                ctx.wait(scalar, "MM", mm_tag)
                ctx.fin(scalar.copy(ckvT[:, m2, csl], ps_s[m2 % 2]), "SE")

            ctx.on("scalar", b)

        ckv_se = ctx.c["SE"]

        for ot in range(DT):
            psw_free = ctx.t(f"psw_free_{ot % 2}")
            q_spill = ctx.t(f"q_spill_{c-1}")
            mm_tag = ctx.res("MM")

            def b(tensor, c=c, ot=ot, xn=xn, psw_free=psw_free, q_spill=q_spill):
                ctx.wait(tensor, "SE", psw_free)
                if ot == 0:
                    ctx.wait(tensor, "QS", q_spill)
                for a in range(DT):
                    mm = tensor.matmul(
                        ps_s[ot % 2], wuq_t[:, a, ot * 128 : (ot + 1) * 128],
                        xn[:, a, :], start=(a == 0), stop=(a == DT - 1),
                    )
                ctx.fin(mm, "MM")

            ctx.on("tensor", b)
            i = ctx.res("SE")
            ctx.tag[f"psw_free_{ot % 2}"] = i

            def b(scalar, ot=ot, mm_tag=mm_tag):
                ctx.wait(scalar, "MM", mm_tag)
                ctx.fin(scalar.copy(qtb[:, ot, :], ps_s[ot % 2]), "SE")

            ctx.on("scalar", b)

        q_se = ctx.c["SE"]
        ctx.res("QS", 16)
        ctx.tag[f"q_spill_{c}"] = ctx.c["QS"]

        def b(gpsimd, c=c, q_se=q_se):
            ctx.wait(gpsimd, "SE", q_se)
            gpsimd.dma_start(out=A["qth"][c], in_=qtb).then_inc(ctx.sems["QS"], 16)

        ctx.on("gpsimd", b)

        for ot in range(DT):
            psw_free = ctx.t(f"psw_free_{ot % 2}")
            k_spill = ctx.t(f"k_spill_{c-1}")
            mm_tag = ctx.res("MM")

            def b(tensor, c=c, ot=ot, csl=csl, ckv_se=ckv_se, psw_free=psw_free,
                  k_spill=k_spill):
                if ot == 0:
                    ctx.wait(tensor, "SE", ckv_se)
                    ctx.wait(tensor, "KS", k_spill)
                ctx.wait(tensor, "SE", psw_free)
                for c2 in range(2):
                    mm = tensor.matmul(
                        ps_s[ot % 2], wuk_t[:, c2, ot * 128 : (ot + 1) * 128],
                        ckvT[:, c2, csl], start=(c2 == 0), stop=(c2 == 1),
                    )
                ctx.fin(mm, "MM")

            ctx.on("tensor", b)
            i = ctx.res("SE")
            ctx.tag[f"psw_free_{ot % 2}"] = i

            def b(scalar, ot=ot, mm_tag=mm_tag):
                ctx.wait(scalar, "MM", mm_tag)
                ctx.fin(scalar.copy(ktb[:, ot, :], ps_s[ot % 2]), "SE")

            ctx.on("scalar", b)

        k_se = ctx.c["SE"]
        ctx.res("KS", 16)
        ctx.tag[f"k_spill_{c}"] = ctx.c["KS"]

        def b(gpsimd, c=c, k_se=k_se):
            ctx.wait(gpsimd, "SE", k_se)
            gpsimd.dma_start(out=A["kth"][c], in_=ktb).then_inc(ctx.sems["KS"], 16)

        ctx.on("gpsimd", b)

        for tt in range(4):
            psbc_free = ctx.t("psbc_free")
            v_spill = ctx.t(f"v_spill_{c-1}")
            mm_tag = ctx.res("MM")

            def b(tensor, c=c, tt=tt, ckv_se=ckv_se, psbc_free=psbc_free,
                  v_spill=v_spill):
                if tt == 0:
                    ctx.wait(tensor, "SE", ckv_se)
                    ctx.wait(tensor, "VS", v_spill)
                ctx.wait(tensor, "VE", psbc_free)
                for half in range(2):
                    for c2 in range(2):
                        mm = tensor.matmul(
                            ps_bc[:, half * CH : (half + 1) * CH],
                            ckvT[:, c2, c * CH + tt * 128 : c * CH + (tt + 1) * 128],
                            wuv_t[:, c2, half * CH : (half + 1) * CH],
                            start=(c2 == 0), stop=(c2 == 1),
                        )
                ctx.fin(mm, "MM")

            ctx.on("tensor", b)
            i = ctx.res("VE")
            ctx.tag["psbc_free"] = i

            def b(vector, tt=tt, mm_tag=mm_tag):
                ctx.wait(vector, "MM", mm_tag)
                ctx.fin(
                    vector.tensor_copy(
                        vab.rearrange("p w (h e) -> p w h e", e=65)[
                            :, tt, :, 0:64
                        ],
                        ps_bc[:].rearrange("p (h e) -> p h e", e=64),
                    ),
                    "VE",
                )

            ctx.on("vector", b)

        v_ve = ctx.c["VE"]
        ctx.res("VS", 16)
        ctx.tag[f"v_spill_{c}"] = ctx.c["VS"]

        def b(gpsimd, c=c, v_ve=v_ve):
            ctx.wait(gpsimd, "VE", v_ve)
            gpsimd.dma_start(out=A["vah"][c], in_=vab).then_inc(ctx.sems["VS"], 16)

        ctx.on("gpsimd", b)

        ctx.tag[f"xn_used_{c}"] = ctx.c["MM"]
        ctx.flush()

    stageN_mm = ctx.c["MM"]
    stageN_ve = ctx.c["VE"]
    stageN_se = ctx.c["SE"]

    for qc in range(NCH):
        qb = qc % 2
        qt_used = ctx.t(f"qt2_used_{qc-2}")
        out_done = ctx.t("out_done")
        xf_free = ctx.t("xf_free")
        q_spill = ctx.t(f"q_spill_{qc}")
        q_ld = ctx.res("LD", 32)

        def b(sync, qc=qc, qb=qb, qt_used=qt_used, out_done=out_done,
              xf_free=xf_free, q_spill=q_spill):
            ctx.wait(sync, "QS", q_spill)
            if qc < 2:
                ctx.wait(sync, "MM", stageN_mm)
                ctx.wait(sync, "VE", stageN_ve)
            else:
                ctx.wait(sync, "MM", qt_used)
            sync.dma_start(out=qt2[:, qb, :, :], in_=A["qth"][qc]).then_inc(
                ctx.sems["LD"], 16
            )
            ctx.wait(sync, "VE", out_done)
            if qc == 0:
                ctx.wait(sync, "VE", xf_free)
            sync.dma_start(out=xf[:], in_=A["x"][qc]).then_inc(ctx.sems["LD"], 16)

        ctx.on("sync", b)

        for kc in range(qc + 1):
            it = qc * (qc + 1) // 2 + kc
            kb = it % 2
            diag = kc == qc

            kv_used = ctx.t(f"kv2_used_{it-2}")
            k_spill = ctx.t(f"k_spill_{kc}")
            v_spill = ctx.t(f"v_spill_{kc}")
            kv_ld = ctx.res("LD", 32)

            def b(sync, qc=qc, kc=kc, it=it, kb=kb, kv_used=kv_used,
                  k_spill=k_spill, v_spill=v_spill):
                ctx.wait(sync, "KS", k_spill)
                ctx.wait(sync, "VS", v_spill)
                if it >= 2:
                    ctx.wait(sync, "MM", kv_used)
                elif qc < 2:
                    ctx.wait(sync, "MM", stageN_mm)
                sync.dma_start(out=kt2[:, kb, :, :], in_=A["kth"][kc]).then_inc(
                    ctx.sems["LD"], 16
                )
                sync.dma_start(out=va2[:, kb, :, :], in_=A["vah"][kc]).then_inc(
                    ctx.sems["LD"], 16
                )

            ctx.on("sync", b)

            se_base = ctx.c["SE"]
            rbc_read = ctx.t("rbc_read")

            mm_tags = {}
            ve_mask_tags = {}
            n_ve_pre = ctx.c["VE"]

            LAGM = 3
            msched = []
            for j in range(NH * 4 + LAGM):
                if j < NH * 4:
                    msched.append(("S", j))
                if j >= LAGM:
                    msched.append(("P", j - LAGM))
            for kind, j in msched:
                if kind == "S":
                    mm_tags[("s", j // 4, j % 4)] = ctx.res("MM")
                else:
                    mm_tags[("pv", j // 4, j % 4)] = ctx.res("MM")
            for h in range(NH):
                for kt in range(4):
                    ctx.res("SE")
                    if diag:
                        ve_mask_tags[(h, kt)] = ctx.res("VE")
                mm_tags[("acc", h)] = ctx.res("VE")

            pv_tail = [ctx.t(f"pv_tail_{i}") for i in range(4)]
            for i in range(4):
                ctx.tag[f"pv_tail_{i}"] = mm_tags[("pv", NH - 1, i)]
            pso_p = [ctx.t("pso_free_0"), ctx.t("pso_free_1")]
            pv_used_prev = [ctx.t("pv_used_0"), ctx.t("pv_used_1")]
            ctx.tag["pv_used_0"] = mm_tags[("acc", NH - 2)]
            ctx.tag["pv_used_1"] = mm_tags[("acc", NH - 1)]
            st_used = ctx.t("out_stored")

            def b(tensor, qc=qc, kc=kc, it=it, kb=kb, qb=qb, diag=diag,
                  kv_ld=kv_ld, q_ld=q_ld, se_base=se_base, mm_tags=mm_tags,
                  ve_mask_tags=ve_mask_tags, msched=msched, pso_p=pso_p,
                  rbc_read=rbc_read, pv_used_prev=pv_used_prev):
                def eS(j):
                    h, kt = j // 4, j % 4
                    a = h // 2
                    base = 64 * (h % 2)
                    if j == 0:
                        ctx.wait(tensor, "LD", kv_ld)
                        ctx.wait(tensor, "LD", q_ld - 16)
                        if kc == 0:
                            if qc == 0:
                                ctx.wait(tensor, "SE", stageN_se)
                            ctx.wait(tensor, "VE", rbc_read)
                    if kc == 0 and j in (2, 3):
                        ctx.wait(tensor, "VE", pso_p[j - 2])
                    if j >= 4:
                        ctx.wait(tensor, "SE", se_base + j - 3)
                    ctx.fin(
                        tensor.matmul(
                            ps_s4[j % 4],
                            kt2[base : base + 64, kb, a,
                                kt * 128 : (kt + 1) * 128],
                            qt2[base : base + 64, qb, a, :],
                            start=True, stop=True,
                            tile_position=(base, 0),
                        ),
                        "MM",
                    )

                def eP(j):
                    h, kt = j // 4, j % 4
                    if diag:
                        ctx.wait(tensor, "VE", ve_mask_tags[(h, kt)])
                    else:
                        ctx.wait(tensor, "SE", se_base + j + 1)
                    if kt == 0:
                        ctx.wait(
                            tensor, "VE",
                            mm_tags[("acc", h - 2)] if h >= 2
                            else pv_used_prev[h % 2],
                        )
                    ctx.fin(
                        tensor.matmul(
                            ps_pv[h % 2][:],
                            va2[:, kb, kt, h * 65 : (h + 1) * 65],
                            pb4[:, j % 4, :],
                            start=(kt == 0), stop=(kt == 3),
                        ),
                        "MM",
                    )

                for kind, j in msched:
                    (eS if kind == "S" else eP)(j)

            ctx.on("tensor", b)

            def b(scalar, diag=diag, mm_tags=mm_tags, pv_tail=pv_tail):
                for h in range(NH):
                    for kt in range(4):
                        j = h * 4 + kt
                        ctx.wait(scalar, "MM", mm_tags[("s", h, kt)])
                        if j >= 4:
                            ctx.wait(scalar, "MM",
                                     mm_tags[("pv", (j - 4) // 4, (j - 4) % 4)])
                        else:
                            ctx.wait(scalar, "MM", pv_tail[j])
                        ctx.fin(
                            scalar.activation(
                                pb4[:, j % 4, :], ps_s4[j % 4], AF.Exp,
                                bias=neg10[:], scale=0.125,
                            ),
                            "SE",
                        )

            ctx.on("scalar", b)

            def b(vector, qc=qc, kc=kc, diag=diag, mm_tags=mm_tags,
                  se_base=se_base, st_used=st_used):
                for h in range(NH):
                    for kt in range(4):
                        j = h * 4 + kt
                        if diag:
                            ctx.wait(vector, "SE", se_base + j + 1)
                            ctx.fin(
                                vector.tensor_tensor(
                                    pb4[:, j % 4, :], pb4[:, j % 4, :],
                                    mask_t[:, kt, :], op=ALU.mult,
                                ),
                                "VE",
                            )
                    ctx.wait(vector, "MM", mm_tags[("pv", h, 3)])
                    if kc == 0 and h < 2:
                        ctx.wait(vector, "ST", st_used)
                    dst = ctxEO[h % 2][0:65, h // 2, :]
                    if kc == 0:
                        ctx.fin(vector.tensor_copy(dst, ps_pv[h % 2][:]), "VE")
                    else:
                        ctx.fin(
                            vector.tensor_tensor(
                                dst, dst, ps_pv[h % 2][:], op=ALU.add
                            ),
                            "VE",
                        )

            ctx.on("vector", b)
            ctx.tag[f"kv2_used_{it}"] = ctx.c["MM"]
            ctx.flush()

        ctx.tag[f"qt2_used_{qc}"] = ctx.c["MM"]
        acc_ve = ctx.c["VE"]

        for batch in range(4):
            par = batch // 2
            alo = 4 * (batch % 2)
            rbc_used = ctx.t("rbc_used")
            r_ve = ctx.res("VE")

            def b(vector, batch=batch, par=par, alo=alo, acc_ve=acc_ve,
                  rbc_used=rbc_used):
                if batch == 0:
                    ctx.wait(vector, "VE", acc_ve)
                ctx.wait(vector, "MM", rbc_used)
                vector.drain()
                ctx.fin(
                    vector.reciprocal(
                        rden[:, :, :], ctxEO[par][64:65, alo : alo + 4, :]
                    ),
                    "VE",
                )

            ctx.on("vector", b)

            for i4 in range(4):
                a = alo + i4
                h = 2 * a + par
                base = 64 * par
                rbc_read2 = ctx.t("rbc_read")
                b_tag = ctx.res("MM")
                ctx.tag["rbc_used"] = b_tag

                def b(tensor, i4=i4, r_ve=r_ve, rbc_read2=rbc_read2):
                    if i4 == 0:
                        ctx.wait(tensor, "VE", r_ve)
                    ctx.wait(tensor, "VE", rbc_read2)
                    ctx.fin(
                        tensor.matmul(
                            ps_s[0][0:64, :], ones1f[:, 0:64],
                            rden[:, i4, :], start=True, stop=True,
                        ),
                        "MM",
                    )

                ctx.on("tensor", b)

                i = ctx.res("VE")
                ctx.tag["rbc_read"] = i

                def b(vector, par=par, a=a, base=base, b_tag=b_tag):
                    ctx.wait(vector, "MM", b_tag)
                    ctx.fin(
                        vector.tensor_tensor(
                            ctx_bf[base : base + 64, a, :],
                            ctxEO[par][0:64, a, :],
                            ps_s[0][0:64, :], op=ALU.mult,
                        ),
                        "VE",
                    )

                ctx.on("vector", b)

        norm_ve = ctx.c["VE"]

        for ot in range(DT):
            pso_free = ctx.t(f"pso_free_{ot % 2}")
            o_tag = ctx.res("MM")

            def b(tensor, qc=qc, ot=ot, norm_ve=norm_ve, pso_free=pso_free):
                if ot == 0:
                    ctx.wait(tensor, "VE", norm_ve)
                ctx.wait(tensor, "VE", pso_free)
                for a in range(DT):
                    mm = tensor.matmul(
                        ps_o[ot % 2], wo_t[:, a, ot * 128 : (ot + 1) * 128],
                        ctx_bf[:, a, :], start=(a == 0), stop=(a == DT - 1),
                    )
                ctx.fin(mm, "MM")

            ctx.on("tensor", b)

            st_prev = ctx.t("out_stored")
            i = ctx.res("VE")
            ctx.tag[f"pso_free_{ot % 2}"] = i
            if ot == DT - 1:
                ctx.tag["out_done"] = i

            def b(vector, qc=qc, ot=ot, o_tag=o_tag, q_ld=q_ld, st_prev=st_prev):
                ctx.wait(vector, "MM", o_tag)
                ctx.wait(vector, "LD", q_ld)
                if ot == 0:
                    ctx.wait(vector, "ST", st_prev)
                ctx.fin(
                    vector.tensor_tensor(
                        outb[:, ot, :], ps_o[ot % 2], xf[:, ot, :], op=ALU.add
                    ),
                    "VE",
                )

            ctx.on("vector", b)

        out_ve = ctx.c["VE"]
        ctx.res("ST", 16)
        ctx.tag["out_stored"] = ctx.c["ST"]

        def b(sync, qc=qc, out_ve=out_ve):
            ctx.wait(sync, "VE", out_ve)
            sync.dma_start(out=A["y"][qc], in_=outb).then_inc(ctx.sems["ST"], 16)

        ctx.on("sync", b)
        ctx.flush()

    st_final = ctx.c["ST"]

    def b(sync):
        ctx.wait(sync, "ST", st_final)

    ctx.on("sync", b)
    ctx.flush()


def build_mla():
    nc = bass.Bass("TRN2", target_bir_lowering=False)
    A = mla_drams(nc)
    es = contextlib.ExitStack()
    with es, nc.Block() as block:
        ctx = Ctx(nc, es, block)
        emit_mla(nc, block, ctx, es, A)
    return nc


def lora_drams(nc, pfx="", io=True):
    A = {}
    if io:
        A["x"] = nc.dram_tensor(pfx + "x", [NCH, 128, DT, CH], F32, kind="ExternalInput").ap()
    A["wqt"] = nc.dram_tensor(pfx + "wqt", [128, DT, D], F32, kind="ExternalInput").ap()
    A["wkt"] = nc.dram_tensor(pfx + "wkt", [128, DT, D], F32, kind="ExternalInput").ap()
    A["wvt"] = nc.dram_tensor(pfx + "wvt", [128, DT, D], F32, kind="ExternalInput").ap()
    A["wot"] = nc.dram_tensor(pfx + "wot", [128, DT, D], F32, kind="ExternalInput").ap()
    A["lat"] = nc.dram_tensor(pfx + "lat", [128, 4, DT, 16], F32, kind="ExternalInput").ap()
    A["lbt"] = nc.dram_tensor(pfx + "lbt", [16, 4, D], F32, kind="ExternalInput").ap()
    A["mask4"] = nc.dram_tensor(pfx + "mask4", [128, 4, CH], BF16, kind="ExternalInput").ap()
    if io:
        A["y"] = nc.dram_tensor(pfx + "y", [NCH, 128, DT, CH], F32, kind="ExternalOutput").ap()
    A["qth"] = nc.dram_tensor(pfx + "qth", [NCH, 128, DT, CH], BF16, kind="Internal").ap()
    A["kth"] = nc.dram_tensor(pfx + "kth", [NCH, 128, DT, CH], BF16, kind="Internal").ap()
    A["vah"] = nc.dram_tensor(pfx + "vah", [NCH, 128, 4, NH * 65], BF16, kind="Internal").ap()

    return A


def emit_lora(nc, block, ctx, es, A):
    sb = lambda *a: es.enter_context(nc.sbuf_tensor(*a, allow_name_mangling=True))
    psa = lambda *a: es.enter_context(nc.psum_tensor(*a, allow_name_mangling=True))

    xf = sb("xf", [128, DT, CH], F32)
    xnc = sb("xnc", [128, 2, DT, CH], BF16)
    sq = sb("sqb", [128, 1, CH], BF16)
    bc = sb("bcb", [128, 2 * CH], BF16)
    rowm = sb("rowm", [1, CH], F32)
    rowv = sb("rowv", [1, CH], F32)
    rows = sb("rows", [1, CH], F32)
    onesb = sb("onesb", [128, 1], BF16)
    eps1 = sb("eps1", [1, 1], F32)
    ones1f = sb("ones1f", [1, 128], F32)
    neg10 = sb("neg10", [128, 1], F32)
    wq_t = sb("wq_t", [128, DT, D], BF16)
    wk_t = sb("wk_t", [128, DT, D], BF16)
    wv_t = sb("wv_t", [128, DT, D], BF16)
    la_t = sb("la_t", [128, 4, DT, 16], BF16)
    lb_t = sb("lb_t", [16, 4, D], BF16)
    lraw = sb("lraw", [16, 3, CH], BF16)
    lo_raw = sb("lo_raw", [16, CH], BF16)
    wo_t = wq_t
    qt2 = sb("qt2", [128, 2, DT, CH], BF16)
    kt2 = sb("kt2", [128, 2, DT, CH], BF16)
    va2 = sb("va2", [128, 2, 4, NH * 65], BF16)
    pb = sb("pb", [128, 2, CH], BF16)
    mask_t = sb("mask_t", [128, 4, CH], BF16)
    ctxEO = [sb("ctxE", [128, DT, CH], F32), sb("ctxO", [128, DT, CH], F32)]
    ctx_bf = sb("ctx_bf", [128, DT, CH], BF16)
    rden = sb("rden", [1, 4, CH], F32)

    outb = ctxEO[1][:]

    ps_stats = psa("ps_stats", [1, 2 * CH], F32)
    ps_bc = psa("ps_bc", [128, 2 * CH], F32)
    ps_w = psa("ps_w", [128, 2 * CH], F32)
    ps_pv = [psa("ps_pv0", [65, CH], F32), psa("ps_pv1", [65, CH], F32)]
    ps_s = [ps_w[:, 0:CH], ps_w[:, CH : 2 * CH]]
    ps_o = [ps_bc[:, 0:CH], ps_bc[:, CH : 2 * CH]]
    ps_s4 = [ps_w[:, 0:CH], ps_w[:, CH : 2 * CH],
             ps_bc[:, 0:CH], ps_bc[:, CH : 2 * CH]]
    pb4 = xnc[:, 0, 0:4, :]

    qtb = qt2[:, 0, :, :]
    ktb = kt2[:, 0, :, :]
    vab = va2[:, 0, :, :]


    bufs = dict(
        sq=sq, bc=bc, rowm=rowm, rowv=rowv, rows=rows, onesb=onesb,
        ones1f=ones1f, ps_stats=ps_stats, ps_bc=ps_bc, eps1=eps1,
    )

    w_tag = ctx.res("LW", 96)

    def b(gpsimd):
        for dst, srcd in [(wq_t, A["wqt"]), (wk_t, A["wkt"]), (wv_t, A["wvt"]),
                          (la_t, A["lat"]), (lb_t, A["lbt"]), (mask_t, A["mask4"])]:
            gpsimd.dma_start(out=dst[:], in_=srcd).then_inc(ctx.sems["LW"], 16)

    ctx.on("gpsimd", b)

    init_ve = ctx.res("VE")

    def b(vector):
        vector.memset(onesb[:], 1.0)
        vector.memset(eps1[:], EPS_LN)
        vector.memset(neg10[:], -10.0)
        vector.memset(
            va2[:].rearrange("p b w (h e) -> p b w h e", e=65)[:, :, :, :, 64:65],
            1.0,
        )
        ctx.fin(vector.memset(ones1f[:], 1.0), "VE")

    ctx.on("vector", b)

    for c in range(NCH):
        xn = xnc[:, c % 2, :, :]
        csl = slice(c * CH, (c + 1) * CH)
        xf_free = ctx.t("xf_free")
        ld_tag = ctx.res("LD", 16)

        def b(sync, c=c, xf_free=xf_free):
            ctx.wait(sync, "VE", xf_free)
            sync.dma_start(out=xf[:], in_=A["x"][c]).then_inc(ctx.sems["LD"], 16)

        ctx.on("sync", b)

        xn_used = ctx.t(f"xn_used_{c-2}")

        def pre(vector, c=c, xn_used=xn_used, init_ve=init_ve):
            if c == 0:
                ctx.wait(vector, "VE", init_ve)
            if c >= 2:
                ctx.wait(vector, "MM", xn_used)

        ve_tag = q_ln_chunk(ctx, bufs, xf[:], xn, "LD", ld_tag, pre)
        ctx.tag["xf_free"] = ctx.t("ln_cast")

        lraw_used = ctx.t("lraw_used")
        for w3 in range(3):
            psw_free = ctx.t(f"psw_free_{w3 % 2}")
            mm_tag = ctx.res("MM")

            def b(tensor, c=c, w3=w3, xn=xn, ve_tag=ve_tag, psw_free=psw_free,
                  w_tag=w_tag):
                if c == 0 and w3 == 0:
                    ctx.wait(tensor, "LW", w_tag)
                ctx.wait(tensor, "VE", ve_tag)
                ctx.wait(tensor, "SE", psw_free)
                for a in range(DT):
                    mm = tensor.matmul(
                        ps_s[w3 % 2][0:16, :], la_t[:, w3, a, :],
                        xn[:, a, :], start=(a == 0), stop=(a == DT - 1),
                    )
                ctx.fin(mm, "MM")

            ctx.on("tensor", b)
            i = ctx.res("SE")
            ctx.tag[f"psw_free_{w3 % 2}"] = i

            def b(scalar, w3=w3, mm_tag=mm_tag, lraw_used=lraw_used):
                ctx.wait(scalar, "MM", mm_tag)
                if w3 == 0:
                    ctx.wait(scalar, "MM", lraw_used)
                ctx.fin(
                    scalar.activation(
                        lraw[:, w3, :], ps_s[w3 % 2][0:16, :], AF.Identity,
                        scale=1.0 / 16.0,
                    ),
                    "SE",
                )

            ctx.on("scalar", b)

        ckv_se = ctx.c["SE"]

        for ot in range(DT):
            psw_free = ctx.t(f"psw_free_{ot % 2}")
            q_spill = ctx.t(f"q_spill_{c-1}")
            mm_tag = ctx.res("MM")

            def b(tensor, c=c, ot=ot, xn=xn, psw_free=psw_free, q_spill=q_spill,
                  ckv_se=ckv_se):
                ctx.wait(tensor, "SE", psw_free)
                if ot == 0:
                    ctx.wait(tensor, "QS", q_spill)
                    ctx.wait(tensor, "SE", ckv_se)
                for a in range(DT):
                    tensor.matmul(
                        ps_s[ot % 2], wq_t[:, a, ot * 128 : (ot + 1) * 128],
                        xn[:, a, :], start=(a == 0), stop=False,
                    )
                mm = tensor.matmul(
                    ps_s[ot % 2], lb_t[:, 0, ot * 128 : (ot + 1) * 128],
                    lraw[:, 0, :], start=False, stop=True,
                )
                ctx.fin(mm, "MM")

            ctx.on("tensor", b)
            i = ctx.res("SE")
            ctx.tag[f"psw_free_{ot % 2}"] = i

            def b(scalar, ot=ot, mm_tag=mm_tag):
                ctx.wait(scalar, "MM", mm_tag)
                ctx.fin(scalar.copy(qtb[:, ot, :], ps_s[ot % 2]), "SE")

            ctx.on("scalar", b)

        q_se = ctx.c["SE"]
        ctx.res("QS", 16)
        ctx.tag[f"q_spill_{c}"] = ctx.c["QS"]

        def b(gpsimd, c=c, q_se=q_se):
            ctx.wait(gpsimd, "SE", q_se)
            gpsimd.dma_start(out=A["qth"][c], in_=qtb).then_inc(ctx.sems["QS"], 16)

        ctx.on("gpsimd", b)

        for ot in range(DT):
            psw_free = ctx.t(f"psw_free_{ot % 2}")
            k_spill = ctx.t(f"k_spill_{c-1}")
            mm_tag = ctx.res("MM")

            def b(tensor, c=c, ot=ot, xn=xn, ckv_se=ckv_se, psw_free=psw_free,
                  k_spill=k_spill):
                if ot == 0:
                    ctx.wait(tensor, "KS", k_spill)
                ctx.wait(tensor, "SE", psw_free)
                for a in range(DT):
                    tensor.matmul(
                        ps_s[ot % 2], wk_t[:, a, ot * 128 : (ot + 1) * 128],
                        xn[:, a, :], start=(a == 0), stop=False,
                    )
                mm = tensor.matmul(
                    ps_s[ot % 2], lb_t[:, 1, ot * 128 : (ot + 1) * 128],
                    lraw[:, 1, :], start=False, stop=True,
                )
                ctx.fin(mm, "MM")

            ctx.on("tensor", b)
            i = ctx.res("SE")
            ctx.tag[f"psw_free_{ot % 2}"] = i

            def b(scalar, ot=ot, mm_tag=mm_tag):
                ctx.wait(scalar, "MM", mm_tag)
                ctx.fin(scalar.copy(ktb[:, ot, :], ps_s[ot % 2]), "SE")

            ctx.on("scalar", b)

        k_se = ctx.c["SE"]
        ctx.res("KS", 16)
        ctx.tag[f"k_spill_{c}"] = ctx.c["KS"]

        def b(gpsimd, c=c, k_se=k_se):
            ctx.wait(gpsimd, "SE", k_se)
            gpsimd.dma_start(out=A["kth"][c], in_=ktb).then_inc(ctx.sems["KS"], 16)

        ctx.on("gpsimd", b)

        for tt in range(4):
            psbc_free = ctx.t("psbc_free")
            v_spill = ctx.t(f"v_spill_{c-1}")
            mm_tag = ctx.res("MM")

            def b(tensor, c=c, tt=tt, xn=xn, ckv_se=ckv_se, psbc_free=psbc_free,
                  v_spill=v_spill):
                if tt == 0:
                    ctx.wait(tensor, "VS", v_spill)
                ctx.wait(tensor, "VE", psbc_free)
                for half in range(2):
                    for a in range(DT):
                        tensor.matmul(
                            ps_bc[:, half * CH : (half + 1) * CH],
                            xn[:, a, tt * 128 : (tt + 1) * 128],
                            wv_t[:, a, half * CH : (half + 1) * CH],
                            start=(a == 0), stop=False,
                        )
                    mm = tensor.matmul(
                        ps_bc[:, half * CH : (half + 1) * CH],
                        lraw[:, 2, tt * 128 : (tt + 1) * 128],
                        lb_t[:, 2, half * CH : (half + 1) * CH],
                        start=False, stop=True,
                    )
                ctx.fin(mm, "MM")

            ctx.on("tensor", b)
            i = ctx.res("VE")
            ctx.tag["psbc_free"] = i

            def b(vector, tt=tt, mm_tag=mm_tag):
                ctx.wait(vector, "MM", mm_tag)
                ctx.fin(
                    vector.tensor_copy(
                        vab.rearrange("p w (h e) -> p w h e", e=65)[
                            :, tt, :, 0:64
                        ],
                        ps_bc[:].rearrange("p (h e) -> p h e", e=64),
                    ),
                    "VE",
                )

            ctx.on("vector", b)

        v_ve = ctx.c["VE"]
        ctx.res("VS", 16)
        ctx.tag[f"v_spill_{c}"] = ctx.c["VS"]

        def b(gpsimd, c=c, v_ve=v_ve):
            ctx.wait(gpsimd, "VE", v_ve)
            gpsimd.dma_start(out=A["vah"][c], in_=vab).then_inc(ctx.sems["VS"], 16)

        ctx.on("gpsimd", b)

        ctx.tag[f"xn_used_{c}"] = ctx.c["MM"]
        ctx.tag["lraw_used"] = ctx.c["MM"]
        ctx.flush()

    stageN_mm = ctx.c["MM"]
    stageN_ve = ctx.c["VE"]
    stageN_se = ctx.c["SE"]

    wo_tag = ctx.res("LW", 16)

    def b(gpsimd):
        ctx.wait(gpsimd, "MM", stageN_mm)
        gpsimd.dma_start(out=wo_t[:], in_=A["wot"]).then_inc(ctx.sems["LW"], 16)

    ctx.on("gpsimd", b)

    for qc in range(NCH):
        qb = qc % 2
        qt_used = ctx.t(f"qt2_used_{qc-2}")
        out_done = ctx.t("out_done")
        xf_free = ctx.t("xf_free")
        q_spill = ctx.t(f"q_spill_{qc}")
        q_ld = ctx.res("LD", 32)

        def b(sync, qc=qc, qb=qb, qt_used=qt_used, out_done=out_done,
              xf_free=xf_free, q_spill=q_spill):
            ctx.wait(sync, "QS", q_spill)
            if qc < 2:
                ctx.wait(sync, "MM", stageN_mm)
                ctx.wait(sync, "VE", stageN_ve)
            else:
                ctx.wait(sync, "MM", qt_used)
            sync.dma_start(out=qt2[:, qb, :, :], in_=A["qth"][qc]).then_inc(
                ctx.sems["LD"], 16
            )
            ctx.wait(sync, "VE", out_done)
            if qc == 0:
                ctx.wait(sync, "VE", xf_free)
            sync.dma_start(out=xf[:], in_=A["x"][qc]).then_inc(ctx.sems["LD"], 16)

        ctx.on("sync", b)

        for kc in range(qc + 1):
            it = qc * (qc + 1) // 2 + kc
            kb = it % 2
            diag = kc == qc

            kv_used = ctx.t(f"kv2_used_{it-2}")
            k_spill = ctx.t(f"k_spill_{kc}")
            v_spill = ctx.t(f"v_spill_{kc}")
            kv_ld = ctx.res("LD", 32)

            def b(sync, qc=qc, kc=kc, it=it, kb=kb, kv_used=kv_used,
                  k_spill=k_spill, v_spill=v_spill):
                ctx.wait(sync, "KS", k_spill)
                ctx.wait(sync, "VS", v_spill)
                if it >= 2:
                    ctx.wait(sync, "MM", kv_used)
                elif qc < 2:
                    ctx.wait(sync, "MM", stageN_mm)
                sync.dma_start(out=kt2[:, kb, :, :], in_=A["kth"][kc]).then_inc(
                    ctx.sems["LD"], 16
                )
                sync.dma_start(out=va2[:, kb, :, :], in_=A["vah"][kc]).then_inc(
                    ctx.sems["LD"], 16
                )

            ctx.on("sync", b)

            se_base = ctx.c["SE"]
            rbc_read = ctx.t("rbc_read")

            mm_tags = {}
            ve_mask_tags = {}
            n_ve_pre = ctx.c["VE"]

            LAGM = 3
            msched = []
            for j in range(NH * 4 + LAGM):
                if j < NH * 4:
                    msched.append(("S", j))
                if j >= LAGM:
                    msched.append(("P", j - LAGM))
            for kind, j in msched:
                if kind == "S":
                    mm_tags[("s", j // 4, j % 4)] = ctx.res("MM")
                else:
                    mm_tags[("pv", j // 4, j % 4)] = ctx.res("MM")
            for h in range(NH):
                for kt in range(4):
                    ctx.res("SE")
                    if diag:
                        ve_mask_tags[(h, kt)] = ctx.res("VE")
                mm_tags[("acc", h)] = ctx.res("VE")

            pv_tail = [ctx.t(f"pv_tail_{i}") for i in range(4)]
            for i in range(4):
                ctx.tag[f"pv_tail_{i}"] = mm_tags[("pv", NH - 1, i)]
            pso_p = [ctx.t("pso_free_0"), ctx.t("pso_free_1")]
            pv_used_prev = [ctx.t("pv_used_0"), ctx.t("pv_used_1")]
            ctx.tag["pv_used_0"] = mm_tags[("acc", NH - 2)]
            ctx.tag["pv_used_1"] = mm_tags[("acc", NH - 1)]
            st_used = ctx.t("out_stored")

            def b(tensor, qc=qc, kc=kc, it=it, kb=kb, qb=qb, diag=diag,
                  kv_ld=kv_ld, q_ld=q_ld, se_base=se_base, mm_tags=mm_tags,
                  ve_mask_tags=ve_mask_tags, msched=msched, pso_p=pso_p,
                  rbc_read=rbc_read, pv_used_prev=pv_used_prev):
                def eS(j):
                    h, kt = j // 4, j % 4
                    a = h // 2
                    base = 64 * (h % 2)
                    if j == 0:
                        ctx.wait(tensor, "LD", kv_ld)
                        ctx.wait(tensor, "LD", q_ld - 16)
                        if kc == 0:
                            if qc == 0:
                                ctx.wait(tensor, "SE", stageN_se)
                            ctx.wait(tensor, "VE", rbc_read)
                    if kc == 0 and j in (2, 3):
                        ctx.wait(tensor, "VE", pso_p[j - 2])
                    if j >= 4:
                        ctx.wait(tensor, "SE", se_base + j - 3)
                    ctx.fin(
                        tensor.matmul(
                            ps_s4[j % 4],
                            kt2[base : base + 64, kb, a,
                                kt * 128 : (kt + 1) * 128],
                            qt2[base : base + 64, qb, a, :],
                            start=True, stop=True,
                            tile_position=(base, 0),
                        ),
                        "MM",
                    )

                def eP(j):
                    h, kt = j // 4, j % 4
                    if diag:
                        ctx.wait(tensor, "VE", ve_mask_tags[(h, kt)])
                    else:
                        ctx.wait(tensor, "SE", se_base + j + 1)
                    if kt == 0:
                        ctx.wait(
                            tensor, "VE",
                            mm_tags[("acc", h - 2)] if h >= 2
                            else pv_used_prev[h % 2],
                        )
                        if h == 0 and kc == 0:
                            ctx.wait(tensor, "SE", ctx.t("lo_done_se"))
                    ctx.fin(
                        tensor.matmul(
                            ps_pv[h % 2][:],
                            va2[:, kb, kt, h * 65 : (h + 1) * 65],
                            pb4[:, j % 4, :],
                            start=(kt == 0), stop=(kt == 3),
                        ),
                        "MM",
                    )

                for kind, j in msched:
                    (eS if kind == "S" else eP)(j)

            ctx.on("tensor", b)

            def b(scalar, diag=diag, mm_tags=mm_tags, pv_tail=pv_tail):
                for h in range(NH):
                    for kt in range(4):
                        j = h * 4 + kt
                        ctx.wait(scalar, "MM", mm_tags[("s", h, kt)])
                        if j >= 4:
                            ctx.wait(scalar, "MM",
                                     mm_tags[("pv", (j - 4) // 4, (j - 4) % 4)])
                        else:
                            ctx.wait(scalar, "MM", pv_tail[j])
                        ctx.fin(
                            scalar.activation(
                                pb4[:, j % 4, :], ps_s4[j % 4], AF.Exp,
                                bias=neg10[:], scale=0.125,
                            ),
                            "SE",
                        )

            ctx.on("scalar", b)

            def b(vector, qc=qc, kc=kc, diag=diag, mm_tags=mm_tags,
                  se_base=se_base, st_used=st_used):
                for h in range(NH):
                    for kt in range(4):
                        j = h * 4 + kt
                        if diag:
                            ctx.wait(vector, "SE", se_base + j + 1)
                            ctx.fin(
                                vector.tensor_tensor(
                                    pb4[:, j % 4, :], pb4[:, j % 4, :],
                                    mask_t[:, kt, :], op=ALU.mult,
                                ),
                                "VE",
                            )
                    ctx.wait(vector, "MM", mm_tags[("pv", h, 3)])
                    if kc == 0 and h < 2:
                        ctx.wait(vector, "ST", st_used)
                    dst = ctxEO[h % 2][0:65, h // 2, :]
                    if kc == 0:
                        ctx.fin(vector.tensor_copy(dst, ps_pv[h % 2][:]), "VE")
                    else:
                        ctx.fin(
                            vector.tensor_tensor(
                                dst, dst, ps_pv[h % 2][:], op=ALU.add
                            ),
                            "VE",
                        )

            ctx.on("vector", b)
            ctx.tag[f"kv2_used_{it}"] = ctx.c["MM"]
            ctx.flush()

        ctx.tag[f"qt2_used_{qc}"] = ctx.c["MM"]
        acc_ve = ctx.c["VE"]

        for batch in range(4):
            par = batch // 2
            alo = 4 * (batch % 2)
            rbc_used = ctx.t("rbc_used")
            r_ve = ctx.res("VE")

            def b(vector, batch=batch, par=par, alo=alo, acc_ve=acc_ve,
                  rbc_used=rbc_used):
                if batch == 0:
                    ctx.wait(vector, "VE", acc_ve)
                ctx.wait(vector, "MM", rbc_used)
                vector.drain()
                ctx.fin(
                    vector.reciprocal(
                        rden[:, :, :], ctxEO[par][64:65, alo : alo + 4, :]
                    ),
                    "VE",
                )

            ctx.on("vector", b)

            for i4 in range(4):
                a = alo + i4
                h = 2 * a + par
                base = 64 * par
                rbc_read2 = ctx.t("rbc_read")
                b_tag = ctx.res("MM")
                ctx.tag["rbc_used"] = b_tag

                def b(tensor, i4=i4, r_ve=r_ve, rbc_read2=rbc_read2):
                    if i4 == 0:
                        ctx.wait(tensor, "VE", r_ve)
                    ctx.wait(tensor, "VE", rbc_read2)
                    ctx.fin(
                        tensor.matmul(
                            ps_s[0][0:64, :], ones1f[:, 0:64],
                            rden[:, i4, :], start=True, stop=True,
                        ),
                        "MM",
                    )

                ctx.on("tensor", b)

                i = ctx.res("VE")
                ctx.tag["rbc_read"] = i

                def b(vector, par=par, a=a, base=base, b_tag=b_tag):
                    ctx.wait(vector, "MM", b_tag)
                    ctx.fin(
                        vector.tensor_tensor(
                            ctx_bf[base : base + 64, a, :],
                            ctxEO[par][0:64, a, :],
                            ps_s[0][0:64, :], op=ALU.mult,
                        ),
                        "VE",
                    )

                ctx.on("vector", b)

        norm_ve = ctx.c["VE"]

        lo_guard = ctx.t("pv_used_0")
        lo_mm = ctx.res("MM")

        def b(tensor, qc=qc, norm_ve=norm_ve, lo_guard=lo_guard, wo_tag=wo_tag):
            ctx.wait(tensor, "VE", norm_ve)
            if qc == 0:
                ctx.wait(tensor, "LW", wo_tag)
            ctx.wait(tensor, "VE", lo_guard)
            for a in range(DT):
                mm = tensor.matmul(
                    ps_pv[0][0:16, :], la_t[:, 3, a, :], ctx_bf[:, a, :],
                    start=(a == 0), stop=(a == DT - 1),
                )
            ctx.fin(mm, "MM")

        ctx.on("tensor", b)

        lo_se = ctx.res("SE")
        ctx.tag["lo_done_se"] = lo_se

        def b(scalar, lo_mm=lo_mm):
            ctx.wait(scalar, "MM", lo_mm)
            ctx.fin(
                scalar.activation(
                    lo_raw[:], ps_pv[0][0:16, :], AF.Identity, scale=1.0 / 16.0
                ),
                "SE",
            )

        ctx.on("scalar", b)

        for ot in range(DT):
            pso_free = ctx.t(f"pso_free_{ot % 2}")
            o_tag = ctx.res("MM")

            def b(tensor, qc=qc, ot=ot, pso_free=pso_free, lo_se=lo_se):
                if ot == 0:
                    ctx.wait(tensor, "SE", lo_se)
                ctx.wait(tensor, "VE", pso_free)
                for a in range(DT):
                    tensor.matmul(
                        ps_o[ot % 2], wo_t[:, a, ot * 128 : (ot + 1) * 128],
                        ctx_bf[:, a, :], start=(a == 0), stop=False,
                    )
                mm = tensor.matmul(
                    ps_o[ot % 2], lb_t[:, 3, ot * 128 : (ot + 1) * 128],
                    lo_raw[:], start=False, stop=True,
                )
                ctx.fin(mm, "MM")

            ctx.on("tensor", b)

            st_prev = ctx.t("out_stored")
            i = ctx.res("VE")
            ctx.tag[f"pso_free_{ot % 2}"] = i
            if ot == DT - 1:
                ctx.tag["out_done"] = i

            def b(vector, qc=qc, ot=ot, o_tag=o_tag, q_ld=q_ld, st_prev=st_prev):
                ctx.wait(vector, "MM", o_tag)
                ctx.wait(vector, "LD", q_ld)
                if ot == 0:
                    ctx.wait(vector, "ST", st_prev)
                ctx.fin(
                    vector.tensor_tensor(
                        outb[:, ot, :], ps_o[ot % 2], xf[:, ot, :], op=ALU.add
                    ),
                    "VE",
                )

            ctx.on("vector", b)

        out_ve = ctx.c["VE"]
        ctx.res("ST", 16)
        ctx.tag["out_stored"] = ctx.c["ST"]

        def b(sync, qc=qc, out_ve=out_ve):
            ctx.wait(sync, "VE", out_ve)
            sync.dma_start(out=A["y"][qc], in_=outb).then_inc(ctx.sems["ST"], 16)

        ctx.on("sync", b)
        ctx.flush()

    st_final = ctx.c["ST"]

    def b(sync):
        ctx.wait(sync, "ST", st_final)

    ctx.on("sync", b)
    ctx.flush()


def build_lora():
    nc = bass.Bass("TRN2", target_bir_lowering=False)
    A = lora_drams(nc)
    es = contextlib.ExitStack()
    with es, nc.Block() as block:
        ctx = Ctx(nc, es, block)
        emit_lora(nc, block, ctx, es, A)
    return nc


def lora_host_inputs(xt_core, W):
    import ml_dtypes

    m4 = np.zeros((128, 4, CH), dtype=np.float32)
    for kt in range(4):
        for kl in range(128):
            m4[kl, kt, :] = (np.arange(CH) >= (128 * kt + kl)).astype(np.float32)
    lat = np.stack(
        [tile_w(np.asarray(W["a" + nm]).T) for nm in ["q", "k", "v", "o"]], axis=1
    )
    lbt = np.stack(
        [np.asarray(W["b" + nm]).T for nm in ["q", "k", "v", "o"]], axis=1
    )
    return {
        "x": xt_core,
        "wqt": tile_w(np.asarray(W["wq"]).T),
        "wkt": tile_w(np.asarray(W["wk"]).T),
        "wvt": tile_w(np.asarray(W["wv"]).T),
        "wot": tile_w(np.asarray(W["wo"]).T),
        "lat": np.ascontiguousarray(lat),
        "lbt": np.ascontiguousarray(lbt),
        "mask4": m4.astype(ml_dtypes.bfloat16),
    }


def mla_host_inputs(xt_core, W):
    import ml_dtypes

    m4 = np.zeros((128, 4, CH), dtype=np.float32)
    for kt in range(4):
        for kl in range(128):
            m4[kl, kt, :] = (np.arange(CH) >= (128 * kt + kl)).astype(np.float32)
    return {
        "x": xt_core,
        "wdkv": tile_w(np.asarray(W["W_DKV"])),
        "wuk": tile_w(np.asarray(W["W_UK"])),
        "wuv": tile_w(np.asarray(W["W_UV"])),
        "wuq": tile_w(np.asarray(W["W_UQ"])),
        "wo": tile_w(np.asarray(W["W_O"])),
        "mask4": m4.astype(ml_dtypes.bfloat16),
    }



DEPTH6 = 6
DI = 2730
DI2 = 5460
MT1 = 22


def l1ffn_drams(nc, pfx="", io=True):
    A = {}
    if io:
        A["x"] = nc.dram_tensor(pfx + "x", [NCH, 128, DT, CH], F32, kind="ExternalInput").ap()
    A["wqkv"] = nc.dram_tensor(pfx + "wqkv", [DEPTH6, 128, DT, 3 * D], F32, kind="ExternalInput").ap()
    A["wod"] = nc.dram_tensor(pfx + "wod", [DEPTH6, 128, DT, D], F32, kind="ExternalInput").ap()
    A["w1d"] = nc.dram_tensor(pfx + "w1d", [DEPTH6, 128, DT, DI2], F32, kind="ExternalInput").ap()
    A["w2da"] = nc.dram_tensor(pfx + "w2da", [DEPTH6, 128, 21, D], F32, kind="ExternalInput").ap()
    A["w2db"] = nc.dram_tensor(pfx + "w2db", [DEPTH6, 42, D], F32, kind="ExternalInput").ap()
    A["qscol_d"] = nc.dram_tensor(pfx + "qscol", [128, DEPTH6, 2], F32, kind="ExternalInput").ap()
    A["kscol_d"] = nc.dram_tensor(pfx + "kscol", [128, DEPTH6, 2], F32, kind="ExternalInput").ap()
    A["lng_d"] = nc.dram_tensor(pfx + "lng", [128, DEPTH6, DT], F32, kind="ExternalInput").ap()
    A["lnb_d"] = nc.dram_tensor(pfx + "lnb", [128, DEPTH6, DT], F32, kind="ExternalInput").ap()
    A["b1a_d"] = nc.dram_tensor(pfx + "b1a", [128, DEPTH6, MT1], F32, kind="ExternalInput").ap()
    A["b1g_d"] = nc.dram_tensor(pfx + "b1g", [128, DEPTH6, MT1], F32, kind="ExternalInput").ap()
    A["b2_d"] = nc.dram_tensor(pfx + "b2c", [128, DEPTH6, DT], F32, kind="ExternalInput").ap()
    A["tabq_d"] = nc.dram_tensor(pfx + "tabq", [128, 2, 2, T], BF16, kind="ExternalInput").ap()
    A["tabk_d"] = nc.dram_tensor(pfx + "tabk", [128, 2, 2, T], BF16, kind="ExternalInput").ap()
    A["hp16_d"] = nc.dram_tensor(pfx + "hp16", [128, 4, 16], BF16, kind="ExternalInput").ap()
    A["hpbc_d"] = nc.dram_tensor(pfx + "hpbc", [16, 4, 128], BF16, kind="ExternalInput").ap()
    A["mdiag_d"] = nc.dram_tensor(pfx + "mdiag", [128, 128], F32, kind="ExternalInput").ap()
    A["mpast_d"] = nc.dram_tensor(pfx + "mpast", [128, 128], BF16, kind="ExternalInput").ap()
    if io:
        A["y"] = nc.dram_tensor(pfx + "y", [NCH, 128, DT, CH], F32, kind="ExternalOutput").ap()
    A["zh"] = [
        nc.dram_tensor(pfx + "zh0", [NCH, 128, DT, CH], F32, kind="Internal").ap(),
        nc.dram_tensor(pfx + "zh1", [NCH, 128, DT, CH], F32, kind="Internal").ap(),
    ]

    return A


def emit_l1ffn(nc, block, ctx, es, A, depth=DEPTH6):
    sb = lambda *a: es.enter_context(nc.sbuf_tensor(*a, allow_name_mangling=True))
    psa = lambda *a: es.enter_context(nc.psum_tensor(*a, allow_name_mangling=True))

    zf = sb("zf", [128, DT, CH], F32)
    zsb = sb("zsb", [128, DT, CH], BF16)
    pcol = sb("pcol", [128, 4, 1], F32)
    sq = sb("sqb", [128, 1, CH], BF16)
    bcb = sb("bcb", [128, 2 * CH], BF16)
    rowm = sb("rowm", [1, CH], F32)
    rowv = sb("rowv", [1, CH], F32)
    rows = sb("rows", [1, CH], F32)
    rowt2 = [rowm, rowv]
    onesb = sb("onesb", [128, 1], BF16)
    eps1 = sb("eps1", [1, 1], F32)
    ones1f = sb("ones1f", [1, 128], F32)
    w1_sb = sb("w1_sb", [128, DT, DI2], BF16)
    w2_sb = sb("w2_sb", [128, MT1, D], BF16)
    w1flat = w1_sb[:].rearrange("p a n -> p (a n)")
    w2flat = w2_sb[:].rearrange("p a n -> p (a n)")
    wqkv_sb = w1flat[:, 0 : DT * 3 * D].rearrange("p (a n) -> p a n", n=3 * D)
    wo_sb = w1flat[:, DT * 3 * D : DT * 4 * D].rearrange("p (a n) -> p a n", n=D)
    kt_ring = w2flat[:, 0 : 2 * DT * CH].rearrange("p (b a n) -> p b a n", b=2, n=CH)
    va_ring = w2flat[:, 2 * DT * CH : 2 * DT * CH + 8 * 1040].rearrange(
        "p (w f) -> p w f", f=1040
    )
    h2_sb = sb("h2_sb", [128, MT1, CH], BF16)
    h2flat = h2_sb[:].rearrange("p a n -> p (a n)")
    rqb = h2flat[:, 0 : DT * CH].rearrange("p (a n) -> p a n", n=CH)
    qraw = h2flat[:, DT * CH : 2 * DT * CH].rearrange("p (a n) -> p a n", n=CH)
    sc0 = h2flat[:, 2 * DT * CH : 2 * DT * CH + CH]
    sc1 = h2flat[:, 2 * DT * CH + CH : 2 * DT * CH + 2 * CH]
    sc2 = h2flat[:, 2 * DT * CH + 2 * CH : 2 * DT * CH + 3 * CH]
    xv = h2flat[:, 0 : 2 * DT * CH].bitcast(F32).rearrange("p (a n) -> p a n", n=CH)
    abuf = sb("abuf", [128, 2, CH], BF16)
    tabs = sb("tabs", [128, 2, 2, CH], BF16)
    gbuf = tabs[:, :, 0, :]
    rsrow = sb("rsrow", [16, CH], BF16)
    ctx_bf = sb("ctx_bf", [128, DT, CH], BF16)
    pT2 = h2flat[:, 2 * DT * CH + 3 * CH :].rearrange(
        "p (s r n) -> p s r n", s=4, r=3
    )
    rbsb = sb("rbsb", [64, 2, CH], BF16)
    ctxu = sb("ctxu", [64, 2, CH], BF16)
    qscol = sb("qscol_t", [128, DEPTH6, 2], F32)
    kscol = sb("kscol_t", [128, DEPTH6, 2], F32)
    lng = sb("lng_t", [128, DEPTH6, DT], F32)
    lnb = sb("lnb_t", [128, DEPTH6, DT], F32)
    b1a = sb("b1a_t", [128, DEPTH6, MT1], F32)
    b1g = sb("b1g_t", [128, DEPTH6, MT1], F32)
    b2c = sb("b2c_t", [128, DEPTH6, DT], F32)
    hp16 = sb("hp16_t", [128, 4, 16], BF16)
    hpbc = sb("hpbc_t", [16, 4, 128], BF16)
    mdiag = sb("mdiag_t", [128, 128], F32)
    mpast = sb("mpast_t", [128, 128], BF16)

    psA = psa("psA", [128, 2 * CH], F32)
    psB = psa("psB", [128, 2 * CH], F32)
    psC = psa("psC", [128, 2 * CH], F32)
    psD = psa("psD", [65, 2 * CH], F32)
    ps_stats = psA[0:1, :]
    ps_nq = psA[0:16, 0:CH]
    ps_rb2 = [psA[0:64, CH : 2 * CH], psA[64:128, 0:CH]]
    psBh = [psB[:, 0:CH], psB[:, CH : 2 * CH]]
    psCh = [psC[:, 0:CH], psC[:, CH : 2 * CH]]
    ps_st2 = [psC[:, 0:384], psC[:, CH : CH + 384],
              psB[:, 0:384], psB[:, CH : CH + 384]]
    psDh = [psD[:, 0:CH], psD[:, CH : 2 * CH]]


    bufs = dict(
        sq=sq, bc=bcb, rowm=rowm, rowv=rowv, rows=rows, onesb=onesb,
        ones1f=ones1f, ps_stats=ps_stats, ps_bc=psB, eps1=eps1,
    )

    const_ld = ctx.res("LD", 176)

    def b(sync):
        for dst, srcd in [
            (qscol, A["qscol_d"]), (kscol, A["kscol_d"]), (lng, A["lng_d"]), (lnb, A["lnb_d"]),
            (b1a, A["b1a_d"]), (b1g, A["b1g_d"]), (b2c, A["b2_d"]), (hp16, A["hp16_d"]),
            (hpbc, A["hpbc_d"]), (mdiag, A["mdiag_d"]), (mpast, A["mpast_d"]),
        ]:
            sync.dma_start(out=dst[:], in_=srcd).then_inc(ctx.sems["LD"], 16)

    ctx.on("sync", b)

    init_ve = ctx.res("VE")

    def b(vector):
        ctx.wait(vector, "LD", const_ld)
        vector.memset(onesb[:], 1.0)
        vector.memset(eps1[:], EPS_LN)
        ctx.fin(vector.memset(ones1f[:], 1.0), "VE")

    ctx.on("vector", b)

    def q_zload(zin, c, first_wait=0, mm_guard=0):
        zf_free = ctx.t("zf_free")
        z_store = ctx.t("z_store")
        z_phase = ctx.t("z_phase")
        ld_tag = ctx.res("LD", 16)
        ctx.tag["zf_ld"] = ld_tag
        cap_pair = ("SE", ctx.t("zcap"))

        def b(sync, c=c, zf_free=zf_free, z_store=z_store,
              first_wait=first_wait, cap_pair=cap_pair):
            ctx.wait(sync, "VE", zf_free)
            ctx.waitp(sync, cap_pair)
            ctx.wait(sync, "ZS", z_store)
            if first_wait:
                ctx.wait(sync, "VE", first_wait)
            sync.dma_start(out=zf[:], in_=zin[c]).then_inc(ctx.sems["LD"], 16)

        ctx.on("sync", b)

        zsb_used = ctx.t("zsb_used")
        ld2 = ctx.res("ZS", 32)

        def b(gpsimd, c=c, z_phase=z_phase, mm_guard=mm_guard,
              zsb_used=zsb_used):
            ctx.wait(gpsimd, "ZS", z_phase)
            ctx.wait(gpsimd, "MM", zsb_used)
            ctx.wait(gpsimd, "MM", mm_guard)
            gpsimd.dma_start(
                out=zsb[:, 0:4, :], in_=zin[c][:, 0:4, :]
            ).then_inc(ctx.sems["ZS"], 16)
            gpsimd.dma_start(
                out=zsb[:, 4:8, 1:CH], in_=zin[c][:, 4:8, 0 : CH - 1]
            ).then_inc(ctx.sems["ZS"], 16)

        ctx.on("gpsimd", b)

        if c == 0:
            zs_ve = ctx.res("VE")

            def b(vector, ld2=ld2):
                ctx.wait(vector, "ZS", ld2)
                ctx.fin(vector.memset(zsb[:, 4:8, 0:1], 0.0), "VE")

            ctx.on("vector", b)
            zs = ("VE", zs_ve)
        else:
            zs_se = ctx.res("SE")

            def b(scalar, ld2=ld2):
                ctx.wait(scalar, "ZS", ld2)
                ctx.fin(
                    scalar.copy(
                        zsb[:, 4:8, 0:1].rearrange("p a n -> p (a n)"),
                        pcol[:].rearrange("p a n -> p (a n)"),
                    ),
                    "SE",
                )

            ctx.on("scalar", b)
            zs = ("SE", zs_se)
        return ld_tag, zs

    def q_zcap(ld_tag):
        cap_se = ctx.res("SE")
        ctx.tag["zcap"] = cap_se

        def b(scalar, ld_tag=ld_tag):
            ctx.wait(scalar, "LD", ld_tag)
            ctx.fin(
                scalar.copy(
                    pcol[:].rearrange("p a n -> p (a n)"),
                    zf[:, 4:8, CH - 1 : CH].rearrange("p a n -> p (a n)"),
                ),
                "SE",
            )

        ctx.on("scalar", b)
        return cap_se

    def q_qk(L, c, which, dst, w_off, ve_need):
        raw_tags = {}
        for ot in range(DT):
            Btag = ctx.tt(f"Bps{ot % 2}")
            mm_tag = ctx.res("MM")

            def b(tensor, ot=ot, ve_need=ve_need, Btag=Btag, w_off=w_off):
                if ot == 0:
                    ctx.waitp(tensor, ve_need)
                ctx.waitp(tensor, Btag)
                for a in range(DT):
                    mm = tensor.matmul(
                        psBh[ot % 2],
                        wqkv_sb[:, a, w_off + ot * 128 : w_off + (ot + 1) * 128],
                        zsb[:, a, :], start=(a == 0), stop=(a == DT - 1),
                    )
                ctx.fin(mm, "MM")

            ctx.on("tensor", b)

            qraw_free = ctx.t("qraw_free")
            i = ctx.res("SE")
            ctx.sett(f"Bps{ot % 2}", "SE", i)
            raw_tags[ot] = i

            def b(scalar, ot=ot, mm_tag=mm_tag, qraw_free=qraw_free):
                ctx.wait(scalar, "MM", mm_tag)
                if ot < 2:
                    ctx.wait(scalar, "VE", qraw_free)
                ctx.fin(scalar.copy(qraw[:, ot, :], psBh[ot % 2]), "SE")

            ctx.on("scalar", b)

        raw_se = ctx.c["SE"]

        sq_mm = {}
        for ot in range(DT):
            sqg = ctx.t("sq_guard")
            abg = [ctx.t("ab_gp_0"), ctx.t("ab_gp_1")]
            se_tag = ctx.res("SE")

            def b(scalar, ot=ot, sqg=sqg, abg=abg):
                if ot < 2:
                    ctx.wait(scalar, "MM", sqg)
                    ctx.wait(scalar, "GP", abg[ot])
                else:
                    ctx.wait(scalar, "MM", sq_mm[ot - 2])
                ctx.fin(
                    scalar.square(abuf[:, ot % 2, :], qraw[:, ot, :]), "SE"
                )

            ctx.on("scalar", b)

            Anq = ctx.t("Anq")
            stats_free = ctx.t("stats_free")
            sq_mm[ot] = ctx.res("MM")

            def b(tensor, ot=ot, se_tag=se_tag, Anq=Anq, stats_free=stats_free):
                ctx.wait(tensor, "SE", se_tag)
                if ot == 0:
                    ctx.wait(tensor, "VE", Anq)
                    ctx.wait(tensor, "VE", stats_free)
                ctx.fin(
                    tensor.matmul(
                        ps_nq, hp16[:, ot % 4, :], abuf[:, ot % 2, :],
                        start=(ot == 0), stop=(ot == DT - 1),
                    ),
                    "MM",
                )

            ctx.on("tensor", b)

        ctx.tag["sq_guard"] = ctx.c["MM"]
        nq_mm = ctx.c["MM"]
        rs_free = ctx.t("rsrow_free")
        nq_ve = ctx.res("VE")
        ctx.tag["Anq"] = nq_ve

        def b(vector, nq_mm=nq_mm, rs_free=rs_free):
            ctx.wait(vector, "MM", nq_mm)
            ctx.wait(vector, "MM", rs_free)
            ctx.fin(
                vector.tensor_scalar(rsrow[:], ps_nq, 1e-24, None, op0=ALU.max),
                "VE",
            )

        ctx.on("vector", b)

        rs_se = ctx.res("SE")

        def b(scalar, nq_ve=nq_ve):
            ctx.wait(scalar, "VE", nq_ve)
            scalar.activation(rsrow[:], rsrow[:], AF.Ln)
            scalar.drain()
            ctx.fin(scalar.activation(rsrow[:], rsrow[:], AF.Exp, scale=-0.5),
                    "SE")

        ctx.on("scalar", b)

        tabs_free = ctx.t("tabs_free")
        tab_ld = ctx.res("LD", 16)

        def b(sync, c=c, which=which, tabs_free=tabs_free):
            ctx.wait(sync, "VE", tabs_free)
            tab = A["tabq_d"] if which == "q" else A["tabk_d"]
            sync.dma_start(
                out=tabs[:], in_=tab[:, :, :, c * CH : (c + 1) * CH]
            ).then_inc(ctx.sems["LD"], 16)

        ctx.on("sync", b)

        scol = qscol if which == "q" else kscol
        rope_done = 0
        rope_slots = [psCh[0], psCh[1], psBh[0], psBh[1]]
        rope_names = ["Cps0", "Cps1", "Bps0", "Bps1"]
        for ot in range(DT):
            g = 0 if ot < 4 else 1
            gp = 1 - g
            pair = ot + 4 if ot < 4 else ot - 4
            sl = ot % 4
            Ctag = ctx.tt(rope_names[sl])
            bc_mm = ctx.res("MM")

            def b(tensor, sl=sl, rs_se=rs_se, Ctag=Ctag):
                ctx.wait(tensor, "SE", rs_se)
                ctx.waitp(tensor, Ctag)
                ctx.fin(
                    tensor.matmul(
                        rope_slots[sl], hpbc[:, sl, :], rsrow[:],
                        start=True, stop=True,
                    ),
                    "MM",
                )

            ctx.on("tensor", b)

            i = ctx.res("VE")
            ctx.sett(rope_names[sl], "VE", i)
            rope_done = i

            def b(vector, L=L, ot=ot, sl=sl, g=g, gp=gp, pair=pair, bc_mm=bc_mm,
                  tab_ld=tab_ld, raw_se=raw_se, which=which):
                ctx.wait(vector, "MM", bc_mm)
                if ot == 0:
                    ctx.wait(vector, "LD", tab_ld)
                    ctx.wait(vector, "SE", raw_se)
                vector.scalar_tensor_tensor(
                    sc0, qraw[:, ot, :], scol[:, L, g : g + 1], tabs[:, g, 0, :],
                    op0=ALU.mult, op1=ALU.mult,
                )
                vector.scalar_tensor_tensor(
                    sc1, qraw[:, pair, :], scol[:, L, gp : gp + 1],
                    tabs[:, g, 1, :], op0=ALU.mult, op1=ALU.mult,
                )
                vector.drain()
                vector.tensor_tensor(sc2, sc0, sc1, op=ALU.add)
                vector.drain()
                ctx.fin(
                    vector.tensor_tensor(
                        dst[:, ot, :], sc2, rope_slots[sl], op=ALU.mult
                    ),
                    "VE",
                )

            ctx.on("vector", b)

        ctx.tag["rsrow_free"] = ctx.c["MM"]
        ctx.tag["qraw_free"] = rope_done
        ctx.tag["tabs_free"] = rope_done
        return rope_done

    for c in range(NCH):
        zf_free = ctx.t("zf_free")
        z_store = ctx.t("z_store")
        ld_tag = ctx.res("LD", 16)

        def b(sync, c=c, zf_free=zf_free, z_store=z_store, init_ve=init_ve):
            if c == 0:
                ctx.wait(sync, "VE", init_ve)
            ctx.wait(sync, "VE", zf_free)
            ctx.wait(sync, "ZS", z_store)
            sync.dma_start(out=zf[:], in_=A["x"][c]).then_inc(ctx.sems["LD"], 16)

        ctx.on("sync", b)

        ve_tag = q_ln_chunk(ctx, bufs, zf[:], zsb[:], "LD", ld_tag)

        z0_ve = ctx.res("VE")
        ctx.tag["zf_free"] = z0_ve

        def b(vector, ve_tag=ve_tag):
            ctx.wait(vector, "VE", ve_tag)
            vector.drain()
            for a in range(DT):
                vector.tensor_tensor(
                    zf[:, a, :], zf[:, a, :], bcb[:, 0:CH], op=ALU.mult
                )
            vector.drain()
            for a in range(DT):
                mm = vector.tensor_tensor(
                    zf[:, a, :], zf[:, a, :], bcb[:, CH : 2 * CH], op=ALU.add
                )
            ctx.fin(mm, "VE")

        ctx.on("vector", b)

        ctx.res("ZS", 16)
        ctx.tag["z_store"] = ctx.c["ZS"]

        def b(gpsimd, c=c, z0_ve=z0_ve):
            ctx.wait(gpsimd, "VE", z0_ve)
            gpsimd.dma_start(out=A["zh"][0][c], in_=zf[:]).then_inc(
                ctx.sems["ZS"], 16
            )

        ctx.on("gpsimd", b)
        ctx.flush()

    for L in range(depth):
        for phase in ("attn", "ffn"):
            step = 2 * L + (0 if phase == "attn" else 1)
            zin = A["zh"][step % 2]
            zout = A["zh"][(step + 1) % 2]
            ctx.tag["sq_guard"] = ctx.c["MM"]
            phase_mm = ctx.c["MM"]
            phase_se = ctx.c["SE"]
            phase_ve = ctx.c["VE"]
            phase_gp = ctx.c["GP"]

            ctx.tag["z_phase"] = ctx.c["ZS"]
            w_tag = ctx.res("LW", 32 if phase == "attn" else 48)

            phase_ld = ctx.c["LD"]

            def b(gpsimd, L=L, phase=phase, phase_mm=phase_mm,
                  phase_se=phase_se, phase_ve=phase_ve, phase_ld=phase_ld):
                ctx.wait(gpsimd, "MM", phase_mm)
                ctx.wait(gpsimd, "SE", phase_se)
                ctx.wait(gpsimd, "VE", phase_ve)
                ctx.wait(gpsimd, "LD", phase_ld)
                if phase == "attn":
                    gpsimd.dma_start(out=wqkv_sb, in_=A["wqkv"][L]).then_inc(
                        ctx.sems["LW"], 16
                    )
                    gpsimd.dma_start(out=wo_sb, in_=A["wod"][L]).then_inc(
                        ctx.sems["LW"], 16
                    )
                else:
                    gpsimd.dma_start(out=w1_sb[:], in_=A["w1d"][L]).then_inc(
                        ctx.sems["LW"], 16
                    )
                    gpsimd.dma_start(out=w2_sb[:, 0:21, :], in_=A["w2da"][L]).then_inc(
                        ctx.sems["LW"], 16
                    )
                    gpsimd.dma_start(out=w2_sb[0:42, 21, :], in_=A["w2db"][L]).then_inc(
                        ctx.sems["LW"], 16
                    )

            ctx.on("gpsimd", b)

            if phase == "attn":
                for c in range(NCH):
                    s_prev = ctx.t(f"s_done_{c-1}")
                    ld_tag, zs_ve = q_zload(zin, c, mm_guard=s_prev)

                    if c == 0:
                        def b(tensor, w_tag=w_tag):
                            ctx.wait(tensor, "LW", w_tag)
                        ctx.on("tensor", b)

                    q_qk(L, c, "k", kt_ring[:, c % 2, :, :], D, zs_ve)
                    q_ve = q_qk(L, c, "q", rqb, 0, zs_ve)

                    if c == 0 and L == 0:
                        ones_ve = ctx.res("VE")

                        def b(vector, phase_mm=phase_mm):
                            ctx.wait(vector, "MM", phase_mm)
                            ctx.fin(
                                vector.memset(
                                    va_ring.rearrange(
                                        "p w (h e) -> p w h e", e=65
                                    )[:, :, :, 64:65],
                                    1.0,
                                ),
                                "VE",
                            )

                        ctx.on("vector", b)
                    elif c == 0:
                        ones_ve = ctx.res("VE")
                        lw_now = ctx.c["LW"]

                        def b(vector, lw_now=lw_now):
                            ctx.wait(vector, "LW", lw_now)
                            ctx.fin(
                                vector.memset(
                                    va_ring.rearrange(
                                        "p w (h e) -> p w h e", e=65
                                    )[:, :, :, 64:65],
                                    1.0,
                                ),
                                "VE",
                            )

                        ctx.on("vector", b)
                    else:
                        ones_ve = 0

                    v_prev = 0
                    for tt4 in range(4):
                        Cp0 = ctx.tt("Cps0") if tt4 == 0 else None
                        Cp1 = ctx.tt("Cps1") if tt4 == 0 else None
                        vp = v_prev
                        v_mm = ctx.res("MM")

                        def b(tensor, c=c, tt4=tt4, zs_ve=zs_ve,
                              ones_ve=ones_ve, Cp0=Cp0, Cp1=Cp1, vp=vp):
                            if tt4 == 0:
                                ctx.waitp(tensor, zs_ve)
                                ctx.wait(tensor, "VE", ones_ve)
                                ctx.waitp(tensor, Cp0)
                                ctx.waitp(tensor, Cp1)
                            else:
                                ctx.wait(tensor, "VE", vp)
                            for half in range(2):
                                for a in range(DT):
                                    mm = tensor.matmul(
                                        psC[:, half * CH : (half + 1) * CH],
                                        zsb[:, a, tt4 * 128 : (tt4 + 1) * 128],
                                        wqkv_sb[
                                            :, a,
                                            2 * D + half * CH
                                            : 2 * D + (half + 1) * CH,
                                        ],
                                        start=(a == 0), stop=(a == DT - 1),
                                    )
                            ctx.fin(mm, "MM")

                        ctx.on("tensor", b)

                        i = ctx.res("VE")
                        v_prev = i
                        if tt4 == 3:
                            ctx.sett("Cps0", "VE", i)
                            ctx.sett("Cps1", "VE", i)

                        def b(vector, c=c, tt4=tt4, v_mm=v_mm):
                            ctx.wait(vector, "MM", v_mm)
                            ctx.fin(
                                vector.tensor_copy(
                                    va_ring.rearrange(
                                        "p (b w) f -> p b w f", b=2
                                    )[:, c % 2, tt4, :].rearrange(
                                        "p (h e) -> p h e", e=65
                                    )[:, :, 0:64],
                                    psC[:].rearrange("p (h e) -> p h e", e=64),
                                ),
                                "VE",
                            )

                        ctx.on("vector", b)

                    kvv_ve = ctx.c["VE"]
                    ctx.tag["zsb_used"] = ctx.c["MM"]

                    units = []
                    for hi in range(4):
                        for pr in ((0, 2), (1, 3)):
                            for j in range(4):
                                for hm in pr:
                                    h = 4 * hi + hm
                                    units.append(
                                        (h, j, [4 * c + j - 2 + r for r in range(3)])
                                    )
                    dslot = lambda h: (h % 4) // 2
                    st_free0 = [ctx.t("st_free_se0"), ctx.t("st_free_se1"),
                                0, 0]
                    bps_first = [ctx.tt("Bps0"), ctx.tt("Bps1")]
                    rb_prev0 = [ctx.t("rb_se0"), ctx.t("rb_se1")]
                    D_prev = [ctx.t("D0se"), ctx.t("D1se")]
                    wo_done = ctx.t("wo_done")
                    cm_prev0 = [ctx.t("gp_cm0"), ctx.t("gp_cm1")]
                    mm_s = {}
                    mm_pv = {}
                    mm_rb = {}
                    ve_dm = {}
                    se_exp = {}
                    se_r = {}
                    se_cu = {}
                    gp_pm = {}
                    gp_cm = {}
                    lhseq = [u[0] for u in units if u[1] == 3]
                    hslot = {h: i % 2 for i, h in enumerate(lhseq)}
                    cu_after = {}
                    _pend = []
                    for ui, (h, j, gk) in enumerate(units):
                        if _pend:
                            cu_after[ui] = _pend.pop(0)
                        if j == 3:
                            _pend.append(h)
                    tail_cu = list(_pend)
                    for ui, (h, j, gk) in enumerate(units):
                        ve_dm[(h, j)] = ctx.res("VE")
                        se_exp[(h, j)] = ctx.res("SE")
                        if gk[0] >= 0:
                            gp_pm[(h, j)] = ctx.res("GP")
                        if j == 3:
                            se_r[h] = ctx.res("SE")
                        if ui in cu_after:
                            se_cu[cu_after[ui]] = ctx.res("SE")
                            gp_cm[cu_after[ui]] = ctx.res("GP")
                    for h in tail_cu:
                        se_cu[h] = ctx.res("SE")
                        gp_cm[h] = ctx.res("GP")
                    LAG = 3
                    sched = []
                    rbq = []
                    nu = len(units)
                    for ui in range(nu + LAG):
                        if ui < nu:
                            sched.append(("S", ui))
                        if ui >= LAG:
                            if rbq:
                                sched.append(("RB", rbq.pop(0)))
                            sched.append(("PV", ui - LAG))
                            if units[ui - LAG][1] == 3:
                                rbq.append(units[ui - LAG][0])
                    for h in rbq:
                        sched.append(("RB", h))
                    for kind, v in sched:
                        if kind == "S":
                            mm_s[units[v][:2]] = ctx.res("MM")
                        elif kind == "PV":
                            mm_pv[units[v][:2]] = ctx.res("MM")
                        else:
                            mm_rb[v] = ctx.res("MM")
                    uex = lambda k: se_exp[(units[k][0], units[k][1])]
                    ctx.tag["st_free_se0"] = uex(nu - 4)
                    ctx.tag["st_free_se1"] = uex(nu - 3)
                    ctx.sett("Cps0", "SE", uex(nu - 4))
                    ctx.sett("Cps1", "SE", uex(nu - 3))
                    ctx.sett("Bps0", "SE", uex(nu - 2))
                    ctx.sett("Bps1", "SE", uex(nu - 1))
                    dlast = {}
                    for h in lhseq:
                        dlast[dslot(h)] = h
                    ctx.tag["rb_se"] = se_cu[lhseq[-1]]
                    ctx.tag["rb_se0"] = se_cu[lhseq[-2]]
                    ctx.tag["rb_se1"] = se_cu[lhseq[-1]]
                    ctx.tag["D0se"] = se_cu[dlast[0]]
                    ctx.tag["D1se"] = se_cu[dlast[1]]
                    ctx.tag["gp_cm0"] = gp_cm[lhseq[-2]]
                    ctx.tag["gp_cm1"] = gp_cm[lhseq[-1]]
                    attn_gp = max(gp_cm.values())

                    def b(tensor, c=c, units=units, kvv_ve=kvv_ve, q_ve=q_ve,
                          mm_s=mm_s, mm_pv=mm_pv, se_exp=se_exp, gp_pm=gp_pm,
                          se_cu=se_cu, se_r=se_r, mm_rb=mm_rb,
                          st_free0=st_free0, bps_first=bps_first,
                          rb_prev0=rb_prev0, D_prev=D_prev,
                          dslot=dslot, hslot=hslot, sched=sched):
                        prev_exp = list(st_free0)
                        prev_cu = {0: D_prev[0], 1: D_prev[1]}
                        prev_rb = list(rb_prev0)

                        def emit_S(ui):
                            h, j, gk = units[ui]
                            b0 = 32 * (h % 4)
                            tg0, tg1 = h // 4, 4 + h // 4
                            slot = ui % 4
                            if ui in (2, 3):
                                ctx.waitp(tensor, bps_first[ui - 2])
                            ctx.wait(tensor, "SE", prev_exp[slot])
                            for r in range(3):
                                g = gk[r]
                                if g < 0:
                                    continue
                                kc, ktl = g // 4, g % 4
                                for gg, tgx in ((0, tg0), (1, tg1)):
                                    mm = tensor.matmul(
                                        ps_st2[slot][:, r * 128 : (r + 1) * 128],
                                        kt_ring[
                                            b0 : b0 + 32, kc % 2, tgx,
                                            ktl * 128 : (ktl + 1) * 128,
                                        ],
                                        rqb[
                                            b0 : b0 + 32, tgx,
                                            j * 128 : (j + 1) * 128,
                                        ],
                                        start=(gg == 0), stop=(gg == 1),
                                        tile_position=(b0, 0),
                                    )
                            ctx.fin(mm, "MM")
                            prev_exp[slot] = se_exp[(h, j)]

                        def emit_PV(ui):
                            h, j, gk = units[ui]
                            slot = ui % 4
                            ds = dslot(h)
                            if gk[0] >= 0:
                                ctx.wait(tensor, "GP", gp_pm[(h, j)])
                            else:
                                ctx.wait(tensor, "SE", se_exp[(h, j)])
                            if j == 0:
                                ctx.wait(tensor, "SE", prev_cu[ds])
                            first = True
                            for r in range(3):
                                g = gk[r]
                                if g < 0:
                                    continue
                                kc, ktl = g // 4, g % 4
                                mm = tensor.matmul(
                                    psDh[ds][:, j * 128 : (j + 1) * 128],
                                    va_ring.rearrange(
                                        "p (b w) f -> p b w f", b=2
                                    )[:, kc % 2, ktl, h * 65 : (h + 1) * 65],
                                    pT2[:, slot, r, :],
                                    start=first, stop=(r == 2),
                                )
                                first = False
                            ctx.fin(mm, "MM")
                            if j == 3:
                                prev_cu[ds] = se_cu[h]

                        def emit_RB(h):
                            hs = hslot[h]
                            ctx.wait(tensor, "SE", se_r[h])
                            ctx.wait(tensor, "SE", prev_rb[hs])
                            prev_rb[hs] = se_cu[h]
                            ctx.fin(
                                tensor.matmul(
                                    ps_rb2[hs], ones1f[:, 0:64],
                                    rowt2[hs][:],
                                    start=True, stop=True,
                                ),
                                "MM",
                            )

                        ctx.wait(tensor, "VE", kvv_ve)
                        ctx.wait(tensor, "VE", q_ve)
                        for kind, v in sched:
                            if kind == "S":
                                emit_S(v)
                            elif kind == "PV":
                                emit_PV(v)
                            else:
                                emit_RB(v)

                    ctx.on("tensor", b)

                    def b(scalar, units=units, ve_dm=ve_dm, mm_pv=mm_pv,
                          mm_rb=mm_rb, gp_cm=gp_cm, cm_prev0=cm_prev0,
                          dslot=dslot, hslot=hslot, cu_after=cu_after,
                          tail_cu=tail_cu):
                        pv_hist = [0, 0, 0, 0]
                        prev_cm = list(cm_prev0)

                        def emit_cu(h):
                            ds = dslot(h)
                            hs = hslot[h]
                            ctx.wait(scalar, "MM", mm_rb[h])
                            ctx.wait(scalar, "GP", prev_cm[hs])
                            prev_cm[hs] = gp_cm[h]
                            scalar.copy(rbsb[:, hs, :], ps_rb2[hs])
                            ctx.fin(
                                scalar.copy(
                                    ctxu[:, hs, :], psDh[ds][0:64, :]
                                ),
                                "SE",
                            )

                        for ui, (h, j, gk) in enumerate(units):
                            slot = ui % 4
                            ctx.wait(scalar, "VE", ve_dm[(h, j)])
                            ctx.wait(scalar, "MM", pv_hist[slot])
                            ctx.fin(
                                scalar.activation(
                                    pT2[:, slot, :, :].rearrange(
                                        "p a n -> p (a n)"
                                    ),
                                    ps_st2[slot][:], AF.Exp, scale=8.0,
                                ),
                                "SE",
                            )
                            pv_hist[slot] = mm_pv[(h, j)]
                            if j == 3:
                                ds = dslot(h)
                                hs = hslot[h]
                                ctx.wait(scalar, "MM", mm_pv[(h, 3)])
                                scalar.activation(
                                    rowt2[hs][:], psDh[ds][64:65, :], AF.Ln
                                )
                                scalar.drain()
                                ctx.fin(
                                    scalar.activation(
                                        rowt2[hs][:], rowt2[hs][:],
                                        AF.Exp, scale=-1.0
                                    ),
                                    "SE",
                                )
                            if ui in cu_after:
                                emit_cu(cu_after[ui])
                        for h in tail_cu:
                            emit_cu(h)

                    ctx.on("scalar", b)

                    def b(vector, units=units, mm_s=mm_s):
                        for ui, (h, j, gk) in enumerate(units):
                            slot = ui % 4
                            ctx.wait(vector, "MM", mm_s[(h, j)])
                            ctx.fin(
                                vector.tensor_tensor(
                                    ps_st2[slot][:, 256:384],
                                    ps_st2[slot][:, 256:384],
                                    mdiag[:], op=ALU.add,
                                ),
                                "VE",
                            )

                    ctx.on("vector", b)

                    def b(gpsimd, units=units, se_exp=se_exp, se_cu=se_cu,
                          gp_pm=gp_pm, wo_done=wo_done, hslot=hslot,
                          cu_after=cu_after, tail_cu=tail_cu):
                        first = [True]

                        def emit_cm(h):
                            hs = hslot[h]
                            ctx.wait(gpsimd, "SE", se_cu[h])
                            if first[0]:
                                ctx.wait(gpsimd, "MM", wo_done)
                                first[0] = False
                            ctx.fin(
                                gpsimd.tensor_mul(
                                    ctx_bf[
                                        64 * (h % 2) : 64 * (h % 2) + 64,
                                        h // 2, :,
                                    ],
                                    ctxu[:, hs, :], rbsb[:, hs, :],
                                ),
                                "GP",
                            )

                        for ui, (h, j, gk) in enumerate(units):
                            slot = ui % 4
                            if gk[0] >= 0:
                                ctx.wait(gpsimd, "SE", se_exp[(h, j)])
                                ctx.fin(
                                    gpsimd.tensor_mul(
                                        pT2[:, slot, 0, :],
                                        pT2[:, slot, 0, :], mpast[:],
                                    ),
                                    "GP",
                                )
                            if ui in cu_after:
                                emit_cm(cu_after[ui])
                        for h in tail_cu:
                            emit_cm(h)

                    ctx.on("gpsimd", b)

                    cap_se = q_zcap(ld_tag)
                    for ot in range(DT):
                        Bp = ctx.tt(f"Bps{ot % 2}")
                        o_mm = ctx.res("MM")
                        if ot == DT - 1:
                            ctx.tag["wo_done"] = o_mm

                        def b(tensor, ot=ot, attn_gp=attn_gp, Bp=Bp):
                            if ot == 0:
                                ctx.wait(tensor, "GP", attn_gp)
                            ctx.waitp(tensor, Bp)
                            for a in range(DT):
                                mm = tensor.matmul(
                                    psBh[ot % 2],
                                    wo_sb[:, a, ot * 128 : (ot + 1) * 128],
                                    ctx_bf[:, a, :],
                                    start=(a == 0), stop=(a == DT - 1),
                                )
                            ctx.fin(mm, "MM")

                        ctx.on("tensor", b)

                        i = ctx.res("VE")
                        ctx.sett(f"Bps{ot % 2}", "VE", i)
                        if ot == DT - 1:
                            ctx.tag["zf_free"] = i

                        def b(vector, ot=ot, o_mm=o_mm, cap_se=cap_se):
                            ctx.wait(vector, "MM", o_mm)
                            if ot == 0:
                                ctx.wait(vector, "SE", cap_se)
                            ctx.fin(
                                vector.tensor_tensor(
                                    zf[:, ot, :], zf[:, ot, :], psBh[ot % 2],
                                    op=ALU.add,
                                ),
                                "VE",
                            )

                        ctx.on("vector", b)

                    zn_ve = ctx.c["VE"]
                    ctx.res("ZS", 16)
                    ctx.tag["z_store"] = ctx.c["ZS"]
                    ctx.tag[f"s_done_{c}"] = ctx.c["MM"]

                    def b(gpsimd, c=c, zn_ve=zn_ve, zout=zout):
                        ctx.wait(gpsimd, "VE", zn_ve)
                        gpsimd.dma_start(out=zout[c], in_=zf[:]).then_inc(
                            ctx.sems["ZS"], 16
                        )

                    ctx.on("gpsimd", b)
                    ctx.flush()

            else:
                for c in range(NCH):
                    ld_tag, zs_ve = q_zload(zin, c)

                    Bp0 = ctx.tt("Bps0")
                    Bp1 = ctx.tt("Bps1")
                    Anq = ctx.t("Anq")
                    rb_se = ctx.t("rb_se")

                    def extra_bc(tensor, Bp0=Bp0, Bp1=Bp1):
                        ctx.waitp(tensor, Bp0)
                        ctx.waitp(tensor, Bp1)

                    def extra_stats(tensor, Anq=Anq, rb_se=rb_se):
                        ctx.wait(tensor, "VE", Anq)
                        ctx.wait(tensor, "SE", rb_se)

                    ve_tag = q_ln_chunk(
                        ctx, dict(bufs, sq=ctx_bf[:, 0:4, :], nsq=4),
                        None, zsb[:], "LD", ld_tag,
                        ready_ve=zs_ve, extra_bc=extra_bc,
                        extra_stats=extra_stats,
                    )

                    xn_ve = ctx.res("VE")

                    def b(vector, L=L, ve_tag=ve_tag):
                        ctx.wait(vector, "VE", ve_tag)
                        vector.drain()
                        for a in range(DT):
                            mm = vector.tensor_scalar(
                                zsb[:, a, :], zsb[:, a, :],
                                lng[:, L, a : a + 1], lnb[:, L, a : a + 1],
                                op0=ALU.mult, op1=ALU.add,
                            )
                        ctx.fin(mm, "VE")

                    ctx.on("vector", b)

                    hslots = [psBh[0], psBh[1], psCh[0], psCh[1]]
                    hnames = ["Bps0", "Bps1", "Cps0", "Cps1"]
                    for m in range(MT1):
                        rows_n = 128 if m < MT1 - 1 else 42
                        for part in range(2):
                            coff = part * DI + m * 128
                            sl = (2 * m + part) % 4
                            Bp = ctx.tt(hnames[sl])
                            hg_mm = ctx.res("MM")

                            def b(tensor, c=c, m=m, part=part, coff=coff,
                                  rows_n=rows_n, xn_ve=xn_ve, Bp=Bp, sl=sl,
                                  w_tag=w_tag):
                                if m == 0 and part == 0:
                                    ctx.wait(tensor, "VE", xn_ve)
                                    if c == 0:
                                        ctx.wait(tensor, "LW", w_tag)
                                ctx.waitp(tensor, Bp)
                                for a in range(DT):
                                    mm = tensor.matmul(
                                        hslots[sl][0:rows_n, :],
                                        w1_sb[:, a, coff : coff + rows_n],
                                        zsb[:, a, :],
                                        start=(a == 0), stop=(a == DT - 1),
                                    )
                                ctx.fin(mm, "MM")

                            ctx.on("tensor", b)

                            if part == 0:
                                ab_gp = ctx.t(f"ab_gp_{m % 2}")
                                i = ctx.res("VE")
                                ctx.sett(hnames[sl], "VE", i)

                                def b(vector, L=L, m=m, rows_n=rows_n,
                                      hg_mm=hg_mm, ab_gp=ab_gp, sl=sl):
                                    ctx.wait(vector, "MM", hg_mm)
                                    ctx.wait(vector, "GP", ab_gp)
                                    ctx.fin(
                                        vector.tensor_scalar(
                                            abuf[0:rows_n, m % 2, :],
                                            hslots[sl][0:rows_n, :],
                                            b1a[0:rows_n, L, m : m + 1], None,
                                            op0=ALU.add,
                                        ),
                                        "VE",
                                    )

                                ctx.on("vector", b)
                            else:
                                gb_gp = ctx.t(f"ab_gp_{m % 2}")
                                i = ctx.res("SE")
                                ctx.sett(hnames[sl], "SE", i)

                                def b(scalar, L=L, m=m, rows_n=rows_n,
                                      hg_mm=hg_mm, gb_gp=gb_gp, sl=sl):
                                    ctx.wait(scalar, "MM", hg_mm)
                                    ctx.wait(scalar, "GP", gb_gp)
                                    ctx.fin(
                                        scalar.activation(
                                            gbuf[0:rows_n, m % 2, :],
                                            hslots[sl][0:rows_n, :],
                                            AF.Gelu,
                                            bias=b1g[0:rows_n, L, m : m + 1],
                                        ),
                                        "SE",
                                    )

                                ctx.on("scalar", b)

                        ab_ve = ctx.c["VE"]
                        g_se = ctx.c["SE"]
                        h2_used = ctx.t("h2_used")
                        i = ctx.res("GP")
                        ctx.tag[f"ab_gp_{m % 2}"] = i

                        def b(gpsimd, m=m, rows_n=rows_n, ab_ve=ab_ve,
                              g_se=g_se, h2_used=h2_used):
                            ctx.wait(gpsimd, "VE", ab_ve)
                            ctx.wait(gpsimd, "SE", g_se)
                            ctx.wait(gpsimd, "MM", h2_used)
                            ctx.fin(
                                gpsimd.tensor_mul(
                                    h2_sb[0:rows_n, m, :],
                                    abuf[0:rows_n, m % 2, :],
                                    gbuf[0:rows_n, m % 2, :],
                                ),
                                "GP",
                            )

                        ctx.on("gpsimd", b)

                    h2_gp = ctx.c["GP"]
                    ctx.tag["zsb_used"] = ctx.c["MM"]

                    wslots = [psCh[0], psCh[1], psBh[0], psBh[1]]
                    wnames = ["Cps0", "Cps1", "Bps0", "Bps1"]
                    cap_se = q_zcap(ld_tag)
                    for ot in range(DT):
                        sl = ot % 4
                        Cp = ctx.tt(wnames[sl])
                        o_mm = ctx.res("MM")
                        if ot == DT - 1:
                            ctx.tag["h2_used"] = o_mm

                        def b(tensor, ot=ot, sl=sl, h2_gp=h2_gp, Cp=Cp):
                            if ot == 0:
                                ctx.wait(tensor, "GP", h2_gp)
                            ctx.waitp(tensor, Cp)
                            for m in range(MT1):
                                rows_n = 128 if m < MT1 - 1 else 42
                                mm = tensor.matmul(
                                    wslots[sl],
                                    w2_sb[0:rows_n, m, ot * 128 : (ot + 1) * 128],
                                    h2_sb[0:rows_n, m, :],
                                    start=(m == 0), stop=(m == MT1 - 1),
                                )
                            ctx.fin(mm, "MM")

                        ctx.on("tensor", b)

                        i = ctx.res("VE")
                        ctx.sett(wnames[sl], "VE", i)
                        if ot == DT - 1:
                            ctx.tag["zf_free"] = i

                        def b(vector, L=L, ot=ot, sl=sl, o_mm=o_mm,
                              cap_se=cap_se):
                            ctx.wait(vector, "MM", o_mm)
                            if ot == 0:
                                ctx.wait(vector, "SE", cap_se)
                            ctx.fin(
                                vector.scalar_tensor_tensor(
                                    zf[:, ot, :], wslots[sl],
                                    b2c[:, L, ot : ot + 1], zf[:, ot, :],
                                    op0=ALU.add, op1=ALU.add,
                                ),
                                "VE",
                            )

                        ctx.on("vector", b)

                    zn_ve = ctx.c["VE"]
                    ctx.res("ZS", 16)
                    ctx.tag["z_store"] = ctx.c["ZS"]

                    def b(gpsimd, c=c, zn_ve=zn_ve, zout=zout):
                        ctx.wait(gpsimd, "VE", zn_ve)
                        gpsimd.dma_start(out=zout[c], in_=zf[:]).then_inc(
                            ctx.sems["ZS"], 16
                        )

                    ctx.on("gpsimd", b)
                    ctx.flush()

    for c in range(NCH):
        zf_free = ctx.t("zf_free")
        z_store = ctx.t("z_store")
        xv_guard_mm = ctx.c["MM"]
        xv_guard_gp = ctx.c["GP"]
        xv_free = ctx.t("xv_free")
        ld_tag = ctx.res("LD", 32)

        def b(sync, c=c, zf_free=zf_free, z_store=z_store,
              xv_guard_mm=xv_guard_mm, xv_guard_gp=xv_guard_gp,
              xv_free=xv_free):
            ctx.wait(sync, "VE", zf_free)
            ctx.wait(sync, "ZS", z_store)
            sync.dma_start(out=zf[:], in_=A["zh"][0][c]).then_inc(ctx.sems["LD"], 16)
            if c == 0:
                ctx.wait(sync, "MM", xv_guard_mm)
                ctx.wait(sync, "GP", xv_guard_gp)
            else:
                ctx.wait(sync, "VE", xv_free)
            sync.dma_start(out=xv, in_=A["x"][c]).then_inc(ctx.sems["LD"], 16)

        ctx.on("sync", b)

        st_used = ctx.c["ST"]
        out_ve = ctx.res("VE")
        ctx.tag["zf_free"] = out_ve
        ctx.tag["xv_free"] = out_ve

        def b(vector, c=c, ld_tag=ld_tag, st_used=st_used):
            ctx.wait(vector, "LD", ld_tag)
            if c > 0:
                ctx.wait(vector, "ST", st_used)
            vector.drain()
            for ot in range(DT):
                mm = vector.scalar_tensor_tensor(
                    zf[:, ot, :], zf[:, ot, :], 0.5, xv[:, ot, :],
                    op0=ALU.mult, op1=ALU.add,
                )
            ctx.fin(mm, "VE")

        ctx.on("vector", b)

        ctx.res("ST", 16)

        def b(sync, c=c, out_ve=out_ve):
            ctx.wait(sync, "VE", out_ve)
            sync.dma_start(out=A["y"][c], in_=zf[:]).then_inc(ctx.sems["ST"], 16)

        ctx.on("sync", b)
        ctx.flush()

    st_final = ctx.c["ST"]

    def b(sync):
        ctx.wait(sync, "ST", st_final)

    ctx.on("sync", b)
    ctx.flush()


def build_l1ffn(depth=DEPTH6):
    nc = bass.Bass("TRN2", target_bir_lowering=False)
    A = l1ffn_drams(nc)
    es = contextlib.ExitStack()
    with es, nc.Block() as block:
        ctx = Ctx(nc, es, block)
        emit_l1ffn(nc, block, ctx, es, A, depth=depth)
    return nc


def l1ffn_host_inputs(xt_core, W):
    import ml_dtypes

    DIM_HEAD = 64
    WINDOW = 256
    perm = np.zeros(D, dtype=np.int64)
    for n in range(D):
        i, p = n // 128, n % 128
        g = i // 4
        h = 4 * (i % 4) + p // 32
        d = 32 * g + (p % 32)
        perm[n] = h * 64 + d
    Wqkv = np.asarray(W["Wqkv"])
    wqkv_p = Wqkv.copy()
    wqkv_p[:, :, 0:D] = Wqkv[:, :, 0:D][:, :, perm]
    wqkv_p[:, :, D : 2 * D] = Wqkv[:, :, D : 2 * D][:, :, perm]

    qscol = np.zeros((128, DEPTH6, 2), dtype=np.float32)
    kscol = np.zeros((128, DEPTH6, 2), dtype=np.float32)
    qs = np.asarray(W["q_scale"])
    ks = np.asarray(W["k_scale"])
    for p in range(128):
        for g in range(2):
            qscol[p, :, g] = qs[:, 32 * g + p % 32]
            kscol[p, :, g] = ks[:, 32 * g + p % 32]

    half = np.arange(0, DIM_HEAD, 2, dtype=np.float64)
    inv_freq = 1.0 / 10000.0 ** (half / DIM_HEAD)
    t = np.arange(T, dtype=np.float64)
    freqs = t[:, None] * inv_freq[None, :]
    base_scale = (half + 0.4 * DIM_HEAD) / (1.4 * DIM_HEAD)
    power = (t - T // 2) / (WINDOW // 2)
    scale = base_scale[None, :] ** power[:, None]
    cos_t = np.cos(freqs)
    sin_t = np.sin(freqs)
    tabq = np.zeros((128, 2, 2, T), dtype=np.float64)
    tabk = np.zeros((128, 2, 2, T), dtype=np.float64)
    for p in range(128):
        dp = p % 32
        for g in range(2):
            sgn = -1.0 if g == 0 else 1.0
            tabq[p, g, 0] = cos_t[:, dp] * scale[:, dp]
            tabq[p, g, 1] = sgn * sin_t[:, dp] * scale[:, dp]
            tabk[p, g, 0] = cos_t[:, dp] / scale[:, dp]
            tabk[p, g, 1] = sgn * sin_t[:, dp] / scale[:, dp]

    hp16 = np.zeros((128, 4, 16), dtype=np.float32)
    hpbc = np.zeros((16, 4, 128), dtype=np.float32)
    for p in range(128):
        for v in range(4):
            h = 4 * v + p // 32
            hp16[p, v, h] = 1.0
            hpbc[h, v, p] = 1.0

    mdiag = np.zeros((128, 128), dtype=np.float32)
    mpast = np.zeros((128, 128), dtype=np.float32)
    for kl in range(128):
        mdiag[kl, :] = np.where(kl > np.arange(128), -1e30, 0.0)
        mpast[kl, :] = (kl >= np.arange(128)).astype(np.float32)

    def padb1(bpart):
        out = np.zeros((DEPTH6, MT1 * 128), dtype=np.float32)
        out[:, :DI] = bpart
        return np.ascontiguousarray(out.reshape(DEPTH6, MT1, 128).transpose(2, 0, 1))

    def tile_wd(w):
        d0, K, N = w.shape
        return np.ascontiguousarray(
            w.reshape(d0, K // 128, 128, N).transpose(0, 2, 1, 3)
        )

    bf = lambda a: np.ascontiguousarray(a).astype(ml_dtypes.bfloat16)
    b1 = np.asarray(W["b1"])
    W2 = np.asarray(W["W2"])
    return {
        "x": xt_core,
        "wqkv": tile_wd(wqkv_p),
        "wod": tile_wd(np.asarray(W["Wo"])),
        "w1d": tile_wd(np.asarray(W["W1"])),
        "w2da": tile_wd(W2[:, 0 : 21 * 128, :]),
        "w2db": np.ascontiguousarray(W2[:, 21 * 128 :, :]),
        "qscol": qscol,
        "kscol": kscol,
        "lng": np.ascontiguousarray(
            np.asarray(W["ln_g"]).reshape(DEPTH6, DT, 128).transpose(2, 0, 1)
        ),
        "lnb": np.ascontiguousarray(
            np.asarray(W["ln_b"]).reshape(DEPTH6, DT, 128).transpose(2, 0, 1)
        ),
        "b1a": padb1(b1[:, :DI]),
        "b1g": padb1(b1[:, DI:]),
        "b2c": np.ascontiguousarray(
            np.asarray(W["b2"]).reshape(DEPTH6, DT, 128).transpose(2, 0, 1)
        ),
        "tabq": bf(tabq),
        "tabk": bf(tabk),
        "hp16": bf(hp16),
        "hpbc": bf(hpbc),
        "mdiag": mdiag,
        "mpast": bf(mpast),
    }









def build_fused(nstages=6):
    nc = bass.Bass("TRN2", target_bir_lowering=False)
    Am = moe_drams(nc, "moe_", io=False)
    Aa = mla_drams(nc, "mla_", io=False)
    Al = lora_drams(nc, "lora_", io=False)
    Af = l1ffn_drams(nc, "l1_", io=False)
    xin = nc.dram_tensor("x", [NCH, 128, DT, CH], F32, kind="ExternalInput").ap()
    yout = nc.dram_tensor("y", [NCH, 128, DT, CH], F32, kind="ExternalOutput").ap()
    xh = [
        nc.dram_tensor("xh0", [NCH, 128, DT, CH], F32, kind="Internal").ap(),
        nc.dram_tensor("xh1", [NCH, 128, DT, CH], F32, kind="Internal").ap(),
    ]
    es = contextlib.ExitStack()
    with es, nc.Block() as block:
        ctx = Ctx(nc, es, block)
        seq = [
            (emit_moe, Am, {}),
            (emit_mla, Aa, {}),
            (emit_moe, Am, {}),
            (emit_l1ffn, Af, {}),
            (emit_lora, Al, {}),
            (emit_l1ffn, Af, {}),
        ]
        seq = seq[:nstages]
        cur = xin
        for i, (fn, A, kw) in enumerate(seq):
            out = yout if i == len(seq) - 1 else xh[i % 2]
            A2 = dict(A)
            A2["x"] = cur
            A2["y"] = out
            with contextlib.ExitStack() as es_stage:
                fn(nc, block, ctx, es_stage, A2, **kw)
            if i < len(seq) - 1:
                ctx.barrier()
            cur = out
    return nc


def fused_host_maps(xt, moe_W, mla_W, lora_W, l1_W, cores):
    maps = []
    for b in cores:
        m = {}
        for pfx, hm in [
            ("moe_", moe_host_inputs(xt[b], moe_W)),
            ("mla_", mla_host_inputs(xt[b], mla_W)),
            ("lora_", lora_host_inputs(xt[b], lora_W)),
            ("l1_", l1ffn_host_inputs(xt[b], l1_W)),
        ]:
            for k, v in hm.items():
                if k == "x":
                    continue
                m[pfx + k] = v
        m["x"] = xt[b]
        maps.append(m)
    return maps


_NC_CACHE = {}


def _nc(name, fn, *a, **kw):
    if name not in _NC_CACHE:
        _NC_CACHE[name] = fn(*a, **kw)
    return _NC_CACHE[name]


def kernel(**inputs):
    inp = {k: np.asarray(v) for k, v in inputs.items()}
    cores = list(range(NCORES))
    x = np.ascontiguousarray(inp["L0_attn_x"])
    xt = [tile_x(np.ascontiguousarray(x[b])) for b in cores]

    moe_W = {
        "Wg": inp["L0_ffn_Wg"], "bg": inp["L0_ffn_bg"], "W1": inp["L0_ffn_W1"],
        "b1": inp["L0_ffn_b1"], "W2": inp["L0_ffn_W2"], "b2": inp["L0_ffn_b2"],
    }
    mla_W = {
        "W_DKV": inp["L0_attn_W_DKV"], "W_UK": inp["L0_attn_W_UK"],
        "W_UV": inp["L0_attn_W_UV"], "W_UQ": inp["L0_attn_W_UQ"],
        "W_O": inp["L0_attn_W_O"],
    }
    lora_W = {}
    for nm in ["q", "k", "v", "o"]:
        lora_W["w" + nm] = inp["L1_attn_w" + nm]
        lora_W["a" + nm] = inp["L1_attn_a" + nm]
        lora_W["b" + nm] = inp["L1_attn_b" + nm]
    l1_W = {
        "Wqkv": inp["L1_ffn_Wqkv"], "Wo": inp["L1_ffn_Wo"],
        "W1": inp["L1_ffn_W1"], "W2": inp["L1_ffn_W2"],
        "q_scale": inp["L1_ffn_q_scale"], "k_scale": inp["L1_ffn_k_scale"],
        "ln_g": inp["L1_ffn_ln_g"], "ln_b": inp["L1_ffn_ln_b"],
        "b1": inp["L1_ffn_b1"], "b2": inp["L1_ffn_b2"],
    }

    nc = _nc("fused", build_fused)
    maps = fused_host_maps(xt, moe_W, mla_W, lora_W, l1_W, cores)
    res = run_bass_kernel_spmd(nc, maps, core_ids=cores)
    return np.stack(
        [untile_y(res.results[b]["y"]) for b in cores]
    ).astype(np.float32)


# version 09 (final)
# speedup vs baseline: 1.4196x; 1.0512x over previous
import contextlib
import numpy as np
import concourse.bass as bass
import concourse.mybir as mybir
from concourse.bass_utils import run_bass_kernel_spmd

F32 = mybir.dt.float32
BF16 = mybir.dt.bfloat16
AF = mybir.ActivationFunctionType
ALU = mybir.AluOpType
AX = mybir.AxisListType

T = 4096
D = 1024
NCORES = 8
CH = 512
NCH = T // CH
DT = D // 128
EPS_LN = 1e-5
ENGS = ("sync", "gpsimd", "tensor", "scalar", "vector")


class Ctx:

    def __init__(self, nc, es, block):
        self.nc = nc
        self.es = es
        self.block = block
        self.sems = {}
        self.c = {}
        self.tag = {}
        self.q = {e: [] for e in ENGS}
        for s in ("LD", "LW", "MM", "VE", "SE", "GP", "ZS", "ST", "QS", "KS", "VS", "XN"):
            self.sem(s)

    def sem(self, name):
        if name not in self.sems:
            self.sems[name] = self.es.enter_context(self.nc.semaphore(name))
            self.c[name] = 0
        return self.sems[name]

    def res(self, name, k=1):
        self.c[name] += k
        return self.c[name]

    def t(self, key):
        return self.tag.get(key, 0)

    def on(self, eng, body):
        self.q[eng].append(body)

    def wait(self, e, name, val):
        if val > 0:
            e.wait_ge(self.sems[name], val)

    def fin(self, instr, name, k=1):
        instr.then_inc(self.sems[name], k)

    def sett(self, key, sem, val):
        self.tag[key] = (sem, val)

    def tt(self, key):
        return self.tag.get(key, None)

    def waitp(self, eng, pair):
        if pair:
            self.wait(eng, pair[0], pair[1])

    DATA_SEMS = ("LD", "LW", "MM", "VE", "SE", "GP", "ZS", "ST", "QS", "KS",
                 "VS", "XN")

    def barrier(self):
        if "BAR" not in self.sems:
            self.sem("BAR")
        bar0 = self.c["BAR"]
        dma_counts = [(s, self.c[s]) for s in
                      ("LD", "LW", "ST", "QS", "KS", "VS", "XN", "ZS")
                      if self.c.get(s, 0) > 0]
        noclear = True
        self.res("BAR", 5)
        target = bar0 + 5

        for e in ENGS:
            def b(eng, e=e, dma_counts=dma_counts, bar0=bar0, noclear=noclear):
                if e == "sync":
                    for s, v in dma_counts:
                        self.wait(eng, s, v)
                    eng.nop().then_inc(self.sems["BAR"], 1)
                    if not noclear:
                        eng.wait_ge(self.sems["BAR"], bar0 + 5)
                        for s in self.DATA_SEMS:
                            eng.sem_clear(self.sems[s])
                        eng.nop().then_inc(self.sems["BAR"], 1)
                else:
                    eng.drain().then_inc(self.sems["BAR"], 1)

            self.on(e, b)
        self.flush()
        if not noclear:
            for s in self.DATA_SEMS:
                self.c[s] = 0
        self.tag = {}
        for e in ENGS:
            def b(eng, target=target):
                self.wait(eng, "BAR", target)

            self.on(e, b)

    def flush(self):
        for name in ENGS:
            items = self.q[name]
            if not items:
                continue
            self.q[name] = []
            dec = getattr(self.block, name)

            def emit(eng, items=items):
                for b in items:
                    b(eng)

            dec(emit)




def q_ln_chunk(ctx, bufs, src_f32, dst_bf16, ld_sem, ld_tag, pre_wait=None,
               ready_ve=None, extra_bc=None, extra_stats=None):
    sq = bufs["sq"]
    nsq = bufs.get("nsq", 1)
    bc = bufs["bc"]
    rowm, rowv, rows = bufs["rowm"], bufs["rowv"], bufs["rows"]
    onesb, ones1f, eps1 = bufs["onesb"], bufs["ones1f"], bufs["eps1"]
    ps_stats, ps_bc = bufs["ps_stats"], bufs["ps_bc"]

    if src_f32 is not None:
        cast_tag = ctx.res("VE")

        def b(vector, pre_wait=pre_wait, ld_tag=ld_tag):
            if pre_wait is not None:
                pre_wait(vector)
            ctx.wait(vector, ld_sem, ld_tag)
            ctx.fin(vector.tensor_copy(dst_bf16, src_f32), "VE")

        ctx.on("vector", b)
        ctx.tag["ln_cast"] = cast_tag
    else:
        cast_tag = ready_ve
        ctx.tag["ln_cast"] = ready_ve
    cast_p = cast_tag if isinstance(cast_tag, tuple) else ("VE", cast_tag)

    stats_free = ctx.t("stats_free")
    sq_guard = ctx.t("sq_guard")
    sq_mm = {}
    for a in range(DT):
        se_tag = ctx.res("SE")

        def b(scalar, a=a, sq_guard=sq_guard, cast_p=cast_p):
            if a < nsq:
                ctx.waitp(scalar, cast_p)
                ctx.wait(scalar, "MM", sq_guard)
            else:
                ctx.wait(scalar, "MM", sq_mm[a - nsq])
            ctx.fin(scalar.square(sq[:, a % nsq, :], dst_bf16[:, a, :]), "SE")

        ctx.on("scalar", b)
        sq_mm[a] = ctx.res("MM")

        def b(tensor, a=a, se_tag=se_tag, cast_p=cast_p, stats_free=stats_free,
              extra_stats=extra_stats):
            if a == 0:
                ctx.waitp(tensor, cast_p)
                ctx.wait(tensor, "VE", stats_free)
                if extra_stats is not None:
                    extra_stats(tensor)
            ctx.wait(tensor, "SE", se_tag)
            tensor.matmul(
                ps_stats[:, 0:CH], onesb[:], dst_bf16[:, a, :],
                start=(a == 0), stop=(a == DT - 1),
            )
            ctx.fin(
                tensor.matmul(
                    ps_stats[:, CH : 2 * CH], onesb[:], sq[:, a % nsq, :],
                    start=(a == 0), stop=(a == DT - 1),
                ),
                "MM",
            )

        ctx.on("tensor", b)

    stats_tag = ctx.c["MM"]
    var_tag = ctx.res("VE")
    ctx.tag["stats_free"] = var_tag

    def b(vector, stats_tag=stats_tag):
        ctx.wait(vector, "MM", stats_tag)
        vector.tensor_scalar_mul(rowm[:], ps_stats[:, 0:CH], 1.0 / D)
        vector.tensor_scalar_mul(rowv[:], ps_stats[:, CH : 2 * CH], 1.0 / D)
        vector.drain()
        vector.tensor_tensor(rows[:], rowm[:], rowm[:], op=ALU.mult)
        vector.drain()
        ctx.fin(
            vector.tensor_tensor(rowv[:], rowv[:], rows[:], op=ALU.subtract), "VE"
        )

    ctx.on("vector", b)

    rstd_tag = ctx.res("SE")

    def b(scalar, var_tag=var_tag):
        ctx.wait(scalar, "VE", var_tag)
        scalar.activation(rows[:], rowv[:], AF.Ln, bias=eps1[:])
        scalar.drain()
        ctx.fin(scalar.activation(rows[:], rows[:], AF.Exp, scale=-0.5), "SE")

    ctx.on("scalar", b)

    rows_tag = ctx.res("VE")

    def b(vector, rstd_tag=rstd_tag):
        ctx.wait(vector, "SE", rstd_tag)
        vector.drain()
        ctx.fin(
            vector.scalar_tensor_tensor(
                rowm[:], rowm[:], -1.0, rows[:], op0=ALU.mult, op1=ALU.mult
            ),
            "VE",
        )

    ctx.on("vector", b)

    bc_free = ctx.t("bc_free")
    bc_mm_tag = ctx.res("MM")

    def b(tensor, rows_tag=rows_tag, bc_free=bc_free, extra_bc=extra_bc):
        ctx.wait(tensor, "VE", rows_tag)
        ctx.wait(tensor, "SE", bc_free)
        if extra_bc is not None:
            extra_bc(tensor)
        tensor.matmul(ps_bc[:, 0:CH], ones1f[:], rows[:], start=True, stop=True)
        ctx.fin(
            tensor.matmul(ps_bc[:, CH : 2 * CH], ones1f[:], rowm[:],
                          start=True, stop=True),
            "MM",
        )

    ctx.on("tensor", b)

    bc_tag = ctx.res("SE")
    ctx.tag["bc_free"] = bc_tag

    def b(scalar, bc_mm_tag=bc_mm_tag):
        ctx.wait(scalar, "MM", bc_mm_tag)
        ctx.fin(scalar.copy(bc[:], ps_bc[:]), "SE")

    ctx.on("scalar", b)

    out_tag = ctx.res("VE")

    def b(vector, bc_tag=bc_tag):
        ctx.wait(vector, "SE", bc_tag)
        for a in range(DT):
            vector.tensor_tensor(
                dst_bf16[:, a, :], dst_bf16[:, a, :], bc[:, 0:CH], op=ALU.mult
            )
        vector.drain()
        for a in range(DT):
            mm = vector.tensor_tensor(
                dst_bf16[:, a, :], dst_bf16[:, a, :], bc[:, CH : 2 * CH], op=ALU.add
            )
        ctx.fin(mm, "VE")

    ctx.on("vector", b)
    return out_tag




def tile_x(x2d):
    return np.ascontiguousarray(
        x2d.reshape(NCH, CH, DT, 128).transpose(0, 3, 2, 1)
    )


def untile_y(yt):
    return np.ascontiguousarray(yt.transpose(0, 3, 2, 1).reshape(T, D))


def tile_w(w):
    K, N = w.shape
    return np.ascontiguousarray(w.reshape(K // 128, 128, N).transpose(1, 0, 2))



NE = 8
HD = 2048
HT = HD // 128


def moe_drams(nc, pfx="", io=True):
    A = {}
    if io:
        A["x"] = nc.dram_tensor(pfx + "x", [NCH, 128, DT, CH], F32, kind="ExternalInput").ap()
    A["wg"] = nc.dram_tensor(pfx + "wg", [128, DT, NE], F32, kind="ExternalInput").ap()
    A["bg"] = nc.dram_tensor(pfx + "bg", [NE, 1], F32, kind="ExternalInput").ap()
    A["w1"] = nc.dram_tensor(pfx + "w1", [NE, 128, DT, HD], F32, kind="ExternalInput").ap()
    A["b1t"] = nc.dram_tensor(pfx + "b1t", [128, HT, NE], F32, kind="ExternalInput").ap()
    A["w2"] = nc.dram_tensor(pfx + "w2", [NE, 128, HT, D], F32, kind="ExternalInput").ap()
    A["b2"] = nc.dram_tensor(pfx + "b2", [NE, D], F32, kind="ExternalInput").ap()
    A["idf"] = nc.dram_tensor(pfx + "idf", [8, 8], F32, kind="ExternalInput").ap()
    A["idb"] = nc.dram_tensor(pfx + "idb", [128, 128], BF16, kind="ExternalInput").ap()
    A["eb8"] = nc.dram_tensor(pfx + "eb8", [NE, NE * 128], BF16, kind="ExternalInput").ap()
    if io:
        A["y"] = nc.dram_tensor(pfx + "y", [NCH, 128, DT, CH], F32, kind="ExternalOutput").ap()
    A["xnh"] = nc.dram_tensor(pfx + "xnh", [NCH, 128, DT, CH], BF16, kind="Internal").ap()

    return A


def emit_moe(nc, block, ctx, es, A):
    sb = lambda *a: es.enter_context(nc.sbuf_tensor(*a, allow_name_mangling=True))
    psa = lambda *a: es.enter_context(nc.psum_tensor(*a, allow_name_mangling=True))

    xf = sb("xf", [128, DT, CH], F32)
    xnc = sb("xnc", [128, 2, DT, CH], BF16)
    sq = sb("sqb", [128, 1, CH], BF16)
    bc = sb("bcb", [128, 2 * CH], BF16)
    rowm = sb("rowm", [1, CH], F32)
    rowv = sb("rowv", [1, CH], F32)
    rows = sb("rows", [1, CH], F32)
    onesb = sb("onesb", [128, 1], BF16)
    eps1 = sb("eps1", [1, 1], F32)
    ones1f = sb("ones1f", [1, 128], F32)
    idf_t = sb("idf_t", [8, 8], F32)
    idb_t = sb("idb_t", [128, 128], BF16)
    eb8_t = sb("eb8_t", [NE, NE * 128], BF16)
    wg_t = sb("wg_t", [128, DT, NE], BF16)
    bg_t = sb("bg_t", [NE, 1], F32)
    b1t_t = sb("b1t_t", [128, HT, NE], F32)
    b2_t = sb("b2_t", [NE, D], BF16)
    lg = sb("lg", [NE, CH], F32)
    lgn = sb("lgn", [128, 4 * NE], F32)
    gt1 = sb("gt1", [128, 4 * NE], F32)
    gt2 = sb("gt2", [128, 4 * NE], F32)
    gmax = sb("gmax", [128, 4], F32)
    gsum = sb("gsum", [128, 4], F32)
    gm1p = sb("gm1p", [128, 4], F32)
    gm2 = sb("gm2", [128, 4], F32)
    gws = sb("gws", [128, 4], F32)
    combT = sb("combT", [NE, T], BF16)
    comb_n = sb("comb_n", [128, 4, NE], BF16)
    w1_sb = sb("w1_sb", [128, DT, HD], BF16)
    w2_sb = sb("w2_sb", [128, HT, D], BF16)
    h_bf = sb("h_bf", [128, HT, CH], BF16)
    outb = (
        h_bf[:]
        .rearrange("p a n -> p (a n)")
        .bitcast(F32)
        .rearrange("p (a n) -> p a n", a=DT)
    )
    cbb = sb("cbb", [128, 2, CH], BF16)
    acc = sb("acc", [128, DT, T], BF16)

    ps_stats = psa("ps_stats", [1, 2 * CH], F32)
    ps_bc = psa("ps_bc", [128, 2 * CH], F32)
    psA = psa("psA", [128, CH], F32)
    ps_g = psA[0:NE, 0:CH]
    ps_t = psA[:, 0 : 4 * NE]
    ps_ct = psA[0:NE, 0 : CH // 2].rearrange("p n -> p (n)").bitcast(BF16)
    ps_h = [psa("ps_h0", [128, CH], F32), psa("ps_h1", [128, CH], F32)]
    ps_y = [ps_bc[:, 0:CH], ps_bc[:, CH : 2 * CH]]


    bufs = dict(
        sq=sq, bc=bc, rowm=rowm, rowv=rowv, rows=rows, onesb=onesb,
        ones1f=ones1f, ps_stats=ps_stats, ps_bc=ps_bc, eps1=eps1,
    )

    ctx.res("LD", 64)

    def b(sync):
        sync.dma_start(out=idf_t[:], in_=A["idf"]).then_inc(ctx.sems["LD"], 16)
        sync.dma_start(out=idb_t[:], in_=A["idb"]).then_inc(ctx.sems["LD"], 16)
        sync.dma_start(out=bg_t[:], in_=A["bg"]).then_inc(ctx.sems["LD"], 16)
        sync.dma_start(out=b1t_t[:], in_=A["b1t"]).then_inc(ctx.sems["LD"], 16)

    ctx.on("sync", b)
    ctx.res("LW", 48)

    def b(gpsimd):
        gpsimd.dma_start(out=eb8_t[:], in_=A["eb8"]).then_inc(ctx.sems["LW"], 16)
        gpsimd.dma_start(out=wg_t[:], in_=A["wg"]).then_inc(ctx.sems["LW"], 16)
        gpsimd.dma_start(out=b2_t[:], in_=A["b2"]).then_inc(ctx.sems["LW"], 16)

    ctx.on("gpsimd", b)
    init_ve = ctx.res("VE")

    def b(vector):
        vector.memset(onesb[:], 1.0)
        vector.memset(eps1[:], EPS_LN)
        ctx.fin(vector.memset(ones1f[:], 1.0), "VE")

    ctx.on("vector", b)

    for c in range(NCH):
        xn = xnc[:, c % 2, :, :]
        xf_free = ctx.t("xf_free")
        ld_tag = ctx.res("LD", 16)

        def b(sync, c=c, xf_free=xf_free):
            ctx.wait(sync, "VE", xf_free)
            sync.dma_start(out=xf[:], in_=A["x"][c]).then_inc(ctx.sems["LD"], 16)

        ctx.on("sync", b)

        gm_prev = ctx.t(f"gate_mm_{c-2}")

        def pre(vector, c=c, gm_prev=gm_prev, init_ve=init_ve):
            if c == 0:
                ctx.wait(vector, "VE", init_ve)
            if c >= 2:
                ctx.wait(vector, "XN", 16 * (c - 1))
                ctx.wait(vector, "MM", gm_prev)

        ve_tag = q_ln_chunk(ctx, bufs, xf[:], xn, "LD", ld_tag, pre)
        ctx.tag["xf_free"] = ctx.t("ln_cast")

        ctx.res("XN", 16)

        def b(gpsimd, c=c, xn=xn, ve_tag=ve_tag):
            ctx.wait(gpsimd, "VE", ve_tag)
            gpsimd.dma_start(out=A["xnh"][c], in_=xn).then_inc(ctx.sems["XN"], 16)

        ctx.on("gpsimd", b)

        psA_free = ctx.t("psA_free")
        g_tag = ctx.res("MM")
        ctx.tag[f"gate_mm_{c}"] = g_tag

        def b(tensor, c=c, xn=xn, ve_tag=ve_tag, psA_free=psA_free):
            ctx.wait(tensor, "LW", 32)
            ctx.wait(tensor, "VE", ve_tag)
            ctx.wait(tensor, "SE", psA_free)
            for a in range(DT):
                mm = tensor.matmul(
                    ps_g, wg_t[:, a, :], xn[:, a, :],
                    start=(a == 0), stop=(a == DT - 1),
                )
            ctx.fin(mm, "MM")

        ctx.on("tensor", b)

        lg_tag = ctx.res("SE")

        def b(scalar, g_tag=g_tag):
            ctx.wait(scalar, "MM", g_tag)
            ctx.fin(
                scalar.activation(lg[:], ps_g, AF.Identity, bias=bg_t[:]), "SE"
            )

        ctx.on("scalar", b)

        t_tag = ctx.res("MM")

        def b(tensor, lg_tag=lg_tag):
            ctx.wait(tensor, "SE", lg_tag)
            for i in range(4):
                mm = tensor.transpose(
                    ps_t[:, i * NE : (i + 1) * NE],
                    lg[:, i * 128 : (i + 1) * 128], idf_t[:],
                )
            ctx.fin(mm, "MM")

        ctx.on("tensor", b)

        lgn_tag = ctx.res("SE")

        def b(scalar, t_tag=t_tag):
            ctx.wait(scalar, "MM", t_tag)
            ctx.fin(scalar.copy(lgn[:], ps_t), "SE")

        ctx.on("scalar", b)

        nm_tag = ctx.res("VE")

        def b(vector, lgn_tag=lgn_tag):
            ctx.wait(vector, "SE", lgn_tag)
            for i in range(4):
                mm = vector.reduce_max(
                    gmax[:, i : i + 1], lgn[:, i * NE : (i + 1) * NE],
                    axis=AX.X, negate=True,
                )
            ctx.fin(mm, "VE")

        ctx.on("vector", b)

        e_tag = ctx.res("SE")

        def b(scalar, nm_tag=nm_tag):
            ctx.wait(scalar, "VE", nm_tag)
            for i in range(4):
                mm = scalar.activation(
                    gt1[:, i * NE : (i + 1) * NE],
                    lgn[:, i * NE : (i + 1) * NE],
                    AF.Exp, bias=gmax[:, i : i + 1],
                )
            ctx.fin(mm, "SE")

        ctx.on("scalar", b)

        comb_tag = ctx.res("VE")

        def b(vector, e_tag=e_tag):
            ctx.wait(vector, "SE", e_tag)
            sl = lambda t, i: t[:, i * NE : (i + 1) * NE]
            for i in range(4):
                vector.reduce_sum(gsum[:, i : i + 1], sl(gt1, i), axis=AX.X)
            vector.drain()
            vector.reciprocal(gsum[:], gsum[:])
            vector.drain()
            for i in range(4):
                vector.tensor_scalar_mul(sl(gt1, i), sl(gt1, i), gsum[:, i : i + 1])
            vector.drain()
            for i in range(4):
                vector.reduce_max(gm1p[:, i : i + 1], sl(gt1, i), axis=AX.X)
            vector.drain()
            for i in range(4):
                vector.tensor_scalar(
                    sl(gt2, i), sl(gt1, i), gm1p[:, i : i + 1], None, op0=ALU.is_lt
                )
            vector.drain()
            vector.tensor_tensor(gt2[:], gt1[:], gt2[:], op=ALU.mult)
            vector.drain()
            for i in range(4):
                vector.reduce_max(gm2[:, i : i + 1], sl(gt2, i), axis=AX.X)
            vector.drain()
            for i in range(4):
                vector.tensor_scalar(
                    sl(gt2, i), sl(gt1, i), gm2[:, i : i + 1], None, op0=ALU.is_ge
                )
            vector.tensor_tensor(gws[:], gm1p[:], gm2[:], op=ALU.add)
            vector.drain()
            vector.reciprocal(gws[:], gws[:])
            vector.tensor_tensor(gt1[:], gt1[:], gt2[:], op=ALU.mult)
            vector.drain()
            for i in range(4):
                mm = vector.tensor_scalar_mul(
                    comb_n[:, i, :], sl(gt1, i), gws[:, i : i + 1]
                )
            ctx.fin(mm, "VE")

        ctx.on("vector", b)

        ct_tag = ctx.res("MM")

        def b(tensor, comb_tag=comb_tag):
            ctx.wait(tensor, "VE", comb_tag)
            for i in range(4):
                mm = tensor.transpose(
                    ps_ct[:, i * 128 : (i + 1) * 128], comb_n[:, i, :], idb_t[:]
                )
            ctx.fin(mm, "MM")

        ctx.on("tensor", b)

        psA_free2 = ctx.res("SE")
        ctx.tag["psA_free"] = psA_free2

        def b(scalar, c=c, ct_tag=ct_tag):
            ctx.wait(scalar, "MM", ct_tag)
            ctx.fin(scalar.copy(combT[:, c * CH : (c + 1) * CH], ps_ct), "SE")

        ctx.on("scalar", b)
        ctx.flush()

    stageA_se = ctx.c["SE"]
    stageA_mm = ctx.c["MM"]
    stageA_xn = ctx.c["XN"]

    for e in range(NE):
        w_used = ctx.t("w_used")
        w_tag = ctx.res("LW", 32)

        def b(gpsimd, e=e, w_used=w_used):
            if e > 0:
                ctx.wait(gpsimd, "MM", w_used)
            gpsimd.dma_start(out=w1_sb[:], in_=A["w1"][e]).then_inc(
                ctx.sems["LW"], 16
            )
            gpsimd.dma_start(out=w2_sb[:], in_=A["w2"][e]).then_inc(
                ctx.sems["LW"], 16
            )

        ctx.on("gpsimd", b)

        for c in range(NCH):
            it = e * NCH + c
            xn = xnc[:, it % 2, :, :]
            cbuf = it % 2

            xn_used = ctx.t(f"xn_used_{it-2}")
            xn_tag = ctx.res("LD", 16)

            def b(sync, e=e, c=c, it=it, xn=xn, xn_used=xn_used):
                ctx.wait(sync, "XN", 16 * (c + 1))
                if it < 2:
                    ctx.wait(sync, "XN", stageA_xn)
                    ctx.wait(sync, "MM", stageA_mm)
                else:
                    ctx.wait(sync, "MM", xn_used)
                sync.dma_start(out=xn, in_=A["xnh"][c]).then_inc(ctx.sems["LD"], 16)

            ctx.on("sync", b)

            y_evac0 = ctx.t("y_evac_0")
            cbm_tag = ctx.res("MM")

            def b(tensor, e=e, c=c, it=it, w_tag=w_tag, y_evac0=y_evac0):
                if c == 0:
                    ctx.wait(tensor, "LW", w_tag)
                if it == 0:
                    ctx.wait(tensor, "SE", stageA_se)
                ctx.wait(tensor, "VE", y_evac0)
                ctx.fin(
                    tensor.matmul(
                        ps_y[0], eb8_t[:, e * 128 : (e + 1) * 128],
                        combT[:, c * CH : (c + 1) * CH],
                        start=True, stop=True,
                    ),
                    "MM",
                )

            ctx.on("tensor", b)

            cb_used = ctx.t(f"cb_used_{it-2}")
            cb_tag = ctx.res("SE")

            def b(scalar, cbuf=cbuf, cbm_tag=cbm_tag, cb_used=cb_used):
                ctx.wait(scalar, "MM", cbm_tag)
                ctx.wait(scalar, "GP", cb_used)
                ctx.fin(scalar.copy(cbb[:, cbuf, :], ps_y[0]), "SE")

            ctx.on("scalar", b)

            h_ev = []
            for m in range(HT):
                bank = m % 2
                h_evac = ctx.t(f"h_evac_{bank}")
                g_tag2 = ctx.res("MM")

                def b(tensor, m=m, bank=bank, xn=xn, xn_tag=xn_tag,
                      h_evac=h_evac):
                    if m == 0:
                        ctx.wait(tensor, "LD", xn_tag)
                    ctx.wait(tensor, "VE", h_evac)
                    for a in range(DT):
                        mm = tensor.matmul(
                            ps_h[bank][:],
                            w1_sb[:, a, m * 128 : (m + 1) * 128],
                            xn[:, a, :], start=(a == 0), stop=(a == DT - 1),
                        )
                    ctx.fin(mm, "MM")

                ctx.on("tensor", b)

                hm_used = ctx.t(f"hm_used_{it-1}")
                st_used = ctx.c["ST"]
                i = ctx.res("VE")
                ctx.tag[f"h_evac_{bank}"] = i
                h_ev.append(i)

                def b(vector, e=e, c=c, m=m, bank=bank, it=it, g_tag2=g_tag2,
                      hm_used=hm_used, st_used=st_used):
                    ctx.wait(vector, "MM", g_tag2)
                    if m == 0:
                        ctx.wait(vector, "GP", hm_used)
                        if e == NE - 1 and c > 0:
                            ctx.wait(vector, "ST", st_used)
                    ctx.fin(
                        vector.tensor_scalar(
                            h_bf[:, m, :], ps_h[bank][:],
                            b1t_t[:, m, e : e + 1], 0.0,
                            op0=ALU.add, op1=ALU.max,
                        ),
                        "VE",
                    )

                ctx.on("vector", b)

            gp_mults = [ctx.res("GP") for _ in range(HT)]
            hm_tag = gp_mults[-1]
            ctx.tag[f"cb_used_{it}"] = hm_tag
            ctx.tag[f"hm_used_{it}"] = hm_tag

            def b(gpsimd, cbuf=cbuf, h_ev=h_ev, cb_tag=cb_tag):
                for m in range(HT):
                    if m == 0:
                        ctx.wait(gpsimd, "SE", cb_tag)
                    ctx.wait(gpsimd, "VE", h_ev[m])
                    ctx.fin(
                        gpsimd.tensor_mul(
                            h_bf[:, m, :], h_bf[:, m, :], cbb[:, cbuf, :]
                        ),
                        "GP",
                    )

            ctx.on("gpsimd", b)

            for o in range(DT):
                bank = o % 2
                y_evac = ctx.t(f"y_evac_{bank}")
                y_tag = ctx.res("MM")

                def b(tensor, e=e, c=c, o=o, bank=bank, hm_tag=hm_tag,
                      cb_tag=cb_tag, y_evac=y_evac):
                    if o == 0:
                        ctx.wait(tensor, "GP", hm_tag)
                        ctx.wait(tensor, "SE", cb_tag)
                    ctx.wait(tensor, "VE", y_evac)
                    for hh in range(HT):
                        mm = tensor.matmul(
                            ps_y[bank],
                            w2_sb[:, hh, o * 128 : (o + 1) * 128],
                            h_bf[:, hh, :],
                            start=(hh == 0), stop=(hh == HT - 1 and e != 0),
                        )
                    if e == 0:
                        mm = tensor.matmul(
                            ps_y[bank], b2_t[:, o * 128 : (o + 1) * 128],
                            combT[:, c * CH : (c + 1) * CH],
                            start=False, stop=True,
                        )
                    ctx.fin(mm, "MM")

                ctx.on("tensor", b)

                i = ctx.res("VE")
                ctx.tag[f"y_evac_{bank}"] = i

                def b(vector, e=e, c=c, o=o, bank=bank, y_tag=y_tag):
                    ctx.wait(vector, "MM", y_tag)
                    at = acc[:, o, c * CH : (c + 1) * CH]
                    if e == 0:
                        ctx.fin(vector.tensor_copy(at, ps_y[bank]), "VE")
                    else:
                        ctx.fin(
                            vector.tensor_tensor(at, at, ps_y[bank], op=ALU.add),
                            "VE",
                        )

                ctx.on("vector", b)

            ctx.tag[f"xn_used_{it}"] = ctx.c["MM"]

            if e == NE - 1:
                guard = ctx.t("xf_free") if c == 0 else ctx.t("out_done")
                xr_tag = ctx.res("LD", 16)

                def b(sync, c=c, guard=guard):
                    ctx.wait(sync, "VE", guard)
                    sync.dma_start(out=xf[:], in_=A["x"][c]).then_inc(
                        ctx.sems["LD"], 16
                    )

                ctx.on("sync", b)

                out_tag = ctx.res("VE")
                ctx.tag["out_done"] = out_tag

                def b(vector, c=c, xr_tag=xr_tag):
                    ctx.wait(vector, "LD", xr_tag)
                    vector.drain()
                    for o in range(DT):
                        mm = vector.scalar_tensor_tensor(
                            outb[:, o, :], acc[:, o, c * CH : (c + 1) * CH],
                            0.5, xf[:, o, :], op0=ALU.mult, op1=ALU.add,
                        )
                    ctx.fin(mm, "VE")

                ctx.on("vector", b)

                ctx.res("ST", 16)

                def b(sync, c=c, out_tag=out_tag):
                    ctx.wait(sync, "VE", out_tag)
                    sync.dma_start(out=A["y"][c], in_=outb).then_inc(
                        ctx.sems["ST"], 16
                    )

                ctx.on("sync", b)
            ctx.flush()

        ctx.tag["w_used"] = ctx.c["MM"]

    st_final = ctx.c["ST"]

    def b(sync):
        ctx.wait(sync, "ST", st_final)

    ctx.on("sync", b)
    ctx.flush()


def build_moe():
    nc = bass.Bass("TRN2", target_bir_lowering=False)
    A = moe_drams(nc)
    es = contextlib.ExitStack()
    with es, nc.Block() as block:
        ctx = Ctx(nc, es, block)
        emit_moe(nc, block, ctx, es, A)
    return nc


def moe_host_inputs(xt_core, W):
    import ml_dtypes

    eb8 = np.zeros((NE, NE * 128), dtype=ml_dtypes.bfloat16)
    for e in range(NE):
        eb8[e, e * 128 : (e + 1) * 128] = 1.0
    w1 = np.ascontiguousarray(
        np.asarray(W["W1"]).reshape(NE, DT, 128, HD).transpose(0, 2, 1, 3)
    )
    w2 = np.ascontiguousarray(
        np.asarray(W["W2"]).reshape(NE, HT, 128, D).transpose(0, 2, 1, 3)
    )
    b1t = np.ascontiguousarray(
        np.asarray(W["b1"]).reshape(NE, HT, 128).transpose(2, 1, 0)
    )
    return {
        "x": xt_core,
        "wg": tile_w(np.asarray(W["Wg"])),
        "bg": np.ascontiguousarray(np.asarray(W["bg"]).reshape(NE, 1)),
        "w1": w1,
        "b1t": b1t,
        "w2": w2,
        "b2": np.ascontiguousarray(np.asarray(W["b2"])),
        "idf": np.eye(8, dtype=np.float32),
        "idb": np.eye(128, dtype=np.float32).astype(ml_dtypes.bfloat16),
        "eb8": eb8,
    }



DL = 256
NH = 16


def mla_drams(nc, pfx="", io=True):
    A = {}
    if io:
        A["x"] = nc.dram_tensor(pfx + "x", [NCH, 128, DT, CH], F32, kind="ExternalInput").ap()
    A["wdkv"] = nc.dram_tensor(pfx + "wdkv", [128, DT, DL], F32, kind="ExternalInput").ap()
    A["wuk"] = nc.dram_tensor(pfx + "wuk", [128, 2, D], F32, kind="ExternalInput").ap()
    A["wuv"] = nc.dram_tensor(pfx + "wuv", [128, 2, D], F32, kind="ExternalInput").ap()
    A["wuq"] = nc.dram_tensor(pfx + "wuq", [128, DT, D], F32, kind="ExternalInput").ap()
    A["wo"] = nc.dram_tensor(pfx + "wo", [128, DT, D], F32, kind="ExternalInput").ap()
    A["mask4"] = nc.dram_tensor(pfx + "mask4", [128, 4, CH], BF16, kind="ExternalInput").ap()
    if io:
        A["y"] = nc.dram_tensor(pfx + "y", [NCH, 128, DT, CH], F32, kind="ExternalOutput").ap()
    A["qth"] = nc.dram_tensor(pfx + "qth", [NCH, 128, DT, CH], BF16, kind="Internal").ap()
    A["kth"] = nc.dram_tensor(pfx + "kth", [NCH, 128, DT, CH], BF16, kind="Internal").ap()
    A["vah"] = nc.dram_tensor(pfx + "vah", [NCH, 128, 4, NH * 65], BF16, kind="Internal").ap()

    return A


def emit_mla(nc, block, ctx, es, A):
    sb = lambda *a: es.enter_context(nc.sbuf_tensor(*a, allow_name_mangling=True))
    psa = lambda *a: es.enter_context(nc.psum_tensor(*a, allow_name_mangling=True))

    xf = sb("xf", [128, DT, CH], F32)
    xnc = sb("xnc", [128, 2, DT, CH], BF16)
    sq = sb("sqb", [128, 1, CH], BF16)
    bc = sb("bcb", [128, 2 * CH], BF16)
    rowm = sb("rowm", [1, CH], F32)
    rowv = sb("rowv", [1, CH], F32)
    rows = sb("rows", [1, CH], F32)
    onesb = sb("onesb", [128, 1], BF16)
    eps1 = sb("eps1", [1, 1], F32)
    ones1f = sb("ones1f", [1, 128], F32)
    neg10 = sb("neg10", [128, 1], F32)
    ckvT = sb("ckvT", [128, 2, T], BF16)
    wdkv_t = sb("wdkv_t", [128, DT, DL], BF16)
    wuk_t = sb("wuk_t", [128, 2, D], BF16)
    wuv_t = sb("wuv_t", [128, 2, D], BF16)
    wuq_t = sb("wuq_t", [128, DT, D], BF16)
    wo_t = sb("wo_t", [128, DT, D], BF16)
    qt2 = sb("qt2", [128, 2, DT, CH], BF16)
    kt2 = sb("kt2", [128, 2, DT, CH], BF16)
    va2 = sb("va2", [128, 2, 4, NH * 65], BF16)
    pb = sb("pb", [128, 2, CH], BF16)
    mask_t = sb("mask_t", [128, 4, CH], BF16)
    ctxEO = [sb("ctxE", [128, DT, CH], F32), sb("ctxO", [128, DT, CH], F32)]
    ctx_bf = sb("ctx_bf", [128, DT, CH], BF16)
    rden = sb("rden", [1, 4, CH], F32)

    outb = ctxEO[1][:]

    ps_stats = psa("ps_stats", [1, 2 * CH], F32)
    ps_bc = psa("ps_bc", [128, 2 * CH], F32)
    ps_w = psa("ps_w", [128, 2 * CH], F32)
    ps_pv = [psa("ps_pv0", [65, CH], F32), psa("ps_pv1", [65, CH], F32)]
    ps_s = [ps_w[:, 0:CH], ps_w[:, CH : 2 * CH]]
    ps_o = [ps_bc[:, 0:CH], ps_bc[:, CH : 2 * CH]]
    ps_s4 = [ps_w[:, 0:CH], ps_w[:, CH : 2 * CH],
             ps_bc[:, 0:CH], ps_bc[:, CH : 2 * CH]]
    pb4 = xnc[:, 0, 0:4, :]

    qtb = qt2[:, 0, :, :]
    ktb = kt2[:, 0, :, :]
    vab = va2[:, 0, :, :]


    bufs = dict(
        sq=sq, bc=bc, rowm=rowm, rowv=rowv, rows=rows, onesb=onesb,
        ones1f=ones1f, ps_stats=ps_stats, ps_bc=ps_bc, eps1=eps1,
    )

    w_tag = ctx.res("LW", 96)

    def b(gpsimd):
        for dst, srcd in [(wdkv_t, A["wdkv"]), (wuk_t, A["wuk"]), (wuv_t, A["wuv"]),
                          (wuq_t, A["wuq"]), (wo_t, A["wo"]), (mask_t, A["mask4"])]:
            gpsimd.dma_start(out=dst[:], in_=srcd).then_inc(ctx.sems["LW"], 16)

    ctx.on("gpsimd", b)

    init_ve = ctx.res("VE")

    def b(vector):
        vector.memset(onesb[:], 1.0)
        vector.memset(eps1[:], EPS_LN)
        vector.memset(neg10[:], -10.0)
        vector.memset(
            va2[:].rearrange("p b w (h e) -> p b w h e", e=65)[:, :, :, :, 64:65],
            1.0,
        )
        ctx.fin(vector.memset(ones1f[:], 1.0), "VE")

    ctx.on("vector", b)

    for c in range(NCH):
        xn = xnc[:, c % 2, :, :]
        csl = slice(c * CH, (c + 1) * CH)
        xf_free = ctx.t("xf_free")
        ld_tag = ctx.res("LD", 16)

        def b(sync, c=c, xf_free=xf_free):
            ctx.wait(sync, "VE", xf_free)
            sync.dma_start(out=xf[:], in_=A["x"][c]).then_inc(ctx.sems["LD"], 16)

        ctx.on("sync", b)

        xn_used = ctx.t(f"xn_used_{c-2}")

        def pre(vector, c=c, xn_used=xn_used, init_ve=init_ve):
            if c == 0:
                ctx.wait(vector, "VE", init_ve)
            if c >= 2:
                ctx.wait(vector, "MM", xn_used)

        ve_tag = q_ln_chunk(ctx, bufs, xf[:], xn, "LD", ld_tag, pre)
        ctx.tag["xf_free"] = ctx.t("ln_cast")

        for m2 in range(2):
            psw_free = ctx.t(f"psw_free_{m2 % 2}")
            mm_tag = ctx.res("MM")

            def b(tensor, c=c, m2=m2, xn=xn, ve_tag=ve_tag, psw_free=psw_free,
                  w_tag=w_tag):
                if c == 0 and m2 == 0:
                    ctx.wait(tensor, "LW", w_tag)
                ctx.wait(tensor, "VE", ve_tag)
                ctx.wait(tensor, "SE", psw_free)
                for a in range(DT):
                    mm = tensor.matmul(
                        ps_s[m2 % 2], wdkv_t[:, a, m2 * 128 : (m2 + 1) * 128],
                        xn[:, a, :], start=(a == 0), stop=(a == DT - 1),
                    )
                ctx.fin(mm, "MM")

            ctx.on("tensor", b)
            i = ctx.res("SE")
            ctx.tag[f"psw_free_{m2 % 2}"] = i

            def b(scalar, m2=m2, csl=csl, mm_tag=mm_tag):
                ctx.wait(scalar, "MM", mm_tag)
                ctx.fin(scalar.copy(ckvT[:, m2, csl], ps_s[m2 % 2]), "SE")

            ctx.on("scalar", b)

        ckv_se = ctx.c["SE"]

        for ot in range(DT):
            psw_free = ctx.t(f"psw_free_{ot % 2}")
            q_spill = ctx.t(f"q_spill_{c-1}")
            mm_tag = ctx.res("MM")

            def b(tensor, c=c, ot=ot, xn=xn, psw_free=psw_free, q_spill=q_spill):
                ctx.wait(tensor, "SE", psw_free)
                if ot == 0:
                    ctx.wait(tensor, "QS", q_spill)
                for a in range(DT):
                    mm = tensor.matmul(
                        ps_s[ot % 2], wuq_t[:, a, ot * 128 : (ot + 1) * 128],
                        xn[:, a, :], start=(a == 0), stop=(a == DT - 1),
                    )
                ctx.fin(mm, "MM")

            ctx.on("tensor", b)
            i = ctx.res("SE")
            ctx.tag[f"psw_free_{ot % 2}"] = i

            def b(scalar, ot=ot, mm_tag=mm_tag):
                ctx.wait(scalar, "MM", mm_tag)
                ctx.fin(scalar.copy(qtb[:, ot, :], ps_s[ot % 2]), "SE")

            ctx.on("scalar", b)

        q_se = ctx.c["SE"]
        ctx.res("QS", 16)
        ctx.tag[f"q_spill_{c}"] = ctx.c["QS"]

        def b(gpsimd, c=c, q_se=q_se):
            ctx.wait(gpsimd, "SE", q_se)
            gpsimd.dma_start(out=A["qth"][c], in_=qtb).then_inc(ctx.sems["QS"], 16)

        ctx.on("gpsimd", b)

        for ot in range(DT):
            psw_free = ctx.t(f"psw_free_{ot % 2}")
            k_spill = ctx.t(f"k_spill_{c-1}")
            mm_tag = ctx.res("MM")

            def b(tensor, c=c, ot=ot, csl=csl, ckv_se=ckv_se, psw_free=psw_free,
                  k_spill=k_spill):
                if ot == 0:
                    ctx.wait(tensor, "SE", ckv_se)
                    ctx.wait(tensor, "KS", k_spill)
                ctx.wait(tensor, "SE", psw_free)
                for c2 in range(2):
                    mm = tensor.matmul(
                        ps_s[ot % 2], wuk_t[:, c2, ot * 128 : (ot + 1) * 128],
                        ckvT[:, c2, csl], start=(c2 == 0), stop=(c2 == 1),
                    )
                ctx.fin(mm, "MM")

            ctx.on("tensor", b)
            i = ctx.res("SE")
            ctx.tag[f"psw_free_{ot % 2}"] = i

            def b(scalar, ot=ot, mm_tag=mm_tag):
                ctx.wait(scalar, "MM", mm_tag)
                ctx.fin(scalar.copy(ktb[:, ot, :], ps_s[ot % 2]), "SE")

            ctx.on("scalar", b)

        k_se = ctx.c["SE"]
        ctx.res("KS", 16)
        ctx.tag[f"k_spill_{c}"] = ctx.c["KS"]

        def b(gpsimd, c=c, k_se=k_se):
            ctx.wait(gpsimd, "SE", k_se)
            gpsimd.dma_start(out=A["kth"][c], in_=ktb).then_inc(ctx.sems["KS"], 16)

        ctx.on("gpsimd", b)

        for tt in range(4):
            psbc_free = ctx.t("psbc_free")
            v_spill = ctx.t(f"v_spill_{c-1}")
            mm_tag = ctx.res("MM")

            def b(tensor, c=c, tt=tt, ckv_se=ckv_se, psbc_free=psbc_free,
                  v_spill=v_spill):
                if tt == 0:
                    ctx.wait(tensor, "SE", ckv_se)
                    ctx.wait(tensor, "VS", v_spill)
                ctx.wait(tensor, "VE", psbc_free)
                for half in range(2):
                    for c2 in range(2):
                        mm = tensor.matmul(
                            ps_bc[:, half * CH : (half + 1) * CH],
                            ckvT[:, c2, c * CH + tt * 128 : c * CH + (tt + 1) * 128],
                            wuv_t[:, c2, half * CH : (half + 1) * CH],
                            start=(c2 == 0), stop=(c2 == 1),
                        )
                ctx.fin(mm, "MM")

            ctx.on("tensor", b)
            i = ctx.res("VE")
            ctx.tag["psbc_free"] = i

            def b(vector, tt=tt, mm_tag=mm_tag):
                ctx.wait(vector, "MM", mm_tag)
                ctx.fin(
                    vector.tensor_copy(
                        vab.rearrange("p w (h e) -> p w h e", e=65)[
                            :, tt, :, 0:64
                        ],
                        ps_bc[:].rearrange("p (h e) -> p h e", e=64),
                    ),
                    "VE",
                )

            ctx.on("vector", b)

        v_ve = ctx.c["VE"]
        ctx.res("VS", 16)
        ctx.tag[f"v_spill_{c}"] = ctx.c["VS"]

        def b(gpsimd, c=c, v_ve=v_ve):
            ctx.wait(gpsimd, "VE", v_ve)
            gpsimd.dma_start(out=A["vah"][c], in_=vab).then_inc(ctx.sems["VS"], 16)

        ctx.on("gpsimd", b)

        ctx.tag[f"xn_used_{c}"] = ctx.c["MM"]
        ctx.flush()

    stageN_mm = ctx.c["MM"]
    stageN_ve = ctx.c["VE"]
    stageN_se = ctx.c["SE"]

    for qc in range(NCH):
        qb = qc % 2
        qt_used = ctx.t(f"qt2_used_{qc-2}")
        out_done = ctx.t("out_done")
        xf_free = ctx.t("xf_free")
        q_spill = ctx.t(f"q_spill_{qc}")
        q_ld = ctx.res("LD", 32)

        def b(sync, qc=qc, qb=qb, qt_used=qt_used, out_done=out_done,
              xf_free=xf_free, q_spill=q_spill):
            ctx.wait(sync, "QS", q_spill)
            if qc < 2:
                ctx.wait(sync, "MM", stageN_mm)
                ctx.wait(sync, "VE", stageN_ve)
            else:
                ctx.wait(sync, "MM", qt_used)
            sync.dma_start(out=qt2[:, qb, :, :], in_=A["qth"][qc]).then_inc(
                ctx.sems["LD"], 16
            )
            ctx.wait(sync, "VE", out_done)
            if qc == 0:
                ctx.wait(sync, "VE", xf_free)
            sync.dma_start(out=xf[:], in_=A["x"][qc]).then_inc(ctx.sems["LD"], 16)

        ctx.on("sync", b)

        for kc in range(qc + 1):
            it = qc * (qc + 1) // 2 + kc
            kb = it % 2
            diag = kc == qc

            kv_used = ctx.t(f"kv2_used_{it-2}")
            k_spill = ctx.t(f"k_spill_{kc}")
            v_spill = ctx.t(f"v_spill_{kc}")
            kv_ld = ctx.res("LD", 32)

            def b(sync, qc=qc, kc=kc, it=it, kb=kb, kv_used=kv_used,
                  k_spill=k_spill, v_spill=v_spill):
                ctx.wait(sync, "KS", k_spill)
                ctx.wait(sync, "VS", v_spill)
                if it >= 2:
                    ctx.wait(sync, "MM", kv_used)
                elif qc < 2:
                    ctx.wait(sync, "MM", stageN_mm)
                sync.dma_start(out=kt2[:, kb, :, :], in_=A["kth"][kc]).then_inc(
                    ctx.sems["LD"], 16
                )
                sync.dma_start(out=va2[:, kb, :, :], in_=A["vah"][kc]).then_inc(
                    ctx.sems["LD"], 16
                )

            ctx.on("sync", b)

            se_base = ctx.c["SE"]
            rbc_read = ctx.t("rbc_read")

            mm_tags = {}
            ve_mask_tags = {}
            n_ve_pre = ctx.c["VE"]

            LAGM = 3
            msched = []
            for j in range(NH * 4 + LAGM):
                if j < NH * 4:
                    msched.append(("S", j))
                if j >= LAGM:
                    msched.append(("P", j - LAGM))
            for kind, j in msched:
                if kind == "S":
                    mm_tags[("s", j // 4, j % 4)] = ctx.res("MM")
                else:
                    mm_tags[("pv", j // 4, j % 4)] = ctx.res("MM")
            for h in range(NH):
                for kt in range(4):
                    ctx.res("SE")
                    if diag:
                        ve_mask_tags[(h, kt)] = ctx.res("VE")
                mm_tags[("acc", h)] = ctx.res("VE")

            pv_tail = [ctx.t(f"pv_tail_{i}") for i in range(4)]
            for i in range(4):
                ctx.tag[f"pv_tail_{i}"] = mm_tags[("pv", NH - 1, i)]
            pso_p = [ctx.t("pso_free_0"), ctx.t("pso_free_1")]
            pv_used_prev = [ctx.t("pv_used_0"), ctx.t("pv_used_1")]
            ctx.tag["pv_used_0"] = mm_tags[("acc", NH - 2)]
            ctx.tag["pv_used_1"] = mm_tags[("acc", NH - 1)]
            st_used = ctx.t("out_stored")

            def b(tensor, qc=qc, kc=kc, it=it, kb=kb, qb=qb, diag=diag,
                  kv_ld=kv_ld, q_ld=q_ld, se_base=se_base, mm_tags=mm_tags,
                  ve_mask_tags=ve_mask_tags, msched=msched, pso_p=pso_p,
                  rbc_read=rbc_read, pv_used_prev=pv_used_prev):
                def eS(j):
                    h, kt = j // 4, j % 4
                    a = h // 2
                    base = 64 * (h % 2)
                    if j == 0:
                        ctx.wait(tensor, "LD", kv_ld)
                        ctx.wait(tensor, "LD", q_ld - 16)
                        if kc == 0:
                            if qc == 0:
                                ctx.wait(tensor, "SE", stageN_se)
                            ctx.wait(tensor, "VE", rbc_read)
                    if kc == 0 and j in (2, 3):
                        ctx.wait(tensor, "VE", pso_p[j - 2])
                    if j >= 4:
                        ctx.wait(tensor, "SE", se_base + j - 3)
                    ctx.fin(
                        tensor.matmul(
                            ps_s4[j % 4],
                            kt2[base : base + 64, kb, a,
                                kt * 128 : (kt + 1) * 128],
                            qt2[base : base + 64, qb, a, :],
                            start=True, stop=True,
                            tile_position=(base, 0),
                        ),
                        "MM",
                    )

                def eP(j):
                    h, kt = j // 4, j % 4
                    if diag:
                        ctx.wait(tensor, "VE", ve_mask_tags[(h, kt)])
                    else:
                        ctx.wait(tensor, "SE", se_base + j + 1)
                    if kt == 0:
                        ctx.wait(
                            tensor, "VE",
                            mm_tags[("acc", h - 2)] if h >= 2
                            else pv_used_prev[h % 2],
                        )
                    ctx.fin(
                        tensor.matmul(
                            ps_pv[h % 2][:],
                            va2[:, kb, kt, h * 65 : (h + 1) * 65],
                            pb4[:, j % 4, :],
                            start=(kt == 0), stop=(kt == 3),
                        ),
                        "MM",
                    )

                for kind, j in msched:
                    (eS if kind == "S" else eP)(j)

            ctx.on("tensor", b)

            def b(scalar, diag=diag, mm_tags=mm_tags, pv_tail=pv_tail):
                for h in range(NH):
                    for kt in range(4):
                        j = h * 4 + kt
                        ctx.wait(scalar, "MM", mm_tags[("s", h, kt)])
                        if j >= 4:
                            ctx.wait(scalar, "MM",
                                     mm_tags[("pv", (j - 4) // 4, (j - 4) % 4)])
                        else:
                            ctx.wait(scalar, "MM", pv_tail[j])
                        ctx.fin(
                            scalar.activation(
                                pb4[:, j % 4, :], ps_s4[j % 4], AF.Exp,
                                bias=neg10[:], scale=0.125,
                            ),
                            "SE",
                        )

            ctx.on("scalar", b)

            def b(vector, qc=qc, kc=kc, diag=diag, mm_tags=mm_tags,
                  se_base=se_base, st_used=st_used):
                for h in range(NH):
                    for kt in range(4):
                        j = h * 4 + kt
                        if diag:
                            ctx.wait(vector, "SE", se_base + j + 1)
                            ctx.fin(
                                vector.tensor_tensor(
                                    pb4[:, j % 4, :], pb4[:, j % 4, :],
                                    mask_t[:, kt, :], op=ALU.mult,
                                ),
                                "VE",
                            )
                    ctx.wait(vector, "MM", mm_tags[("pv", h, 3)])
                    if kc == 0 and h < 2:
                        ctx.wait(vector, "ST", st_used)
                    dst = ctxEO[h % 2][0:65, h // 2, :]
                    if kc == 0:
                        ctx.fin(vector.tensor_copy(dst, ps_pv[h % 2][:]), "VE")
                    else:
                        ctx.fin(
                            vector.tensor_tensor(
                                dst, dst, ps_pv[h % 2][:], op=ALU.add
                            ),
                            "VE",
                        )

            ctx.on("vector", b)
            ctx.tag[f"kv2_used_{it}"] = ctx.c["MM"]
            ctx.flush()

        ctx.tag[f"qt2_used_{qc}"] = ctx.c["MM"]
        acc_ve = ctx.c["VE"]

        for batch in range(4):
            par = batch // 2
            alo = 4 * (batch % 2)
            rbc_used = ctx.t("rbc_used")
            r_ve = ctx.res("VE")

            def b(vector, batch=batch, par=par, alo=alo, acc_ve=acc_ve,
                  rbc_used=rbc_used):
                if batch == 0:
                    ctx.wait(vector, "VE", acc_ve)
                ctx.wait(vector, "MM", rbc_used)
                vector.drain()
                ctx.fin(
                    vector.reciprocal(
                        rden[:, :, :], ctxEO[par][64:65, alo : alo + 4, :]
                    ),
                    "VE",
                )

            ctx.on("vector", b)

            for i4 in range(4):
                a = alo + i4
                h = 2 * a + par
                base = 64 * par
                rbc_read2 = ctx.t("rbc_read")
                b_tag = ctx.res("MM")
                ctx.tag["rbc_used"] = b_tag

                def b(tensor, i4=i4, r_ve=r_ve, rbc_read2=rbc_read2):
                    if i4 == 0:
                        ctx.wait(tensor, "VE", r_ve)
                    ctx.wait(tensor, "VE", rbc_read2)
                    ctx.fin(
                        tensor.matmul(
                            ps_s[0][0:64, :], ones1f[:, 0:64],
                            rden[:, i4, :], start=True, stop=True,
                        ),
                        "MM",
                    )

                ctx.on("tensor", b)

                i = ctx.res("VE")
                ctx.tag["rbc_read"] = i

                def b(vector, par=par, a=a, base=base, b_tag=b_tag):
                    ctx.wait(vector, "MM", b_tag)
                    ctx.fin(
                        vector.tensor_tensor(
                            ctx_bf[base : base + 64, a, :],
                            ctxEO[par][0:64, a, :],
                            ps_s[0][0:64, :], op=ALU.mult,
                        ),
                        "VE",
                    )

                ctx.on("vector", b)

        norm_ve = ctx.c["VE"]

        for ot in range(DT):
            pso_free = ctx.t(f"pso_free_{ot % 2}")
            o_tag = ctx.res("MM")

            def b(tensor, qc=qc, ot=ot, norm_ve=norm_ve, pso_free=pso_free):
                if ot == 0:
                    ctx.wait(tensor, "VE", norm_ve)
                ctx.wait(tensor, "VE", pso_free)
                for a in range(DT):
                    mm = tensor.matmul(
                        ps_o[ot % 2], wo_t[:, a, ot * 128 : (ot + 1) * 128],
                        ctx_bf[:, a, :], start=(a == 0), stop=(a == DT - 1),
                    )
                ctx.fin(mm, "MM")

            ctx.on("tensor", b)

            st_prev = ctx.t("out_stored")
            i = ctx.res("VE")
            ctx.tag[f"pso_free_{ot % 2}"] = i
            if ot == DT - 1:
                ctx.tag["out_done"] = i

            def b(vector, qc=qc, ot=ot, o_tag=o_tag, q_ld=q_ld, st_prev=st_prev):
                ctx.wait(vector, "MM", o_tag)
                ctx.wait(vector, "LD", q_ld)
                if ot == 0:
                    ctx.wait(vector, "ST", st_prev)
                ctx.fin(
                    vector.tensor_tensor(
                        outb[:, ot, :], ps_o[ot % 2], xf[:, ot, :], op=ALU.add
                    ),
                    "VE",
                )

            ctx.on("vector", b)

        out_ve = ctx.c["VE"]
        ctx.res("ST", 16)
        ctx.tag["out_stored"] = ctx.c["ST"]

        def b(sync, qc=qc, out_ve=out_ve):
            ctx.wait(sync, "VE", out_ve)
            sync.dma_start(out=A["y"][qc], in_=outb).then_inc(ctx.sems["ST"], 16)

        ctx.on("sync", b)
        ctx.flush()

    st_final = ctx.c["ST"]

    def b(sync):
        ctx.wait(sync, "ST", st_final)

    ctx.on("sync", b)
    ctx.flush()


def build_mla():
    nc = bass.Bass("TRN2", target_bir_lowering=False)
    A = mla_drams(nc)
    es = contextlib.ExitStack()
    with es, nc.Block() as block:
        ctx = Ctx(nc, es, block)
        emit_mla(nc, block, ctx, es, A)
    return nc


def lora_drams(nc, pfx="", io=True):
    A = {}
    if io:
        A["x"] = nc.dram_tensor(pfx + "x", [NCH, 128, DT, CH], F32, kind="ExternalInput").ap()
    A["wqt"] = nc.dram_tensor(pfx + "wqt", [128, DT, D], F32, kind="ExternalInput").ap()
    A["wkt"] = nc.dram_tensor(pfx + "wkt", [128, DT, D], F32, kind="ExternalInput").ap()
    A["wvt"] = nc.dram_tensor(pfx + "wvt", [128, DT, D], F32, kind="ExternalInput").ap()
    A["wot"] = nc.dram_tensor(pfx + "wot", [128, DT, D], F32, kind="ExternalInput").ap()
    A["lat"] = nc.dram_tensor(pfx + "lat", [128, 4, DT, 16], F32, kind="ExternalInput").ap()
    A["lbt"] = nc.dram_tensor(pfx + "lbt", [16, 4, D], F32, kind="ExternalInput").ap()
    A["mask4"] = nc.dram_tensor(pfx + "mask4", [128, 4, CH], BF16, kind="ExternalInput").ap()
    if io:
        A["y"] = nc.dram_tensor(pfx + "y", [NCH, 128, DT, CH], F32, kind="ExternalOutput").ap()
    A["qth"] = nc.dram_tensor(pfx + "qth", [NCH, 128, DT, CH], BF16, kind="Internal").ap()
    A["kth"] = nc.dram_tensor(pfx + "kth", [NCH, 128, DT, CH], BF16, kind="Internal").ap()
    A["vah"] = nc.dram_tensor(pfx + "vah", [NCH, 128, 4, NH * 65], BF16, kind="Internal").ap()

    return A


def emit_lora(nc, block, ctx, es, A):
    sb = lambda *a: es.enter_context(nc.sbuf_tensor(*a, allow_name_mangling=True))
    psa = lambda *a: es.enter_context(nc.psum_tensor(*a, allow_name_mangling=True))

    xf = sb("xf", [128, DT, CH], F32)
    xnc = sb("xnc", [128, 2, DT, CH], BF16)
    sq = sb("sqb", [128, 1, CH], BF16)
    bc = sb("bcb", [128, 2 * CH], BF16)
    rowm = sb("rowm", [1, CH], F32)
    rowv = sb("rowv", [1, CH], F32)
    rows = sb("rows", [1, CH], F32)
    onesb = sb("onesb", [128, 1], BF16)
    eps1 = sb("eps1", [1, 1], F32)
    ones1f = sb("ones1f", [1, 128], F32)
    neg10 = sb("neg10", [128, 1], F32)
    wq_t = sb("wq_t", [128, DT, D], BF16)
    wk_t = sb("wk_t", [128, DT, D], BF16)
    wv_t = sb("wv_t", [128, DT, D], BF16)
    la_t = sb("la_t", [128, 4, DT, 16], BF16)
    lb_t = sb("lb_t", [16, 4, D], BF16)
    lraw = sb("lraw", [16, 3, CH], BF16)
    lo_raw = sb("lo_raw", [16, CH], BF16)
    wo_t = wq_t
    qt2 = sb("qt2", [128, 2, DT, CH], BF16)
    kt2 = sb("kt2", [128, 2, DT, CH], BF16)
    va2 = sb("va2", [128, 2, 4, NH * 65], BF16)
    pb = sb("pb", [128, 2, CH], BF16)
    mask_t = sb("mask_t", [128, 4, CH], BF16)
    ctxEO = [sb("ctxE", [128, DT, CH], F32), sb("ctxO", [128, DT, CH], F32)]
    ctx_bf = sb("ctx_bf", [128, DT, CH], BF16)
    rden = sb("rden", [1, 4, CH], F32)

    outb = ctxEO[1][:]

    ps_stats = psa("ps_stats", [1, 2 * CH], F32)
    ps_bc = psa("ps_bc", [128, 2 * CH], F32)
    ps_w = psa("ps_w", [128, 2 * CH], F32)
    ps_pv = [psa("ps_pv0", [65, CH], F32), psa("ps_pv1", [65, CH], F32)]
    ps_s = [ps_w[:, 0:CH], ps_w[:, CH : 2 * CH]]
    ps_o = [ps_bc[:, 0:CH], ps_bc[:, CH : 2 * CH]]
    ps_s4 = [ps_w[:, 0:CH], ps_w[:, CH : 2 * CH],
             ps_bc[:, 0:CH], ps_bc[:, CH : 2 * CH]]
    pb4 = xnc[:, 0, 0:4, :]

    qtb = qt2[:, 0, :, :]
    ktb = kt2[:, 0, :, :]
    vab = va2[:, 0, :, :]


    bufs = dict(
        sq=sq, bc=bc, rowm=rowm, rowv=rowv, rows=rows, onesb=onesb,
        ones1f=ones1f, ps_stats=ps_stats, ps_bc=ps_bc, eps1=eps1,
    )

    w_tag = ctx.res("LW", 96)

    def b(gpsimd):
        for dst, srcd in [(wq_t, A["wqt"]), (wk_t, A["wkt"]), (wv_t, A["wvt"]),
                          (la_t, A["lat"]), (lb_t, A["lbt"]), (mask_t, A["mask4"])]:
            gpsimd.dma_start(out=dst[:], in_=srcd).then_inc(ctx.sems["LW"], 16)

    ctx.on("gpsimd", b)

    init_ve = ctx.res("VE")

    def b(vector):
        vector.memset(onesb[:], 1.0)
        vector.memset(eps1[:], EPS_LN)
        vector.memset(neg10[:], -10.0)
        vector.memset(
            va2[:].rearrange("p b w (h e) -> p b w h e", e=65)[:, :, :, :, 64:65],
            1.0,
        )
        ctx.fin(vector.memset(ones1f[:], 1.0), "VE")

    ctx.on("vector", b)

    for c in range(NCH):
        xn = xnc[:, c % 2, :, :]
        csl = slice(c * CH, (c + 1) * CH)
        xf_free = ctx.t("xf_free")
        ld_tag = ctx.res("LD", 16)

        def b(sync, c=c, xf_free=xf_free):
            ctx.wait(sync, "VE", xf_free)
            sync.dma_start(out=xf[:], in_=A["x"][c]).then_inc(ctx.sems["LD"], 16)

        ctx.on("sync", b)

        xn_used = ctx.t(f"xn_used_{c-2}")

        def pre(vector, c=c, xn_used=xn_used, init_ve=init_ve):
            if c == 0:
                ctx.wait(vector, "VE", init_ve)
            if c >= 2:
                ctx.wait(vector, "MM", xn_used)

        ve_tag = q_ln_chunk(ctx, bufs, xf[:], xn, "LD", ld_tag, pre)
        ctx.tag["xf_free"] = ctx.t("ln_cast")

        lraw_used = ctx.t("lraw_used")
        for w3 in range(3):
            psw_free = ctx.t(f"psw_free_{w3 % 2}")
            mm_tag = ctx.res("MM")

            def b(tensor, c=c, w3=w3, xn=xn, ve_tag=ve_tag, psw_free=psw_free,
                  w_tag=w_tag):
                if c == 0 and w3 == 0:
                    ctx.wait(tensor, "LW", w_tag)
                ctx.wait(tensor, "VE", ve_tag)
                ctx.wait(tensor, "SE", psw_free)
                for a in range(DT):
                    mm = tensor.matmul(
                        ps_s[w3 % 2][0:16, :], la_t[:, w3, a, :],
                        xn[:, a, :], start=(a == 0), stop=(a == DT - 1),
                    )
                ctx.fin(mm, "MM")

            ctx.on("tensor", b)
            i = ctx.res("SE")
            ctx.tag[f"psw_free_{w3 % 2}"] = i

            def b(scalar, w3=w3, mm_tag=mm_tag, lraw_used=lraw_used):
                ctx.wait(scalar, "MM", mm_tag)
                if w3 == 0:
                    ctx.wait(scalar, "MM", lraw_used)
                ctx.fin(
                    scalar.activation(
                        lraw[:, w3, :], ps_s[w3 % 2][0:16, :], AF.Identity,
                        scale=1.0 / 16.0,
                    ),
                    "SE",
                )

            ctx.on("scalar", b)

        ckv_se = ctx.c["SE"]

        for ot in range(DT):
            psw_free = ctx.t(f"psw_free_{ot % 2}")
            q_spill = ctx.t(f"q_spill_{c-1}")
            mm_tag = ctx.res("MM")

            def b(tensor, c=c, ot=ot, xn=xn, psw_free=psw_free, q_spill=q_spill,
                  ckv_se=ckv_se):
                ctx.wait(tensor, "SE", psw_free)
                if ot == 0:
                    ctx.wait(tensor, "QS", q_spill)
                    ctx.wait(tensor, "SE", ckv_se)
                for a in range(DT):
                    tensor.matmul(
                        ps_s[ot % 2], wq_t[:, a, ot * 128 : (ot + 1) * 128],
                        xn[:, a, :], start=(a == 0), stop=False,
                    )
                mm = tensor.matmul(
                    ps_s[ot % 2], lb_t[:, 0, ot * 128 : (ot + 1) * 128],
                    lraw[:, 0, :], start=False, stop=True,
                )
                ctx.fin(mm, "MM")

            ctx.on("tensor", b)
            i = ctx.res("SE")
            ctx.tag[f"psw_free_{ot % 2}"] = i

            def b(scalar, ot=ot, mm_tag=mm_tag):
                ctx.wait(scalar, "MM", mm_tag)
                ctx.fin(scalar.copy(qtb[:, ot, :], ps_s[ot % 2]), "SE")

            ctx.on("scalar", b)

        q_se = ctx.c["SE"]
        ctx.res("QS", 16)
        ctx.tag[f"q_spill_{c}"] = ctx.c["QS"]

        def b(gpsimd, c=c, q_se=q_se):
            ctx.wait(gpsimd, "SE", q_se)
            gpsimd.dma_start(out=A["qth"][c], in_=qtb).then_inc(ctx.sems["QS"], 16)

        ctx.on("gpsimd", b)

        for ot in range(DT):
            psw_free = ctx.t(f"psw_free_{ot % 2}")
            k_spill = ctx.t(f"k_spill_{c-1}")
            mm_tag = ctx.res("MM")

            def b(tensor, c=c, ot=ot, xn=xn, ckv_se=ckv_se, psw_free=psw_free,
                  k_spill=k_spill):
                if ot == 0:
                    ctx.wait(tensor, "KS", k_spill)
                ctx.wait(tensor, "SE", psw_free)
                for a in range(DT):
                    tensor.matmul(
                        ps_s[ot % 2], wk_t[:, a, ot * 128 : (ot + 1) * 128],
                        xn[:, a, :], start=(a == 0), stop=False,
                    )
                mm = tensor.matmul(
                    ps_s[ot % 2], lb_t[:, 1, ot * 128 : (ot + 1) * 128],
                    lraw[:, 1, :], start=False, stop=True,
                )
                ctx.fin(mm, "MM")

            ctx.on("tensor", b)
            i = ctx.res("SE")
            ctx.tag[f"psw_free_{ot % 2}"] = i

            def b(scalar, ot=ot, mm_tag=mm_tag):
                ctx.wait(scalar, "MM", mm_tag)
                ctx.fin(scalar.copy(ktb[:, ot, :], ps_s[ot % 2]), "SE")

            ctx.on("scalar", b)

        k_se = ctx.c["SE"]
        ctx.res("KS", 16)
        ctx.tag[f"k_spill_{c}"] = ctx.c["KS"]

        def b(gpsimd, c=c, k_se=k_se):
            ctx.wait(gpsimd, "SE", k_se)
            gpsimd.dma_start(out=A["kth"][c], in_=ktb).then_inc(ctx.sems["KS"], 16)

        ctx.on("gpsimd", b)

        for tt in range(4):
            psbc_free = ctx.t("psbc_free")
            v_spill = ctx.t(f"v_spill_{c-1}")
            mm_tag = ctx.res("MM")

            def b(tensor, c=c, tt=tt, xn=xn, ckv_se=ckv_se, psbc_free=psbc_free,
                  v_spill=v_spill):
                if tt == 0:
                    ctx.wait(tensor, "VS", v_spill)
                ctx.wait(tensor, "VE", psbc_free)
                for half in range(2):
                    for a in range(DT):
                        tensor.matmul(
                            ps_bc[:, half * CH : (half + 1) * CH],
                            xn[:, a, tt * 128 : (tt + 1) * 128],
                            wv_t[:, a, half * CH : (half + 1) * CH],
                            start=(a == 0), stop=False,
                        )
                    mm = tensor.matmul(
                        ps_bc[:, half * CH : (half + 1) * CH],
                        lraw[:, 2, tt * 128 : (tt + 1) * 128],
                        lb_t[:, 2, half * CH : (half + 1) * CH],
                        start=False, stop=True,
                    )
                ctx.fin(mm, "MM")

            ctx.on("tensor", b)
            i = ctx.res("VE")
            ctx.tag["psbc_free"] = i

            def b(vector, tt=tt, mm_tag=mm_tag):
                ctx.wait(vector, "MM", mm_tag)
                ctx.fin(
                    vector.tensor_copy(
                        vab.rearrange("p w (h e) -> p w h e", e=65)[
                            :, tt, :, 0:64
                        ],
                        ps_bc[:].rearrange("p (h e) -> p h e", e=64),
                    ),
                    "VE",
                )

            ctx.on("vector", b)

        v_ve = ctx.c["VE"]
        ctx.res("VS", 16)
        ctx.tag[f"v_spill_{c}"] = ctx.c["VS"]

        def b(gpsimd, c=c, v_ve=v_ve):
            ctx.wait(gpsimd, "VE", v_ve)
            gpsimd.dma_start(out=A["vah"][c], in_=vab).then_inc(ctx.sems["VS"], 16)

        ctx.on("gpsimd", b)

        ctx.tag[f"xn_used_{c}"] = ctx.c["MM"]
        ctx.tag["lraw_used"] = ctx.c["MM"]
        ctx.flush()

    stageN_mm = ctx.c["MM"]
    stageN_ve = ctx.c["VE"]
    stageN_se = ctx.c["SE"]

    wo_tag = ctx.res("LW", 16)

    def b(gpsimd):
        ctx.wait(gpsimd, "MM", stageN_mm)
        gpsimd.dma_start(out=wo_t[:], in_=A["wot"]).then_inc(ctx.sems["LW"], 16)

    ctx.on("gpsimd", b)

    for qc in range(NCH):
        qb = qc % 2
        qt_used = ctx.t(f"qt2_used_{qc-2}")
        out_done = ctx.t("out_done")
        xf_free = ctx.t("xf_free")
        q_spill = ctx.t(f"q_spill_{qc}")
        q_ld = ctx.res("LD", 32)

        def b(sync, qc=qc, qb=qb, qt_used=qt_used, out_done=out_done,
              xf_free=xf_free, q_spill=q_spill):
            ctx.wait(sync, "QS", q_spill)
            if qc < 2:
                ctx.wait(sync, "MM", stageN_mm)
                ctx.wait(sync, "VE", stageN_ve)
            else:
                ctx.wait(sync, "MM", qt_used)
            sync.dma_start(out=qt2[:, qb, :, :], in_=A["qth"][qc]).then_inc(
                ctx.sems["LD"], 16
            )
            ctx.wait(sync, "VE", out_done)
            if qc == 0:
                ctx.wait(sync, "VE", xf_free)
            sync.dma_start(out=xf[:], in_=A["x"][qc]).then_inc(ctx.sems["LD"], 16)

        ctx.on("sync", b)

        for kc in range(qc + 1):
            it = qc * (qc + 1) // 2 + kc
            kb = it % 2
            diag = kc == qc

            kv_used = ctx.t(f"kv2_used_{it-2}")
            k_spill = ctx.t(f"k_spill_{kc}")
            v_spill = ctx.t(f"v_spill_{kc}")
            kv_ld = ctx.res("LD", 32)

            def b(sync, qc=qc, kc=kc, it=it, kb=kb, kv_used=kv_used,
                  k_spill=k_spill, v_spill=v_spill):
                ctx.wait(sync, "KS", k_spill)
                ctx.wait(sync, "VS", v_spill)
                if it >= 2:
                    ctx.wait(sync, "MM", kv_used)
                elif qc < 2:
                    ctx.wait(sync, "MM", stageN_mm)
                sync.dma_start(out=kt2[:, kb, :, :], in_=A["kth"][kc]).then_inc(
                    ctx.sems["LD"], 16
                )
                sync.dma_start(out=va2[:, kb, :, :], in_=A["vah"][kc]).then_inc(
                    ctx.sems["LD"], 16
                )

            ctx.on("sync", b)

            se_base = ctx.c["SE"]
            rbc_read = ctx.t("rbc_read")

            mm_tags = {}
            ve_mask_tags = {}
            n_ve_pre = ctx.c["VE"]

            LAGM = 3
            msched = []
            for j in range(NH * 4 + LAGM):
                if j < NH * 4:
                    msched.append(("S", j))
                if j >= LAGM:
                    msched.append(("P", j - LAGM))
            for kind, j in msched:
                if kind == "S":
                    mm_tags[("s", j // 4, j % 4)] = ctx.res("MM")
                else:
                    mm_tags[("pv", j // 4, j % 4)] = ctx.res("MM")
            for h in range(NH):
                for kt in range(4):
                    ctx.res("SE")
                    if diag:
                        ve_mask_tags[(h, kt)] = ctx.res("VE")
                mm_tags[("acc", h)] = ctx.res("VE")

            pv_tail = [ctx.t(f"pv_tail_{i}") for i in range(4)]
            for i in range(4):
                ctx.tag[f"pv_tail_{i}"] = mm_tags[("pv", NH - 1, i)]
            pso_p = [ctx.t("pso_free_0"), ctx.t("pso_free_1")]
            pv_used_prev = [ctx.t("pv_used_0"), ctx.t("pv_used_1")]
            ctx.tag["pv_used_0"] = mm_tags[("acc", NH - 2)]
            ctx.tag["pv_used_1"] = mm_tags[("acc", NH - 1)]
            st_used = ctx.t("out_stored")

            def b(tensor, qc=qc, kc=kc, it=it, kb=kb, qb=qb, diag=diag,
                  kv_ld=kv_ld, q_ld=q_ld, se_base=se_base, mm_tags=mm_tags,
                  ve_mask_tags=ve_mask_tags, msched=msched, pso_p=pso_p,
                  rbc_read=rbc_read, pv_used_prev=pv_used_prev):
                def eS(j):
                    h, kt = j // 4, j % 4
                    a = h // 2
                    base = 64 * (h % 2)
                    if j == 0:
                        ctx.wait(tensor, "LD", kv_ld)
                        ctx.wait(tensor, "LD", q_ld - 16)
                        if kc == 0:
                            if qc == 0:
                                ctx.wait(tensor, "SE", stageN_se)
                            ctx.wait(tensor, "VE", rbc_read)
                    if kc == 0 and j in (2, 3):
                        ctx.wait(tensor, "VE", pso_p[j - 2])
                    if j >= 4:
                        ctx.wait(tensor, "SE", se_base + j - 3)
                    ctx.fin(
                        tensor.matmul(
                            ps_s4[j % 4],
                            kt2[base : base + 64, kb, a,
                                kt * 128 : (kt + 1) * 128],
                            qt2[base : base + 64, qb, a, :],
                            start=True, stop=True,
                            tile_position=(base, 0),
                        ),
                        "MM",
                    )

                def eP(j):
                    h, kt = j // 4, j % 4
                    if diag:
                        ctx.wait(tensor, "VE", ve_mask_tags[(h, kt)])
                    else:
                        ctx.wait(tensor, "SE", se_base + j + 1)
                    if kt == 0:
                        ctx.wait(
                            tensor, "VE",
                            mm_tags[("acc", h - 2)] if h >= 2
                            else pv_used_prev[h % 2],
                        )
                        if h == 0 and kc == 0:
                            ctx.wait(tensor, "SE", ctx.t("lo_done_se"))
                    ctx.fin(
                        tensor.matmul(
                            ps_pv[h % 2][:],
                            va2[:, kb, kt, h * 65 : (h + 1) * 65],
                            pb4[:, j % 4, :],
                            start=(kt == 0), stop=(kt == 3),
                        ),
                        "MM",
                    )

                for kind, j in msched:
                    (eS if kind == "S" else eP)(j)

            ctx.on("tensor", b)

            def b(scalar, diag=diag, mm_tags=mm_tags, pv_tail=pv_tail):
                for h in range(NH):
                    for kt in range(4):
                        j = h * 4 + kt
                        ctx.wait(scalar, "MM", mm_tags[("s", h, kt)])
                        if j >= 4:
                            ctx.wait(scalar, "MM",
                                     mm_tags[("pv", (j - 4) // 4, (j - 4) % 4)])
                        else:
                            ctx.wait(scalar, "MM", pv_tail[j])
                        ctx.fin(
                            scalar.activation(
                                pb4[:, j % 4, :], ps_s4[j % 4], AF.Exp,
                                bias=neg10[:], scale=0.125,
                            ),
                            "SE",
                        )

            ctx.on("scalar", b)

            def b(vector, qc=qc, kc=kc, diag=diag, mm_tags=mm_tags,
                  se_base=se_base, st_used=st_used):
                for h in range(NH):
                    for kt in range(4):
                        j = h * 4 + kt
                        if diag:
                            ctx.wait(vector, "SE", se_base + j + 1)
                            ctx.fin(
                                vector.tensor_tensor(
                                    pb4[:, j % 4, :], pb4[:, j % 4, :],
                                    mask_t[:, kt, :], op=ALU.mult,
                                ),
                                "VE",
                            )
                    ctx.wait(vector, "MM", mm_tags[("pv", h, 3)])
                    if kc == 0 and h < 2:
                        ctx.wait(vector, "ST", st_used)
                    dst = ctxEO[h % 2][0:65, h // 2, :]
                    if kc == 0:
                        ctx.fin(vector.tensor_copy(dst, ps_pv[h % 2][:]), "VE")
                    else:
                        ctx.fin(
                            vector.tensor_tensor(
                                dst, dst, ps_pv[h % 2][:], op=ALU.add
                            ),
                            "VE",
                        )

            ctx.on("vector", b)
            ctx.tag[f"kv2_used_{it}"] = ctx.c["MM"]
            ctx.flush()

        ctx.tag[f"qt2_used_{qc}"] = ctx.c["MM"]
        acc_ve = ctx.c["VE"]

        for batch in range(4):
            par = batch // 2
            alo = 4 * (batch % 2)
            rbc_used = ctx.t("rbc_used")
            r_ve = ctx.res("VE")

            def b(vector, batch=batch, par=par, alo=alo, acc_ve=acc_ve,
                  rbc_used=rbc_used):
                if batch == 0:
                    ctx.wait(vector, "VE", acc_ve)
                ctx.wait(vector, "MM", rbc_used)
                vector.drain()
                ctx.fin(
                    vector.reciprocal(
                        rden[:, :, :], ctxEO[par][64:65, alo : alo + 4, :]
                    ),
                    "VE",
                )

            ctx.on("vector", b)

            for i4 in range(4):
                a = alo + i4
                h = 2 * a + par
                base = 64 * par
                rbc_read2 = ctx.t("rbc_read")
                b_tag = ctx.res("MM")
                ctx.tag["rbc_used"] = b_tag

                def b(tensor, i4=i4, r_ve=r_ve, rbc_read2=rbc_read2):
                    if i4 == 0:
                        ctx.wait(tensor, "VE", r_ve)
                    ctx.wait(tensor, "VE", rbc_read2)
                    ctx.fin(
                        tensor.matmul(
                            ps_s[0][0:64, :], ones1f[:, 0:64],
                            rden[:, i4, :], start=True, stop=True,
                        ),
                        "MM",
                    )

                ctx.on("tensor", b)

                i = ctx.res("VE")
                ctx.tag["rbc_read"] = i

                def b(vector, par=par, a=a, base=base, b_tag=b_tag):
                    ctx.wait(vector, "MM", b_tag)
                    ctx.fin(
                        vector.tensor_tensor(
                            ctx_bf[base : base + 64, a, :],
                            ctxEO[par][0:64, a, :],
                            ps_s[0][0:64, :], op=ALU.mult,
                        ),
                        "VE",
                    )

                ctx.on("vector", b)

        norm_ve = ctx.c["VE"]

        lo_guard = ctx.t("pv_used_0")
        lo_mm = ctx.res("MM")

        def b(tensor, qc=qc, norm_ve=norm_ve, lo_guard=lo_guard, wo_tag=wo_tag):
            ctx.wait(tensor, "VE", norm_ve)
            if qc == 0:
                ctx.wait(tensor, "LW", wo_tag)
            ctx.wait(tensor, "VE", lo_guard)
            for a in range(DT):
                mm = tensor.matmul(
                    ps_pv[0][0:16, :], la_t[:, 3, a, :], ctx_bf[:, a, :],
                    start=(a == 0), stop=(a == DT - 1),
                )
            ctx.fin(mm, "MM")

        ctx.on("tensor", b)

        lo_se = ctx.res("SE")
        ctx.tag["lo_done_se"] = lo_se

        def b(scalar, lo_mm=lo_mm):
            ctx.wait(scalar, "MM", lo_mm)
            ctx.fin(
                scalar.activation(
                    lo_raw[:], ps_pv[0][0:16, :], AF.Identity, scale=1.0 / 16.0
                ),
                "SE",
            )

        ctx.on("scalar", b)

        for ot in range(DT):
            pso_free = ctx.t(f"pso_free_{ot % 2}")
            o_tag = ctx.res("MM")

            def b(tensor, qc=qc, ot=ot, pso_free=pso_free, lo_se=lo_se):
                if ot == 0:
                    ctx.wait(tensor, "SE", lo_se)
                ctx.wait(tensor, "VE", pso_free)
                for a in range(DT):
                    tensor.matmul(
                        ps_o[ot % 2], wo_t[:, a, ot * 128 : (ot + 1) * 128],
                        ctx_bf[:, a, :], start=(a == 0), stop=False,
                    )
                mm = tensor.matmul(
                    ps_o[ot % 2], lb_t[:, 3, ot * 128 : (ot + 1) * 128],
                    lo_raw[:], start=False, stop=True,
                )
                ctx.fin(mm, "MM")

            ctx.on("tensor", b)

            st_prev = ctx.t("out_stored")
            i = ctx.res("VE")
            ctx.tag[f"pso_free_{ot % 2}"] = i
            if ot == DT - 1:
                ctx.tag["out_done"] = i

            def b(vector, qc=qc, ot=ot, o_tag=o_tag, q_ld=q_ld, st_prev=st_prev):
                ctx.wait(vector, "MM", o_tag)
                ctx.wait(vector, "LD", q_ld)
                if ot == 0:
                    ctx.wait(vector, "ST", st_prev)
                ctx.fin(
                    vector.tensor_tensor(
                        outb[:, ot, :], ps_o[ot % 2], xf[:, ot, :], op=ALU.add
                    ),
                    "VE",
                )

            ctx.on("vector", b)

        out_ve = ctx.c["VE"]
        ctx.res("ST", 16)
        ctx.tag["out_stored"] = ctx.c["ST"]

        def b(sync, qc=qc, out_ve=out_ve):
            ctx.wait(sync, "VE", out_ve)
            sync.dma_start(out=A["y"][qc], in_=outb).then_inc(ctx.sems["ST"], 16)

        ctx.on("sync", b)
        ctx.flush()

    st_final = ctx.c["ST"]

    def b(sync):
        ctx.wait(sync, "ST", st_final)

    ctx.on("sync", b)
    ctx.flush()


def build_lora():
    nc = bass.Bass("TRN2", target_bir_lowering=False)
    A = lora_drams(nc)
    es = contextlib.ExitStack()
    with es, nc.Block() as block:
        ctx = Ctx(nc, es, block)
        emit_lora(nc, block, ctx, es, A)
    return nc


def lora_host_inputs(xt_core, W):
    import ml_dtypes

    m4 = np.zeros((128, 4, CH), dtype=np.float32)
    for kt in range(4):
        for kl in range(128):
            m4[kl, kt, :] = (np.arange(CH) >= (128 * kt + kl)).astype(np.float32)
    lat = np.stack(
        [tile_w(np.asarray(W["a" + nm]).T) for nm in ["q", "k", "v", "o"]], axis=1
    )
    lbt = np.stack(
        [np.asarray(W["b" + nm]).T for nm in ["q", "k", "v", "o"]], axis=1
    )
    return {
        "x": xt_core,
        "wqt": tile_w(np.asarray(W["wq"]).T),
        "wkt": tile_w(np.asarray(W["wk"]).T),
        "wvt": tile_w(np.asarray(W["wv"]).T),
        "wot": tile_w(np.asarray(W["wo"]).T),
        "lat": np.ascontiguousarray(lat),
        "lbt": np.ascontiguousarray(lbt),
        "mask4": m4.astype(ml_dtypes.bfloat16),
    }


def mla_host_inputs(xt_core, W):
    import ml_dtypes

    m4 = np.zeros((128, 4, CH), dtype=np.float32)
    for kt in range(4):
        for kl in range(128):
            m4[kl, kt, :] = (np.arange(CH) >= (128 * kt + kl)).astype(np.float32)
    return {
        "x": xt_core,
        "wdkv": tile_w(np.asarray(W["W_DKV"])),
        "wuk": tile_w(np.asarray(W["W_UK"])),
        "wuv": tile_w(np.asarray(W["W_UV"])),
        "wuq": tile_w(np.asarray(W["W_UQ"])),
        "wo": tile_w(np.asarray(W["W_O"])),
        "mask4": m4.astype(ml_dtypes.bfloat16),
    }



DEPTH6 = 6
DI = 2730
DI2 = 5460
MT1 = 22


def l1ffn_drams(nc, pfx="", io=True):
    A = {}
    if io:
        A["x"] = nc.dram_tensor(pfx + "x", [NCH, 128, DT, CH], F32, kind="ExternalInput").ap()
    A["wqkv"] = nc.dram_tensor(pfx + "wqkv", [DEPTH6, 128, DT, 3 * D], F32, kind="ExternalInput").ap()
    A["wod"] = nc.dram_tensor(pfx + "wod", [DEPTH6, 128, DT, D], F32, kind="ExternalInput").ap()
    A["w1d"] = nc.dram_tensor(pfx + "w1d", [DEPTH6, 128, DT, DI2], F32, kind="ExternalInput").ap()
    A["w2da"] = nc.dram_tensor(pfx + "w2da", [DEPTH6, 128, 21, D], F32, kind="ExternalInput").ap()
    A["w2db"] = nc.dram_tensor(pfx + "w2db", [DEPTH6, 42, D], F32, kind="ExternalInput").ap()
    A["qscol_d"] = nc.dram_tensor(pfx + "qscol", [128, DEPTH6, 2], F32, kind="ExternalInput").ap()
    A["kscol_d"] = nc.dram_tensor(pfx + "kscol", [128, DEPTH6, 2], F32, kind="ExternalInput").ap()
    A["lng_d"] = nc.dram_tensor(pfx + "lng", [128, DEPTH6, DT], F32, kind="ExternalInput").ap()
    A["lnb_d"] = nc.dram_tensor(pfx + "lnb", [128, DEPTH6, DT], F32, kind="ExternalInput").ap()
    A["b1a_d"] = nc.dram_tensor(pfx + "b1a", [128, DEPTH6, MT1], F32, kind="ExternalInput").ap()
    A["b1g_d"] = nc.dram_tensor(pfx + "b1g", [128, DEPTH6, MT1], F32, kind="ExternalInput").ap()
    A["b2_d"] = nc.dram_tensor(pfx + "b2c", [128, DEPTH6, DT], F32, kind="ExternalInput").ap()
    A["tabq_d"] = nc.dram_tensor(pfx + "tabq", [128, 2, 2, T], BF16, kind="ExternalInput").ap()
    A["tabk_d"] = nc.dram_tensor(pfx + "tabk", [128, 2, 2, T], BF16, kind="ExternalInput").ap()
    A["hp16_d"] = nc.dram_tensor(pfx + "hp16", [128, 4, 16], BF16, kind="ExternalInput").ap()
    A["hpbc_d"] = nc.dram_tensor(pfx + "hpbc", [16, 4, 128], BF16, kind="ExternalInput").ap()
    A["mdiag_d"] = nc.dram_tensor(pfx + "mdiag", [128, 128], F32, kind="ExternalInput").ap()
    A["mpast_d"] = nc.dram_tensor(pfx + "mpast", [128, 128], BF16, kind="ExternalInput").ap()
    if io:
        A["y"] = nc.dram_tensor(pfx + "y", [NCH, 128, DT, CH], F32, kind="ExternalOutput").ap()
    A["zh"] = [
        nc.dram_tensor(pfx + "zh0", [NCH, 128, DT, CH], F32, kind="Internal").ap(),
        nc.dram_tensor(pfx + "zh1", [NCH, 128, DT, CH], F32, kind="Internal").ap(),
    ]

    return A


def emit_l1ffn(nc, block, ctx, es, A, depth=DEPTH6):
    sb = lambda *a: es.enter_context(nc.sbuf_tensor(*a, allow_name_mangling=True))
    psa = lambda *a: es.enter_context(nc.psum_tensor(*a, allow_name_mangling=True))

    zf = sb("zf", [128, DT, CH], F32)
    zsb = sb("zsb", [128, DT, CH], BF16)
    pcol = sb("pcol", [128, 4, 1], F32)
    sq = sb("sqb", [128, 1, CH], BF16)
    bcb = sb("bcb", [128, 2 * CH], BF16)
    rowm = sb("rowm", [1, CH], F32)
    rowv = sb("rowv", [1, CH], F32)
    rows = sb("rows", [1, CH], F32)
    rowt2 = [rowm, rowv]
    onesb = sb("onesb", [128, 1], BF16)
    eps1 = sb("eps1", [1, 1], F32)
    ones1f = sb("ones1f", [1, 128], F32)
    w1_sb = sb("w1_sb", [128, DT, DI2], BF16)
    w2_sb = sb("w2_sb", [128, MT1, D], BF16)
    w1flat = w1_sb[:].rearrange("p a n -> p (a n)")
    w2flat = w2_sb[:].rearrange("p a n -> p (a n)")
    wqkv_sb = w1flat[:, 0 : DT * 3 * D].rearrange("p (a n) -> p a n", n=3 * D)
    wo_sb = w1flat[:, DT * 3 * D : DT * 4 * D].rearrange("p (a n) -> p a n", n=D)
    kt_ring = w2flat[:, 0 : 2 * DT * CH].rearrange("p (b a n) -> p b a n", b=2, n=CH)
    va_ring = w2flat[:, 2 * DT * CH : 2 * DT * CH + 8 * 1040].rearrange(
        "p (w f) -> p w f", f=1040
    )
    h2_sb = sb("h2_sb", [128, MT1, CH], BF16)
    h2flat = h2_sb[:].rearrange("p a n -> p (a n)")
    rqb = h2flat[:, 0 : DT * CH].rearrange("p (a n) -> p a n", n=CH)
    qraw = h2flat[:, DT * CH : 2 * DT * CH].rearrange("p (a n) -> p a n", n=CH)
    sc0 = h2flat[:, 2 * DT * CH : 2 * DT * CH + CH]
    sc1 = h2flat[:, 2 * DT * CH + CH : 2 * DT * CH + 2 * CH]
    sc2 = h2flat[:, 2 * DT * CH + 2 * CH : 2 * DT * CH + 3 * CH]
    xv = h2flat[:, 0 : 2 * DT * CH].bitcast(F32).rearrange("p (a n) -> p a n", n=CH)
    abuf = sb("abuf", [128, 2, CH], BF16)
    tabs = sb("tabs", [128, 2, 2, CH], BF16)
    gbuf = tabs[:, :, 0, :]
    rsrow = sb("rsrow", [16, CH], BF16)
    ctx_bf = sb("ctx_bf", [128, DT, CH], BF16)
    pT2 = h2flat[:, 2 * DT * CH + 3 * CH :].rearrange(
        "p (s r n) -> p s r n", s=4, r=3
    )
    rbsb = sb("rbsb", [64, 2, CH], BF16)
    ctxu = sb("ctxu", [64, 2, CH], BF16)
    qscol = sb("qscol_t", [128, DEPTH6, 2], F32)
    kscol = sb("kscol_t", [128, DEPTH6, 2], F32)
    lng = sb("lng_t", [128, DEPTH6, DT], F32)
    lnb = sb("lnb_t", [128, DEPTH6, DT], F32)
    b1a = sb("b1a_t", [128, DEPTH6, MT1], F32)
    b1g = sb("b1g_t", [128, DEPTH6, MT1], F32)
    b2c = sb("b2c_t", [128, DEPTH6, DT], F32)
    hp16 = sb("hp16_t", [128, 4, 16], BF16)
    hpbc = sb("hpbc_t", [16, 4, 128], BF16)
    mdiag = sb("mdiag_t", [128, 128], F32)
    mpast = sb("mpast_t", [128, 128], BF16)

    psA = psa("psA", [128, 2 * CH], F32)
    psB = psa("psB", [128, 2 * CH], F32)
    psC = psa("psC", [128, 2 * CH], F32)
    psD = psa("psD", [65, 2 * CH], F32)
    ps_stats = psA[0:1, :]
    ps_nq = psA[0:16, 0:CH]
    ps_rb2 = [psA[0:64, CH : 2 * CH], psA[64:128, 0:CH]]
    psBh = [psB[:, 0:CH], psB[:, CH : 2 * CH]]
    psCh = [psC[:, 0:CH], psC[:, CH : 2 * CH]]
    ps_st2 = [psC[:, 0:384], psC[:, CH : CH + 384],
              psB[:, 0:384], psB[:, CH : CH + 384]]
    psDh = [psD[:, 0:CH], psD[:, CH : 2 * CH]]


    bufs = dict(
        sq=sq, bc=bcb, rowm=rowm, rowv=rowv, rows=rows, onesb=onesb,
        ones1f=ones1f, ps_stats=ps_stats, ps_bc=psB, eps1=eps1,
    )

    const_ld = ctx.res("LD", 176)

    def b(sync):
        for dst, srcd in [
            (qscol, A["qscol_d"]), (kscol, A["kscol_d"]), (lng, A["lng_d"]), (lnb, A["lnb_d"]),
            (b1a, A["b1a_d"]), (b1g, A["b1g_d"]), (b2c, A["b2_d"]), (hp16, A["hp16_d"]),
            (hpbc, A["hpbc_d"]), (mdiag, A["mdiag_d"]), (mpast, A["mpast_d"]),
        ]:
            sync.dma_start(out=dst[:], in_=srcd).then_inc(ctx.sems["LD"], 16)

    ctx.on("sync", b)

    init_ve = ctx.res("VE")

    def b(vector):
        ctx.wait(vector, "LD", const_ld)
        vector.memset(onesb[:], 1.0)
        vector.memset(eps1[:], EPS_LN)
        ctx.fin(vector.memset(ones1f[:], 1.0), "VE")

    ctx.on("vector", b)

    def q_zload(zin, c, first_wait=0, mm_guard=0):
        zf_free = ctx.t("zf_free")
        z_store = ctx.t("z_store")
        z_phase = ctx.t("z_phase")
        ld_tag = ctx.res("LD", 16)
        ctx.tag["zf_ld"] = ld_tag
        cap_pair = ("SE", ctx.t("zcap"))

        def b(sync, c=c, zf_free=zf_free, z_store=z_store,
              first_wait=first_wait, cap_pair=cap_pair):
            ctx.wait(sync, "VE", zf_free)
            ctx.waitp(sync, cap_pair)
            ctx.wait(sync, "ZS", z_store)
            if first_wait:
                ctx.wait(sync, "VE", first_wait)
            sync.dma_start(out=zf[:], in_=zin[c]).then_inc(ctx.sems["LD"], 16)

        ctx.on("sync", b)

        zsb_used = ctx.t("zsb_used")
        ld2 = ctx.res("ZS", 32)

        def b(gpsimd, c=c, z_phase=z_phase, mm_guard=mm_guard,
              zsb_used=zsb_used):
            ctx.wait(gpsimd, "ZS", z_phase)
            ctx.wait(gpsimd, "MM", zsb_used)
            ctx.wait(gpsimd, "MM", mm_guard)
            gpsimd.dma_start(
                out=zsb[:, 0:4, :], in_=zin[c][:, 0:4, :]
            ).then_inc(ctx.sems["ZS"], 16)
            gpsimd.dma_start(
                out=zsb[:, 4:8, 1:CH], in_=zin[c][:, 4:8, 0 : CH - 1]
            ).then_inc(ctx.sems["ZS"], 16)

        ctx.on("gpsimd", b)

        if c == 0:
            zs_ve = ctx.res("VE")

            def b(vector, ld2=ld2):
                ctx.wait(vector, "ZS", ld2)
                ctx.fin(vector.memset(zsb[:, 4:8, 0:1], 0.0), "VE")

            ctx.on("vector", b)
            zs = ("VE", zs_ve)
        else:
            zs_se = ctx.res("SE")

            def b(scalar, ld2=ld2):
                ctx.wait(scalar, "ZS", ld2)
                ctx.fin(
                    scalar.copy(
                        zsb[:, 4:8, 0:1].rearrange("p a n -> p (a n)"),
                        pcol[:].rearrange("p a n -> p (a n)"),
                    ),
                    "SE",
                )

            ctx.on("scalar", b)
            zs = ("SE", zs_se)
        return ld_tag, zs

    def q_zcap(ld_tag):
        cap_se = ctx.res("SE")
        ctx.tag["zcap"] = cap_se

        def b(scalar, ld_tag=ld_tag):
            ctx.wait(scalar, "LD", ld_tag)
            ctx.fin(
                scalar.copy(
                    pcol[:].rearrange("p a n -> p (a n)"),
                    zf[:, 4:8, CH - 1 : CH].rearrange("p a n -> p (a n)"),
                ),
                "SE",
            )

        ctx.on("scalar", b)
        return cap_se

    def q_qk(L, c, which, dst, w_off, ve_need):
        raw_tags = {}
        for ot in range(DT):
            Btag = ctx.tt(f"Bps{ot % 2}")
            mm_tag = ctx.res("MM")

            def b(tensor, ot=ot, ve_need=ve_need, Btag=Btag, w_off=w_off):
                if ot == 0:
                    ctx.waitp(tensor, ve_need)
                ctx.waitp(tensor, Btag)
                for a in range(DT):
                    mm = tensor.matmul(
                        psBh[ot % 2],
                        wqkv_sb[:, a, w_off + ot * 128 : w_off + (ot + 1) * 128],
                        zsb[:, a, :], start=(a == 0), stop=(a == DT - 1),
                    )
                ctx.fin(mm, "MM")

            ctx.on("tensor", b)

            qraw_free = ctx.t("qraw_free")
            i = ctx.res("SE")
            ctx.sett(f"Bps{ot % 2}", "SE", i)
            raw_tags[ot] = i

            def b(scalar, ot=ot, mm_tag=mm_tag, qraw_free=qraw_free):
                ctx.wait(scalar, "MM", mm_tag)
                if ot < 2:
                    ctx.wait(scalar, "VE", qraw_free)
                ctx.fin(scalar.copy(qraw[:, ot, :], psBh[ot % 2]), "SE")

            ctx.on("scalar", b)

        raw_se = ctx.c["SE"]

        sq_mm = {}
        for ot in range(DT):
            sqg = ctx.t("sq_guard")
            abg = [ctx.t("ab_gp_0"), ctx.t("ab_gp_1")]
            se_tag = ctx.res("SE")

            def b(scalar, ot=ot, sqg=sqg, abg=abg):
                if ot < 2:
                    ctx.wait(scalar, "MM", sqg)
                    ctx.wait(scalar, "GP", abg[ot])
                else:
                    ctx.wait(scalar, "MM", sq_mm[ot - 2])
                ctx.fin(
                    scalar.square(abuf[:, ot % 2, :], qraw[:, ot, :]), "SE"
                )

            ctx.on("scalar", b)

            Anq = ctx.t("Anq")
            stats_free = ctx.t("stats_free")
            sq_mm[ot] = ctx.res("MM")

            def b(tensor, ot=ot, se_tag=se_tag, Anq=Anq, stats_free=stats_free):
                ctx.wait(tensor, "SE", se_tag)
                if ot == 0:
                    ctx.wait(tensor, "VE", Anq)
                    ctx.wait(tensor, "VE", stats_free)
                ctx.fin(
                    tensor.matmul(
                        ps_nq, hp16[:, ot % 4, :], abuf[:, ot % 2, :],
                        start=(ot == 0), stop=(ot == DT - 1),
                    ),
                    "MM",
                )

            ctx.on("tensor", b)

        ctx.tag["sq_guard"] = ctx.c["MM"]
        nq_mm = ctx.c["MM"]
        rs_free = ctx.t("rsrow_free")
        nq_ve = ctx.res("VE")
        ctx.tag["Anq"] = nq_ve

        def b(vector, nq_mm=nq_mm, rs_free=rs_free):
            ctx.wait(vector, "MM", nq_mm)
            ctx.wait(vector, "MM", rs_free)
            ctx.fin(
                vector.tensor_scalar(rsrow[:], ps_nq, 1e-24, None, op0=ALU.max),
                "VE",
            )

        ctx.on("vector", b)

        rs_se = ctx.res("SE")

        def b(scalar, nq_ve=nq_ve):
            ctx.wait(scalar, "VE", nq_ve)
            scalar.activation(rsrow[:], rsrow[:], AF.Ln)
            scalar.drain()
            ctx.fin(scalar.activation(rsrow[:], rsrow[:], AF.Exp, scale=-0.5),
                    "SE")

        ctx.on("scalar", b)

        tabs_free = ctx.t("tabs_free")
        tab_ld = ctx.res("LD", 16)

        def b(sync, c=c, which=which, tabs_free=tabs_free):
            ctx.wait(sync, "VE", tabs_free)
            tab = A["tabq_d"] if which == "q" else A["tabk_d"]
            sync.dma_start(
                out=tabs[:], in_=tab[:, :, :, c * CH : (c + 1) * CH]
            ).then_inc(ctx.sems["LD"], 16)

        ctx.on("sync", b)

        scol = qscol if which == "q" else kscol
        rope_done = 0
        rope_slots = [psCh[0], psCh[1], psBh[0], psBh[1]]
        rope_names = ["Cps0", "Cps1", "Bps0", "Bps1"]
        for ot in range(DT):
            g = 0 if ot < 4 else 1
            gp = 1 - g
            pair = ot + 4 if ot < 4 else ot - 4
            sl = ot % 4
            Ctag = ctx.tt(rope_names[sl])
            bc_mm = ctx.res("MM")

            def b(tensor, sl=sl, rs_se=rs_se, Ctag=Ctag):
                ctx.wait(tensor, "SE", rs_se)
                ctx.waitp(tensor, Ctag)
                ctx.fin(
                    tensor.matmul(
                        rope_slots[sl], hpbc[:, sl, :], rsrow[:],
                        start=True, stop=True,
                    ),
                    "MM",
                )

            ctx.on("tensor", b)

            i = ctx.res("VE")
            ctx.sett(rope_names[sl], "VE", i)
            rope_done = i

            def b(vector, L=L, ot=ot, sl=sl, g=g, gp=gp, pair=pair, bc_mm=bc_mm,
                  tab_ld=tab_ld, raw_se=raw_se, which=which):
                ctx.wait(vector, "MM", bc_mm)
                if ot == 0:
                    ctx.wait(vector, "LD", tab_ld)
                    ctx.wait(vector, "SE", raw_se)
                vector.scalar_tensor_tensor(
                    sc0, qraw[:, ot, :], scol[:, L, g : g + 1], tabs[:, g, 0, :],
                    op0=ALU.mult, op1=ALU.mult,
                )
                vector.scalar_tensor_tensor(
                    sc1, qraw[:, pair, :], scol[:, L, gp : gp + 1],
                    tabs[:, g, 1, :], op0=ALU.mult, op1=ALU.mult,
                )
                vector.drain()
                vector.tensor_tensor(sc2, sc0, sc1, op=ALU.add)
                vector.drain()
                ctx.fin(
                    vector.tensor_tensor(
                        dst[:, ot, :], sc2, rope_slots[sl], op=ALU.mult
                    ),
                    "VE",
                )

            ctx.on("vector", b)

        ctx.tag["rsrow_free"] = ctx.c["MM"]
        ctx.tag["qraw_free"] = rope_done
        ctx.tag["tabs_free"] = rope_done
        return rope_done

    for c in range(NCH):
        zf_free = ctx.t("zf_free")
        z_store = ctx.t("z_store")
        ld_tag = ctx.res("LD", 16)

        def b(sync, c=c, zf_free=zf_free, z_store=z_store, init_ve=init_ve):
            if c == 0:
                ctx.wait(sync, "VE", init_ve)
            ctx.wait(sync, "VE", zf_free)
            ctx.wait(sync, "ZS", z_store)
            sync.dma_start(out=zf[:], in_=A["x"][c]).then_inc(ctx.sems["LD"], 16)

        ctx.on("sync", b)

        ve_tag = q_ln_chunk(ctx, bufs, zf[:], zsb[:], "LD", ld_tag)

        z0_ve = ctx.res("VE")
        ctx.tag["zf_free"] = z0_ve

        def b(vector, ve_tag=ve_tag):
            ctx.wait(vector, "VE", ve_tag)
            vector.drain()
            for a in range(DT):
                vector.tensor_tensor(
                    zf[:, a, :], zf[:, a, :], bcb[:, 0:CH], op=ALU.mult
                )
            vector.drain()
            for a in range(DT):
                mm = vector.tensor_tensor(
                    zf[:, a, :], zf[:, a, :], bcb[:, CH : 2 * CH], op=ALU.add
                )
            ctx.fin(mm, "VE")

        ctx.on("vector", b)

        ctx.res("ZS", 16)
        ctx.tag["z_store"] = ctx.c["ZS"]

        def b(gpsimd, c=c, z0_ve=z0_ve):
            ctx.wait(gpsimd, "VE", z0_ve)
            gpsimd.dma_start(out=A["zh"][0][c], in_=zf[:]).then_inc(
                ctx.sems["ZS"], 16
            )

        ctx.on("gpsimd", b)
        ctx.flush()

    for L in range(depth):
        for phase in ("attn", "ffn"):
            step = 2 * L + (0 if phase == "attn" else 1)
            zin = A["zh"][step % 2]
            zout = A["zh"][(step + 1) % 2]
            ctx.tag["sq_guard"] = ctx.c["MM"]
            phase_mm = ctx.c["MM"]
            phase_se = ctx.c["SE"]
            phase_ve = ctx.c["VE"]
            phase_gp = ctx.c["GP"]

            ctx.tag["z_phase"] = ctx.c["ZS"]
            w_tag = ctx.res("LW", 32 if phase == "attn" else 48)

            phase_ld = ctx.c["LD"]

            def b(gpsimd, L=L, phase=phase, phase_mm=phase_mm,
                  phase_se=phase_se, phase_ve=phase_ve, phase_ld=phase_ld):
                ctx.wait(gpsimd, "MM", phase_mm)
                ctx.wait(gpsimd, "SE", phase_se)
                ctx.wait(gpsimd, "VE", phase_ve)
                ctx.wait(gpsimd, "LD", phase_ld)
                if phase == "attn":
                    gpsimd.dma_start(out=wqkv_sb, in_=A["wqkv"][L]).then_inc(
                        ctx.sems["LW"], 16
                    )
                    gpsimd.dma_start(out=wo_sb, in_=A["wod"][L]).then_inc(
                        ctx.sems["LW"], 16
                    )
                else:
                    gpsimd.dma_start(out=w1_sb[:], in_=A["w1d"][L]).then_inc(
                        ctx.sems["LW"], 16
                    )
                    gpsimd.dma_start(out=w2_sb[:, 0:21, :], in_=A["w2da"][L]).then_inc(
                        ctx.sems["LW"], 16
                    )
                    gpsimd.dma_start(out=w2_sb[0:42, 21, :], in_=A["w2db"][L]).then_inc(
                        ctx.sems["LW"], 16
                    )

            ctx.on("gpsimd", b)

            if phase == "attn":
                for c in range(NCH):
                    s_prev = ctx.t(f"s_done_{c-1}")
                    ld_tag, zs_ve = q_zload(zin, c, mm_guard=s_prev)

                    if c == 0:
                        def b(tensor, w_tag=w_tag):
                            ctx.wait(tensor, "LW", w_tag)
                        ctx.on("tensor", b)

                    q_qk(L, c, "k", kt_ring[:, c % 2, :, :], D, zs_ve)
                    q_ve = q_qk(L, c, "q", rqb, 0, zs_ve)

                    if c == 0 and L == 0:
                        ones_ve = ctx.res("VE")

                        def b(vector, phase_mm=phase_mm):
                            ctx.wait(vector, "MM", phase_mm)
                            ctx.fin(
                                vector.memset(
                                    va_ring.rearrange(
                                        "p w (h e) -> p w h e", e=65
                                    )[:, :, :, 64:65],
                                    1.0,
                                ),
                                "VE",
                            )

                        ctx.on("vector", b)
                    elif c == 0:
                        ones_ve = ctx.res("VE")
                        lw_now = ctx.c["LW"]

                        def b(vector, lw_now=lw_now):
                            ctx.wait(vector, "LW", lw_now)
                            ctx.fin(
                                vector.memset(
                                    va_ring.rearrange(
                                        "p w (h e) -> p w h e", e=65
                                    )[:, :, :, 64:65],
                                    1.0,
                                ),
                                "VE",
                            )

                        ctx.on("vector", b)
                    else:
                        ones_ve = 0

                    v_prev = 0
                    for tt4 in range(4):
                        Cp0 = ctx.tt("Cps0") if tt4 == 0 else None
                        Cp1 = ctx.tt("Cps1") if tt4 == 0 else None
                        vp = v_prev
                        v_mm = ctx.res("MM")

                        def b(tensor, c=c, tt4=tt4, zs_ve=zs_ve,
                              ones_ve=ones_ve, Cp0=Cp0, Cp1=Cp1, vp=vp):
                            if tt4 == 0:
                                ctx.waitp(tensor, zs_ve)
                                ctx.wait(tensor, "VE", ones_ve)
                                ctx.waitp(tensor, Cp0)
                                ctx.waitp(tensor, Cp1)
                            else:
                                ctx.wait(tensor, "VE", vp)
                            for half in range(2):
                                for a in range(DT):
                                    mm = tensor.matmul(
                                        psC[:, half * CH : (half + 1) * CH],
                                        zsb[:, a, tt4 * 128 : (tt4 + 1) * 128],
                                        wqkv_sb[
                                            :, a,
                                            2 * D + half * CH
                                            : 2 * D + (half + 1) * CH,
                                        ],
                                        start=(a == 0), stop=(a == DT - 1),
                                    )
                            ctx.fin(mm, "MM")

                        ctx.on("tensor", b)

                        i = ctx.res("VE")
                        v_prev = i
                        if tt4 == 3:
                            ctx.sett("Cps0", "VE", i)
                            ctx.sett("Cps1", "VE", i)

                        def b(vector, c=c, tt4=tt4, v_mm=v_mm):
                            ctx.wait(vector, "MM", v_mm)
                            ctx.fin(
                                vector.tensor_copy(
                                    va_ring.rearrange(
                                        "p (b w) f -> p b w f", b=2
                                    )[:, c % 2, tt4, :].rearrange(
                                        "p (h e) -> p h e", e=65
                                    )[:, :, 0:64],
                                    psC[:].rearrange("p (h e) -> p h e", e=64),
                                ),
                                "VE",
                            )

                        ctx.on("vector", b)

                    kvv_ve = ctx.c["VE"]
                    ctx.tag["zsb_used"] = ctx.c["MM"]

                    units = []
                    for hi in range(4):
                        for pr in ((0, 2), (1, 3)):
                            for j in range(4):
                                for hm in pr:
                                    h = 4 * hi + hm
                                    units.append(
                                        (h, j, [4 * c + j - 2 + r for r in range(3)])
                                    )
                    dslot = lambda h: (h % 4) // 2
                    st_free0 = [ctx.t("st_free_se0"), ctx.t("st_free_se1"),
                                0, 0]
                    bps_first = [ctx.tt("Bps0"), ctx.tt("Bps1")]
                    rb_prev0 = [ctx.t("rb_se0"), ctx.t("rb_se1")]
                    D_prev = [ctx.t("D0se"), ctx.t("D1se")]
                    wo_done = ctx.t("wo_done")
                    cm_prev0 = [ctx.t("gp_cm0"), ctx.t("gp_cm1")]
                    mm_s = {}
                    mm_pv = {}
                    mm_rb = {}
                    ve_dm = {}
                    se_exp = {}
                    se_r = {}
                    se_cu = {}
                    gp_pm = {}
                    gp_cm = {}
                    lhseq = [u[0] for u in units if u[1] == 3]
                    hslot = {h: i % 2 for i, h in enumerate(lhseq)}
                    cu_after = {}
                    _pend = []
                    for ui, (h, j, gk) in enumerate(units):
                        if _pend:
                            cu_after[ui] = _pend.pop(0)
                        if j == 3:
                            _pend.append(h)
                    tail_cu = list(_pend)
                    for ui, (h, j, gk) in enumerate(units):
                        ve_dm[(h, j)] = ctx.res("VE")
                        se_exp[(h, j)] = ctx.res("SE")
                        if gk[0] >= 0:
                            gp_pm[(h, j)] = ctx.res("GP")
                        if j == 3:
                            se_r[h] = ctx.res("SE")
                        if ui in cu_after:
                            se_cu[cu_after[ui]] = ctx.res("SE")
                            gp_cm[cu_after[ui]] = ctx.res("GP")
                    for h in tail_cu:
                        se_cu[h] = ctx.res("SE")
                        gp_cm[h] = ctx.res("GP")
                    LAG = 3
                    sched = []
                    rbq = []
                    nu = len(units)
                    for ui in range(nu + LAG):
                        if ui < nu:
                            sched.append(("S", ui))
                        if ui >= LAG:
                            if rbq:
                                sched.append(("RB", rbq.pop(0)))
                            sched.append(("PV", ui - LAG))
                            if units[ui - LAG][1] == 3:
                                rbq.append(units[ui - LAG][0])
                    for h in rbq:
                        sched.append(("RB", h))
                    for kind, v in sched:
                        if kind == "S":
                            mm_s[units[v][:2]] = ctx.res("MM")
                        elif kind == "PV":
                            mm_pv[units[v][:2]] = ctx.res("MM")
                        else:
                            mm_rb[v] = ctx.res("MM")
                    uex = lambda k: se_exp[(units[k][0], units[k][1])]
                    ctx.tag["st_free_se0"] = uex(nu - 4)
                    ctx.tag["st_free_se1"] = uex(nu - 3)
                    ctx.sett("Cps0", "SE", uex(nu - 4))
                    ctx.sett("Cps1", "SE", uex(nu - 3))
                    ctx.sett("Bps0", "SE", uex(nu - 2))
                    ctx.sett("Bps1", "SE", uex(nu - 1))
                    dlast = {}
                    for h in lhseq:
                        dlast[dslot(h)] = h
                    ctx.tag["rb_se"] = se_cu[lhseq[-1]]
                    ctx.tag["rb_se0"] = se_cu[lhseq[-2]]
                    ctx.tag["rb_se1"] = se_cu[lhseq[-1]]
                    ctx.tag["D0se"] = se_cu[dlast[0]]
                    ctx.tag["D1se"] = se_cu[dlast[1]]
                    ctx.tag["gp_cm0"] = gp_cm[lhseq[-2]]
                    ctx.tag["gp_cm1"] = gp_cm[lhseq[-1]]
                    attn_gp = max(gp_cm.values())

                    def b(tensor, c=c, units=units, kvv_ve=kvv_ve, q_ve=q_ve,
                          mm_s=mm_s, mm_pv=mm_pv, se_exp=se_exp, gp_pm=gp_pm,
                          se_cu=se_cu, se_r=se_r, mm_rb=mm_rb,
                          st_free0=st_free0, bps_first=bps_first,
                          rb_prev0=rb_prev0, D_prev=D_prev,
                          dslot=dslot, hslot=hslot, sched=sched):
                        prev_exp = list(st_free0)
                        prev_cu = {0: D_prev[0], 1: D_prev[1]}
                        prev_rb = list(rb_prev0)

                        def emit_S(ui):
                            h, j, gk = units[ui]
                            b0 = 32 * (h % 4)
                            tg0, tg1 = h // 4, 4 + h // 4
                            slot = ui % 4
                            if ui in (2, 3):
                                ctx.waitp(tensor, bps_first[ui - 2])
                            ctx.wait(tensor, "SE", prev_exp[slot])
                            for r in range(3):
                                g = gk[r]
                                if g < 0:
                                    continue
                                kc, ktl = g // 4, g % 4
                                for gg, tgx in ((0, tg0), (1, tg1)):
                                    mm = tensor.matmul(
                                        ps_st2[slot][:, r * 128 : (r + 1) * 128],
                                        kt_ring[
                                            b0 : b0 + 32, kc % 2, tgx,
                                            ktl * 128 : (ktl + 1) * 128,
                                        ],
                                        rqb[
                                            b0 : b0 + 32, tgx,
                                            j * 128 : (j + 1) * 128,
                                        ],
                                        start=(gg == 0), stop=(gg == 1),
                                        tile_position=(b0, 0),
                                    )
                            ctx.fin(mm, "MM")
                            prev_exp[slot] = se_exp[(h, j)]

                        def emit_PV(ui):
                            h, j, gk = units[ui]
                            slot = ui % 4
                            ds = dslot(h)
                            if gk[0] >= 0:
                                ctx.wait(tensor, "GP", gp_pm[(h, j)])
                            else:
                                ctx.wait(tensor, "SE", se_exp[(h, j)])
                            if j == 0:
                                ctx.wait(tensor, "SE", prev_cu[ds])
                            first = True
                            for r in range(3):
                                g = gk[r]
                                if g < 0:
                                    continue
                                kc, ktl = g // 4, g % 4
                                mm = tensor.matmul(
                                    psDh[ds][:, j * 128 : (j + 1) * 128],
                                    va_ring.rearrange(
                                        "p (b w) f -> p b w f", b=2
                                    )[:, kc % 2, ktl, h * 65 : (h + 1) * 65],
                                    pT2[:, slot, r, :],
                                    start=first, stop=(r == 2),
                                )
                                first = False
                            ctx.fin(mm, "MM")
                            if j == 3:
                                prev_cu[ds] = se_cu[h]

                        def emit_RB(h):
                            hs = hslot[h]
                            ctx.wait(tensor, "SE", se_r[h])
                            ctx.wait(tensor, "SE", prev_rb[hs])
                            prev_rb[hs] = se_cu[h]
                            ctx.fin(
                                tensor.matmul(
                                    ps_rb2[hs], ones1f[:, 0:64],
                                    rowt2[hs][:],
                                    start=True, stop=True,
                                ),
                                "MM",
                            )

                        ctx.wait(tensor, "VE", kvv_ve)
                        ctx.wait(tensor, "VE", q_ve)
                        for kind, v in sched:
                            if kind == "S":
                                emit_S(v)
                            elif kind == "PV":
                                emit_PV(v)
                            else:
                                emit_RB(v)

                    ctx.on("tensor", b)

                    def b(scalar, units=units, ve_dm=ve_dm, mm_pv=mm_pv,
                          mm_rb=mm_rb, gp_cm=gp_cm, cm_prev0=cm_prev0,
                          dslot=dslot, hslot=hslot, cu_after=cu_after,
                          tail_cu=tail_cu):
                        pv_hist = [0, 0, 0, 0]
                        prev_cm = list(cm_prev0)

                        def emit_cu(h):
                            ds = dslot(h)
                            hs = hslot[h]
                            ctx.wait(scalar, "MM", mm_rb[h])
                            ctx.wait(scalar, "GP", prev_cm[hs])
                            prev_cm[hs] = gp_cm[h]
                            scalar.copy(rbsb[:, hs, :], ps_rb2[hs])
                            ctx.fin(
                                scalar.copy(
                                    ctxu[:, hs, :], psDh[ds][0:64, :]
                                ),
                                "SE",
                            )

                        for ui, (h, j, gk) in enumerate(units):
                            slot = ui % 4
                            ctx.wait(scalar, "VE", ve_dm[(h, j)])
                            ctx.wait(scalar, "MM", pv_hist[slot])
                            ctx.fin(
                                scalar.activation(
                                    pT2[:, slot, :, :].rearrange(
                                        "p a n -> p (a n)"
                                    ),
                                    ps_st2[slot][:], AF.Exp, scale=8.0,
                                ),
                                "SE",
                            )
                            pv_hist[slot] = mm_pv[(h, j)]
                            if j == 3:
                                ds = dslot(h)
                                hs = hslot[h]
                                ctx.wait(scalar, "MM", mm_pv[(h, 3)])
                                scalar.activation(
                                    rowt2[hs][:], psDh[ds][64:65, :], AF.Ln
                                )
                                scalar.drain()
                                ctx.fin(
                                    scalar.activation(
                                        rowt2[hs][:], rowt2[hs][:],
                                        AF.Exp, scale=-1.0
                                    ),
                                    "SE",
                                )
                            if ui in cu_after:
                                emit_cu(cu_after[ui])
                        for h in tail_cu:
                            emit_cu(h)

                    ctx.on("scalar", b)

                    def b(vector, units=units, mm_s=mm_s):
                        for ui, (h, j, gk) in enumerate(units):
                            slot = ui % 4
                            ctx.wait(vector, "MM", mm_s[(h, j)])
                            ctx.fin(
                                vector.tensor_tensor(
                                    ps_st2[slot][:, 256:384],
                                    ps_st2[slot][:, 256:384],
                                    mdiag[:], op=ALU.add,
                                ),
                                "VE",
                            )

                    ctx.on("vector", b)

                    def b(gpsimd, units=units, se_exp=se_exp, se_cu=se_cu,
                          gp_pm=gp_pm, wo_done=wo_done, hslot=hslot,
                          cu_after=cu_after, tail_cu=tail_cu):
                        first = [True]

                        def emit_cm(h):
                            hs = hslot[h]
                            ctx.wait(gpsimd, "SE", se_cu[h])
                            if first[0]:
                                ctx.wait(gpsimd, "MM", wo_done)
                                first[0] = False
                            ctx.fin(
                                gpsimd.tensor_mul(
                                    ctx_bf[
                                        64 * (h % 2) : 64 * (h % 2) + 64,
                                        h // 2, :,
                                    ],
                                    ctxu[:, hs, :], rbsb[:, hs, :],
                                ),
                                "GP",
                            )

                        for ui, (h, j, gk) in enumerate(units):
                            slot = ui % 4
                            if gk[0] >= 0:
                                ctx.wait(gpsimd, "SE", se_exp[(h, j)])
                                ctx.fin(
                                    gpsimd.tensor_mul(
                                        pT2[:, slot, 0, :],
                                        pT2[:, slot, 0, :], mpast[:],
                                    ),
                                    "GP",
                                )
                            if ui in cu_after:
                                emit_cm(cu_after[ui])
                        for h in tail_cu:
                            emit_cm(h)

                    ctx.on("gpsimd", b)

                    cap_se = q_zcap(ld_tag)
                    for ot in range(DT):
                        Bp = ctx.tt(f"Bps{ot % 2}")
                        o_mm = ctx.res("MM")
                        if ot == DT - 1:
                            ctx.tag["wo_done"] = o_mm

                        def b(tensor, ot=ot, attn_gp=attn_gp, Bp=Bp):
                            if ot == 0:
                                ctx.wait(tensor, "GP", attn_gp)
                            ctx.waitp(tensor, Bp)
                            for a in range(DT):
                                mm = tensor.matmul(
                                    psBh[ot % 2],
                                    wo_sb[:, a, ot * 128 : (ot + 1) * 128],
                                    ctx_bf[:, a, :],
                                    start=(a == 0), stop=(a == DT - 1),
                                )
                            ctx.fin(mm, "MM")

                        ctx.on("tensor", b)

                        i = ctx.res("VE")
                        ctx.sett(f"Bps{ot % 2}", "VE", i)
                        if ot == DT - 1:
                            ctx.tag["zf_free"] = i

                        def b(vector, ot=ot, o_mm=o_mm, cap_se=cap_se):
                            ctx.wait(vector, "MM", o_mm)
                            if ot == 0:
                                ctx.wait(vector, "SE", cap_se)
                            ctx.fin(
                                vector.tensor_tensor(
                                    zf[:, ot, :], zf[:, ot, :], psBh[ot % 2],
                                    op=ALU.add,
                                ),
                                "VE",
                            )

                        ctx.on("vector", b)

                    zn_ve = ctx.c["VE"]
                    ctx.res("ZS", 16)
                    ctx.tag["z_store"] = ctx.c["ZS"]
                    ctx.tag[f"s_done_{c}"] = ctx.c["MM"]

                    def b(gpsimd, c=c, zn_ve=zn_ve, zout=zout):
                        ctx.wait(gpsimd, "VE", zn_ve)
                        gpsimd.dma_start(out=zout[c], in_=zf[:]).then_inc(
                            ctx.sems["ZS"], 16
                        )

                    ctx.on("gpsimd", b)
                    ctx.flush()

            else:
                for c in range(NCH):
                    ld_tag, zs_ve = q_zload(zin, c)

                    Bp0 = ctx.tt("Bps0")
                    Bp1 = ctx.tt("Bps1")
                    Anq = ctx.t("Anq")
                    rb_se = ctx.t("rb_se")

                    def extra_bc(tensor, Bp0=Bp0, Bp1=Bp1):
                        ctx.waitp(tensor, Bp0)
                        ctx.waitp(tensor, Bp1)

                    def extra_stats(tensor, Anq=Anq, rb_se=rb_se):
                        ctx.wait(tensor, "VE", Anq)
                        ctx.wait(tensor, "SE", rb_se)

                    ve_tag = q_ln_chunk(
                        ctx, dict(bufs, sq=ctx_bf[:, 0:4, :], nsq=4),
                        None, zsb[:], "LD", ld_tag,
                        ready_ve=zs_ve, extra_bc=extra_bc,
                        extra_stats=extra_stats,
                    )

                    xn_ve = ctx.res("VE")

                    def b(vector, L=L, ve_tag=ve_tag):
                        ctx.wait(vector, "VE", ve_tag)
                        vector.drain()
                        for a in range(DT):
                            mm = vector.tensor_scalar(
                                zsb[:, a, :], zsb[:, a, :],
                                lng[:, L, a : a + 1], lnb[:, L, a : a + 1],
                                op0=ALU.mult, op1=ALU.add,
                            )
                        ctx.fin(mm, "VE")

                    ctx.on("vector", b)

                    hslots = [psBh[0], psBh[1], psCh[0], psCh[1]]
                    hnames = ["Bps0", "Bps1", "Cps0", "Cps1"]
                    for m in range(MT1):
                        rows_n = 128 if m < MT1 - 1 else 42
                        for part in range(2):
                            coff = part * DI + m * 128
                            sl = (2 * m + part) % 4
                            Bp = ctx.tt(hnames[sl])
                            hg_mm = ctx.res("MM")

                            def b(tensor, c=c, m=m, part=part, coff=coff,
                                  rows_n=rows_n, xn_ve=xn_ve, Bp=Bp, sl=sl,
                                  w_tag=w_tag):
                                if m == 0 and part == 0:
                                    ctx.wait(tensor, "VE", xn_ve)
                                    if c == 0:
                                        ctx.wait(tensor, "LW", w_tag)
                                ctx.waitp(tensor, Bp)
                                for a in range(DT):
                                    mm = tensor.matmul(
                                        hslots[sl][0:rows_n, :],
                                        w1_sb[:, a, coff : coff + rows_n],
                                        zsb[:, a, :],
                                        start=(a == 0), stop=(a == DT - 1),
                                    )
                                ctx.fin(mm, "MM")

                            ctx.on("tensor", b)

                            if part == 0:
                                ab_gp = ctx.t(f"ab_gp_{m % 2}")
                                i = ctx.res("VE")
                                ctx.sett(hnames[sl], "VE", i)

                                def b(vector, L=L, m=m, rows_n=rows_n,
                                      hg_mm=hg_mm, ab_gp=ab_gp, sl=sl):
                                    ctx.wait(vector, "MM", hg_mm)
                                    ctx.wait(vector, "GP", ab_gp)
                                    ctx.fin(
                                        vector.tensor_scalar(
                                            abuf[0:rows_n, m % 2, :],
                                            hslots[sl][0:rows_n, :],
                                            b1a[0:rows_n, L, m : m + 1], None,
                                            op0=ALU.add,
                                        ),
                                        "VE",
                                    )

                                ctx.on("vector", b)
                            else:
                                gb_gp = ctx.t(f"ab_gp_{m % 2}")
                                i = ctx.res("SE")
                                ctx.sett(hnames[sl], "SE", i)

                                def b(scalar, L=L, m=m, rows_n=rows_n,
                                      hg_mm=hg_mm, gb_gp=gb_gp, sl=sl):
                                    ctx.wait(scalar, "MM", hg_mm)
                                    ctx.wait(scalar, "GP", gb_gp)
                                    ctx.fin(
                                        scalar.activation(
                                            gbuf[0:rows_n, m % 2, :],
                                            hslots[sl][0:rows_n, :],
                                            AF.Gelu,
                                            bias=b1g[0:rows_n, L, m : m + 1],
                                        ),
                                        "SE",
                                    )

                                ctx.on("scalar", b)

                        ab_ve = ctx.c["VE"]
                        g_se = ctx.c["SE"]
                        h2_used = ctx.t("h2_used")
                        i = ctx.res("GP")
                        ctx.tag[f"ab_gp_{m % 2}"] = i

                        def b(gpsimd, m=m, rows_n=rows_n, ab_ve=ab_ve,
                              g_se=g_se, h2_used=h2_used):
                            ctx.wait(gpsimd, "VE", ab_ve)
                            ctx.wait(gpsimd, "SE", g_se)
                            ctx.wait(gpsimd, "MM", h2_used)
                            ctx.fin(
                                gpsimd.tensor_mul(
                                    h2_sb[0:rows_n, m, :],
                                    abuf[0:rows_n, m % 2, :],
                                    gbuf[0:rows_n, m % 2, :],
                                ),
                                "GP",
                            )

                        ctx.on("gpsimd", b)

                    h2_gp = ctx.c["GP"]
                    ctx.tag["zsb_used"] = ctx.c["MM"]

                    wslots = [psCh[0], psCh[1], psBh[0], psBh[1]]
                    wnames = ["Cps0", "Cps1", "Bps0", "Bps1"]
                    cap_se = q_zcap(ld_tag)
                    for ot in range(DT):
                        sl = ot % 4
                        Cp = ctx.tt(wnames[sl])
                        o_mm = ctx.res("MM")
                        if ot == DT - 1:
                            ctx.tag["h2_used"] = o_mm

                        def b(tensor, ot=ot, sl=sl, h2_gp=h2_gp, Cp=Cp):
                            if ot == 0:
                                ctx.wait(tensor, "GP", h2_gp)
                            ctx.waitp(tensor, Cp)
                            for m in range(MT1):
                                rows_n = 128 if m < MT1 - 1 else 42
                                mm = tensor.matmul(
                                    wslots[sl],
                                    w2_sb[0:rows_n, m, ot * 128 : (ot + 1) * 128],
                                    h2_sb[0:rows_n, m, :],
                                    start=(m == 0), stop=(m == MT1 - 1),
                                )
                            ctx.fin(mm, "MM")

                        ctx.on("tensor", b)

                        i = ctx.res("VE")
                        ctx.sett(wnames[sl], "VE", i)
                        if ot == DT - 1:
                            ctx.tag["zf_free"] = i

                        def b(vector, L=L, ot=ot, sl=sl, o_mm=o_mm,
                              cap_se=cap_se):
                            ctx.wait(vector, "MM", o_mm)
                            if ot == 0:
                                ctx.wait(vector, "SE", cap_se)
                            ctx.fin(
                                vector.scalar_tensor_tensor(
                                    zf[:, ot, :], wslots[sl],
                                    b2c[:, L, ot : ot + 1], zf[:, ot, :],
                                    op0=ALU.add, op1=ALU.add,
                                ),
                                "VE",
                            )

                        ctx.on("vector", b)

                    zn_ve = ctx.c["VE"]
                    ctx.res("ZS", 16)
                    ctx.tag["z_store"] = ctx.c["ZS"]

                    def b(gpsimd, c=c, zn_ve=zn_ve, zout=zout):
                        ctx.wait(gpsimd, "VE", zn_ve)
                        gpsimd.dma_start(out=zout[c], in_=zf[:]).then_inc(
                            ctx.sems["ZS"], 16
                        )

                    ctx.on("gpsimd", b)
                    ctx.flush()

    for c in range(NCH):
        zf_free = ctx.t("zf_free")
        z_store = ctx.t("z_store")
        xv_guard_mm = ctx.c["MM"]
        xv_guard_gp = ctx.c["GP"]
        xv_free = ctx.t("xv_free")
        ld_tag = ctx.res("LD", 32)

        def b(sync, c=c, zf_free=zf_free, z_store=z_store,
              xv_guard_mm=xv_guard_mm, xv_guard_gp=xv_guard_gp,
              xv_free=xv_free):
            ctx.wait(sync, "VE", zf_free)
            ctx.wait(sync, "ZS", z_store)
            sync.dma_start(out=zf[:], in_=A["zh"][0][c]).then_inc(ctx.sems["LD"], 16)
            if c == 0:
                ctx.wait(sync, "MM", xv_guard_mm)
                ctx.wait(sync, "GP", xv_guard_gp)
            else:
                ctx.wait(sync, "VE", xv_free)
            sync.dma_start(out=xv, in_=A["x"][c]).then_inc(ctx.sems["LD"], 16)

        ctx.on("sync", b)

        st_used = ctx.c["ST"]
        out_ve = ctx.res("VE")
        ctx.tag["zf_free"] = out_ve
        ctx.tag["xv_free"] = out_ve

        def b(vector, c=c, ld_tag=ld_tag, st_used=st_used):
            ctx.wait(vector, "LD", ld_tag)
            if c > 0:
                ctx.wait(vector, "ST", st_used)
            vector.drain()
            for ot in range(DT):
                mm = vector.scalar_tensor_tensor(
                    zf[:, ot, :], zf[:, ot, :], 0.5, xv[:, ot, :],
                    op0=ALU.mult, op1=ALU.add,
                )
            ctx.fin(mm, "VE")

        ctx.on("vector", b)

        ctx.res("ST", 16)

        def b(sync, c=c, out_ve=out_ve):
            ctx.wait(sync, "VE", out_ve)
            sync.dma_start(out=A["y"][c], in_=zf[:]).then_inc(ctx.sems["ST"], 16)

        ctx.on("sync", b)
        ctx.flush()

    st_final = ctx.c["ST"]

    def b(sync):
        ctx.wait(sync, "ST", st_final)

    ctx.on("sync", b)
    ctx.flush()


def build_l1ffn(depth=DEPTH6):
    nc = bass.Bass("TRN2", target_bir_lowering=False)
    A = l1ffn_drams(nc)
    es = contextlib.ExitStack()
    with es, nc.Block() as block:
        ctx = Ctx(nc, es, block)
        emit_l1ffn(nc, block, ctx, es, A, depth=depth)
    return nc


def l1ffn_host_inputs(xt_core, W):
    import ml_dtypes

    DIM_HEAD = 64
    WINDOW = 256
    perm = np.zeros(D, dtype=np.int64)
    for n in range(D):
        i, p = n // 128, n % 128
        g = i // 4
        h = 4 * (i % 4) + p // 32
        d = 32 * g + (p % 32)
        perm[n] = h * 64 + d
    Wqkv = np.asarray(W["Wqkv"])
    wqkv_p = Wqkv.copy()
    wqkv_p[:, :, 0:D] = Wqkv[:, :, 0:D][:, :, perm]
    wqkv_p[:, :, D : 2 * D] = Wqkv[:, :, D : 2 * D][:, :, perm]

    qscol = np.zeros((128, DEPTH6, 2), dtype=np.float32)
    kscol = np.zeros((128, DEPTH6, 2), dtype=np.float32)
    qs = np.asarray(W["q_scale"])
    ks = np.asarray(W["k_scale"])
    for p in range(128):
        for g in range(2):
            qscol[p, :, g] = qs[:, 32 * g + p % 32]
            kscol[p, :, g] = ks[:, 32 * g + p % 32]

    half = np.arange(0, DIM_HEAD, 2, dtype=np.float64)
    inv_freq = 1.0 / 10000.0 ** (half / DIM_HEAD)
    t = np.arange(T, dtype=np.float64)
    freqs = t[:, None] * inv_freq[None, :]
    base_scale = (half + 0.4 * DIM_HEAD) / (1.4 * DIM_HEAD)
    power = (t - T // 2) / (WINDOW // 2)
    scale = base_scale[None, :] ** power[:, None]
    cos_t = np.cos(freqs)
    sin_t = np.sin(freqs)
    tabq = np.zeros((128, 2, 2, T), dtype=np.float64)
    tabk = np.zeros((128, 2, 2, T), dtype=np.float64)
    for p in range(128):
        dp = p % 32
        for g in range(2):
            sgn = -1.0 if g == 0 else 1.0
            tabq[p, g, 0] = cos_t[:, dp] * scale[:, dp]
            tabq[p, g, 1] = sgn * sin_t[:, dp] * scale[:, dp]
            tabk[p, g, 0] = cos_t[:, dp] / scale[:, dp]
            tabk[p, g, 1] = sgn * sin_t[:, dp] / scale[:, dp]

    hp16 = np.zeros((128, 4, 16), dtype=np.float32)
    hpbc = np.zeros((16, 4, 128), dtype=np.float32)
    for p in range(128):
        for v in range(4):
            h = 4 * v + p // 32
            hp16[p, v, h] = 1.0
            hpbc[h, v, p] = 1.0

    mdiag = np.zeros((128, 128), dtype=np.float32)
    mpast = np.zeros((128, 128), dtype=np.float32)
    for kl in range(128):
        mdiag[kl, :] = np.where(kl > np.arange(128), -1e30, 0.0)
        mpast[kl, :] = (kl >= np.arange(128)).astype(np.float32)

    def padb1(bpart):
        out = np.zeros((DEPTH6, MT1 * 128), dtype=np.float32)
        out[:, :DI] = bpart
        return np.ascontiguousarray(out.reshape(DEPTH6, MT1, 128).transpose(2, 0, 1))

    def tile_wd(w):
        d0, K, N = w.shape
        return np.ascontiguousarray(
            w.reshape(d0, K // 128, 128, N).transpose(0, 2, 1, 3)
        )

    bf = lambda a: np.ascontiguousarray(a).astype(ml_dtypes.bfloat16)
    b1 = np.asarray(W["b1"])
    W2 = np.asarray(W["W2"])
    return {
        "x": xt_core,
        "wqkv": tile_wd(wqkv_p),
        "wod": tile_wd(np.asarray(W["Wo"])),
        "w1d": tile_wd(np.asarray(W["W1"])),
        "w2da": tile_wd(W2[:, 0 : 21 * 128, :]),
        "w2db": np.ascontiguousarray(W2[:, 21 * 128 :, :]),
        "qscol": qscol,
        "kscol": kscol,
        "lng": np.ascontiguousarray(
            np.asarray(W["ln_g"]).reshape(DEPTH6, DT, 128).transpose(2, 0, 1)
        ),
        "lnb": np.ascontiguousarray(
            np.asarray(W["ln_b"]).reshape(DEPTH6, DT, 128).transpose(2, 0, 1)
        ),
        "b1a": padb1(b1[:, :DI]),
        "b1g": padb1(b1[:, DI:]),
        "b2c": np.ascontiguousarray(
            np.asarray(W["b2"]).reshape(DEPTH6, DT, 128).transpose(2, 0, 1)
        ),
        "tabq": bf(tabq),
        "tabk": bf(tabk),
        "hp16": bf(hp16),
        "hpbc": bf(hpbc),
        "mdiag": mdiag,
        "mpast": bf(mpast),
    }









def build_fused(nstages=6):
    nc = bass.Bass("TRN2", target_bir_lowering=False)
    Am = moe_drams(nc, "moe_", io=False)
    Aa = mla_drams(nc, "mla_", io=False)
    Al = lora_drams(nc, "lora_", io=False)
    Af = l1ffn_drams(nc, "l1_", io=False)
    xin = nc.dram_tensor("x", [NCH, 128, DT, CH], F32, kind="ExternalInput").ap()
    yout = nc.dram_tensor("y", [NCH, 128, DT, CH], F32, kind="ExternalOutput").ap()
    xh = [
        nc.dram_tensor("xh0", [NCH, 128, DT, CH], F32, kind="Internal").ap(),
        nc.dram_tensor("xh1", [NCH, 128, DT, CH], F32, kind="Internal").ap(),
    ]
    es = contextlib.ExitStack()
    with es, nc.Block() as block:
        ctx = Ctx(nc, es, block)
        seq = [
            (emit_moe, Am, {}),
            (emit_mla, Aa, {}),
            (emit_moe, Am, {}),
            (emit_l1ffn, Af, {}),
            (emit_lora, Al, {}),
            (emit_l1ffn, Af, {}),
        ]
        seq = seq[:nstages]
        cur = xin
        for i, (fn, A, kw) in enumerate(seq):
            out = yout if i == len(seq) - 1 else xh[i % 2]
            A2 = dict(A)
            A2["x"] = cur
            A2["y"] = out
            with contextlib.ExitStack() as es_stage:
                fn(nc, block, ctx, es_stage, A2, **kw)
            if i < len(seq) - 1:
                ctx.barrier()
            cur = out
    return nc


def fused_host_maps(xt, moe_W, mla_W, lora_W, l1_W, cores):
    maps = []
    for b in cores:
        m = {}
        for pfx, hm in [
            ("moe_", moe_host_inputs(xt[b], moe_W)),
            ("mla_", mla_host_inputs(xt[b], mla_W)),
            ("lora_", lora_host_inputs(xt[b], lora_W)),
            ("l1_", l1ffn_host_inputs(xt[b], l1_W)),
        ]:
            for k, v in hm.items():
                if k == "x":
                    continue
                m[pfx + k] = v
        m["x"] = xt[b]
        maps.append(m)
    return maps


_NC_CACHE = {}


def _nc(name, fn, *a, **kw):
    if name not in _NC_CACHE:
        _NC_CACHE[name] = fn(*a, **kw)
    return _NC_CACHE[name]


def kernel(**inputs):
    inp = {k: np.asarray(v) for k, v in inputs.items()}
    cores = list(range(NCORES))
    x = np.ascontiguousarray(inp["L0_attn_x"])
    xt = [tile_x(np.ascontiguousarray(x[b])) for b in cores]

    moe_W = {
        "Wg": inp["L0_ffn_Wg"], "bg": inp["L0_ffn_bg"], "W1": inp["L0_ffn_W1"],
        "b1": inp["L0_ffn_b1"], "W2": inp["L0_ffn_W2"], "b2": inp["L0_ffn_b2"],
    }
    mla_W = {
        "W_DKV": inp["L0_attn_W_DKV"], "W_UK": inp["L0_attn_W_UK"],
        "W_UV": inp["L0_attn_W_UV"], "W_UQ": inp["L0_attn_W_UQ"],
        "W_O": inp["L0_attn_W_O"],
    }
    lora_W = {}
    for nm in ["q", "k", "v", "o"]:
        lora_W["w" + nm] = inp["L1_attn_w" + nm]
        lora_W["a" + nm] = inp["L1_attn_a" + nm]
        lora_W["b" + nm] = inp["L1_attn_b" + nm]
    l1_W = {
        "Wqkv": inp["L1_ffn_Wqkv"], "Wo": inp["L1_ffn_Wo"],
        "W1": inp["L1_ffn_W1"], "W2": inp["L1_ffn_W2"],
        "q_scale": inp["L1_ffn_q_scale"], "k_scale": inp["L1_ffn_k_scale"],
        "ln_g": inp["L1_ffn_ln_g"], "ln_b": inp["L1_ffn_ln_b"],
        "b1": inp["L1_ffn_b1"], "b2": inp["L1_ffn_b2"],
    }

    nc = _nc("fused", build_fused)
    maps = fused_host_maps(xt, moe_W, mla_W, lora_W, l1_W, cores)
    res = run_bass_kernel_spmd(nc, maps, core_ids=cores)
    return np.stack(
        [untile_y(res.results[b]["y"]) for b in cores]
    ).astype(np.float32)
